# version 05
# baseline (speedup 1.0000x reference)
import math
import contextlib
import numpy as np
import concourse.bass as bass
import concourse.mybir as mybir
from concourse.alu_op_type import AluOpType
from concourse.bass_utils import run_bass_kernel_spmd
from concourse import library_config

F32 = mybir.dt.float32
BF16 = mybir.dt.bfloat16
I32 = mybir.dt.int32
U8 = mybir.dt.uint8
AF = mybir.ActivationFunctionType
AX = mybir.AxisListType.X
OP = AluOpType

NCORES = 8
B = 4
SEQ = 4096
TPC = SEQ // NCORES
D = 1024
DCH = D // 128

LN2 = float(np.float32(0.6931471805599453))
INV_LN2 = float(np.float32(1.4426950408889634))
LN2_LO = float(np.float32(np.float64(0.69314718055994530941723) - np.float64(np.float32(0.6931471805599453))))



class Prog:

    def __init__(self, name):
        self.nc = bass.Bass("TRN2", target_bir_lowering=False)
        self.es = contextlib.ExitStack()
        self.block = None
        self.semc = {}
        self.inputs = {}
        self.outputs = {}
        self._sb = {}
        self.pf = ""
        self.rstack = []

    def push_region(self):
        self.rstack.append(contextlib.ExitStack())

    def pop_region(self):
        self.rstack.pop().close()

    def _es(self):
        return self.rstack[-1] if self.rstack else self.es

    def start_block(self):
        self.block = self.es.enter_context(self.nc.Block())

    def sem(self, name):
        if name not in self.semc:
            h = self.es.enter_context(self.nc.semaphore(name))
            self.semc[name] = [h, 0]
        return self.semc[name]

    def inp(self, name, shape, dtype=F32):
        name = self.pf + name
        ap = self.nc.dram_tensor(name, list(shape), dtype, kind="ExternalInput").ap()
        self.inputs[name] = ap
        return ap

    def outp(self, name, shape, dtype=F32):
        name = self.pf + name
        ap = self.nc.dram_tensor(name, list(shape), dtype, kind="ExternalOutput").ap()
        self.outputs[name] = ap
        return ap

    def dram(self, name, shape, dtype=F32):
        return self.nc.dram_tensor(self.pf + name, list(shape), dtype).ap()

    def sb(self, name, shape, dtype=F32):
        name = self.pf + name
        t = self._es().enter_context(self.nc.sbuf_tensor(name, list(shape), dtype))
        self._sb[name] = t
        return t

    def ps(self, name, shape=(128, 512), dtype=F32):
        return self._es().enter_context(
            self.nc.psum_tensor(self.pf + name, list(shape), dtype))

    def close(self):
        self.es.close()


class Stager:

    ENGS = ("tensor", "vector", "scalar", "gpsimd", "sync")
    DSEMS = ("dma_sync", "dma_gpsimd")

    def __init__(self, prog):
        self.p = prog
        for e in self.ENGS:
            prog.sem("st_" + e)
        for d in self.DSEMS:
            prog.sem(d)

    def _engine_section(self, ename, fn, waits):
        p = self.p

        def body(eng):
            for (h, c) in waits:
                if c > 0:
                    eng.wait_ge(h, c)
            n_dma = fn(eng)
            if n_dma is None:
                n_dma = 0
            sem = p.sem("st_" + ename)
            eng.drain().then_inc(sem[0], 1)
            sem[1] += 1
            if n_dma:
                p.sem("dma_" + ename)[1] += 16 * n_dma

        getattr(self.p.block, ename)(body)

    def stage(self, **emits):
        waits = [(self.p.sem("st_" + e)[0], self.p.sem("st_" + e)[1])
                 for e in self.ENGS]
        waits += [(self.p.sem(d)[0], self.p.sem(d)[1]) for d in self.DSEMS]
        for ename, fn in emits.items():
            self._engine_section(ename, fn, waits)


def dma(eng, p, qname, out, in_):
    sem = p.sem("dma_" + qname)
    eng.dma_start(out=out, in_=in_).then_inc(sem[0], 16)
    return 1


def cast_stream(st, p, items, stg0, stg1):
    n = len(items)
    for k in range(n + 1):
        em = {}
        if k < n:
            d_, s_, w_ = items[k]
            stg = (stg0, stg1)[k % 2]

            def ld(e, s_=s_, stg=stg, w_=w_):
                return dma(e, p, "sync", stg[:, 0:w_], s_)
            em["sync"] = ld
        if k > 0:
            d0, s0, w0 = items[k - 1]
            stgp = (stg0, stg1)[(k - 1) % 2]

            def cst(s, d0=d0, stgp=stgp, w0=w0):
                s.activation(out=d0, in_=stgp[:, 0:w0], func=AF.Identity)
            em["scalar"] = cst
        st.stage(**em)



def mm_tiled(t, psum_ap, lhsT_tiles, rhs_tiles, n_cols, start=True):
    n = len(lhsT_tiles)
    for k in range(n):
        t.matmul(psum_ap, lhsT_tiles[k], rhs_tiles[k],
                 start=(start and k == 0), stop=(k == n - 1))



def emit_layernorm(st, env, b, T=TPC, rms=False, nchunks=DCH, x_at=None, xn_at=None,
                   eps_override=None):
    p = st.p
    xs = env["stream"]
    xn = env["xn"]
    sq = env["sq"]
    cone = env["cone"] if nchunks == DCH else env["cone2"]
    onesrow = env["onesrow"]
    srow = env["srow"]
    ps_stat = env["ps_stat"]
    ps_b0 = env["ps_b0"]
    ps_b1 = env["ps_b1"]
    nf = nchunks * 128
    eps = 1e-5 if not rms else 1e-5

    def xat(ch):
        if x_at is not None:
            return x_at(ch)
        return xs[:, (b * DCH + ch) * TPC:(b * DCH + ch) * TPC + T]

    def xnat(ch):
        if xn_at is not None:
            return xn_at(ch)
        return xn[:, (b * DCH + ch) * TPC:(b * DCH + ch) * TPC + T]

    def v_cast(v):
        for ch in range(nchunks):
            v.tensor_copy(out=xnat(ch), in_=xat(ch))
    st.stage(vector=v_cast)

    sqw = env["sqw"]

    def s_sq(s):
        for ch in range(nchunks):
            s.activation(out=sqw[:, ch * T:(ch + 1) * T], in_=xnat(ch), func=AF.Square)
    st.stage(scalar=s_sq)

    def t_stats(t):
        for ch in range(nchunks):
            t.matmul(ps_stat[0:1, 0:T], cone[:, 0:1], xnat(ch),
                     start=(ch == 0), stop=(ch == nchunks - 1))
        for ch in range(nchunks):
            t.matmul(ps_b0[0:1, 0:T], cone[:, 0:1], sqw[:, ch * T:(ch + 1) * T],
                     start=(ch == 0), stop=(ch == nchunks - 1))
    st.stage(tensor=t_stats)

    r_mean = srow[0:1, 0:T]
    r_var = srow[0:1, T:2 * T]
    r_rstd = srow[0:1, 2 * T:3 * T]
    r_negm = srow[0:1, 3 * T:4 * T]

    if not rms:
        def v_row1(v):
            v.tensor_copy(out=r_mean, in_=ps_stat[0:1, 0:T])
        st.stage(vector=v_row1)

        def s_row2(s):
            s.activation(out=r_var, in_=r_mean, func=AF.Square)
        st.stage(scalar=s_row2)

        def v_row3(v):
            v.tensor_tensor(out=r_var, in0=ps_b0[0:1, 0:T], in1=r_var, op=OP.subtract)
        st.stage(vector=v_row3)
    else:
        def v_row3b(v):
            v.tensor_copy(out=r_var, in_=ps_b0[0:1, 0:T])
        st.stage(vector=v_row3b)

    eps_col = env["eps_col"] if eps_override is None else env["eps_col2"]

    def s_row4(s):
        s.activation(out=r_rstd, in_=r_var, func=AF.Ln, bias=eps_col[0:1, 0:1])
        s.drain()
        s.activation(out=r_rstd, in_=r_rstd, func=AF.Exp, scale=-0.5)
    st.stage(scalar=s_row4)

    if not rms:
        def v_row5(v):
            v.scalar_tensor_tensor(out=r_negm, in0=r_mean, scalar=-1.0, in1=r_rstd,
                                   op0=OP.mult, op1=OP.mult)
        st.stage(vector=v_row5)

    def t_bc(t):
        t.matmul(ps_b0[:, 0:T], onesrow[0:1, :], r_rstd, start=True, stop=True)
        if not rms:
            t.matmul(ps_b1[:, 0:T], onesrow[0:1, :], r_negm, start=True, stop=True)
    st.stage(tensor=t_bc)

    tmp = env["lnt"]

    slots = max(1, tmp.shape[1] // T) if not rms else nchunks

    def v_apply(v):
        if rms:
            for ch in range(nchunks):
                v.tensor_tensor(out=xnat(ch), in0=xat(ch), in1=ps_b0[:, 0:T], op=OP.mult)
        else:
            for g0 in range(0, nchunks, slots):
                grp = range(g0, min(g0 + slots, nchunks))
                for ch in grp:
                    v.tensor_tensor(out=tmp[:, (ch - g0) * T:(ch - g0 + 1) * T],
                                    in0=xat(ch), in1=ps_b0[:, 0:T], op=OP.mult)
                v.drain()
                for ch in grp:
                    v.tensor_tensor(out=xnat(ch), in0=tmp[:, (ch - g0) * T:(ch - g0 + 1) * T],
                                    in1=ps_b1[:, 0:T], op=OP.add)
                v.drain()
    st.stage(vector=v_apply)



def emit_l0ffn(p, st, xs_d, xo_d):
    p.pf = "f0_"
    nc = p.nc
    w_names = ["wf", "uf", "wi", "ui", "wc", "uc", "wo", "uo"]
    wd = {n: p.inp("wT_" + n, (D, D)) for n in w_names}
    bd = {n: p.inp("b_" + n, (128, DCH)) for n in ["bf", "bi", "bc", "bo"]}
    inc_d = p.inp("in_c", (128, DCH))

    sbx = p.sb("sbx", (128, DCH * TPC), F32)
    xn = p.sb("xn", (128, B * DCH * TPC), BF16)
    wsb = {n: p.sb("w_" + n, (128, DCH * D), BF16)
           for n in ["wf", "uf", "wi", "ui", "wc", "uc"]}
    wsb["wo"] = wsb["wf"]
    wsb["uo"] = wsb["uf"]
    bias = p.sb("bias", (128, 5 * DCH), F32)
    cstate = p.sb("cstate", (128, DCH * TPC), F32)
    cbf = p.sb("cbf", (128, DCH * TPC), BF16)
    gates = p.sb("gates", (128, 3 * TPC), BF16)
    sqw = p.sb("sqw", (128, DCH * TPC), BF16)
    srow = p.sb("srow", (1, 4 * TPC), F32)
    lnt = p.sb("lnt", (128, DCH * TPC), F32)
    cone = p.sb("cone", (128, 1), BF16)
    onesrow = p.sb("onesrow", (1, 128), F32)
    onescol = p.sb("onescol", (128, TPC), F32)
    htile = p.sb("htile", (128, TPC), F32)

    ps_stat = p.ps("ps_stat")
    ps_b0 = p.ps("ps_b0")
    ps_b1 = p.ps("ps_b1")
    ps_g = [p.ps("ps_g%d" % i) for i in range(3)]
    ps_h = p.ps("ps_h")

    eps_col = p.sb("eps_col", (128, 1), F32)
    env = dict(stream=None, xn=xn, sq=None, sqw=sqw, srow=srow, lnt=lnt,
               cone=cone, onesrow=onesrow, ps_stat=ps_stat, ps_b0=ps_b0,
               ps_b1=ps_b1, eps_col=eps_col)


    def ld_sync(e):
        n = 0
        for gi, bn in enumerate(["bf", "bi", "bc", "bo"]):
            n += dma(e, p, "sync", bias[:, gi * DCH:(gi + 1) * DCH], bd[bn][:, :])
        n += dma(e, p, "sync", bias[:, 4 * DCH:5 * DCH], inc_d[:, :])
        return n

    def ld_gp(e):
        n = 0
        for wn in ["wf", "uf", "wi", "ui", "wc", "uc"]:
            n += dma(e, p, "gpsimd",
                     wsb[wn][:, :].rearrange("q (ic o) -> q ic o", ic=DCH),
                     wd[wn].rearrange("(ic q) o -> q ic o", q=128))
        return n

    def init_v(v):
        v.memset(cone[:, :], 1.0 / 1024.0)
        v.memset(onesrow[:, :], 1.0)
        v.memset(onescol[:, :], 1.0)
        v.memset(eps_col[:, :], 1e-5)
    st.stage(sync=ld_sync, gpsimd=ld_gp, vector=init_v)

    for b in range(B):
        def ld_b(e, b=b):
            return dma(e, p, "sync",
                       sbx[:, :], xs_d[b])
        st.stage(sync=ld_b)
        emit_layernorm(st, env, b,
                       x_at=lambda ch: sbx[:, ch * TPC:(ch + 1) * TPC])

    def v_c0(v):
        for ch in range(DCH):
            v.tensor_scalar(out=cstate[:, ch * TPC:(ch + 1) * TPC],
                            in0=onescol[:, :],
                            scalar1=bias[:, 4 * DCH + ch:4 * DCH + ch + 1],
                            scalar2=None, op0=OP.mult, op1=OP.bypass)
    st.stage(vector=v_c0)

    def wtile(wn, ic, oc):
        return wsb[wn][:, ic * D + oc * 128: ic * D + oc * 128 + 128]

    def xn_at(t_step, ch):
        o = (t_step * DCH + ch) * TPC
        return xn[:, o:o + TPC]

    for t_step in range(B):
        def v_castc(v):
            for ch in range(DCH):
                v.tensor_copy(out=cbf[:, ch * TPC:(ch + 1) * TPC],
                              in_=cstate[:, ch * TPC:(ch + 1) * TPC])
        st.stage(vector=v_castc)

        for oc in range(DCH):
            def t_mm(t, oc=oc, t_step=t_step):
                for gi, (wn, un) in enumerate([("wf", "uf"), ("wi", "ui"), ("wc", "uc")]):
                    for ic in range(DCH):
                        t.matmul(ps_g[gi][:, 0:TPC], wtile(wn, ic, oc),
                                 xn_at(t_step, ic), start=(ic == 0), stop=False)
                    for ic in range(DCH):
                        t.matmul(ps_g[gi][:, 0:TPC], wtile(un, ic, oc),
                                 cbf[:, ic * TPC:(ic + 1) * TPC],
                                 start=False, stop=(ic == DCH - 1))

            def s_evac(s, oc=oc):
                for gi, fn in [(0, AF.Sigmoid), (1, AF.Sigmoid), (2, AF.Tanh)]:
                    s.activation(out=gates[:, gi * TPC:(gi + 1) * TPC],
                                 in_=ps_g[gi][:, 0:TPC], func=fn,
                                 bias=bias[:, gi * DCH + oc:gi * DCH + oc + 1])

            def v_update(v, oc=oc):
                cs = cstate[:, oc * TPC:(oc + 1) * TPC]
                f_ = gates[:, 0:TPC]
                i_ = gates[:, TPC:2 * TPC]
                g_ = gates[:, 2 * TPC:3 * TPC]
                v.tensor_tensor(out=cs, in0=cs, in1=f_, op=OP.mult)
                v.tensor_tensor(out=lnt[:, 0:TPC], in0=i_, in1=g_, op=OP.mult)
                v.drain()
                v.tensor_tensor(out=cs, in0=cs, in1=lnt[:, 0:TPC], op=OP.add)
                v.drain()
            st.stage(tensor=t_mm)
            st.stage(scalar=s_evac)
            st.stage(vector=v_update)

    def v_castc2(v):
        for ch in range(DCH):
            v.tensor_copy(out=cbf[:, ch * TPC:(ch + 1) * TPC],
                          in_=cstate[:, ch * TPC:(ch + 1) * TPC])

    def ld_wo(e):
        n = 0
        for wn in ["wo", "uo"]:
            n += dma(e, p, "gpsimd",
                     wsb[wn][:, :].rearrange("q (ic o) -> q ic o", ic=DCH),
                     wd[wn].rearrange("(ic q) o -> q ic o", q=128))
        return n
    st.stage(vector=v_castc2, gpsimd=ld_wo)

    for t_step in range(B):
        def ld_b2(e, t_step=t_step):
            return dma(e, p, "sync",
                       sbx[:, :], xs_d[t_step])
        st.stage(sync=ld_b2)
        for oc in range(DCH):
            def t_mmh(t, oc=oc, t_step=t_step):
                for ic in range(DCH):
                    t.matmul(ps_h[:, 0:TPC], wtile("wo", ic, oc),
                             xn_at(t_step, ic), start=(ic == 0), stop=False)
                for ic in range(DCH):
                    t.matmul(ps_h[:, 0:TPC], wtile("uo", ic, oc),
                             cbf[:, ic * TPC:(ic + 1) * TPC],
                             start=False, stop=(ic == DCH - 1))

            def s_evh(s, oc=oc):
                s.activation(out=htile[:, 0:TPC], in_=ps_h[:, 0:TPC], func=AF.Tanh,
                             bias=bias[:, 3 * DCH + oc:3 * DCH + oc + 1])

            def v_res(v, oc=oc):
                sl = sbx[:, oc * TPC:(oc + 1) * TPC]
                v.tensor_tensor(out=sl, in0=sl, in1=htile[:, 0:TPC], op=OP.add)
            st.stage(tensor=t_mmh)
            st.stage(scalar=s_evh)
            st.stage(vector=v_res)

        def st_b(e, t_step=t_step):
            return dma(e, p, "sync",
                       xo_d[t_step], sbx[:, :])
        st.stage(sync=st_b)





NH = 16
HD = 64
KCH = SEQ // 128



def build_l0ffn():
    p = Prog("l0ffn")
    p.pf = "f0_"
    xs_d = p.inp("xs", (B, 128, DCH * TPC))
    xo_d = p.outp("xo", (B, 128, DCH * TPC))
    p.start_block()
    st = Stager(p)
    emit_l0ffn(p, st, xs_d, xo_d)
    st.stage(sync=lambda e: None)
    return p

def emit_attn(p, st, layer, xs_d, xo_d):
    causal = (layer == 1)
    p.pf = "a%d_" % layer
    wqT = p.inp("wqT", (D, D))
    wkT = p.inp("wkT", (D, D))
    wvT = p.inp("wvT", (D, D))
    woT = p.inp("woT", (D, D))
    wob = p.inp("wob", (128, DCH))
    cosd = p.inp("cosT", (512, TPC))
    sind = p.inp("sinT", (512, TPC))
    if causal:
        mask_d = p.inp("maskT", (KCH, 128, TPC), BF16)

    kv_loc = p.dram("kv_loc", (2, B, 8, 128, TPC), BF16)
    kv_all = p.dram("kv_all", (NCORES, 2, B, 8, 128, TPC), BF16)
    bnd_loc = p.dram("bnd_loc", (NH, B), F32)
    bnd_all = p.dram("bnd_all", (NH, B), F32)
    brow_d = p.dram("brow_d", (B, NH), F32)

    sbx = p.sb("sbx", (128, DCH * TPC), F32)
    xn = p.sb("xn", (128, B * DCH * TPC), BF16)
    pre = p.sb("pre", (128, DCH * TPC), F32)
    qr = p.sb("qr", (128, B * DCH * TPC), BF16)
    kr = p.sb("kr", (128, DCH * TPC), BF16)
    kr2 = p.sb("kr2", (128, 4 * NCORES * 130), BF16)
    wbuf = p.sb("wbuf", (128, DCH * D), BF16)
    cos_s = p.sb("cos_s", (128, 4 * TPC), F32)
    sin_s = p.sb("sin_s", (128, 4 * TPC), F32)
    sqw = p.sb("sqw", (128, DCH * TPC), BF16)
    srow = p.sb("srow", (1, 4 * TPC), F32)
    lnt = p.sb("lnt", (128, DCH * TPC), F32)
    cone = p.sb("cone", (128, 1), BF16)
    onesrow = p.sb("onesrow", (1, 128), F32)
    eps_col = p.sb("eps_col", (128, 1), F32)
    e16 = p.sb("e16", (128, DCH * NH), BF16)
    idt = p.sb("idt", (128, 128), BF16)
    idt32 = p.sb("idt32", (128, 128), F32)
    iwork = p.sb("iwork", (128, 128), I32)
    perm = p.sb("perm", (128, 4 * 128), BF16)
    mq = p.sb("mq", (128, B), F32)
    mkl = p.sb("mkl", (16, B), F32)
    mkg = p.sb("mkg", (16, B), F32)
    bias_bh = p.sb("bias_bh", (128, NH * B), F32)
    brow = p.sb("brow", (1, NH * B), F32)
    bsml = p.sb("bsml", (16, B), F32)
    kbuf = kr
    vbuf = kr2
    pk = sqw
    rbc = p.sb("rbc", (128, TPC), F32)
    denrow = p.sb("denrow", (128, 4 * TPC), F32)
    ones128 = p.sb("ones128", (128, 128), F32)
    wob_s = p.sb("wob_s", (128, DCH), F32)
    if causal:
        mbuf = p.sb("mbuf", (128, 2 * TPC), BF16)

    actx = xn
    ps_s = [p.ps("ps_s%d" % i) for i in range(4)]
    ps_o = [p.ps("ps_o%d" % i) for i in range(2)]
    ps_x = p.ps("ps_x")
    ps_t = p.ps("ps_t", (128, 128), BF16)
    ps_stat = ps_s[0]
    ps_b0 = ps_s[1]
    ps_b1 = ps_s[2]

    env = dict(stream=None, xn=xn, sq=None, sqw=sqw, srow=srow, lnt=lnt,
               cone=cone, onesrow=onesrow, ps_stat=ps_stat, ps_b0=ps_b0,
               ps_b1=ps_b1, eps_col=eps_col)


    def ld_sync(e):
        n = 0
        n += dma(e, p, "sync", wob_s[:, :], wob[:, :])
        for ch in range(4):
            n += dma(e, p, "sync", cos_s[:, ch * TPC:(ch + 1) * TPC],
                     cosd[ch * 128:(ch + 1) * 128, :])
            n += dma(e, p, "sync", sin_s[:, ch * TPC:(ch + 1) * TPC],
                     sind[ch * 128:(ch + 1) * 128, :])
        return n

    def init_v(v):
        v.memset(cone[:, :], 1.0 / 1024.0)
        v.memset(onesrow[:, :], 1.0)
        v.memset(ones128[:, :], 1.0)
        v.memset(eps_col[:, :], 1e-5)
        v.memset(e16[:, :], 0.0)
        v.drain()
        for ch in range(DCH):
            v.memset(e16[0:64, ch * NH + 2 * ch: ch * NH + 2 * ch + 1], 1.0)
            v.memset(e16[64:128, ch * NH + 2 * ch + 1: ch * NH + 2 * ch + 2], 1.0)

    def init_g(g):
        g.iota(iwork[:, :], pattern=[[1, 128]], base=0, channel_multiplier=-1)
    st.stage(sync=ld_sync, vector=init_v, gpsimd=init_g)

    def v_idt(v):
        v.tensor_scalar(out=idt[:, :], in0=iwork[:, :], scalar1=0.0, scalar2=None,
                        op0=OP.is_equal, op1=OP.bypass)
        v.tensor_scalar(out=idt32[:, :], in0=iwork[:, :], scalar1=0.0, scalar2=None,
                        op0=OP.is_equal, op1=OP.bypass)
        if causal:
            v.tensor_scalar(out=perm[:, 0:128], in0=iwork[:, :], scalar1=0.0,
                            scalar2=None, op0=OP.is_equal, op1=OP.bypass)
            v.tensor_scalar(out=perm[:, 128:256], in0=iwork[:, :], scalar1=-64.0,
                            scalar2=None, op0=OP.is_equal, op1=OP.bypass)
            v.tensor_scalar(out=perm[:, 256:384], in0=iwork[:, :], scalar1=64.0,
                            scalar2=None, op0=OP.is_equal, op1=OP.bypass)
            v.tensor_scalar(out=perm[:, 384:512], in0=iwork[:, :], scalar1=0.0,
                            scalar2=None, op0=OP.is_equal, op1=OP.bypass)
            v.drain()
            v.memset(perm[64:128, 0:128], 0.0)
            v.memset(perm[0:64, 384:512], 0.0)
    st.stage(vector=v_idt)

    def wtile(ic, oc):
        return wbuf[:, ic * D + oc * 128: ic * D + oc * 128 + 128]

    def xn_at(b, ch):
        o = (b * DCH + ch) * TPC
        return xn[:, o:o + TPC]

    for b in range(B):
        def ld_b(e, b=b):
            return dma(e, p, "sync",
                       sbx[:, :], xs_d[b])
        st.stage(sync=ld_b)
        emit_layernorm(st, env, b,
                       x_at=lambda ch: sbx[:, ch * TPC:(ch + 1) * TPC],
                       xn_at=lambda ch, b=b: xn_at(b, ch))

    def rope_into(v, dst_at):
        for r in range(4):
            qe = pre[:, r * TPC:(r + 1) * TPC]
            qo = pre[:, (4 + r) * TPC:(5 + r) * TPC]
            cs = cos_s[:, r * TPC:(r + 1) * TPC]
            sn = sin_s[:, r * TPC:(r + 1) * TPC]
            t1 = lnt[:, 0:TPC]
            t2 = lnt[:, TPC:2 * TPC]
            v.tensor_tensor(out=t1, in0=qe, in1=cs, op=OP.mult)
            v.tensor_tensor(out=t2, in0=qo, in1=sn, op=OP.mult)
            v.drain()
            v.tensor_tensor(out=dst_at(r), in0=t1, in1=t2, op=OP.subtract)
            v.drain()
            v.tensor_tensor(out=t1, in0=qe, in1=sn, op=OP.mult)
            v.tensor_tensor(out=t2, in0=qo, in1=cs, op=OP.mult)
            v.drain()
            v.tensor_tensor(out=dst_at(4 + r), in0=t1, in1=t2, op=OP.add)
            v.drain()

    def emit_proj(dest_fn, b):
        for oc in range(DCH):
            def t_p(t, oc=oc, b=b):
                for ic in range(DCH):
                    t.matmul(ps_x[:, 0:TPC], wtile(ic, oc), xn_at(b, ic),
                             start=(ic == 0), stop=(ic == DCH - 1))

            def v_e(v, oc=oc):
                dest_fn(v, oc)
            st.stage(tensor=t_p)
            st.stage(vector=v_e)

    def emit_regroup(srcbuf, dst_at):
        for hp in range(8):
            a_ch = hp // 2
            lo = hp % 2

            def t_rg(t, a_ch=a_ch, lo=lo, hp=hp):
                selA = perm[:, 0:128] if lo == 0 else perm[:, 128:256]
                selB = perm[:, 256:384] if lo == 0 else perm[:, 384:512]
                t.matmul(ps_x[:, 0:TPC], selA,
                         srcbuf[:, a_ch * TPC:(a_ch + 1) * TPC],
                         start=True, stop=False)
                t.matmul(ps_x[:, 0:TPC], selB,
                         srcbuf[:, (4 + a_ch) * TPC:(5 + a_ch) * TPC],
                         start=False, stop=True)

            def v_rg(v, hp=hp):
                v.tensor_copy(out=dst_at(hp), in_=ps_x[:, 0:TPC])
            st.stage(tensor=t_rg)
            st.stage(vector=v_rg)

    def emit_headnorm(srcat, dst, b):
        def s_sq(s):
            for ch in range(DCH):
                s.activation(out=sqw[:, ch * TPC:(ch + 1) * TPC], in_=srcat(ch),
                             func=AF.Square)
        st.stage(scalar=s_sq)

        def t_hs(t):
            for ch in range(DCH):
                t.matmul(ps_x[0:NH, 0:TPC], e16[:, ch * NH:(ch + 1) * NH],
                         sqw[:, ch * TPC:(ch + 1) * TPC],
                         start=(ch == 0), stop=(ch == DCH - 1))
        st.stage(tensor=t_hs)

        def v_m(v, b=b):
            v.tensor_reduce(out=dst[0:NH, b:b + 1], in_=ps_x[0:NH, 0:TPC],
                            axis=AX, op=OP.max)
        st.stage(vector=v_m)

    def ld_wq(e):
        return dma(e, p, "gpsimd",
                   wbuf[:, :].rearrange("q (ic o) -> q ic o", ic=DCH),
                   wqT.rearrange("(ic q) o -> q ic o", q=128))
    st.stage(gpsimd=ld_wq)
    for b in range(B):
        emit_proj(lambda v, oc: v.tensor_copy(out=pre[:, oc * TPC:(oc + 1) * TPC],
                                              in_=ps_x[:, 0:TPC]), b)
        if causal:
            def v_rq(v):
                rope_into(v, lambda r: kr2[:, r * TPC:(r + 1) * TPC])
            st.stage(vector=v_rq)
            emit_regroup(kr2, lambda hp, b=b: qr[:, (b * DCH + hp) * TPC:
                                                 (b * DCH + hp) * TPC + TPC])
        else:
            def v_rq(v, b=b):
                rope_into(v, lambda r, b=b: qr[:, (b * DCH + r) * TPC:
                                               (b * DCH + r) * TPC + TPC])
            st.stage(vector=v_rq)
        emit_headnorm(lambda ch, b=b: qr[:, (b * DCH + ch) * TPC:
                                         (b * DCH + ch) * TPC + TPC], mq, b)

    def ld_wk(e):
        return dma(e, p, "gpsimd",
                   wbuf[:, :].rearrange("q (ic o) -> q ic o", ic=DCH),
                   wkT.rearrange("(ic q) o -> q ic o", q=128))
    st.stage(gpsimd=ld_wk)
    for b in range(B):
        emit_proj(lambda v, oc: v.tensor_copy(out=pre[:, oc * TPC:(oc + 1) * TPC],
                                              in_=ps_x[:, 0:TPC]), b)
        if causal:
            def v_rk(v):
                rope_into(v, lambda r: kr2[:, r * TPC:(r + 1) * TPC])
            st.stage(vector=v_rk)
            emit_regroup(kr2, lambda hp: kr[:, hp * TPC:(hp + 1) * TPC])
        else:
            def v_rk(v):
                rope_into(v, lambda r: kr[:, r * TPC:(r + 1) * TPC])
            st.stage(vector=v_rk)
        emit_headnorm(lambda ch: kr[:, ch * TPC:(ch + 1) * TPC], mkl, b)

        def st_k(e, b=b):
            n = 0
            for hp in range(8):
                n += dma(e, p, "sync", kv_loc[0, b, hp],
                         kr[:, hp * TPC:(hp + 1) * TPC])
            return n
        st.stage(sync=st_k)

    def ld_wv(e):
        return dma(e, p, "gpsimd",
                   wbuf[:, :].rearrange("q (ic o) -> q ic o", ic=DCH),
                   wvT.rearrange("(ic q) o -> q ic o", q=128))
    st.stage(gpsimd=ld_wv)
    for b in range(B):
        emit_proj(lambda v, oc: v.tensor_copy(out=kr[:, oc * TPC:(oc + 1) * TPC],
                                              in_=ps_x[:, 0:TPC]), b)
        for ch in range(DCH):
            for tq in range(4):
                def t_tr(t, ch=ch, tq=tq):
                    t.transpose(ps_t[:, 0:128],
                                kr[:, ch * TPC + tq * 128: ch * TPC + tq * 128 + 128],
                                idt[:, :])

                def v_tr(v):
                    v.tensor_copy(out=sqw[:, 0:128], in_=ps_t[:, 0:128])

                def st_v(e, ch=ch, tq=tq, b=b):
                    dst = kv_loc[1, b, ch].rearrange("e t -> (e t)")
                    dst = dst.rearrange("(a q f) -> q a f", a=4, q=128)[:, tq, :]
                    return dma(e, p, "sync", dst, sqw[:, 0:128])
                st.stage(tensor=t_tr)
                st.stage(vector=v_tr)
                st.stage(sync=st_v)

    def st_bnd(e):
        return dma(e, p, "sync", bnd_loc[:, :], mkl[0:16, 0:B])
    st.stage(sync=st_bnd)

    def g_cc(g):
        cs = p.sem("cc")
        g.collective_compute("AllGather", OP.bypass,
                             replica_groups=[list(range(NCORES))],
                             ins=[kv_loc[:, :, :, :, :].opt()],
                             outs=[kv_all[:, :, :, :, :, :].opt()]).then_inc(cs[0], 1)
        cs[1] += 1
        g.collective_compute("AllReduce", OP.max,
                             replica_groups=[list(range(NCORES))],
                             ins=[bnd_loc[:, :].opt()],
                             outs=[bnd_all[:, :].opt()]).then_inc(cs[0], 1)
        cs[1] += 1
        g.wait_ge(cs[0], cs[1])
    st.stage(gpsimd=g_cc)

    def ld_mkg(e):
        return dma(e, p, "sync", mkg[0:16, 0:B], bnd_all[:, :])
    st.stage(sync=ld_mkg)

    scl = 1.0 / math.sqrt(HD)

    def s_bnd(s):
        s.activation(out=mq[0:NH, 0:B], in_=mq[0:NH, 0:B], func=AF.Sqrt)
        s.drain()
        s.activation(out=mkg[0:NH, 0:B], in_=mkg[0:NH, 0:B], func=AF.Sqrt)
    st.stage(scalar=s_bnd)

    def v_bnd(v):
        v.tensor_tensor(out=bsml[0:NH, 0:B], in0=mq[0:NH, 0:B], in1=mkg[0:NH, 0:B],
                        op=OP.mult)
        v.drain()
        v.tensor_scalar(out=bsml[0:NH, 0:B], in0=bsml[0:NH, 0:B], scalar1=-scl,
                        scalar2=None, op0=OP.mult, op1=OP.bypass)
    st.stage(vector=v_bnd)

    def t_bt(t):
        t.transpose(ps_x[0:B, 0:NH], bsml[0:NH, 0:B], idt32[0:NH, 0:NH])
    st.stage(tensor=t_bt)

    def v_bt(v):
        v.tensor_copy(out=lnt[0:B, 0:NH], in_=ps_x[0:B, 0:NH])
    st.stage(vector=v_bt)

    def st_br(e):
        return dma(e, p, "sync", brow_d[:, :], lnt[0:B, 0:NH])
    st.stage(sync=st_br)

    def ld_br(e):
        return dma(e, p, "sync", brow[0:1, :], brow_d.rearrange("b h -> (b h)"))
    st.stage(sync=ld_br)

    def t_pb(t):
        t.matmul(ps_x[:, 0:NH * B], onesrow[0:1, 0:128], brow[0:1, :],
                 start=True, stop=True)
    st.stage(tensor=t_pb)

    def v_pb(v):
        v.tensor_copy(out=bias_bh[:, :], in_=ps_x[:, 0:NH * B])
    st.stage(vector=v_pb)

    def v_vones(v):
        for z in range(4 * NCORES):
            for hi in range(2):
                v.memset(vbuf[:, z * 130 + hi * 65 + 64: z * 130 + hi * 65 + 65], 1.0)
    st.stage(vector=v_vones)

    qk_sem = p.sem("qk")
    ex_sem = p.sem("ex")
    exs = p.sem("exs")

    for b in range(B):
        def ld_strm(e, b=b):
            return dma(e, p, "sync",
                       sbx[:, :], xs_d[b])
        st.stage(sync=ld_strm)

        for hp in range(8):
            h0 = 2 * hp
            is_c = causal and hp < 4

            def d_kv(e, b=b, hp=hp):
                n = 0
                for cpr in range(NCORES):
                    n += dma(e, p, "sync",
                             kbuf[:, cpr * TPC:(cpr + 1) * TPC], kv_all[cpr, 0, b, hp])
                    srcv = kv_all[cpr, 1, b, hp].rearrange("e t -> (e t)")
                    srcv = srcv.rearrange("(a q h e) -> q a h e", q=128, h=2, e=64)
                    for hi in range(2):
                        dst = vbuf[:, 0:32 * 130].rearrange("q (z s) -> q z s", s=130)[
                            :, cpr * 4:(cpr + 1) * 4, hi * 65:hi * 65 + 64]
                        n += dma(e, p, "sync", dst, srcv[:, :, hi, :])
                return n
            st.stage(sync=d_kv)

            qk_base = qk_sem[1]
            ex_base = ex_sem[1]
            exs_base = exs[1]

            def t_att(t, b=b, hp=hp, qk_base=qk_base, ex_base=ex_base):
                for kc in range(KCH):
                    par = kc % 2
                    if kc >= 2:
                        t.wait_ge(ex_sem[0], ex_base + (kc - 1))
                    for hi in range(2):
                        mm = t.matmul(ps_s[hi * 2 + par][:, 0:TPC],
                                 kbuf[hi * 64:(hi + 1) * 64, kc * 128:kc * 128 + 128],
                                 qr[hi * 64:(hi + 1) * 64,
                                    (b * DCH + hp) * TPC:(b * DCH + hp) * TPC + TPC],
                                 start=True, stop=True)
                        if hi == 1:
                            mm.then_inc(qk_sem[0], 1)
                    if kc >= 1:
                        t.wait_ge(ex_sem[0], ex_base + kc)
                        pb = (kc - 1) % 2
                        for hi in range(2):
                            t.matmul(ps_o[hi][0:65, 0:TPC],
                                     vbuf[:, (kc - 1) * 130 + hi * 65:
                                          (kc - 1) * 130 + hi * 65 + 65],
                                     pk[:, (hi * 2 + pb) * TPC:(hi * 2 + pb) * TPC + TPC],
                                     start=(kc == 1), stop=False)
                t.wait_ge(ex_sem[0], ex_base + KCH)
                pb = (KCH - 1) % 2
                for hi in range(2):
                    mm = t.matmul(ps_o[hi][0:65, 0:TPC],
                             vbuf[:, (KCH - 1) * 130 + hi * 65:
                                  (KCH - 1) * 130 + hi * 65 + 65],
                             pk[:, (hi * 2 + pb) * TPC:(hi * 2 + pb) * TPC + TPC],
                             start=False, stop=True)
                    if hi == 1:
                        mm.then_inc(qk_sem[0], 1)

            def s_exp(s, b=b, hp=hp, qk_base=qk_base, is_c=is_c):
                tgt = exs if is_c else ex_sem
                for kc in range(KCH):
                    par = kc % 2
                    s.wait_ge(qk_sem[0], qk_base + kc + 1)
                    for hi in range(2):
                        act = s.activation(
                            out=pk[:, (hi * 2 + par) * TPC:
                                   (hi * 2 + par) * TPC + TPC],
                            in_=ps_s[hi * 2 + par][:, 0:TPC], func=AF.Exp,
                            scale=scl,
                            bias=bias_bh[:, b * NH + 2 * hp + hi:
                                         b * NH + 2 * hp + hi + 1])
                        if hi == 1:
                            act.then_inc(tgt[0], 1)

            if not is_c:
                qk_sem[1] += KCH + 1
                ex_sem[1] += KCH
                st.stage(tensor=t_att, scalar=s_exp)
            else:
                mk_base = p.sem("dma_sync")[1]

                def d_mask(e, ex_base=ex_base):
                    n = 0
                    for kc in range(KCH):
                        if kc >= 2:
                            e.wait_ge(ex_sem[0], ex_base + (kc - 1))
                        n += dma(e, p, "sync",
                                 mbuf[:, (kc % 2) * TPC:(kc % 2) * TPC + TPC],
                                 mask_d[kc])
                    return n

                def v_mask(v, exs_base=exs_base, ex_base=ex_base, mk_base=mk_base):
                    for kc in range(KCH):
                        par = kc % 2
                        v.wait_ge(exs[0], exs_base + kc + 1)
                        v.wait_ge(p.sem("dma_sync")[0], mk_base + 16 * (kc + 1))
                        for hi in range(2):
                            sl = pk[:, (hi * 2 + par) * TPC:(hi * 2 + par) * TPC + TPC]
                            tt = v.tensor_tensor(out=sl, in0=sl,
                                                 in1=mbuf[:, par * TPC:(par + 1) * TPC],
                                                 op=OP.mult)
                            if hi == 1:
                                tt.then_inc(ex_sem[0], 1)

                qk_sem[1] += KCH + 1
                exs[1] += KCH
                ex_sem[1] += KCH
                st.stage(tensor=t_att, scalar=s_exp, sync=d_mask, vector=v_mask)

            def v_rec(v):
                v.tensor_copy(out=denrow[64:65, 2 * TPC:3 * TPC],
                              in_=ps_o[0][64:65, 0:TPC])
                v.tensor_copy(out=denrow[64:65, 3 * TPC:4 * TPC],
                              in_=ps_o[1][64:65, 0:TPC])
                v.drain()
                v.reciprocal(out=denrow[64:65, 0:TPC],
                             in_=denrow[64:65, 2 * TPC:3 * TPC])
                v.reciprocal(out=denrow[64:65, TPC:2 * TPC],
                             in_=denrow[64:65, 3 * TPC:4 * TPC])
            st.stage(vector=v_rec)

            def t_rb(t):
                t.matmul(ps_s[0][0:64, 0:TPC], ones128[64:65, 0:64],
                         denrow[64:65, 0:TPC], start=True, stop=True)
                t.matmul(ps_s[1][0:64, 0:TPC], ones128[64:65, 0:64],
                         denrow[64:65, TPC:2 * TPC], start=True, stop=True)
            st.stage(tensor=t_rb)

            def v_rbc(v):
                v.tensor_copy(out=rbc[0:64, 0:TPC], in_=ps_s[0][0:64, 0:TPC])
                v.tensor_copy(out=rbc[64:128, 0:TPC], in_=ps_s[1][0:64, 0:TPC])
            st.stage(vector=v_rbc)

            def v_ctx(v, hp=hp):
                v.tensor_tensor(out=actx[0:64, hp * TPC:hp * TPC + TPC],
                                in0=ps_o[0][0:64, 0:TPC], in1=rbc[0:64, 0:TPC],
                                op=OP.mult)
                v.tensor_tensor(out=actx[64:128, hp * TPC:hp * TPC + TPC],
                                in0=ps_o[1][0:64, 0:TPC], in1=rbc[64:128, 0:TPC],
                                op=OP.mult)
            st.stage(vector=v_ctx)

        if b == 0:
            def ld_wo(e):
                return dma(e, p, "gpsimd",
                           wbuf[:, :].rearrange("q (ic o) -> q ic o", ic=DCH),
                           woT.rearrange("(ic q) o -> q ic o", q=128))
            st.stage(gpsimd=ld_wo)

        for oc in range(DCH):
            def t_o(t, oc=oc):
                for ic in range(DCH):
                    t.matmul(ps_x[:, 0:TPC], wtile(ic, oc),
                             actx[:, ic * TPC:(ic + 1) * TPC],
                             start=(ic == 0), stop=(ic == DCH - 1))

            def v_o(v, oc=oc):
                sl = sbx[:, oc * TPC:(oc + 1) * TPC]
                v.scalar_tensor_tensor(out=sl, in0=ps_x[:, 0:TPC],
                                       scalar=wob_s[:, oc:oc + 1], in1=sl,
                                       op0=OP.add, op1=OP.add)
            st.stage(tensor=t_o)
            st.stage(vector=v_o)

        def st_b(e, b=b):
            return dma(e, p, "sync",
                       xo_d[b], sbx[:, :])
        st.stage(sync=st_b)




def build_attn(layer):
    p = Prog("attn%d" % layer)
    p.pf = "a%d_" % layer
    xs_d = p.inp("xs", (B, 128, DCH * TPC))
    xo_d = p.outp("xo", (B, 128, DCH * TPC))
    p.start_block()
    st = Stager(p)
    emit_attn(p, st, layer, xs_d, xo_d)
    st.stage(sync=lambda e: None)
    return p

def _rope_tables_l2(core):
    pos = np.arange(core * TPC, (core + 1) * TPC, dtype=np.float64)[None, :]
    r = np.arange(512, dtype=np.float64)[:, None]
    freqs = np.exp(r * 2 * (-math.log(10000.0) / 1024.0))
    ang = pos * freqs
    return (np.cos(ang).astype(np.float32), np.sin(ang).astype(np.float32))


def _rope_tables_l1(core):
    inv = 1.0 / (10000.0 ** (np.arange(0, 64, 2, dtype=np.float64) / 64.0))
    t = np.arange(core * TPC, (core + 1) * TPC, dtype=np.float64)
    fr = t[:, None] * inv[None, :]
    cos = np.cos(fr).T
    sin = np.sin(fr).T
    cosT = np.tile(cos, (16, 1)).astype(np.float32)
    sinT = np.tile(sin, (16, 1)).astype(np.float32)
    return np.ascontiguousarray(cosT), np.ascontiguousarray(sinT)


def _perm_cols(wT, layer):
    if layer == 2:
        idx = np.concatenate([np.arange(0, D, 2), np.arange(1, D, 2)])
    else:
        heads = np.arange(NH)
        a_idx = (heads[:, None] * 64 + np.arange(32)[None, :]).ravel()
        b_idx = (heads[:, None] * 64 + 32 + np.arange(32)[None, :]).ravel()
        idx = np.concatenate([a_idx, b_idx])
    return np.ascontiguousarray(wT[:, idx])


def _causal_masks(core):
    import ml_dtypes
    m = np.zeros((KCH, 128, TPC), ml_dtypes.bfloat16)
    q_glob = core * TPC + np.arange(TPC)[None, :]
    for kc in range(KCH):
        k_glob = kc * 128 + np.arange(128)[:, None]
        m[kc] = (q_glob >= k_glob).astype(ml_dtypes.bfloat16)
    return m


def maps_attn(layer, I):
    if layer == 2:
        base = {
            "wqT": _perm_cols(_t(I["L2_attn_wq"]), 2),
            "wkT": _perm_cols(_t(I["L2_attn_wk"]), 2),
            "wvT": _t(I["L2_attn_wv"]),
            "woT": _t(I["L2_attn_wo_w"]),
            "wob": _col(I["L2_attn_wo_b"]),
        }
    else:
        base = {
            "wqT": _perm_cols(np.ascontiguousarray(np.asarray(I["L1_attn_Wq"], np.float32)), 1),
            "wkT": _perm_cols(np.ascontiguousarray(np.asarray(I["L1_attn_Wk"], np.float32)), 1),
            "wvT": np.ascontiguousarray(np.asarray(I["L1_attn_Wv"], np.float32)),
            "woT": np.ascontiguousarray(np.asarray(I["L1_attn_Wo"], np.float32)),
            "wob": np.zeros((128, DCH), np.float32),
        }
    pf = "a%d_" % layer
    in_maps = []
    for c in range(NCORES):
        m = dict(base)
        cosT, sinT = (_rope_tables_l2(c) if layer == 2 else _rope_tables_l1(c))
        m["cosT"] = cosT
        m["sinT"] = sinT
        if layer == 1:
            m["maskT"] = _causal_masks(c)
        in_maps.append({pf + k: v for k, v in m.items()})
    return in_maps


def run_attn(layer, stream_shards, I):
    p = build_attn(layer)
    pf = "a%d_" % layer
    in_maps = maps_attn(layer, I)
    for c in range(NCORES):
        in_maps[c][pf + "xs"] = _to_qmaj(stream_shards[c])
    res = run_bass_kernel_spmd(p.nc, in_maps, core_ids=list(range(NCORES)))
    out = [_from_qmaj(res.results[c][pf + "xo"]) for c in range(NCORES)]
    p.close()
    return out




def emit_l1ffn(p, st, xs_d, xo_d):
    p.pf = "f1_"
    t1r_d = p.inp("t1r", (4, 512, 512))
    t2r_d = p.inp("t2r", (4, 512, 512))
    t1i_d = p.inp("t1i", (4, 512, 512))
    t2i_d = p.inp("t2i", (4, 512, 512))
    br_d = p.inp("br", (128, 4))
    bi_d = p.inp("bi", (128, 4))

    sbx = p.sb("sbx", (128, DCH * TPC), F32)
    xn = p.sb("xn", (128, DCH * TPC), BF16)
    wre = p.sb("wre", (128, DCH * 512), BF16)
    wim = p.sb("wim", (128, DCH * 512), BF16)
    tta = p.sb("tta", (128, 4 * 512), F32)
    ttb = p.sb("ttb", (128, 4 * 512), F32)
    acc = p.sb("acc", (128, 4 * 512), F32)
    tmp = p.sb("tmp", (128, 4 * 512), F32)
    sqw = p.sb("sqw", (128, DCH * TPC), BF16)
    srow = p.sb("srow", (1, 4 * TPC), F32)
    lnt = p.sb("lnt", (128, DCH * TPC), F32)
    cone = p.sb("cone", (128, 1), BF16)
    onesrow = p.sb("onesrow", (1, 128), F32)
    eps_col = p.sb("eps_col", (128, 1), F32)
    bcol = p.sb("bcol", (128, 8), F32)

    ps_s = [p.ps("ps_s%d" % i) for i in range(4)]
    ps_x = p.ps("ps_x")
    env = dict(stream=None, xn=xn, sq=None, sqw=sqw, srow=srow, lnt=lnt,
               cone=cone, onesrow=onesrow, ps_stat=ps_s[0], ps_b0=ps_s[1],
               ps_b1=ps_s[2], eps_col=eps_col)


    def ld0(e):
        n = dma(e, p, "sync", bcol[:, 0:4], br_d[:, :])
        n += dma(e, p, "sync", bcol[:, 4:8], bi_d[:, :])
        return n

    def iv(v):
        v.memset(cone[:, :], 1.0 / 1024.0)
        v.memset(onesrow[:, :], 1.0)
        v.memset(eps_col[:, :], 1e-5)
    st.stage(sync=ld0, vector=iv)

    for part, (t1d, t2d) in [(0, (t1r_d, t2r_d)), (1, (t1i_d, t2i_d))]:
        for c in range(4):
            def ldc(e, t1d=t1d, t2d=t2d, c=c):
                n = dma(e, p, "sync",
                        tta[:, :].rearrange("q (a s) -> q a s", a=4),
                        t1d[c].rearrange("(a q) s -> q a s", q=128))
                n += dma(e, p, "sync",
                         ttb[:, :].rearrange("q (a s) -> q a s", a=4),
                         t2d[c].rearrange("(a q) s -> q a s", q=128))
                return n
            st.stage(sync=ldc)

            def vmul(v, c=c):
                if c == 0:
                    v.tensor_tensor(out=acc[:, :], in0=tta[:, :], in1=ttb[:, :],
                                    op=OP.mult)
                else:
                    v.tensor_tensor(out=tmp[:, :], in0=tta[:, :], in1=ttb[:, :],
                                    op=OP.mult)
                    v.drain()
                    v.tensor_tensor(out=acc[:, :], in0=acc[:, :], in1=tmp[:, :],
                                    op=OP.add)
            st.stage(vector=vmul)

        def vst(v, part=part):
            for ch in range(4):
                sl = acc[:, ch * 512:(ch + 1) * 512]
                if part == 0:
                    v.tensor_copy(out=wre[:, ch * 512:ch * 512 + 512], in_=sl)
                    v.tensor_copy(out=wim[:, (4 + ch) * 512:(4 + ch) * 512 + 512],
                                  in_=sl)
                else:
                    v.tensor_copy(out=wim[:, ch * 512:ch * 512 + 512], in_=sl)
            if part == 1:
                v.drain()
                for ch in range(4):
                    v.tensor_scalar(out=wre[:, (4 + ch) * 512:(4 + ch) * 512 + 512],
                                    in0=acc[:, ch * 512:(ch + 1) * 512],
                                    scalar1=-1.0, scalar2=None,
                                    op0=OP.mult, op1=OP.bypass)
        st.stage(vector=vst)

    for b in range(B):
        def ld_b(e, b=b):
            return dma(e, p, "sync",
                       sbx[:, :], xs_d[b])
        st.stage(sync=ld_b)
        emit_layernorm(st, env, 0,
                       x_at=lambda ch: sbx[:, ch * TPC:(ch + 1) * TPC],
                       xn_at=lambda ch: xn[:, ch * TPC:(ch + 1) * TPC])

        for oc in range(DCH):
            wsel = wre if oc < 4 else wim
            occ = oc % 4

            def t_p(t, wsel=wsel, occ=occ):
                for ic in range(DCH):
                    t.matmul(ps_x[:, 0:TPC],
                             wsel[:, ic * 512 + occ * 128: ic * 512 + occ * 128 + 128],
                             xn[:, ic * TPC:(ic + 1) * TPC],
                             start=(ic == 0), stop=(ic == DCH - 1))

            def v_o(v, oc=oc):
                sl = sbx[:, oc * TPC:(oc + 1) * TPC]
                v.scalar_tensor_tensor(out=sl, in0=ps_x[:, 0:TPC],
                                       scalar=bcol[:, oc:oc + 1], in1=sl,
                                       op0=OP.add, op1=OP.add)
            st.stage(tensor=t_p)
            st.stage(vector=v_o)

        def st_b(e, b=b):
            return dma(e, p, "sync",
                       xo_d[b], sbx[:, :])
        st.stage(sync=st_b)




def build_l1ffn():
    p = Prog("l1ffn")
    p.pf = "f1_"
    xs_d = p.inp("xs", (B, 128, DCH * TPC))
    xo_d = p.outp("xo", (B, 128, DCH * TPC))
    p.start_block()
    st = Stager(p)
    emit_l1ffn(p, st, xs_d, xo_d)
    st.stage(sync=lambda e: None)
    return p

def maps_l1ffn(I):
    G1R = np.asarray(I["L1_ffn_G1_R"], np.float32)[0]
    G2R = np.asarray(I["L1_ffn_G2_R"], np.float32)[:, :, 0, :]
    G1I = np.asarray(I["L1_ffn_G1_I"], np.float32)[0]
    G2I = np.asarray(I["L1_ffn_G2_I"], np.float32)[:, :, 0, :]

    def factors(G1, G2):
        t1 = np.zeros((4, 512, 512), np.float32)
        t2 = np.zeros((4, 512, 512), np.float32)
        for c in range(4):
            A = G1[:, c, :].T
            Bm = G2[c]
            t1[c] = np.repeat(np.repeat(A, 16, 0), 16, 1)
            t2[c] = np.tile(Bm.T, (32, 32))
        return t1, t2

    t1r, t2r = factors(G1R, G2R)
    t1i, t2i = factors(G1I, G2I)
    base = {
        "t1r": t1r, "t2r": t2r, "t1i": t1i, "t2i": t2i,
        "br": _col(I["L1_ffn_bias_real"], 4),
        "bi": _col(I["L1_ffn_bias_imag"], 4),
    }
    return [{"f1_" + k: v for k, v in base.items()} for c in range(NCORES)]


def run_l1ffn(stream_shards, I):
    p = build_l1ffn()
    in_maps = maps_l1ffn(I)
    for c in range(NCORES):
        in_maps[c]["f1_xs"] = _to_qmaj(stream_shards[c])
    res = run_bass_kernel_spmd(p.nc, in_maps, core_ids=list(range(NCORES)))
    out = [_from_qmaj(res.results[c]["f1_xo"]) for c in range(NCORES)]
    p.close()
    return out




def emit_quant(st, Q, src_ap, dst_ap, W, mbits, elo, ehi, signed):
    p = st.p
    f = Q["f"][:, 0:W]
    g_ = Q["g"][:, 0:W]
    t_ = Q["t"][:, 0:W]
    si = Q["si"][:, 0:W]
    u0 = Q["u0"][:, 0:W]
    u1 = Q["u1"][:, 0:W]
    b0 = Q["b0"][:, 0:W]
    ei = Q["ei"][:, 0:W]
    two = Q["two"][:, 0:W]
    sg = Q["sg"][:, 0:W]

    def s1(s):
        if signed:
            s.activation(out=sg, in_=src_ap, func=AF.Sign)
        s.activation(out=g_, in_=src_ap, func=AF.Square)
        s.drain()
        s.activation(out=f, in_=g_, func=AF.Ln)
    st.stage(scalar=s1)

    def v1(v):
        v.tensor_scalar(out=f, in0=f, scalar1=0.5 * INV_LN2, scalar2=float(elo),
                        op0=OP.mult, op1=OP.max)
        v.drain()
        v.tensor_scalar(out=g_, in0=f, scalar1=float(ehi), scalar2=-0.5,
                        op0=OP.min, op1=OP.add)
        v.drain()
        v.tensor_copy(out=ei, in_=g_)
        v.drain()
        v.tensor_copy(out=t_, in_=ei)
        v.drain()
        v.tensor_scalar(out=t_, in0=t_, scalar1=-1.0, scalar2=float(mbits),
                        op0=OP.mult, op1=OP.add)
    st.stage(vector=v1)

    def v_pow(v):
        v.tensor_scalar(out=b0, in0=t_, scalar1=7.0, scalar2=None,
                        op0=OP.add, op1=OP.bypass)
        v.memset(si, 2.0 ** -7)
        v.drain()
        for k in (16, 8, 4, 2, 1):
            v.tensor_scalar(out=u0, in0=b0, scalar1=float(k), scalar2=None,
                            op0=OP.is_ge, op1=OP.bypass)
            v.drain()
            v.scalar_tensor_tensor(out=b0, in0=u0, scalar=-float(k), in1=b0,
                                   op0=OP.mult, op1=OP.add)
            v.drain()
            v.tensor_scalar(out=u0, in0=u0, scalar1=float(2 ** k - 1),
                            scalar2=1.0, op0=OP.mult, op1=OP.add)
            v.drain()
            v.tensor_tensor(out=si, in0=si, in1=u0, op=OP.mult)
            v.drain()
    st.stage(vector=v_pow)

    def v2a(v):
        v.tensor_tensor(out=u0, in0=src_ap, in1=si, op=OP.mult)
    st.stage(vector=v2a)

    if signed:
        def s2(s):
            s.activation(out=u1, in_=u0, func=AF.Abs)
        st.stage(scalar=s2)
        ua = u1
    else:
        ua = u0

    def v2(v):
        if mbits == 3:
            cur = ua
            slots = [g_, t_]
            for i, wt in enumerate([16.0, 8.0, 4.0, 2.0, 1.0]):
                v.tensor_scalar(out=b0, in0=cur, scalar1=wt - 0.5, scalar2=-wt,
                                op0=OP.is_ge, op1=OP.mult)
                v.drain()
                nxt = slots[i % 2]
                v.tensor_tensor(out=nxt, in0=cur, in1=b0, op=OP.add)
                v.drain()
                cur = nxt
            v.scalar_tensor_tensor(out=b0, in0=cur, scalar=-1.0, in1=ua,
                                   op0=OP.mult, op1=OP.add)
            v.drain()
            v.reciprocal(out=si, in_=si)
            v.drain()
            if signed:
                v.tensor_tensor(out=g_, in0=b0, in1=si, op=OP.mult)
                v.drain()
                v.tensor_tensor(out=dst_ap, in0=g_, in1=sg, op=OP.mult)
            else:
                v.tensor_tensor(out=dst_ap, in0=b0, in1=si, op=OP.mult)
        else:
            v.tensor_scalar(out=b0, in0=ua, scalar1=1.0 / 64.0, scalar2=None,
                            op0=OP.mult, op1=OP.bypass)
            v.drain()
            v.tensor_copy(out=ei, in_=b0)
            v.drain()
            v.tensor_copy(out=f, in_=ei)
            v.drain()
            v.scalar_tensor_tensor(out=u0 if signed else g_, in0=f, scalar=-64.0,
                                   in1=ua, op0=OP.mult, op1=OP.add)
            w_ = u0 if signed else g_
            v.drain()
            v.tensor_scalar(out=w_, in0=w_, scalar1=64.0, scalar2=None,
                            op0=OP.add, op1=OP.bypass)
            v.drain()
            cur = w_
            slots = [t_, g_] if signed else [t_, u1]
            for i, wt in enumerate([64.0, 32.0, 16.0, 8.0, 4.0, 2.0, 1.0]):
                v.tensor_scalar(out=b0, in0=cur, scalar1=wt - 0.5, scalar2=-wt,
                                op0=OP.is_ge, op1=OP.mult)
                v.drain()
                nxt = slots[i % 2]
                v.tensor_tensor(out=nxt, in0=cur, in1=b0, op=OP.add)
                v.drain()
                cur = nxt
            v.scalar_tensor_tensor(out=b0, in0=cur, scalar=-1.0, in1=w_,
                                   op0=OP.mult, op1=OP.add)
            v.drain()
            v.scalar_tensor_tensor(out=w_, in0=f, scalar=64.0, in1=b0,
                                   op0=OP.mult, op1=OP.add)
            v.drain()
            v.tensor_scalar(out=w_, in0=w_, scalar1=-64.0, scalar2=None,
                            op0=OP.add, op1=OP.bypass)
            v.drain()
            v.reciprocal(out=si, in_=si)
            v.drain()
            if signed:
                v.tensor_tensor(out=g_, in0=w_, in1=si, op=OP.mult)
                v.drain()
                v.tensor_tensor(out=dst_ap, in0=g_, in1=sg, op=OP.mult)
            else:
                v.tensor_tensor(out=dst_ap, in0=w_, in1=si, op=OP.mult)
    st.stage(vector=v2)


def emit_l2ffn(p, st, xs_d, xo_d):
    p.pf = "f2_"
    HCH = 32
    w1T_d = p.inp("w1T", (D, 4096))
    b1_d = p.inp("b1", (128, HCH))
    b2_d = p.inp("b2", (128, DCH))
    w2slice_d = p.inp("w2slice", (512, D))

    w2q_loc = p.dram("w2q_loc", (512, D), BF16)
    w2q_all = p.dram("w2q_all", (4096, D), BF16)

    sbx = p.sb("sbx", (128, DCH * TPC), F32)
    xn = p.sb("xn", (128, DCH * TPC), BF16)
    w1sb = p.sb("w1sb", (128, DCH * 4096), BF16)
    w2blk = p.sb("w2blk", (128, HCH * 128), BF16)
    hbuf = p.sb("hbuf", (128, 2 * TPC), F32)
    hq = p.sb("hq", (128, HCH * TPC), BF16)
    obuf = p.sb("obuf", (128, DCH * TPC), F32)
    b1c = p.sb("b1c", (128, HCH), F32)
    b2c = p.sb("b2c", (128, DCH), F32)
    b2q = p.sb("b2q", (128, DCH), F32)
    w2src = obuf
    w2qs = hq
    srow = p.sb("srow", (1, 4 * TPC), F32)
    cone = p.sb("cone", (128, 1), BF16)
    onesrow = p.sb("onesrow", (1, 128), F32)
    eps_col = p.sb("eps_col", (128, 1), F32)

    QW = 1024
    Q = {}
    for nm in ["f", "g", "t", "si", "u0", "u1", "b0", "sg", "two"]:
        Q[nm] = p.sb("q_" + nm, (128, QW), F32)
    Q["ei"] = p.sb("q_ei", (128, QW), I32)
    lnt = p.sb("lnt", (128, 2 * TPC), F32)

    ps_s = [p.ps("ps_s%d" % i) for i in range(4)]
    ps_x = p.ps("ps_x")
    env = dict(stream=None, xn=xn, sq=None, sqw=hq, srow=srow, lnt=lnt,
               cone=cone, onesrow=onesrow, ps_stat=ps_s[0], ps_b0=ps_s[1],
               ps_b1=ps_s[2], eps_col=eps_col)


    def ld0(e):
        n = dma(e, p, "sync", b1c[:, :], b1_d[:, :])
        n += dma(e, p, "sync", b2c[:, :], b2_d[:, :])
        n += dma(e, p, "sync",
                 w2src[:, :].rearrange("q (a o) -> q a o", a=4),
                 w2slice_d.rearrange("(a q) o -> q a o", q=128))
        return n

    def iv(v):
        v.memset(cone[:, :], 1.0 / 1024.0)
        v.memset(onesrow[:, :], 1.0)
        v.memset(eps_col[:, :], 1e-5)
        v.memset(Q["two"][:, :], 2.0)
    st.stage(sync=ld0, vector=iv)
    w1r = w1T_d.rearrange("(ic q) o -> q ic o", q=128)
    cast_stream(st, p,
                [(w1sb[:, ic * 4096 + o4 * 1024:ic * 4096 + (o4 + 1) * 1024],
                  w1r[:, ic, o4 * 1024:(o4 + 1) * 1024], 1024)
                 for ic in range(DCH) for o4 in range(4)],
                hbuf[:, 0:1024], lnt[:, 0:1024])

    for a in range(4):
        emit_quant(st, Q, w2src[:, a * D:(a + 1) * D], w2qs[:, a * D:(a + 1) * D],
                   D, 3, -6, 9.9, True)
    emit_quant(st, Q, b2c[:, 0:DCH], b2q[:, 0:DCH], DCH, 10, -14, 15.9, True)

    def stw2(e):
        return dma(e, p, "sync",
                   w2q_loc.rearrange("(a q) o -> q a o", q=128),
                   w2qs[:, 0:4 * D].rearrange("q (a o) -> q a o", a=4))
    st.stage(sync=stw2)

    def g_cc(g):
        cs = p.sem("cc")
        g.collective_compute("AllGather", OP.bypass,
                             replica_groups=[list(range(NCORES))],
                             ins=[w2q_loc[:, :].opt()],
                             outs=[w2q_all[:, :].opt()]).then_inc(cs[0], 1)
        cs[1] += 1
        g.wait_ge(cs[0], cs[1])
    st.stage(gpsimd=g_cc)

    for b in range(B):
        def ld_b(e, b=b):
            return dma(e, p, "sync",
                       sbx[:, :], xs_d[b])
        st.stage(sync=ld_b)
        emit_layernorm(st, env, 0,
                       x_at=lambda ch: sbx[:, ch * TPC:(ch + 1) * TPC],
                       xn_at=lambda ch: xn[:, ch * TPC:(ch + 1) * TPC])

        for hg in range(16):
            for j in range(2):
                hc = hg * 2 + j

                def t_h(t, hc=hc):
                    for ic in range(DCH):
                        t.matmul(ps_x[:, 0:TPC],
                                 w1sb[:, ic * 4096 + hc * 128: ic * 4096 + hc * 128 + 128],
                                 xn[:, ic * TPC:(ic + 1) * TPC],
                                 start=(ic == 0), stop=(ic == DCH - 1))

                def s_h(s, hc=hc, j=j):
                    s.activation(out=hbuf[:, j * TPC:(j + 1) * TPC],
                                 in_=ps_x[:, 0:TPC], func=AF.Relu,
                                 bias=b1c[:, hc:hc + 1])
                st.stage(tensor=t_h)
                st.stage(scalar=s_h)
            emit_quant(st, Q, hbuf[:, 0:2 * TPC],
                       hq[:, hg * 2 * TPC:(hg + 1) * 2 * TPC],
                       2 * TPC, 3, -6, 9.9, False)

        for oc in range(DCH):
            def ld_w2b(e, oc=oc):
                return dma(e, p, "sync",
                           w2blk[:, :].rearrange("q (ic w) -> q ic w", ic=HCH),
                           w2q_all.rearrange("(ic q) o -> q ic o", q=128)[
                               :, :, oc * 128:(oc + 1) * 128])
            st.stage(sync=ld_w2b)

            def t_o(t, oc=oc):
                for ic in range(HCH):
                    t.matmul(ps_x[:, 0:TPC],
                             w2blk[:, ic * 128:ic * 128 + 128],
                             hq[:, ic * TPC:(ic + 1) * TPC],
                             start=(ic == 0), stop=(ic == HCH - 1))

            def s_o(s, oc=oc):
                s.activation(out=obuf[:, oc * TPC:(oc + 1) * TPC],
                             in_=ps_x[:, 0:TPC], func=AF.Identity,
                             bias=b2q[:, oc:oc + 1])
            st.stage(tensor=t_o)
            st.stage(scalar=s_o)

        for quarter in range(4):
            sl = obuf[:, quarter * 2 * TPC:(quarter + 1) * 2 * TPC]
            emit_quant(st, Q, sl, sl, 2 * TPC, 10, -14, 15.9, True)

        def v_res(v):
            for ch in range(DCH):
                sl = sbx[:, ch * TPC:(ch + 1) * TPC]
                v.tensor_tensor(out=sl, in0=sl, in1=obuf[:, ch * TPC:(ch + 1) * TPC],
                                op=OP.add)
        st.stage(vector=v_res)

        def st_b(e, b=b):
            return dma(e, p, "sync",
                       xo_d[b], sbx[:, :])
        st.stage(sync=st_b)




def build_l2ffn():
    p = Prog("l2ffn")
    p.pf = "f2_"
    xs_d = p.inp("xs", (B, 128, DCH * TPC))
    xo_d = p.outp("xo", (B, 128, DCH * TPC))
    p.start_block()
    st = Stager(p)
    emit_l2ffn(p, st, xs_d, xo_d)
    st.stage(sync=lambda e: None)
    return p

def maps_l2ffn(I):
    w2T = _t(I["L2_ffn_w2"])
    base = {
        "w1T": _t(I["L2_ffn_w1"]),
        "b1": _col(I["L2_ffn_b1"], 32),
        "b2": _col(I["L2_ffn_b2"]),
    }
    in_maps = []
    for c in range(NCORES):
        m = dict(base)
        m["w2slice"] = np.ascontiguousarray(w2T[c * 512:(c + 1) * 512])
        in_maps.append({"f2_" + k: v for k, v in m.items()})
    return in_maps


def run_l2ffn(stream_shards, I):
    p = build_l2ffn()
    in_maps = maps_l2ffn(I)
    for c in range(NCORES):
        in_maps[c]["f2_xs"] = _to_qmaj(stream_shards[c])
    res = run_bass_kernel_spmd(p.nc, in_maps, core_ids=list(range(NCORES)))
    out = [_from_qmaj(res.results[c]["f2_xo"]) for c in range(NCORES)]
    p.close()
    return out




def emit_l3ffn(p, st, xs_d, xo_d):
    p.pf = "f3_"
    HCH = 32
    wexpTs_d = p.inp("wexpTs", (4096, 128))
    bexp_d = p.inp("bexpc", (128, HCH))
    wcons_d = p.inp("wcons", (4096, 128))
    stU_d = p.inp("stU", (7, 32, 128, 128))
    stD_d = p.inp("stD", (7, 32, 128, 128))
    wcU_d = p.inp("wcU", (5, 128, 64))
    wcD_d = p.inp("wcD", (5, 128, 64))
    upb_d = p.inp("upb", (128, HCH))
    bsum_d = p.inp("bsumc", (128, DCH))
    dnb_d = p.inp("dnbc", (128, DCH))

    w1_loc = p.dram("w1_loc", (128, 4096), BF16)
    w1_all = p.dram("w1_all", (NCORES, 128, 4096), BF16)
    w2_loc = p.dram("w2_loc", (4096, 128), BF16)
    w2_all = p.dram("w2_all", (NCORES, 4096, 128), BF16)

    sbx = p.sb("sbx", (128, DCH * TPC), F32)
    xn = p.sb("xn", (128, DCH * TPC), BF16)
    w1sb = p.sb("w1sb", (128, DCH * 4096), BF16)
    h2 = p.sb("h2", (128, HCH * TPC), BF16)
    w2blk = p.sb("w2blk", (128, HCH * 128), BF16)
    wt = p.sb("wt", (128, 32 * 132), BF16)
    wt2 = p.sb("wt2", (128, 32 * 132), BF16)
    stile = p.sb("stile", (128, 4 * 128), BF16)
    wcol = p.sb("wcol", (128, 64), F32)
    tmp1 = p.sb("tmp1", (128, 132), F32)
    srow = p.sb("srow", (1, 4 * TPC), F32)
    lnt = p.sb("lnt", (128, 2 * TPC), F32)
    cone = p.sb("cone", (128, 1), BF16)
    onesrow = p.sb("onesrow", (1, 128), F32)
    eps_col = p.sb("eps_col", (128, 1), F32)
    idt = p.sb("idt", (128, 128), BF16)
    iw = p.sb("iw", (128, 128), I32)
    b1c = p.sb("b1c", (128, HCH), F32)
    upbc = p.sb("upbc", (128, HCH), F32)
    boc = p.sb("boc", (128, DCH), F32)
    dnbc = p.sb("dnbc_s", (128, DCH), F32)

    ps_s = [p.ps("ps_s%d" % i) for i in range(4)]
    ps_x = p.ps("ps_x")
    ps_t = p.ps("ps_t", (128, 128), BF16)
    env = dict(stream=None, xn=xn, sq=None, sqw=h2, srow=srow, lnt=lnt,
               cone=cone, onesrow=onesrow, ps_stat=ps_s[0], ps_b0=ps_s[1],
               ps_b1=ps_s[2], eps_col=eps_col)


    def ld0(e):
        n = dma(e, p, "sync", upbc[:, :], upb_d[:, :])
        n += dma(e, p, "sync", boc[:, :], bsum_d[:, :])
        n += dma(e, p, "sync", dnbc[:, :], dnb_d[:, :])
        n += dma(e, p, "sync", b1c[:, :], bexp_d[:, :])
        return n

    def ld0g(e):
        n = dma(e, p, "gpsimd",
                wt[:, :].rearrange("q (jb s) -> q jb s", s=132)[:, :, 0:128],
                wexpTs_d.rearrange("(jb q) d -> q jb d", q=128))
        return n

    def iv(v):
        v.memset(cone[:, :], 1.0 / 1024.0)
        v.memset(onesrow[:, :], 1.0)
        v.memset(eps_col[:, :], 1e-5)
        v.memset(wt[:, :].rearrange("q (jb s) -> q jb s", s=132)[:, :, 128:132], 0.0)
        v.drain()
        for jb in range(HCH):
            v.tensor_copy(out=wt[:, jb * 132 + 128:jb * 132 + 129],
                          in_=b1c[:, jb:jb + 1])
        v.drain()
        v.tensor_tensor(out=boc[:, :], in0=boc[:, :], in1=dnbc[:, :], op=OP.add)
    st.stage(sync=ld0, gpsimd=ld0g)
    st.stage(vector=iv)

    def g_iota(g):
        g.iota(iw[:, :], pattern=[[1, 128]], base=0, channel_multiplier=-1)
    st.stage(gpsimd=g_iota)

    def v_idt(v):
        v.tensor_scalar(out=idt[:, :], in0=iw[:, :], scalar1=0.0, scalar2=None,
                        op0=OP.is_equal, op1=OP.bypass)
    st.stage(vector=v_idt)

    def fold(std_d, wc_d, a_order, b_first, b_trans, width):
        bufs = [wt, wt2]
        cur = 0

        def tile(bufi, jb):
            return bufs[bufi][:, jb * 132:jb * 132 + width]

        def do_bstages():
            nonlocal cur
            for si, s in enumerate((range(4, -1, -1) if b_trans else range(5))):
                sb = s
                def ld_wc(e, sb=sb):
                    return dma(e, p, "sync", wcol[:, :], wc_d[sb])
                st.stage(sync=ld_wc)

                def v_b(v, sb=sb, cur=cur):
                    pairs = [ba for ba in range(32) if not (ba >> sb) & 1]
                    for pi, ba in enumerate(pairs):
                        bb_ = ba + (1 << sb)
                        a_ap = tile(cur, ba)
                        b_ap = tile(cur, bb_)
                        c00 = wcol[:, pi * 4 + 0:pi * 4 + 1]
                        c01 = wcol[:, pi * 4 + 1:pi * 4 + 2]
                        c10 = wcol[:, pi * 4 + 2:pi * 4 + 3]
                        c11 = wcol[:, pi * 4 + 3:pi * 4 + 4]
                        if b_trans:
                            c01, c10 = c10, c01
                        v.tensor_scalar(out=tmp1[:, 0:width], in0=a_ap, scalar1=c00,
                                        scalar2=None, op0=OP.mult, op1=OP.bypass)
                        v.drain()
                        v.scalar_tensor_tensor(out=tile(1 - cur, ba), in0=b_ap,
                                               scalar=c01, in1=tmp1[:, 0:width],
                                               op0=OP.mult, op1=OP.add)
                        v.drain()
                        v.tensor_scalar(out=tmp1[:, 0:width], in0=a_ap, scalar1=c10,
                                        scalar2=None, op0=OP.mult, op1=OP.bypass)
                        v.drain()
                        v.scalar_tensor_tensor(out=tile(1 - cur, bb_), in0=b_ap,
                                               scalar=c11, in1=tmp1[:, 0:width],
                                               op0=OP.mult, op1=OP.add)
                        v.drain()
                st.stage(vector=v_b)
                cur = 1 - cur

        def do_astages():
            nonlocal cur
            for s in a_order:
                def ld_st(e, s=s):
                    n = 0
                    for g4 in range(8):
                        pass
                    return 0
                for g4 in range(8):
                    def ld_s4(e, s=s, g4=g4):
                        return dma(e, p, "gpsimd",
                                   stile[:, :].rearrange("q (k z) -> q k z", k=4),
                                   std_d[s, 4 * g4:4 * g4 + 4].rearrange(
                                       "k q z -> q k z"))
                    st.stage(sync=lambda e: None, gpsimd=ld_s4)

                    def t_a(t, g4=g4, cur=cur):
                        for k in range(4):
                            t.matmul(ps_s[k][:, 0:width],
                                     stile[:, k * 128:(k + 1) * 128],
                                     tile(cur, 4 * g4 + k), start=True, stop=True)
                    st.stage(tensor=t_a)

                    def v_a(v, g4=g4, cur=cur):
                        for k in range(4):
                            v.tensor_copy(out=tile(1 - cur, 4 * g4 + k),
                                          in_=ps_s[k][:, 0:width])
                    st.stage(vector=v_a)
                cur = 1 - cur

        if b_first:
            do_bstages()
            do_astages()
        else:
            do_astages()
            do_bstages()
        return cur

    cur = fold(stU_d, wcU_d, list(range(7)), False, False, 132)

    def v_b1(v, cur=cur):
        for jb in range(HCH):
            v.tensor_copy(out=b1c[:, jb:jb + 1],
                          in_=(wt if cur == 0 else wt2)[:, jb * 132 + 128:jb * 132 + 129])
        v.drain()
        v.tensor_tensor(out=b1c[:, :], in0=b1c[:, :], in1=upbc[:, :], op=OP.add)
    st.stage(vector=v_b1)

    for jb in range(HCH):
        def t_tr(t, jb=jb, cur=cur):
            t.transpose(ps_t[:, 0:128],
                        (wt if cur == 0 else wt2)[:, jb * 132:jb * 132 + 128],
                        idt[:, :])
        st.stage(tensor=t_tr)

        def v_tr(v, jb=jb):
            v.tensor_copy(out=wt2[:, 0:128] if False else stile[:, 0:128],
                          in_=ps_t[:, 0:128])
        st.stage(vector=v_tr)

        def st_w1(e, jb=jb):
            return dma(e, p, "sync", w1_loc[:, jb * 128:(jb + 1) * 128],
                       stile[:, 0:128])
        st.stage(sync=st_w1)

    def g_ag1(g):
        cs = p.sem("cc")
        g.collective_compute("AllGather", OP.bypass,
                             replica_groups=[list(range(NCORES))],
                             ins=[w1_loc[:, :].opt()],
                             outs=[w1_all[:, :, :].opt()]).then_inc(cs[0], 1)
        cs[1] += 1
        g.wait_ge(cs[0], cs[1])
    st.stage(gpsimd=g_ag1)

    def ld_wc2(e):
        return dma(e, p, "gpsimd",
                   wt[:, :].rearrange("q (jb s) -> q jb s", s=132)[:, :, 0:128],
                   wcons_d.rearrange("(jb q) d -> q jb d", q=128))
    st.stage(gpsimd=ld_wc2)
    cur2 = fold(stD_d, wcD_d, list(range(6, -1, -1)), True, True, 128)

    for jb in range(HCH):
        def st_w2(e, jb=jb, cur2=cur2):
            return dma(e, p, "sync", w2_loc[jb * 128:(jb + 1) * 128, :],
                       (wt if cur2 == 0 else wt2)[:, jb * 132:jb * 132 + 128])
        st.stage(sync=st_w2)

    def g_ag2(g):
        cs = p.sem("cc")
        g.collective_compute("AllGather", OP.bypass,
                             replica_groups=[list(range(NCORES))],
                             ins=[w2_loc[:, :].opt()],
                             outs=[w2_all[:, :, :].opt()]).then_inc(cs[0], 1)
        cs[1] += 1
        g.wait_ge(cs[0], cs[1])
    st.stage(gpsimd=g_ag2)

    def ld_w1(e):
        n = 0
        for dc in range(NCORES):
            n += dma(e, p, "sync", w1sb[:, dc * 4096:(dc + 1) * 4096], w1_all[dc])
        return n
    st.stage(sync=ld_w1)

    for b in range(B):
        def ld_b(e, b=b):
            return dma(e, p, "sync",
                       sbx[:, :], xs_d[b])
        st.stage(sync=ld_b)
        emit_layernorm(st, env, 0,
                       x_at=lambda ch: sbx[:, ch * TPC:(ch + 1) * TPC],
                       xn_at=lambda ch: xn[:, ch * TPC:(ch + 1) * TPC])

        for hc in range(HCH):
            def t_h(t, hc=hc):
                for ic in range(DCH):
                    t.matmul(ps_x[:, 0:TPC],
                             w1sb[:, ic * 4096 + hc * 128: ic * 4096 + hc * 128 + 128],
                             xn[:, ic * TPC:(ic + 1) * TPC],
                             start=(ic == 0), stop=(ic == DCH - 1))

            def s_h(s, hc=hc):
                s.activation(out=h2[:, hc * TPC:(hc + 1) * TPC],
                             in_=ps_x[:, 0:TPC], func=AF.Gelu,
                             bias=b1c[:, hc:hc + 1])
            st.stage(tensor=t_h)
            st.stage(scalar=s_h)

        for oc in range(DCH):
            def ld_w2b(e, oc=oc):
                n = 0
                for icq in range(NCORES):
                    pass
                return dma(e, p, "sync",
                           w2blk[:, :].rearrange("q (ic w) -> q ic w", ic=HCH),
                           w2_all[oc].rearrange("(ic q) w -> q ic w", q=128))
            st.stage(sync=ld_w2b)

            def t_o(t, oc=oc):
                for ic in range(HCH):
                    t.matmul(ps_x[:, 0:TPC],
                             w2blk[:, ic * 128:ic * 128 + 128],
                             h2[:, ic * TPC:(ic + 1) * TPC],
                             start=(ic == 0), stop=(ic == HCH - 1))

            def v_o(v, oc=oc):
                sl = sbx[:, oc * TPC:(oc + 1) * TPC]
                v.scalar_tensor_tensor(out=sl, in0=ps_x[:, 0:TPC],
                                       scalar=boc[:, oc:oc + 1], in1=sl,
                                       op0=OP.add, op1=OP.add)
            st.stage(tensor=t_o)
            st.stage(vector=v_o)

        def st_b(e, b=b):
            return dma(e, p, "sync",
                       xo_d[b], sbx[:, :])
        st.stage(sync=st_b)




def build_l3ffn():
    p = Prog("l3ffn")
    p.pf = "f3_"
    xs_d = p.inp("xs", (B, 128, DCH * TPC))
    xo_d = p.outp("xo", (B, 128, DCH * TPC))
    p.start_block()
    st = Stager(p)
    emit_l3ffn(p, st, xs_d, xo_d)
    st.stage(sync=lambda e: None)
    return p

def _l3_stage_tiles(weights, transposed):
    out = np.zeros((7, 32, 128, 128), np.float32)
    w = np.asarray(weights, np.float32)
    for s in range(7):
        sp = 2 ** s
        for blk in range(32):
            for kl in range(64):
                k = blk * 64 + kl
                a = (kl // sp) * 2 * sp + kl % sp
                bcol = a + sp
                w00, w01 = w[s, k, 0, 0], w[s, k, 0, 1]
                w10, w11 = w[s, k, 1, 0], w[s, k, 1, 1]
                out[s, blk, a, a] = w00
                out[s, blk, a, bcol] = w01
                out[s, blk, bcol, a] = w10
                out[s, blk, bcol, bcol] = w11
    if transposed:
        out = np.ascontiguousarray(out.transpose(0, 1, 3, 2))
    return np.ascontiguousarray(out)


def _l3_wcols(weights):
    w = np.asarray(weights, np.float32)
    out = np.zeros((5, 128, 64), np.float32)
    for sb in range(5):
        s = sb + 7
        sp = 2 ** sb
        pairs = [ba for ba in range(32) if not (ba >> sb) & 1]
        for pi, ba in enumerate(pairs):
            kbase = (ba >> (sb + 1)) * 128 * sp + (ba & (sp - 1)) * 128
            for comp in range(4):
                i_, j_ = comp // 2, comp % 2
                out[sb, :, pi * 4 + comp] = w[s, kbase:kbase + 128, i_, j_]
    return np.ascontiguousarray(out)


def maps_l3ffn(I):
    wexpT = _t(np.asarray(I["L3_ffn_W_exp"]))
    wcon = np.ascontiguousarray(np.asarray(I["L3_ffn_W_con"], np.float32))
    base = {
        "stU": _l3_stage_tiles(I["L3_ffn_up_weights"], True),
        "stD": _l3_stage_tiles(I["L3_ffn_down_weights"], False),
        "wcU": _l3_wcols(I["L3_ffn_up_weights"]),
        "wcD": _l3_wcols(I["L3_ffn_down_weights"]),
        "bexpc": _col(I["L3_ffn_b_exp"], 32),
        "upb": _col(I["L3_ffn_up_bias"], 32),
        "bsumc": _col(I["L3_ffn_b_con"]),
        "dnbc": _col(I["L3_ffn_down_bias"]),
    }
    in_maps = []
    for c in range(NCORES):
        m = dict(base)
        m["wexpTs"] = np.ascontiguousarray(wexpT[:, c * 128:(c + 1) * 128])
        m["wcons"] = np.ascontiguousarray(wcon[:, c * 128:(c + 1) * 128])
        in_maps.append({"f3_" + k: v for k, v in m.items()})
    return in_maps


def run_l3ffn(stream_shards, I):
    p = build_l3ffn()
    in_maps = maps_l3ffn(I)
    for c in range(NCORES):
        in_maps[c]["f3_xs"] = _to_qmaj(stream_shards[c])
    res = run_bass_kernel_spmd(p.nc, in_maps, core_ids=list(range(NCORES)))
    out = [_from_qmaj(res.results[c]["f3_xo"]) for c in range(NCORES)]
    p.close()
    return out




EPS_RMS = 1.1920929e-07


def emit_l3attn(p, st, xs_d, xo_d):
    p.pf = "a3_"
    CH2 = 2
    CSZ = 16
    NCH = TPC // CSZ
    NG = B * CH2

    wmT_d = p.inp("wmT", (D, D))
    wpT_d = p.inp("wpT", (512, D))
    vc_d = p.inp("vc", (128, 2 * CH2))
    osc_d = p.inp("osc", (128, DCH))
    ohc_d = p.inp("ohc", (128, NCORES))
    idc_d = p.inp("idc", (128, 1))

    car_loc = p.dram("car_loc", (128, 4 * NG), F32)
    car_all = p.dram("car_all", (NCORES, 128, 4 * NG), F32)

    sbx = p.sb("sbx", (128, DCH * TPC), F32)
    xn = p.sb("xn", (128, DCH * TPC), BF16)
    xr = p.sb("xr", (128, DCH * TPC), BF16)
    wm = p.sb("wm", (128, DCH * D), BF16)
    wp = p.sb("wp", (128, 4 * D), BF16)
    mcall = p.sb("mcall", (128, B * 4 * CH2 * TPC), F32)
    pref = p.sb("pref", (128, NG * 4 * NCH), F32)
    hh = p.sb("hh", (128, 2 * CH2 * TPC), F32)
    hbf = p.sb("hbf", (128, 2 * CH2 * TPC), BF16)
    n2 = p.sb("n2", (128, CH2 * TPC), F32)
    ssc = p.sb("ssc", (128, 16 * NCH), F32)
    car = p.sb("car", (128, 4 * NG), F32)
    carg = p.sb("carg", (128, 4 * NG), F32)
    rrun = p.sb("rrun", (128, 2 * 4 * NG), F32)
    hown = p.sb("hown", (128, 4 * NG), F32)
    gtile = p.sb("gtile", (128, 4 * (NCH + 1)), F32)
    wtl = p.sb("wtl", (128, 6 * NCH), F32)
    ones32 = p.sb("ones32", (128, NCH), F32)
    vcc = p.sb("vcc", (128, 2 * CH2), F32)
    osc = p.sb("osc_s", (128, DCH), F32)
    ohc = p.sb("ohc_s", (128, NCORES), F32)
    idc = p.sb("idc_s", (128, 1), F32)
    sqw = p.sb("sqw", (128, DCH * TPC), BF16)
    srow = p.sb("srow", (1, 4 * TPC), F32)
    lnt = p.sb("lnt", (128, DCH * TPC), F32)
    cone = p.sb("cone", (128, 1), BF16)
    onesrow = p.sb("onesrow", (1, 128), F32)
    eps_col = p.sb("eps_col", (128, 1), F32)
    eps_col2 = p.sb("eps_col2", (128, 1), F32)

    ps_s = [p.ps("ps_s%d" % i) for i in range(4)]
    ps_x = p.ps("ps_x")
    env = dict(stream=None, xn=xn, sq=None, sqw=sqw, srow=srow, lnt=lnt,
               cone=cone, onesrow=onesrow, ps_stat=ps_s[0], ps_b0=ps_s[1],
               ps_b1=ps_s[2], eps_col=eps_col, eps_col2=eps_col2)


    def S(i):
        return ssc[:, i * NCH:(i + 1) * NCH]

    def MC(b, comp, cc):
        o = ((b * 4 + comp) * CH2 + cc) * TPC
        return mcall[:, o:o + TPC]

    def PF(g, comp):
        o = (g * 4 + comp) * NCH
        return pref[:, o:o + NCH]

    def ld0(e):
        n = dma(e, p, "sync", vcc[:, :], vc_d[:, :])
        n += dma(e, p, "sync", osc[:, :], osc_d[:, :])
        n += dma(e, p, "sync", ohc[:, :], ohc_d[:, :])
        n += dma(e, p, "sync", idc[:, :], idc_d[:, :])
        return n

    def ldg(e):
        n = dma(e, p, "gpsimd",
                wm[:, :].rearrange("q (ic o) -> q ic o", ic=DCH),
                wmT_d.rearrange("(ic q) o -> q ic o", q=128))
        n += dma(e, p, "gpsimd",
                 wp[:, :].rearrange("q (ic o) -> q ic o", ic=4),
                 wpT_d.rearrange("(ic q) o -> q ic o", q=128))
        return n

    def iv(v):
        v.memset(cone[:, :], 1.0 / 1024.0)
        v.memset(onesrow[:, :], 1.0)
        v.memset(eps_col[:, :], 1e-5)
        v.memset(eps_col2[:, :], EPS_RMS)
        v.memset(ones32[:, :], 1.0)
    st.stage(sync=ld0, gpsimd=ldg, vector=iv)

    def mm22(v, O, A, Bt, t1, t2):
        for (oo, x1, y1, x2, y2) in [(O[0], A[0], Bt[0], A[1], Bt[2]),
                                     (O[1], A[0], Bt[1], A[1], Bt[3]),
                                     (O[2], A[2], Bt[0], A[3], Bt[2]),
                                     (O[3], A[2], Bt[1], A[3], Bt[3])]:
            v.tensor_tensor(out=t1, in0=x1, in1=y1, op=OP.mult)
            v.tensor_tensor(out=t2, in0=x2, in1=y2, op=OP.mult)
            v.drain()
            v.tensor_tensor(out=oo, in0=t1, in1=t2, op=OP.add)
            v.drain()

    for b in range(B):
        def ld_b(e, b=b):
            return dma(e, p, "sync",
                       sbx[:, :], xs_d[b])
        st.stage(sync=ld_b)
        emit_layernorm(st, env, 0,
                       x_at=lambda ch: sbx[:, ch * TPC:(ch + 1) * TPC],
                       xn_at=lambda ch: xn[:, ch * TPC:(ch + 1) * TPC])
        emit_layernorm(st, env, 0, rms=True, eps_override=EPS_RMS,
                       x_at=lambda ch: xn[:, ch * TPC:(ch + 1) * TPC],
                       xn_at=lambda ch: xr[:, ch * TPC:(ch + 1) * TPC])

        def v_resx(v):
            for ch in range(DCH):
                sl = sbx[:, ch * TPC:(ch + 1) * TPC]
                v.tensor_tensor(out=sl, in0=sl, in1=xn[:, ch * TPC:(ch + 1) * TPC],
                                op=OP.add)
        st.stage(vector=v_resx)

        def st_bx(e, b=b):
            return dma(e, p, "sync",
                       xo_d[b], sbx[:, :])
        st.stage(sync=st_bx)

        for oc in range(DCH):
            def t_p(t, oc=oc):
                for ic in range(DCH):
                    t.matmul(ps_x[:, 0:TPC],
                             wm[:, ic * D + oc * 128: ic * D + oc * 128 + 128],
                             xr[:, ic * TPC:(ic + 1) * TPC],
                             start=(ic == 0), stop=(ic == DCH - 1))

            def v_e(v, oc=oc, b=b):
                comp, cc = oc // 2, oc % 2
                v.tensor_copy(out=MC(b, comp, cc), in_=ps_x[:, 0:TPC])
            st.stage(tensor=t_p)
            st.stage(vector=v_e)

        def s_sq0(s, b=b):
            for cc in range(CH2):
                s.activation(out=n2[:, cc * TPC:(cc + 1) * TPC],
                             in_=MC(b, 0, cc), func=AF.Square)
        st.stage(scalar=s_sq0)
        for comp in range(1, 4):
            def s_sqk(s, comp=comp, b=b):
                for cc in range(CH2):
                    s.activation(out=lnt[:, cc * TPC:(cc + 1) * TPC],
                                 in_=MC(b, comp, cc), func=AF.Square)
            st.stage(scalar=s_sqk)

            def v_acc(v):
                for cc in range(CH2):
                    v.tensor_tensor(out=n2[:, cc * TPC:(cc + 1) * TPC],
                                    in0=n2[:, cc * TPC:(cc + 1) * TPC],
                                    in1=lnt[:, cc * TPC:(cc + 1) * TPC], op=OP.add)
            st.stage(vector=v_acc)

        def v_cl(v):
            for cc in range(CH2):
                v.tensor_scalar(out=n2[:, cc * TPC:(cc + 1) * TPC],
                                in0=n2[:, cc * TPC:(cc + 1) * TPC],
                                scalar1=1e-12, scalar2=None, op0=OP.max,
                                op1=OP.bypass)
        st.stage(vector=v_cl)

        def s_rn(s):
            for cc in range(CH2):
                s.activation(out=n2[:, cc * TPC:(cc + 1) * TPC],
                             in_=n2[:, cc * TPC:(cc + 1) * TPC], func=AF.Ln)
            s.drain()
            for cc in range(CH2):
                s.activation(out=n2[:, cc * TPC:(cc + 1) * TPC],
                             in_=n2[:, cc * TPC:(cc + 1) * TPC], func=AF.Exp,
                             scale=-0.5)
        st.stage(scalar=s_rn)

        def v_nm(v, b=b):
            for comp in range(4):
                for cc in range(CH2):
                    v.tensor_tensor(out=MC(b, comp, cc), in0=MC(b, comp, cc),
                                    in1=n2[:, cc * TPC:(cc + 1) * TPC], op=OP.mult)
                v.drain()
        st.stage(vector=v_nm)

        for cc in range(CH2):
            g = b * CH2 + cc

            def msl(comp, i, b=b, cc=cc):
                return MC(b, comp, cc).rearrange("q (c z) -> q c z", z=CSZ)[
                    :, :, i]

            def v_p12(v, b=b, cc=cc, g=g):
                for comp in range(4):
                    v.tensor_copy(out=S(comp), in_=msl(comp, 0))
                v.drain()
                cur = 0
                for i in range(1, CSZ):
                    co, no = cur * 4, (1 - cur) * 4
                    mm22(v, [S(no + k) for k in range(4)],
                         [msl(k, i) for k in range(4)],
                         [S(co + k) for k in range(4)], S(8), S(9))
                    cur = 1 - cur
                if cur == 1:
                    for comp in range(4):
                        v.tensor_copy(out=S(comp), in_=S(4 + comp))
                    v.drain()
                cur = 0
                for k in range(5):
                    sh = 1 << k
                    co, no = cur * 4, (1 - cur) * 4
                    for comp in range(4):
                        v.tensor_copy(out=S(no + comp)[:, 0:sh],
                                      in_=S(co + comp)[:, 0:sh])
                    v.drain()
                    W = NCH - sh
                    mm22(v, [S(no + j)[:, sh:] for j in range(4)],
                         [S(co + j)[:, sh:] for j in range(4)],
                         [S(co + j)[:, 0:W] for j in range(4)],
                         S(8)[:, 0:W], S(9)[:, 0:W])
                    cur = 1 - cur
                base = cur * 4
                for comp in range(4):
                    v.tensor_copy(out=PF(g, comp), in_=S(base + comp))
                v.drain()
                for comp in range(4):
                    v.tensor_copy(out=car[:, g * 4 + comp:g * 4 + comp + 1],
                                  in_=S(base + comp)[:, NCH - 1:NCH])
            st.stage(vector=v_p12)

    def st_car(e):
        return dma(e, p, "sync", car_loc[:, :], car[:, :])
    st.stage(sync=st_car)

    def g_cc(g):
        cs = p.sem("cc")
        g.collective_compute("AllGather", OP.bypass,
                             replica_groups=[list(range(NCORES))],
                             ins=[car_loc[:, :].opt()],
                             outs=[car_all[:, :, :].opt()]).then_inc(cs[0], 1)
        cs[1] += 1
        g.wait_ge(cs[0], cs[1])
    st.stage(gpsimd=g_cc)

    def v_hinit(v):
        v.memset(hown[:, :], 0.0)
        v.drain()
        for g_ in range(NG):
            v.tensor_scalar(out=hown[:, g_ * 4 + 0:g_ * 4 + 1], in0=idc[:, 0:1],
                            scalar1=1.0, scalar2=None, op0=OP.mult, op1=OP.bypass)
            v.tensor_scalar(out=hown[:, g_ * 4 + 3:g_ * 4 + 4], in0=idc[:, 0:1],
                            scalar1=1.0, scalar2=None, op0=OP.mult, op1=OP.bypass)
    st.stage(vector=v_hinit)

    for j in range(NCORES - 1):
        def ld_cj(e, j=j):
            return dma(e, p, "sync", carg[:, :], car_all[j])
        st.stage(sync=ld_cj)

        def v_comb(v, j=j):
            co = (j % 2) * 4 * NG
            no = ((j + 1) % 2) * 4 * NG
            if j == 0:
                v.tensor_copy(out=rrun[:, no:no + 4 * NG], in_=carg[:, :])
                v.drain()
            else:
                for g_ in range(NG):
                    A = [carg[:, g_ * 4 + k:g_ * 4 + k + 1] for k in range(4)]
                    Bt = [rrun[:, co + g_ * 4 + k:co + g_ * 4 + k + 1]
                          for k in range(4)]
                    O = [rrun[:, no + g_ * 4 + k:no + g_ * 4 + k + 1]
                         for k in range(4)]
                    mm22(v, O, A, Bt, S(8)[:, 0:1], S(9)[:, 0:1])
            v.drain()
            for k in range(4 * NG):
                pass
            v.scalar_tensor_tensor(out=hown[:, :], in0=rrun[:, no:no + 4 * NG],
                                   scalar=ohc[:, j:j + 1], in1=hown[:, :],
                                   op0=OP.mult, op1=OP.add)
        st.stage(vector=v_comb)

    for b in range(B):
        def ld_b2(e, b=b):
            return dma(e, p, "sync",
                       sbx[:, :], xo_d[b])
        st.stage(sync=ld_b2)

        for cc in range(CH2):
            g = b * CH2 + cc

            def hcol(comp, g=g):
                return hown[:, g * 4 + comp:g * 4 + comp + 1]

            def v_G(v, g=g, cc=cc):
                for (comp, x1, h1, x2, h2) in [(0, 0, 0, 1, 2), (1, 0, 1, 1, 3),
                                               (2, 2, 0, 3, 2), (3, 2, 1, 3, 3)]:
                    v.tensor_scalar(out=S(8), in0=PF(g, x1), scalar1=hcol(h1),
                                    scalar2=None, op0=OP.mult, op1=OP.bypass)
                    v.drain()
                    v.scalar_tensor_tensor(out=gtile[:, comp * 33 + 1:comp * 33 + 33],
                                           in0=PF(g, x2), scalar=hcol(h2),
                                           in1=S(8), op0=OP.mult, op1=OP.add)
                    v.drain()
                    v.tensor_scalar(out=gtile[:, comp * 33:comp * 33 + 1],
                                    in0=hcol(comp), scalar1=1.0, scalar2=None,
                                    op0=OP.mult, op1=OP.bypass)
                v.drain()
                v1c = vcc[:, cc:cc + 1]
                v2c = vcc[:, CH2 + cc:CH2 + cc + 1]
                v.tensor_scalar(out=S(8), in0=gtile[:, 0:NCH], scalar1=v1c,
                                scalar2=None, op0=OP.mult, op1=OP.bypass)
                v.drain()
                v.scalar_tensor_tensor(out=wtl[:, 0:NCH], in0=gtile[:, 33:33 + NCH],
                                       scalar=v2c, in1=S(8), op0=OP.mult, op1=OP.add)
                v.tensor_scalar(out=S(8), in0=gtile[:, 66:66 + NCH], scalar1=v1c,
                                scalar2=None, op0=OP.mult, op1=OP.bypass)
                v.drain()
                v.scalar_tensor_tensor(out=wtl[:, NCH:2 * NCH],
                                       in0=gtile[:, 99:99 + NCH],
                                       scalar=v2c, in1=S(8), op0=OP.mult, op1=OP.add)
                v.drain()
            st.stage(vector=v_G)

            def v_inner(v, b=b, cc=cc):
                def msl(comp, i):
                    return MC(b, comp, cc).rearrange("q (c z) -> q c z", z=CSZ)[
                        :, :, i]

                def hsl(h, i):
                    return hh[:, (h * CH2 + cc) * TPC:(h * CH2 + cc) * TPC + TPC]\
                        .rearrange("q (c z) -> q c z", z=CSZ)[:, :, i]
                cur = 0
                for i in range(CSZ):
                    co, no = cur * 2, 2 + (1 - cur) * 2 - 2
                    w1 = wtl[:, cur * 2 * NCH:cur * 2 * NCH + NCH]
                    w2 = wtl[:, cur * 2 * NCH + NCH:cur * 2 * NCH + 2 * NCH]
                    nw1 = wtl[:, (1 - cur) * 2 * NCH:(1 - cur) * 2 * NCH + NCH]
                    nw2 = wtl[:, (1 - cur) * 2 * NCH + NCH:
                              (1 - cur) * 2 * NCH + 2 * NCH]
                    v.tensor_tensor(out=S(8), in0=msl(0, i), in1=w1, op=OP.mult)
                    v.tensor_tensor(out=S(9), in0=msl(1, i), in1=w2, op=OP.mult)
                    v.drain()
                    v.tensor_tensor(out=nw1, in0=S(8), in1=S(9), op=OP.add)
                    v.tensor_tensor(out=S(8), in0=msl(2, i), in1=w1, op=OP.mult)
                    v.tensor_tensor(out=S(9), in0=msl(3, i), in1=w2, op=OP.mult)
                    v.drain()
                    v.tensor_tensor(out=nw2, in0=S(8), in1=S(9), op=OP.add)
                    v.drain()
                    v.tensor_copy(out=hsl(0, i), in_=nw1)
                    v.tensor_copy(out=hsl(1, i), in_=nw2)
                    v.drain()
                    cur = 1 - cur
            st.stage(vector=v_inner)

        def v_hc(v):
            for i in range(2 * CH2):
                v.tensor_copy(out=hbf[:, i * TPC:(i + 1) * TPC],
                              in_=hh[:, i * TPC:(i + 1) * TPC])
        st.stage(vector=v_hc)

        for oc in range(DCH):
            def t_o(t, oc=oc):
                for ic in range(4):
                    t.matmul(ps_x[:, 0:TPC],
                             wp[:, ic * D + oc * 128: ic * D + oc * 128 + 128],
                             hbf[:, ic * TPC:(ic + 1) * TPC],
                             start=(ic == 0), stop=(ic == 3))

            def v_o(v, oc=oc):
                sl = sbx[:, oc * TPC:(oc + 1) * TPC]
                v.scalar_tensor_tensor(out=sl, in0=ps_x[:, 0:TPC],
                                       scalar=osc[:, oc:oc + 1], in1=sl,
                                       op0=OP.mult, op1=OP.add)
            st.stage(tensor=t_o)
            st.stage(vector=v_o)

        def st_b2(e, b=b):
            return dma(e, p, "sync",
                       xo_d[b], sbx[:, :])
        st.stage(sync=st_b2)




def build_l3attn():
    p = Prog("l3attn")
    p.pf = "a3_"
    xs_d = p.inp("xs", (B, 128, DCH * TPC))
    xo_d = p.outp("xo", (B, 128, DCH * TPC))
    p.start_block()
    st = Stager(p)
    emit_l3attn(p, st, xs_d, xo_d)
    st.stage(sync=lambda e: None)
    return p

def maps_l3attn(I):
    CH2 = 2
    wm = np.asarray(I["L3_attn_W_m"], np.float32)
    wmT = _t(wm)
    idx = np.concatenate([np.arange(256) * 4 + comp for comp in range(4)])
    wmTp = np.ascontiguousarray(wmT[:, idx])
    wp = np.asarray(I["L3_attn_W_proj"], np.float32)
    wpT = _t(wp)
    ridx = np.concatenate([np.arange(256) * 2 + h for h in range(2)])
    wpTp = np.ascontiguousarray(wpT[ridx, :])
    v_vec = np.asarray(I["L3_attn_v_vec"], np.float32)[:, :, 0]
    vc = np.zeros((128, 2 * CH2), np.float32)
    for cc in range(CH2):
        vc[:, cc] = v_vec[cc * 128:(cc + 1) * 128, 0]
        vc[:, CH2 + cc] = v_vec[cc * 128:(cc + 1) * 128, 1]
    base = {
        "wmT": wmTp,
        "wpT": wpTp,
        "vc": vc,
        "osc": _col(I["L3_attn_out_scale"]),
    }
    in_maps = []
    for c in range(NCORES):
        m = dict(base)
        oh = np.zeros((128, NCORES), np.float32)
        if c > 0:
            oh[:, c - 1] = 1.0
        m["ohc"] = oh
        m["idc"] = np.full((128, 1), 1.0 if c == 0 else 0.0, np.float32)
        in_maps.append({"a3_" + k: v for k, v in m.items()})
    return in_maps


def run_l3attn(stream_shards, I):
    p = build_l3attn()
    in_maps = maps_l3attn(I)
    for c in range(NCORES):
        in_maps[c]["a3_xs"] = _to_qmaj(stream_shards[c])
    res = run_bass_kernel_spmd(p.nc, in_maps, core_ids=list(range(NCORES)))
    out = [_from_qmaj(res.results[c]["a3_xo"]) for c in range(NCORES)]
    p.close()
    return out




DI = 2048
NHM = 32
HDM = 64
NST = 128
CVD = 2304
DIP = 4384
CCH = 18
MCH = 128
NMC = TPC // MCH


def emit_l0attn(p, st, us_d, xo_d):
    p.pf = "a0_"
    ipT_d = p.inp("ipT", (D, DIP))
    cvw_d = p.inp("cvw", (128, CCH * 4))
    cvb_d = p.inp("cvb", (128, CCH))
    dtb_d = p.inp("dtb", (128, 1))
    alog_d = p.inp("alog", (3, 32))
    dcol_d = p.inp("dcol", (128, 16))
    nw_d = p.inp("nw", (128, 16))
    opT_d = p.inp("opT", (DI, D))
    ohc_d = p.inp("ohc", (128, NCORES))

    op_dram = p.dram("op_dram", (DCH, 128, 16 * 128), BF16)
    ip_bf = p.dram("ip_bf", (128, DCH, DIP), BF16)
    yz_dram = p.dram("yz_dram", (B, 2, 128, 16 * TPC), BF16)
    acg_dram = p.dram("acg_dram", (B, NMC, 32, MCH), F32)
    acl_dram = p.dram("acl_dram", (B, NMC, 32, MCH), F32)
    h_loc = p.dram("h_loc", (B, 128, DI), BF16)
    h_all = p.dram("h_all", (NCORES, B, 128, DI), BF16)
    dk_loc = p.dram("dk_loc", (NHM, B), F32)
    dk_all = p.dram("dk_all", (NCORES, NHM, B), F32)

    HW = 1536
    sbx = p.sb("sbx", (128, DCH * TPC), F32)
    xn = p.sb("xn", (128, DCH * (TPC + 3)), BF16)
    ipw = p.sb("ipw", (128, DCH * HW), BF16)
    opb = p.sb("opb", (128, 16 * 128), BF16)
    cvin = p.sb("cvin", (128, CCH * (TPC + 3)), BF16)
    xfm = p.sb("xfm", (128, 16 * TPC), BF16)
    xtm = p.sb("xtm", (128, 16 * TPC), BF16)
    bct = p.sb("bct", (128, 2 * TPC), BF16)
    call = p.sb("call", (128, B * TPC), BF16)
    ysb = cvin
    gsb = ysb
    zs3 = xfm
    hst = p.sb("hst", (128, DI), F32)
    hnw = p.sb("hnw", (128, DI), F32)
    hjb = p.sb("hjb", (128, DI), BF16)
    dtr = p.sb("dtr", (128, TPC), F32)
    dts = p.sb("dts", (128, 3 * TPC), F32)
    acg = p.sb("acg", (128, TPC), F32)
    offc = p.sb("offc", (128, 2), F32)
    acT = p.sb("acT", (128, NHM), F32)
    dtT = p.sb("dtT", (128, NHM), F32)
    wvt = p.sb("wvt", (128, 3 * NHM), F32)
    rowA = p.sb("rowA", (1, NHM * MCH + 256), F32)
    rowE = p.sb("rowE", (1, NHM * MCH), BF16)
    onesbf = p.sb("onesbf", (1, 128), BF16)
    tri = p.sb("tri", (128, 128), F32)
    iw = p.sb("iw", (128, 128), I32)
    idt = p.sb("idt", (128, 128), BF16)
    idt32 = p.sb("idt32m", (128, 128), F32)
    gwk = p.sb("gwk", (128, 8 * MCH), F32)
    gbf = p.sb("gbf", (128, 2 * MCH), BF16)
    xw = p.sb("xw", (128, DI), BF16)
    dec = p.sb("dec", (128, NHM), F32)
    dkb = p.sb("dkb", (32, B), F32)
    drow = p.sb("drow", (1, NCORES * NHM * B), F32)
    acol = p.sb("acol", (128, 4), F32)
    srow = p.sb("srow", (1, 4 * TPC), F32)
    lnt = p.sb("lnt", (128, 2 * TPC), F32)
    cone = p.sb("cone", (128, 1), BF16)
    c2 = p.sb("c2", (128, 1), BF16)
    onesrow = p.sb("onesrow", (1, 128), F32)
    eps_col = p.sb("eps_col", (128, 1), F32)
    eps_col2 = p.sb("eps_col2", (128, 1), F32)
    cvwc = p.sb("cvwc", (128, CCH * 4), F32)
    cvbc = p.sb("cvbc", (128, CCH), F32)
    dtbc = p.sb("dtbc", (128, 1), F32)
    dcc = p.sb("dcc", (128, 16), F32)
    nwc = p.sb("nwc", (128, 16), F32)
    ohc = p.sb("ohc_s", (128, NCORES), F32)
    alg = p.sb("alg", (3, 32), F32)
    onec = p.sb("onec", (128, 1), F32)

    ps_s = [p.ps("ps_s%d" % i) for i in range(4)]
    ps_x = p.ps("ps_x")
    ps_g = p.ps("ps_g")
    ps_t = p.ps("ps_t", (128, 128), BF16)
    ps_t2 = p.ps("ps_t2", (128, 128), BF16)
    sqw = xtm
    env = dict(stream=None, xn=xn, sq=None, sqw=sqw, srow=srow, lnt=lnt,
               cone=cone, cone2=c2, onesrow=onesrow, ps_stat=ps_s[0],
               ps_b0=ps_s[1], ps_b1=ps_s[2], eps_col=eps_col, eps_col2=eps_col2)


    TP3 = TPC + 3

    def ld0(e):
        n = dma(e, p, "sync", cvwc[:, :], cvw_d[:, :])
        n += dma(e, p, "sync", cvbc[:, :], cvb_d[:, :])
        n += dma(e, p, "sync", dtbc[:, :], dtb_d[:, :])
        n += dma(e, p, "sync", dcc[:, :], dcol_d[:, :])
        n += dma(e, p, "sync", nwc[:, :], nw_d[:, :])
        n += dma(e, p, "sync", ohc[:, :], ohc_d[:, :])
        n += dma(e, p, "sync", alg[:, :], alog_d[:, :])
        return n

    def iv(v):
        v.memset(cone[:, :], 1.0 / 1024.0)
        v.memset(c2[:, :], 1.0 / DI)
        v.memset(onesrow[:, :], 1.0)
        v.memset(eps_col[:, :], 1e-5)
        v.memset(eps_col2[:, :], 1e-5)
        v.memset(onec[:, :], 1.0)
        v.memset(rowA[0:1, NHM * MCH:NHM * MCH + 128], 1.0)
        v.memset(rowA[0:1, NHM * MCH + 128:NHM * MCH + 256], -1.0)
        v.memset(onesbf[0:1, :], 1.0)

    def ig(g):
        g.iota(iw[:, :], pattern=[[1, 128]], base=0, channel_multiplier=-1)
    st.stage(sync=ld0, vector=iv, gpsimd=ig)

    def v_tri(v):
        v.tensor_scalar(out=tri[:, :], in0=iw[:, :], scalar1=0.0, scalar2=None,
                        op0=OP.is_ge, op1=OP.bypass)
        v.drain()
        v.tensor_scalar(out=tri[:, :], in0=tri[:, :], scalar1=-1.0, scalar2=1e30,
                        op0=OP.add, op1=OP.mult)
        v.drain()
        v.tensor_scalar(out=idt[:, :], in0=iw[:, :], scalar1=0.0, scalar2=None,
                        op0=OP.is_equal, op1=OP.bypass)
        v.tensor_scalar(out=idt32[:, :], in0=iw[:, :], scalar1=0.0, scalar2=None,
                        op0=OP.is_equal, op1=OP.bypass)
    st.stage(vector=v_tri)

    def s_a(s):
        s.activation(out=alg[:, :], in_=alg[:, :], func=AF.Exp)
    st.stage(scalar=s_a)

    def v_a0(v):
        v.memset(acol[:, 1:2], -1.0 / 3.0)
    st.stage(vector=v_a0)

    def t_a(t):
        t.matmul(ps_x[0:32, 0:1], alg[0:3, :], acol[0:3, 1:2], start=True, stop=True)
    st.stage(tensor=t_a)

    def v_a1(v):
        v.tensor_copy(out=acol[:, 0:1], in_=ps_x[:, 0:1])
    st.stage(vector=v_a1)

    HH = DIP // 4
    ipchunks = [(ic, hf) for ic in range(DCH) for hf in range(4)]
    for k in range(len(ipchunks) + 2):
        em = {}

        def sy_ip(e, k=k):
            n = 0
            if k < len(ipchunks):
                ic, hf = ipchunks[k]
                n += dma(e, p, "sync", sbx[:, (k % 2) * HH:(k % 2) * HH + HH],
                         ipT_d[ic * 128:(ic + 1) * 128, hf * HH:(hf + 1) * HH])
            if k >= 2:
                ic, hf = ipchunks[k - 2]
                n += dma(e, p, "sync", ip_bf[:, ic, hf * HH:(hf + 1) * HH],
                         xn[:, (k % 2) * HH:(k % 2) * HH + HH])
            return n
        em["sync"] = sy_ip
        if 1 <= k <= len(ipchunks):
            def s_ip(s, k=k):
                s.activation(out=xn[:, ((k - 1) % 2) * HH:((k - 1) % 2) * HH + HH],
                             in_=sbx[:, ((k - 1) % 2) * HH:((k - 1) % 2) * HH + HH],
                             func=AF.Identity)
            em["scalar"] = s_ip
        st.stage(**em)

    for k in range(18):
        em = {}

        def sy_op(e, k=k):
            n = 0
            if k < 16:
                n += dma(e, p, "sync", sbx[:, (k % 2) * 1024:(k % 2) * 1024 + D],
                         opT_d[k * 128:(k + 1) * 128, :])
            if k >= 2:
                ic = k - 2
                n += dma(e, p, "sync",
                         op_dram.rearrange("o q i -> q o i")[
                             :, :, ic * 128:(ic + 1) * 128],
                         opb[:, (ic % 2) * 1024:(ic % 2) * 1024 + D]
                         .rearrange("q (o w) -> q o w", o=DCH))
            return n
        em["sync"] = sy_op
        if 1 <= k <= 16:
            ic = k - 1

            def s_op(s, ic=ic):
                s.activation(out=opb[:, (ic % 2) * 1024:(ic % 2) * 1024 + D],
                             in_=sbx[:, (ic % 2) * 1024:(ic % 2) * 1024 + D],
                             func=AF.Identity)
            em["scalar"] = s_op
        st.stage(**em)

    for b in range(B):
        def ld_b(e, b=b):
            return dma(e, p, "sync",
                       sbx[:, :].rearrange("q (c t) -> q c t", c=DCH),
                       us_d[b].rearrange("q (c t) -> q c t", t=TP3)[:, :, 3:TP3])
        st.stage(sync=ld_b)
        emit_layernorm(st, env, 0,
                       x_at=lambda ch: sbx[:, ch * TPC:(ch + 1) * TPC],
                       xn_at=lambda ch: xn[:, ch * TP3:ch * TP3 + TPC])
        def ld_b3(e, b=b):
            return dma(e, p, "sync",
                       lnt[:, 0:DCH * 3].rearrange("q (c t) -> q c t", c=DCH),
                       us_d[b].rearrange("q (c t) -> q c t", t=TP3)[:, :, 0:3])
        st.stage(sync=ld_b3)
        emit_layernorm(st, env, 0, T=3,
                       x_at=lambda ch: lnt[:, ch * 3:(ch + 1) * 3],
                       xn_at=lambda ch: xn[:, ch * TP3 + TPC:ch * TP3 + TPC + 3])

        def xnat(ic):
            return xn[:, ic * TP3:ic * TP3 + TPC]

        def xnat3(ic):
            return xn[:, ic * TP3 + TPC:ic * TP3 + TPC + 3]

        zst = bct
        for third in range(3):
            base_oc = third * 12
            ocs = list(range(base_oc, min(base_oc + 12, 35)))

            def ld_ip(e, third=third):
                lo = third * HW
                w = min(HW, DIP - lo)
                return dma(e, p, "sync",
                           ipw[:, :].rearrange("q (ic o) -> q ic o", o=HW)[
                               :, :, 0:w],
                           ip_bf[:, :, lo:lo + w])
            st.stage(sync=ld_ip)

            n_oc = len(ocs)
            for k in range(n_oc + 2):
                em = {}
                if k < n_oc:
                    oc = ocs[k]
                    po = 128 if oc < 34 else 32
                    ocl = oc - base_oc
                    pm = ps_x if k % 2 == 0 else ps_s[0]
                    pl = ps_s[3] if k % 2 == 0 else ps_s[1]

                    def t_p(t, oc=oc, po=po, ocl=ocl, pm=pm, pl=pl):
                        for ic in range(DCH):
                            t.matmul(pm[0:po, 0:TPC],
                                     ipw[:, ic * HW + ocl * 128:
                                         ic * HW + ocl * 128 + po],
                                     xnat(ic), start=(ic == 0),
                                     stop=(ic == DCH - 1))
                        if 16 <= oc < 34:
                            for ic in range(DCH):
                                t.matmul(pl[0:po, 0:3],
                                         ipw[:, ic * HW + ocl * 128:
                                             ic * HW + ocl * 128 + po],
                                         xnat3(ic), start=(ic == 0),
                                         stop=(ic == DCH - 1))
                    em["tensor"] = t_p
                if 1 <= k <= n_oc:
                    oc = ocs[k - 1]
                    pm = ps_x if (k - 1) % 2 == 0 else ps_s[0]
                    pl = ps_s[3] if (k - 1) % 2 == 0 else ps_s[1]

                    def v_e(v, oc=oc, pm=pm, pl=pl, k=k):
                        if oc < 16:
                            v.tensor_copy(out=zst[:, ((k - 1) % 2) * TPC:
                                                  ((k - 1) % 2) * TPC + TPC],
                                          in_=pm[:, 0:TPC])
                        elif oc < 34:
                            cc = oc - 16
                            v.tensor_copy(out=cvin[:, cc * TP3 + 3:
                                                   cc * TP3 + 3 + TPC],
                                          in_=pm[:, 0:TPC])
                            v.tensor_copy(out=cvin[:, cc * TP3:cc * TP3 + 3],
                                          in_=pl[:, 0:3])
                        else:
                            v.tensor_copy(out=dtr[0:32, 0:TPC],
                                          in_=pm[0:32, 0:TPC])
                    em["vector"] = v_e
                if 2 <= k and k - 2 < n_oc and ocs[k - 2] < 16:
                    oc2 = ocs[k - 2]

                    def st_z(e, oc2=oc2, k=k, b=b):
                        return dma(e, p, "sync",
                                   yz_dram[b, 1][:, oc2 * TPC:(oc2 + 1) * TPC],
                                   zst[:, ((k - 2) % 2) * TPC:
                                       ((k - 2) % 2) * TPC + TPC])
                    em["sync"] = st_z
                st.stage(**em)

        def s_sp(s):
            s.activation(out=dts[0:32, 0:TPC], in_=dtr[0:32, 0:TPC], func=AF.Abs,
                         bias=dtbc[0:32, 0:1])
            s.activation(out=dts[0:32, TPC:2 * TPC], in_=dtr[0:32, 0:TPC],
                         func=AF.Relu, bias=dtbc[0:32, 0:1])
            s.drain()
            s.activation(out=dts[0:32, 0:TPC], in_=dts[0:32, 0:TPC], func=AF.Exp,
                         scale=-1.0)
            s.drain()
            s.activation(out=dts[0:32, 0:TPC], in_=dts[0:32, 0:TPC], func=AF.Ln,
                         bias=onec[0:32, 0:1])
        st.stage(scalar=s_sp)

        def v_sp(v):
            v.tensor_tensor(out=dts[0:32, 0:TPC], in0=dts[0:32, 0:TPC],
                            in1=dts[0:32, TPC:2 * TPC], op=OP.add)
            v.drain()
            v.tensor_scalar(out=dts[0:32, TPC:2 * TPC], in0=dts[0:32, 0:TPC],
                            scalar1=acol[0:32, 0:1], scalar2=None,
                            op0=OP.mult, op1=OP.bypass)
            v.drain()
            v.memset(offc[0:32, 0:1], 0.0)
        st.stage(vector=v_sp)

        def v_ac(v, b=b):
            for c_ in range(NMC):
                v.tensor_tensor_scan(out=dts[0:32, 2 * TPC + c_ * MCH:
                                             2 * TPC + (c_ + 1) * MCH],
                                     data0=dts[0:32, TPC + c_ * MCH:
                                               TPC + (c_ + 1) * MCH],
                                     data1=dts[0:32, TPC + c_ * MCH:
                                               TPC + (c_ + 1) * MCH],
                                     initial=0.0, op0=OP.add, op1=OP.bypass)
            v.drain()
            for c_ in range(NMC):
                v.tensor_scalar(out=acg[0:32, c_ * MCH:(c_ + 1) * MCH],
                                in0=dts[0:32, 2 * TPC + c_ * MCH:
                                        2 * TPC + (c_ + 1) * MCH],
                                scalar1=offc[0:32, 0:1], scalar2=None,
                                op0=OP.add, op1=OP.bypass)
                v.drain()
                v.tensor_copy(out=offc[0:32, 1:2],
                              in_=acg[0:32, (c_ + 1) * MCH - 1:(c_ + 1) * MCH])
                v.drain()
                v.tensor_copy(out=offc[0:32, 0:1], in_=offc[0:32, 1:2])
                v.drain()
        st.stage(vector=v_ac)

        def s_dk(s, b=b):
            s.activation(out=dkb[0:32, b:b + 1], in_=acg[0:32, TPC - 1:TPC],
                         func=AF.Exp)
        st.stage(scalar=s_dk)

        def st_acg(e, b=b):
            n = dma(e, p, "sync", acg_dram[b].rearrange("c h t -> h c t"),
                    acg[0:32, :].rearrange("h (c t) -> h c t", t=MCH))
            n += dma(e, p, "sync", acl_dram[b].rearrange("c h t -> h c t"),
                     dts[0:32, 2 * TPC:3 * TPC].rearrange("h (c t) -> h c t",
                                                          t=MCH))
            return n
        st.stage(sync=st_acg)

        cbufs = (lnt, sbx)
        for k in range(CCH + 1):
            em = {}
            if k < CCH:
                def v_cv(v, cc=k, buf=cbufs[k % 2]):
                    base = cc * TP3
                    v.tensor_scalar(out=buf[:, 0:TPC],
                                    in0=cvin[:, base:base + TPC],
                                    scalar1=cvwc[:, cc * 4:cc * 4 + 1],
                                    scalar2=None, op0=OP.mult, op1=OP.bypass)
                    v.drain()
                    for j in range(1, 4):
                        v.scalar_tensor_tensor(
                            out=buf[:, TPC:2 * TPC] if j % 2 else buf[:, 0:TPC],
                            in0=cvin[:, base + j:base + j + TPC],
                            scalar=cvwc[:, cc * 4 + j:cc * 4 + j + 1],
                            in1=buf[:, 0:TPC] if j % 2 else buf[:, TPC:2 * TPC],
                            op0=OP.mult, op1=OP.add)
                        v.drain()
                em["vector"] = v_cv
            if k >= 1:
                def s_cv(s, cc=k - 1, buf=cbufs[(k - 1) % 2]):
                    dst = (xfm[:, cc * TPC:(cc + 1) * TPC] if cc < 16 else
                           bct[:, (cc - 16) * TPC:(cc - 15) * TPC])
                    s.activation(out=dst, in_=buf[:, TPC:2 * TPC], func=AF.Silu,
                                 bias=cvbc[:, cc:cc + 1])
                em["scalar"] = s_cv
            st.stage(**em)

        def v_ckeep(v, b=b):
            v.tensor_copy(out=call[:, b * TPC:(b + 1) * TPC],
                          in_=bct[:, TPC:2 * TPC])
        st.stage(vector=v_ckeep)

        pbanks = (ps_t, ps_t2)
        trs = [(cc, tq) for cc in range(16) for tq in range(4)]
        for k in range(len(trs) + 1):
            em = {}
            if k < len(trs):
                cc, tq = trs[k]

                def t_tr(t, cc=cc, tq=tq, bk=pbanks[k % 2]):
                    t.transpose(bk[:, 0:128],
                                xfm[:, cc * TPC + tq * 128:
                                    cc * TPC + tq * 128 + 128],
                                idt[:, :])
                em["tensor"] = t_tr
            if k >= 1:
                cc, tq = trs[k - 1]

                def v_tr(v, cc=cc, tq=tq, bk=pbanks[(k - 1) % 2]):
                    v.tensor_copy(out=xtm[:, tq * 2048 + cc * 128:
                                          tq * 2048 + cc * 128 + 128],
                                  in_=bk[:, 0:128])
                em["vector"] = v_tr
            st.stage(**em)

        def v_h0(v):
            v.memset(hst[:, :], 0.0)
        st.stage(vector=v_h0)

        for c_ in range(NMC):
            def st_r(e, c_=c_, b=b):
                return dma(e, p, "sync", rowA[0:1, 0:NHM * MCH],
                           acl_dram[b, c_].rearrange("h t -> (h t)"))
            st.stage(sync=st_r)

            def s_er(s):
                s.activation(out=rowE[0:1, :], in_=rowA[0:1, 0:NHM * MCH],
                             func=AF.Exp)

            def t_pre(t, c_=c_):
                t.transpose(ps_t[:, 0:128], bct[:, c_ * MCH:(c_ + 1) * MCH],
                            idt[:, :])
                t.matmul(ps_s[3][:, 0:MCH], bct[:, c_ * MCH:(c_ + 1) * MCH],
                         bct[:, TPC + c_ * MCH:TPC + (c_ + 1) * MCH],
                         start=True, stop=True)
                t.transpose(ps_s[0][:, 0:32], dts[0:32, 2 * TPC + c_ * MCH:
                                                  2 * TPC + (c_ + 1) * MCH],
                            idt32[0:32, 0:32])
                t.transpose(ps_s[1][:, 0:32], dts[0:32, c_ * MCH:(c_ + 1) * MCH],
                            idt32[0:32, 0:32])
                t.matmul(ps_s[2][:, 0:32], onesrow[0:1, 0:128],
                         rowA[0:1, 0:NHM * MCH].rearrange(
                             "o (h t) -> o h t", t=MCH)[:, :, MCH - 1],
                         start=True, stop=True)
            st.stage(scalar=s_er, tensor=t_pre)

            def v_pre(v):
                v.tensor_copy(out=gbf[:, MCH:2 * MCH], in_=ps_t[:, 0:128])
                v.tensor_copy(out=acT[:, :], in_=ps_s[0][:, 0:32])
                v.tensor_copy(out=dtT[:, :], in_=ps_s[1][:, 0:32])
                v.drain()
                v.tensor_tensor(out=wvt[:, 0:NHM], in0=ps_s[2][:, 0:32],
                                in1=acT[:, :], op=OP.subtract)
            st.stage(vector=v_pre)

            def s_wv(s):
                s.activation(out=wvt[:, NHM:2 * NHM], in_=wvt[:, 0:NHM],
                             func=AF.Exp)
                s.activation(out=dec[:, :], in_=ps_s[2][:, 0:32], func=AF.Exp)
            st.stage(scalar=s_wv)

            def v_wv2(v):
                v.tensor_tensor(out=wvt[:, 2 * NHM:3 * NHM],
                                in0=wvt[:, NHM:2 * NHM], in1=dtT[:, :], op=OP.mult)
            st.stage(vector=v_wv2)

            for g in range(4):
                hs = g * 8
                W = NHM * MCH

                def t_g1(t, hs=hs):
                    for bank, p4 in ((ps_s[0], 0), (ps_s[1], 1)):
                        h4 = hs + p4 * 4
                        for j in range(4):
                            h = h4 + j
                            t.matmul(bank[:, j * MCH:(j + 1) * MCH],
                                     rowA[0:1, h * MCH:(h + 1) * MCH],
                                     rowA[0:1, W + 128:W + 256],
                                     start=True, stop=False)
                            t.matmul(bank[:, j * MCH:(j + 1) * MCH],
                                     rowA[0:1, W:W + 128],
                                     rowA[0:1, h * MCH:(h + 1) * MCH],
                                     start=False, stop=True)
                    t.matmul(ps_s[2][:, 0:512], onesbf[0:1, 0:128],
                             rowE[0:1, hs * MCH:hs * MCH + 512],
                             start=True, stop=True)
                    t.matmul(ps_g[:, 0:512], onesbf[0:1, 0:128],
                             rowE[0:1, (hs + 4) * MCH:(hs + 4) * MCH + 512],
                             start=True, stop=True)
                st.stage(tensor=t_g1)

                def v_g2(v):
                    for j in range(8):
                        bank = ps_s[0] if j < 4 else ps_s[1]
                        v.tensor_tensor(out=gwk[:, j * MCH:(j + 1) * MCH],
                                        in0=bank[:, (j % 4) * MCH:
                                                 (j % 4 + 1) * MCH],
                                        in1=tri[:, :], op=OP.min)
                st.stage(vector=v_g2)

                def s_g3(s):
                    s.activation(out=sbx[:, 1024:1536], in_=gwk[:, 0:512],
                                 func=AF.Exp)
                    s.activation(out=sbx[:, 1536:2048], in_=gwk[:, 512:1024],
                                 func=AF.Exp)
                st.stage(scalar=s_g3)

                def v_g4(v, hs=hs, c_=c_):
                    for j in range(8):
                        v.tensor_tensor(out=gwk[:, j * MCH:(j + 1) * MCH],
                                        in0=sbx[:, 1024 + j * MCH:
                                                1024 + (j + 1) * MCH],
                                        in1=ps_s[3][:, 0:MCH], op=OP.mult)
                    for j in range(8):
                        pm = ps_s[2] if j < 4 else ps_g
                        v.tensor_tensor(out=sbx[:, j * MCH:(j + 1) * MCH],
                                        in0=bct[:, TPC + c_ * MCH:
                                                TPC + (c_ + 1) * MCH],
                                        in1=pm[:, (j % 4) * MCH:
                                               (j % 4 + 1) * MCH], op=OP.mult)
                    v.drain()
                    for j in range(8):
                        v.tensor_scalar(out=xw[:, j * MCH:(j + 1) * MCH],
                                        in0=gwk[:, j * MCH:(j + 1) * MCH],
                                        scalar1=dtT[:, hs + j:hs + j + 1],
                                        scalar2=None, op0=OP.mult, op1=OP.bypass)
                st.stage(vector=v_g4)

                def t_g5(t, hs=hs, c_=c_):
                    for j in range(8):
                        h = hs + j
                        out = ps_x[(h % 2) * 64:(h % 2) * 64 + 64,
                                   ((h // 2) % 4) * MCH:((h // 2) % 4 + 1) * MCH]
                        t.matmul(out, hst[:, h * 64:(h + 1) * 64],
                                 sbx[:, j * MCH:(j + 1) * MCH],
                                 start=True, stop=False)
                        t.matmul(out, xtm[:, c_ * 2048 + h * 64:
                                          c_ * 2048 + (h + 1) * 64],
                                 xw[:, j * MCH:(j + 1) * MCH],
                                 start=False, stop=True)
                st.stage(tensor=t_g5)

                def v_g6(v, hs=hs, c_=c_):
                    for jp in range(4):
                        hp2 = hs // 2 + jp
                        sl = hp2 * TPC + c_ * MCH
                        v.scalar_tensor_tensor(out=ysb[:, sl:sl + MCH],
                                               in0=xfm[:, sl:sl + MCH],
                                               scalar=dcc[:, hp2:hp2 + 1],
                                               in1=ps_x[:, jp * MCH:
                                                       (jp + 1) * MCH],
                                               op0=OP.mult, op1=OP.add)
                st.stage(vector=v_g6)

            def v_xw(v, c_=c_):
                for h in range(NHM):
                    v.tensor_scalar(out=xw[:, h * 64:(h + 1) * 64],
                                    in0=xtm[:, c_ * 2048 + h * 64:
                                            c_ * 2048 + (h + 1) * 64],
                                    scalar1=wvt[:, 2 * NHM + h:2 * NHM + h + 1],
                                    scalar2=None, op0=OP.mult, op1=OP.bypass)
            st.stage(vector=v_xw)

            for q4 in range(4):
                def t_hd(t, q4=q4):
                    t.matmul(ps_s[q4][:, 0:512], gbf[:, MCH:2 * MCH],
                             xw[:, q4 * 512:(q4 + 1) * 512], start=True, stop=True)
                st.stage(tensor=t_hd)

            def v_hu(v):
                for h in range(NHM):
                    v.tensor_scalar(out=hnw[:, h * 64:(h + 1) * 64],
                                    in0=hst[:, h * 64:(h + 1) * 64],
                                    scalar1=dec[:, h:h + 1], scalar2=None,
                                    op0=OP.mult, op1=OP.bypass)
                v.drain()
                for q4 in range(4):
                    v.tensor_tensor(out=hnw[:, q4 * 512:(q4 + 1) * 512],
                                    in0=hnw[:, q4 * 512:(q4 + 1) * 512],
                                    in1=ps_s[q4][:, 0:512], op=OP.add)
                v.drain()
                for q4 in range(4):
                    v.tensor_copy(out=hst[:, q4 * 512:(q4 + 1) * 512],
                                  in_=hnw[:, q4 * 512:(q4 + 1) * 512])
            st.stage(vector=v_hu)

        def g_hl(e, b=b):
            return dma(e, p, "gpsimd", h_loc[b], hst[:, :])

        def st_y(e, b=b):
            return dma(e, p, "sync", yz_dram[b, 0], ysb[:, 0:16 * TPC])
        st.stage(sync=st_y, gpsimd=g_hl)

    def st_dk(e):
        return dma(e, p, "sync", dk_loc, dkb[0:32, :])
    st.stage(sync=st_dk)

    def g_cc(g):
        cs = p.sem("cc")
        g.collective_compute("AllGather", OP.bypass,
                             replica_groups=[list(range(NCORES))],
                             ins=[h_loc[:, :, :].opt()],
                             outs=[h_all[:, :, :, :].opt()]).then_inc(cs[0], 1)
        cs[1] += 1
        g.collective_compute("AllGather", OP.bypass,
                             replica_groups=[list(range(NCORES))],
                             ins=[dk_loc[:, :].opt()],
                             outs=[dk_all[:, :, :].opt()]).then_inc(cs[0], 1)
        cs[1] += 1
        g.wait_ge(cs[0], cs[1])
    st.stage(gpsimd=g_cc)

    def ld_dk(e):
        return dma(e, p, "sync", drow[0:1, :],
                   dk_all.rearrange("c h b -> (c h b)"))
    st.stage(sync=ld_dk)

    for b in range(B):
        def ld_yz(e, b=b):
            n = dma(e, p, "sync", ysb[:, 0:16 * TPC], yz_dram[b, 0])
            n += dma(e, p, "sync", zs3[:, :], yz_dram[b, 1])
            n += dma(e, p, "sync",
                     sbx[:, :].rearrange("q (c t) -> q c t", c=DCH),
                     us_d[b].rearrange("q (c t) -> q c t", t=TP3)[:, :, 3:TP3])
            return n
        st.stage(sync=ld_yz)

        def v_hi0(v):
            v.memset(hst[:, :], 0.0)
            v.memset(hnw[:, :], 0.0)
        st.stage(vector=v_hi0)

        for j in range(NCORES - 1):
            def ld_hj(e, j=j, b=b):
                return dma(e, p, "sync", hjb[:, :], h_all[j, b])
            st.stage(sync=ld_hj)

            def t_db(t, j=j, b=b):
                t.matmul(ps_s[2][:, 0:NHM], onesrow[0:1, 0:128],
                         drow[0:1, :].rearrange("o (c h b) -> o c h b",
                                                h=NHM, b=B)[:, j, :, b],
                         start=True, stop=True)
            st.stage(tensor=t_db)

            def v_hi(v, j=j):
                v.tensor_copy(out=dec[:, :], in_=ps_s[2][:, 0:NHM])
                v.drain()
                for h in range(NHM):
                    v.tensor_scalar(out=hst[:, h * 64:(h + 1) * 64],
                                    in0=hst[:, h * 64:(h + 1) * 64],
                                    scalar1=dec[:, h:h + 1], scalar2=None,
                                    op0=OP.mult, op1=OP.bypass)
                v.drain()
                v.tensor_tensor(out=hst[:, :], in0=hst[:, :], in1=hjb[:, :],
                                op=OP.add)
                v.drain()
                v.scalar_tensor_tensor(out=hnw[:, :], in0=hst[:, :],
                                       scalar=ohc[:, j:j + 1], in1=hnw[:, :],
                                       op0=OP.mult, op1=OP.add)
                v.drain()
            st.stage(vector=v_hi)

        for c_ in range(NMC):
            def ld_r(e, c_=c_, b=b):
                return dma(e, p, "sync", rowA[0:1, 0:NHM * MCH],
                           acg_dram[b, c_].rearrange("h t -> (h t)"))
            st.stage(sync=ld_r)

            def s_er2(s):
                s.activation(out=rowE[0:1, :], in_=rowA[0:1, 0:NHM * MCH],
                             func=AF.Exp)
            st.stage(scalar=s_er2)

            for g in range(4):
                hs = g * 8

                def t_gv2(t, hs=hs):
                    t.matmul(ps_s[2][:, 0:512], onesbf[0:1, 0:128],
                             rowE[0:1, hs * MCH:hs * MCH + 512],
                             start=True, stop=True)
                    t.matmul(ps_g[:, 0:512], onesbf[0:1, 0:128],
                             rowE[0:1, (hs + 4) * MCH:(hs + 4) * MCH + 512],
                             start=True, stop=True)
                st.stage(tensor=t_gv2)

                def v_gv2(v, c_=c_, b=b):
                    for j in range(8):
                        pm = ps_s[2] if j < 4 else ps_g
                        v.tensor_tensor(out=dts[:, j * MCH:(j + 1) * MCH],
                                        in0=call[:, b * TPC + c_ * MCH:
                                                b * TPC + (c_ + 1) * MCH],
                                        in1=pm[:, (j % 4) * MCH:
                                               (j % 4 + 1) * MCH], op=OP.mult)
                st.stage(vector=v_gv2)

                def t_yc(t, hs=hs):
                    for j in range(8):
                        h = hs + j
                        t.matmul(ps_x[(h % 2) * 64:(h % 2) * 64 + 64,
                                      ((h // 2) % 4) * MCH:
                                      ((h // 2) % 4 + 1) * MCH],
                                 hnw[:, h * 64:(h + 1) * 64],
                                 dts[:, j * MCH:(j + 1) * MCH],
                                 start=True, stop=True)
                st.stage(tensor=t_yc)

                def v_yc(v, hs=hs, c_=c_):
                    for jp in range(4):
                        hp2 = hs // 2 + jp
                        sl = hp2 * TPC + c_ * MCH
                        v.tensor_tensor(out=ysb[:, sl:sl + MCH],
                                        in0=ysb[:, sl:sl + MCH],
                                        in1=ps_x[:, jp * MCH:(jp + 1) * MCH],
                                        op=OP.add)
                st.stage(vector=v_yc)

        emit_layernorm(st, env, 0, rms=True, nchunks=16,
                       x_at=lambda ch: ysb[:, ch * TPC:(ch + 1) * TPC],
                       xn_at=lambda ch: gsb[:, ch * TPC:(ch + 1) * TPC])

        def v_nwm(v):
            for ch in range(16):
                v.tensor_scalar(out=gsb[:, ch * TPC:(ch + 1) * TPC],
                                in0=gsb[:, ch * TPC:(ch + 1) * TPC],
                                scalar1=nwc[:, ch:ch + 1], scalar2=None,
                                op0=OP.mult, op1=OP.bypass)
        st.stage(vector=v_nwm)

        def s_sz(s):
            for ch in range(16):
                s.activation(out=zs3[:, ch * TPC:(ch + 1) * TPC],
                             in_=zs3[:, ch * TPC:(ch + 1) * TPC], func=AF.Silu)
        st.stage(scalar=s_sz)

        def v_gt(v):
            for ch in range(16):
                v.tensor_tensor(out=gsb[:, ch * TPC:(ch + 1) * TPC],
                                in0=gsb[:, ch * TPC:(ch + 1) * TPC],
                                in1=zs3[:, ch * TPC:(ch + 1) * TPC], op=OP.mult)
        st.stage(vector=v_gt)

        for oc in range(DCH):
            def ld_ob(e, oc=oc):
                return dma(e, p, "sync", opb[:, :], op_dram[oc])
            st.stage(sync=ld_ob)

            def t_o(t, oc=oc):
                for ic in range(16):
                    t.matmul(ps_x[:, 0:TPC], opb[:, ic * 128:(ic + 1) * 128],
                             gsb[:, ic * TPC:(ic + 1) * TPC],
                             start=(ic == 0), stop=(ic == 15))
            st.stage(tensor=t_o)

            def v_o(v, oc=oc):
                sl = sbx[:, oc * TPC:(oc + 1) * TPC]
                v.tensor_tensor(out=sl, in0=sl, in1=ps_x[:, 0:TPC], op=OP.add)
            st.stage(vector=v_o)

        def st_b(e, b=b):
            return dma(e, p, "sync",
                       xo_d[b], sbx[:, :])
        st.stage(sync=st_b)




def build_l0attn():
    p = Prog("l0attn")
    p.pf = "a0_"
    us_d = p.inp("us", (B, 128, DCH * (TPC + 3)))
    xo_d = p.outp("xo", (B, 128, DCH * TPC))
    p.start_block()
    st = Stager(p)
    emit_l0attn(p, st, us_d, xo_d)
    st.stage(sync=lambda e: None)
    return p

def maps_l0attn(I):
    cw = np.asarray(I["L0_attn_conv_w"], np.float32)[:, 0, :]
    cvw = np.zeros((128, CCH * 4), np.float32)
    for cc in range(CCH):
        for j in range(4):
            cvw[:, cc * 4 + j] = cw[cc * 128:(cc + 1) * 128, j]
    dtb = np.zeros((128, 1), np.float32)
    dtb[0:32, 0] = np.asarray(I["L0_attn_dt_bias"], np.float32)
    Dv = np.asarray(I["L0_attn_D"], np.float32)
    dcol = np.zeros((128, 16), np.float32)
    for m in range(16):
        dcol[0:64, m] = Dv[2 * m]
        dcol[64:128, m] = Dv[2 * m + 1]
    base = {
        "ipT": _t(I["L0_attn_in_proj_w"]),
        "cvw": cvw,
        "cvb": _col(I["L0_attn_conv_b"], CCH),
        "dtb": dtb,
        "alog": np.asarray(I["L0_attn_A_log"], np.float32),
        "dcol": dcol,
        "nw": _col(I["L0_attn_norm_w"], 16),
        "opT": _t(I["L0_attn_out_proj_w"]),
    }
    in_maps = []
    for c in range(NCORES):
        m = dict(base)
        oh = np.zeros((128, NCORES), np.float32)
        if c > 0:
            oh[:, c - 1] = 1.0
        m["ohc"] = oh
        in_maps.append({"a0_" + k: v for k, v in m.items()})
    return in_maps


def run_l0attn(u_shards3, I):
    p = build_l0attn()
    in_maps = maps_l0attn(I)
    for c in range(NCORES):
        in_maps[c]["a0_us"] = _to_qmaj(u_shards3[c])
    res = run_bass_kernel_spmd(p.nc, in_maps, core_ids=list(range(NCORES)))
    out = [_from_qmaj(res.results[c]["a0_xo"]) for c in range(NCORES)]
    p.close()
    return out



def _t(w):
    return np.ascontiguousarray(np.asarray(w, np.float32).T)


def _col(v, nch=DCH):
    return np.ascontiguousarray(np.asarray(v, np.float32).reshape(nch, 128).T)


def maps_l0ffn(I):
    base = {
        "wT_wf": _t(I["L0_ffn_wf"]), "wT_uf": _t(I["L0_ffn_uf"]),
        "wT_wi": _t(I["L0_ffn_wi"]), "wT_ui": _t(I["L0_ffn_ui"]),
        "wT_wc": _t(I["L0_ffn_wc"]), "wT_uc": _t(I["L0_ffn_uc"]),
        "wT_wo": _t(I["L0_ffn_wo"]), "wT_uo": _t(I["L0_ffn_uo"]),
        "b_bf": _col(I["L0_ffn_bf"]),
        "b_bi": _col(I["L0_ffn_bi"]),
        "b_bc": _col(I["L0_ffn_bc"]),
        "b_bo": _col(I["L0_ffn_bo"]),
        "in_c": _col(I["L0_ffn_in_c"]),
    }
    return [{"f0_" + k: v for k, v in base.items()} for c in range(NCORES)]


def run_l0ffn(stream_shards, I):
    p = build_l0ffn()
    in_maps = maps_l0ffn(I)
    for c in range(NCORES):
        in_maps[c]["f0_xs"] = _to_qmaj(stream_shards[c])
    res = run_bass_kernel_spmd(p.nc, in_maps, core_ids=list(range(NCORES)))
    out = [_from_qmaj(res.results[c]["f0_xo"]) for c in range(NCORES)]
    p.close()
    return out


def _to_qmaj(a):
    Bq, Dq, T = a.shape
    return np.ascontiguousarray(
        a.reshape(Bq, DCH, 128, T).transpose(0, 2, 1, 3).reshape(Bq, 128, DCH * T))


def _from_qmaj(a):
    Bq, _, CT = a.shape
    T = CT // DCH
    return np.ascontiguousarray(
        a.reshape(Bq, 128, DCH, T).transpose(0, 2, 1, 3).reshape(Bq, DCH * 128, T))


def _shard_u3(x):
    fm = np.ascontiguousarray(x.transpose(0, 2, 1))
    out = []
    for c in range(NCORES):
        sl = np.zeros((B, D, TPC + 3), np.float32)
        lo = c * TPC
        lo3 = max(lo - 3, 0)
        pad = lo - lo3
        sl[:, :, 3 - pad:] = fm[:, :, lo3:lo + TPC]
        out.append(sl)
    return out


def build_fused():
    p = Prog("fused")
    p.pf = "a0_"
    us_d = p.inp("us", (B, 128, DCH * (TPC + 3)))
    p.pf = "out_"
    xo_d = p.outp("xo", (B, 128, DCH * TPC))
    strm = [p.dram("strm%d" % k, (B, 128, DCH * TPC)) for k in range(7)]
    p.pf = ""
    p.start_block()
    st = Stager(p)
    blocks = [
        (emit_l0attn, (us_d, strm[0])),
        (emit_l0ffn, (strm[0], strm[1])),
        (emit_attn, (1, strm[1], strm[2])),
        (emit_l1ffn, (strm[2], strm[3])),
        (emit_attn, (2, strm[3], strm[4])),
        (emit_l2ffn, (strm[4], strm[5])),
        (emit_l3attn, (strm[5], strm[6])),
        (emit_l3ffn, (strm[6], xo_d)),
    ]
    for fn, args in blocks:
        p.push_region()
        fn(p, st, *args)
        p.pop_region()
    st.stage(sync=lambda e: None)
    return p


def run_fused(I):
    p = build_fused()
    x = np.ascontiguousarray(np.asarray(I["L0_attn_u"], np.float32))
    us = _shard_u3(x)
    allmaps = [maps_l0attn(I), maps_l0ffn(I), maps_attn(1, I), maps_l1ffn(I),
               maps_attn(2, I), maps_l2ffn(I), maps_l3attn(I), maps_l3ffn(I)]
    in_maps = []
    for c in range(NCORES):
        m = {}
        for bm in allmaps:
            m.update(bm[c])
        m["a0_us"] = _to_qmaj(us[c])
        in_maps.append(m)
    res = run_bass_kernel_spmd(p.nc, in_maps, core_ids=list(range(NCORES)))
    out = [_from_qmaj(res.results[c]["out_xo"]) for c in range(NCORES)]
    p.close()
    fm = np.concatenate(out, axis=2)
    return np.ascontiguousarray(fm.transpose(0, 2, 1)).astype(np.float32)


def kernel(**inputs):
    return run_fused(inputs)


# version 06
# speedup vs baseline: 1.0981x; 1.0981x over previous
import math
import contextlib
import numpy as np
import concourse.bass as bass
import concourse.mybir as mybir
from concourse.alu_op_type import AluOpType
from concourse.bass_utils import run_bass_kernel_spmd
from concourse import library_config

F32 = mybir.dt.float32
BF16 = mybir.dt.bfloat16
I32 = mybir.dt.int32
U8 = mybir.dt.uint8
AF = mybir.ActivationFunctionType
AX = mybir.AxisListType.X
OP = AluOpType

NCORES = 8
B = 4
SEQ = 4096
TPC = SEQ // NCORES
D = 1024
DCH = D // 128

LN2 = float(np.float32(0.6931471805599453))
INV_LN2 = float(np.float32(1.4426950408889634))
LN2_LO = float(np.float32(np.float64(0.69314718055994530941723) - np.float64(np.float32(0.6931471805599453))))



class Prog:

    def __init__(self, name):
        self.nc = bass.Bass("TRN2", target_bir_lowering=False)
        self.es = contextlib.ExitStack()
        self.block = None
        self.semc = {}
        self.inputs = {}
        self.outputs = {}
        self._sb = {}
        self.pf = ""
        self.rstack = []

    def push_region(self):
        self.rstack.append(contextlib.ExitStack())

    def pop_region(self):
        self.rstack.pop().close()

    def _es(self):
        return self.rstack[-1] if self.rstack else self.es

    def start_block(self):
        self.block = self.es.enter_context(self.nc.Block())

    def sem(self, name):
        if name not in self.semc:
            h = self.es.enter_context(self.nc.semaphore(name))
            self.semc[name] = [h, 0]
        return self.semc[name]

    def inp(self, name, shape, dtype=F32):
        name = self.pf + name
        ap = self.nc.dram_tensor(name, list(shape), dtype, kind="ExternalInput").ap()
        self.inputs[name] = ap
        return ap

    def outp(self, name, shape, dtype=F32):
        name = self.pf + name
        ap = self.nc.dram_tensor(name, list(shape), dtype, kind="ExternalOutput").ap()
        self.outputs[name] = ap
        return ap

    def dram(self, name, shape, dtype=F32):
        return self.nc.dram_tensor(self.pf + name, list(shape), dtype).ap()

    def sb(self, name, shape, dtype=F32):
        name = self.pf + name
        t = self._es().enter_context(self.nc.sbuf_tensor(name, list(shape), dtype))
        self._sb[name] = t
        return t

    def ps(self, name, shape=(128, 512), dtype=F32):
        return self._es().enter_context(
            self.nc.psum_tensor(self.pf + name, list(shape), dtype))

    def close(self):
        self.es.close()


class Stager:

    ENGS = ("tensor", "vector", "scalar", "gpsimd", "sync")
    DSEMS = ("dma_sync", "dma_gpsimd")

    def __init__(self, prog):
        self.p = prog
        for e in self.ENGS:
            prog.sem("st_" + e)
        for d in self.DSEMS:
            prog.sem(d)
        self.seen = {e: {} for e in self.ENGS}

    def _engine_section(self, ename, fn, waits):
        p = self.p
        seen = self.seen[ename]

        def body(eng):
            for (nm, h, c) in waits:
                if c > 0 and seen.get(nm, 0) < c:
                    eng.wait_ge(h, c)
                    seen[nm] = c
            n_dma = fn(eng)
            if n_dma is None:
                n_dma = 0
            sem = p.sem("st_" + ename)
            eng.drain().then_inc(sem[0], 1)
            sem[1] += 1
            if n_dma:
                p.sem("dma_" + ename)[1] += 16 * n_dma

        getattr(self.p.block, ename)(body)

    def stage(self, **emits):
        waits = [("st_" + e, self.p.sem("st_" + e)[0], self.p.sem("st_" + e)[1])
                 for e in self.ENGS]
        waits += [(d, self.p.sem(d)[0], self.p.sem(d)[1]) for d in self.DSEMS]
        for ename, fn in emits.items():
            self._engine_section(ename, fn, waits)


def dma(eng, p, qname, out, in_):
    sem = p.sem("dma_" + qname)
    eng.dma_start(out=out, in_=in_).then_inc(sem[0], 16)
    return 1


def cast_stream(st, p, items, stg0, stg1):
    n = len(items)
    for k in range(n + 1):
        em = {}
        if k < n:
            d_, s_, w_ = items[k]
            stg = (stg0, stg1)[k % 2]

            def ld(e, s_=s_, stg=stg, w_=w_):
                return dma(e, p, "sync", stg[:, 0:w_], s_)
            em["sync"] = ld
        if k > 0:
            d0, s0, w0 = items[k - 1]
            stgp = (stg0, stg1)[(k - 1) % 2]

            def cst(s, d0=d0, stgp=stgp, w0=w0):
                s.activation(out=d0, in_=stgp[:, 0:w0], func=AF.Identity)
            em["scalar"] = cst
        st.stage(**em)



def mm_tiled(t, psum_ap, lhsT_tiles, rhs_tiles, n_cols, start=True):
    n = len(lhsT_tiles)
    for k in range(n):
        t.matmul(psum_ap, lhsT_tiles[k], rhs_tiles[k],
                 start=(start and k == 0), stop=(k == n - 1))



def emit_layernorm(st, env, b, T=TPC, rms=False, nchunks=DCH, x_at=None, xn_at=None,
                   eps_override=None):
    p = st.p
    xs = env["stream"]
    xn = env["xn"]
    sq = env["sq"]
    cone = env["cone"] if nchunks == DCH else env["cone2"]
    onesrow = env["onesrow"]
    srow = env["srow"]
    ps_stat = env["ps_stat"]
    ps_b0 = env["ps_b0"]
    ps_b1 = env["ps_b1"]
    nf = nchunks * 128
    eps = 1e-5 if not rms else 1e-5

    def xat(ch):
        if x_at is not None:
            return x_at(ch)
        return xs[:, (b * DCH + ch) * TPC:(b * DCH + ch) * TPC + T]

    def xnat(ch):
        if xn_at is not None:
            return xn_at(ch)
        return xn[:, (b * DCH + ch) * TPC:(b * DCH + ch) * TPC + T]

    def v_cast(v):
        for ch in range(nchunks):
            v.tensor_copy(out=xnat(ch), in_=xat(ch))
    st.stage(vector=v_cast)

    sqw = env["sqw"]

    def s_sq(s):
        for ch in range(nchunks):
            s.activation(out=sqw[:, ch * T:(ch + 1) * T], in_=xnat(ch), func=AF.Square)
    st.stage(scalar=s_sq)

    def t_stats(t):
        for ch in range(nchunks):
            t.matmul(ps_stat[0:1, 0:T], cone[:, 0:1], xnat(ch),
                     start=(ch == 0), stop=(ch == nchunks - 1))
        for ch in range(nchunks):
            t.matmul(ps_b0[0:1, 0:T], cone[:, 0:1], sqw[:, ch * T:(ch + 1) * T],
                     start=(ch == 0), stop=(ch == nchunks - 1))
    st.stage(tensor=t_stats)

    r_mean = srow[0:1, 0:T]
    r_var = srow[0:1, T:2 * T]
    r_rstd = srow[0:1, 2 * T:3 * T]
    r_negm = srow[0:1, 3 * T:4 * T]

    if not rms:
        def v_row1(v):
            v.tensor_copy(out=r_mean, in_=ps_stat[0:1, 0:T])
        st.stage(vector=v_row1)

        def s_row2(s):
            s.activation(out=r_var, in_=r_mean, func=AF.Square)
        st.stage(scalar=s_row2)

        def v_row3(v):
            v.tensor_tensor(out=r_var, in0=ps_b0[0:1, 0:T], in1=r_var, op=OP.subtract)
        st.stage(vector=v_row3)
    else:
        def v_row3b(v):
            v.tensor_copy(out=r_var, in_=ps_b0[0:1, 0:T])
        st.stage(vector=v_row3b)

    eps_col = env["eps_col"] if eps_override is None else env["eps_col2"]

    def s_row4(s):
        s.activation(out=r_rstd, in_=r_var, func=AF.Ln, bias=eps_col[0:1, 0:1])
        s.drain()
        s.activation(out=r_rstd, in_=r_rstd, func=AF.Exp, scale=-0.5)
    st.stage(scalar=s_row4)

    if not rms:
        def v_row5(v):
            v.scalar_tensor_tensor(out=r_negm, in0=r_mean, scalar=-1.0, in1=r_rstd,
                                   op0=OP.mult, op1=OP.mult)
        st.stage(vector=v_row5)

    def t_bc(t):
        t.matmul(ps_b0[:, 0:T], onesrow[0:1, :], r_rstd, start=True, stop=True)
        if not rms:
            t.matmul(ps_b1[:, 0:T], onesrow[0:1, :], r_negm, start=True, stop=True)
    st.stage(tensor=t_bc)

    tmp = env["lnt"]

    slots = max(1, tmp.shape[1] // T) if not rms else nchunks

    def v_apply(v):
        if rms:
            for ch in range(nchunks):
                v.tensor_tensor(out=xnat(ch), in0=xat(ch), in1=ps_b0[:, 0:T], op=OP.mult)
        else:
            for g0 in range(0, nchunks, slots):
                grp = range(g0, min(g0 + slots, nchunks))
                for ch in grp:
                    v.tensor_tensor(out=tmp[:, (ch - g0) * T:(ch - g0 + 1) * T],
                                    in0=xat(ch), in1=ps_b0[:, 0:T], op=OP.mult)
                v.drain()
                for ch in grp:
                    v.tensor_tensor(out=xnat(ch), in0=tmp[:, (ch - g0) * T:(ch - g0 + 1) * T],
                                    in1=ps_b1[:, 0:T], op=OP.add)
                v.drain()
    st.stage(vector=v_apply)



def emit_l0ffn(p, st, xs_d, xo_d):
    p.pf = "f0_"
    nc = p.nc
    w_names = ["wf", "uf", "wi", "ui", "wc", "uc", "wo", "uo"]
    wd = {n: p.inp("wT_" + n, (D, D)) for n in w_names}
    bd = {n: p.inp("b_" + n, (128, DCH)) for n in ["bf", "bi", "bc", "bo"]}
    inc_d = p.inp("in_c", (128, DCH))

    sbx = p.sb("sbx", (128, DCH * TPC), F32)
    xn = p.sb("xn", (128, B * DCH * TPC), BF16)
    wsb = {n: p.sb("w_" + n, (128, DCH * D), BF16)
           for n in ["wf", "uf", "wi", "ui", "wc", "uc"]}
    wsb["wo"] = wsb["wf"]
    wsb["uo"] = wsb["uf"]
    bias = p.sb("bias", (128, 5 * DCH), F32)
    cstate = p.sb("cstate", (128, DCH * TPC), F32)
    cbf = p.sb("cbf", (128, DCH * TPC), BF16)
    gates = p.sb("gates", (128, 3 * TPC), BF16)
    sqw = p.sb("sqw", (128, DCH * TPC), BF16)
    srow = p.sb("srow", (1, 4 * TPC), F32)
    lnt = p.sb("lnt", (128, DCH * TPC), F32)
    cone = p.sb("cone", (128, 1), BF16)
    onesrow = p.sb("onesrow", (1, 128), F32)
    onescol = p.sb("onescol", (128, TPC), F32)
    htile = p.sb("htile", (128, TPC), F32)

    ps_stat = p.ps("ps_stat")
    ps_b0 = p.ps("ps_b0")
    ps_b1 = p.ps("ps_b1")
    ps_g = [p.ps("ps_g%d" % i) for i in range(3)]
    ps_h = p.ps("ps_h")

    eps_col = p.sb("eps_col", (128, 1), F32)
    env = dict(stream=None, xn=xn, sq=None, sqw=sqw, srow=srow, lnt=lnt,
               cone=cone, onesrow=onesrow, ps_stat=ps_stat, ps_b0=ps_b0,
               ps_b1=ps_b1, eps_col=eps_col)


    def ld_sync(e):
        n = 0
        for gi, bn in enumerate(["bf", "bi", "bc", "bo"]):
            n += dma(e, p, "sync", bias[:, gi * DCH:(gi + 1) * DCH], bd[bn][:, :])
        n += dma(e, p, "sync", bias[:, 4 * DCH:5 * DCH], inc_d[:, :])
        return n

    def ld_gp(e):
        n = 0
        for wn in ["wf", "uf", "wi", "ui", "wc", "uc"]:
            n += dma(e, p, "gpsimd",
                     wsb[wn][:, :].rearrange("q (ic o) -> q ic o", ic=DCH),
                     wd[wn].rearrange("(ic q) o -> q ic o", q=128))
        return n

    def init_v(v):
        v.memset(cone[:, :], 1.0 / 1024.0)
        v.memset(onesrow[:, :], 1.0)
        v.memset(onescol[:, :], 1.0)
        v.memset(eps_col[:, :], 1e-5)
    st.stage(sync=ld_sync, gpsimd=ld_gp, vector=init_v)

    for b in range(B):
        def ld_b(e, b=b):
            return dma(e, p, "sync",
                       sbx[:, :], xs_d[b])
        st.stage(sync=ld_b)
        emit_layernorm(st, env, b,
                       x_at=lambda ch: sbx[:, ch * TPC:(ch + 1) * TPC])

    def v_c0(v):
        for ch in range(DCH):
            v.tensor_scalar(out=cstate[:, ch * TPC:(ch + 1) * TPC],
                            in0=onescol[:, :],
                            scalar1=bias[:, 4 * DCH + ch:4 * DCH + ch + 1],
                            scalar2=None, op0=OP.mult, op1=OP.bypass)
    st.stage(vector=v_c0)

    def wtile(wn, ic, oc):
        return wsb[wn][:, ic * D + oc * 128: ic * D + oc * 128 + 128]

    def xn_at(t_step, ch):
        o = (t_step * DCH + ch) * TPC
        return xn[:, o:o + TPC]

    for t_step in range(B):
        def v_castc(v):
            for ch in range(DCH):
                v.tensor_copy(out=cbf[:, ch * TPC:(ch + 1) * TPC],
                              in_=cstate[:, ch * TPC:(ch + 1) * TPC])
        st.stage(vector=v_castc)

        for oc in range(DCH):
            def t_mm(t, oc=oc, t_step=t_step):
                for gi, (wn, un) in enumerate([("wf", "uf"), ("wi", "ui"), ("wc", "uc")]):
                    for ic in range(DCH):
                        t.matmul(ps_g[gi][:, 0:TPC], wtile(wn, ic, oc),
                                 xn_at(t_step, ic), start=(ic == 0), stop=False)
                    for ic in range(DCH):
                        t.matmul(ps_g[gi][:, 0:TPC], wtile(un, ic, oc),
                                 cbf[:, ic * TPC:(ic + 1) * TPC],
                                 start=False, stop=(ic == DCH - 1))

            def s_evac(s, oc=oc):
                for gi, fn in [(0, AF.Sigmoid), (1, AF.Sigmoid), (2, AF.Tanh)]:
                    s.activation(out=gates[:, gi * TPC:(gi + 1) * TPC],
                                 in_=ps_g[gi][:, 0:TPC], func=fn,
                                 bias=bias[:, gi * DCH + oc:gi * DCH + oc + 1])

            def v_update(v, oc=oc):
                cs = cstate[:, oc * TPC:(oc + 1) * TPC]
                f_ = gates[:, 0:TPC]
                i_ = gates[:, TPC:2 * TPC]
                g_ = gates[:, 2 * TPC:3 * TPC]
                v.tensor_tensor(out=cs, in0=cs, in1=f_, op=OP.mult)
                v.tensor_tensor(out=lnt[:, 0:TPC], in0=i_, in1=g_, op=OP.mult)
                v.drain()
                v.tensor_tensor(out=cs, in0=cs, in1=lnt[:, 0:TPC], op=OP.add)
                v.drain()
            st.stage(tensor=t_mm)
            st.stage(scalar=s_evac)
            st.stage(vector=v_update)

    def v_castc2(v):
        for ch in range(DCH):
            v.tensor_copy(out=cbf[:, ch * TPC:(ch + 1) * TPC],
                          in_=cstate[:, ch * TPC:(ch + 1) * TPC])

    def ld_wo(e):
        n = 0
        for wn in ["wo", "uo"]:
            n += dma(e, p, "gpsimd",
                     wsb[wn][:, :].rearrange("q (ic o) -> q ic o", ic=DCH),
                     wd[wn].rearrange("(ic q) o -> q ic o", q=128))
        return n
    st.stage(vector=v_castc2, gpsimd=ld_wo)

    for t_step in range(B):
        def ld_b2(e, t_step=t_step):
            return dma(e, p, "sync",
                       sbx[:, :], xs_d[t_step])
        st.stage(sync=ld_b2)
        for oc in range(DCH):
            def t_mmh(t, oc=oc, t_step=t_step):
                for ic in range(DCH):
                    t.matmul(ps_h[:, 0:TPC], wtile("wo", ic, oc),
                             xn_at(t_step, ic), start=(ic == 0), stop=False)
                for ic in range(DCH):
                    t.matmul(ps_h[:, 0:TPC], wtile("uo", ic, oc),
                             cbf[:, ic * TPC:(ic + 1) * TPC],
                             start=False, stop=(ic == DCH - 1))

            def s_evh(s, oc=oc):
                s.activation(out=htile[:, 0:TPC], in_=ps_h[:, 0:TPC], func=AF.Tanh,
                             bias=bias[:, 3 * DCH + oc:3 * DCH + oc + 1])

            def v_res(v, oc=oc):
                sl = sbx[:, oc * TPC:(oc + 1) * TPC]
                v.tensor_tensor(out=sl, in0=sl, in1=htile[:, 0:TPC], op=OP.add)
            st.stage(tensor=t_mmh)
            st.stage(scalar=s_evh)
            st.stage(vector=v_res)

        def st_b(e, t_step=t_step):
            return dma(e, p, "sync",
                       xo_d[t_step], sbx[:, :])
        st.stage(sync=st_b)





NH = 16
HD = 64
KCH = SEQ // 128



def build_l0ffn():
    p = Prog("l0ffn")
    p.pf = "f0_"
    xs_d = p.inp("xs", (B, 128, DCH * TPC))
    xo_d = p.outp("xo", (B, 128, DCH * TPC))
    p.start_block()
    st = Stager(p)
    emit_l0ffn(p, st, xs_d, xo_d)
    st.stage(sync=lambda e: None)
    return p

def emit_attn(p, st, layer, xs_d, xo_d):
    causal = (layer == 1)
    p.pf = "a%d_" % layer
    wqT = p.inp("wqT", (D, D))
    wkT = p.inp("wkT", (D, D))
    wvT = p.inp("wvT", (D, D))
    woT = p.inp("woT", (D, D))
    wob = p.inp("wob", (128, DCH))
    cosd = p.inp("cosT", (512, TPC))
    sind = p.inp("sinT", (512, TPC))
    if causal:
        mask_d = p.inp("maskT", (KCH, 128, TPC), BF16)

    kv_loc = p.dram("kv_loc", (2, B, 8, 128, TPC), BF16)
    kv_all = p.dram("kv_all", (NCORES, 2, B, 8, 128, TPC), BF16)
    bnd_loc = p.dram("bnd_loc", (NH, B), F32)
    bnd_all = p.dram("bnd_all", (NH, B), F32)
    brow_d = p.dram("brow_d", (B, NH), F32)

    sbx = p.sb("sbx", (128, DCH * TPC), F32)
    xn = p.sb("xn", (128, B * DCH * TPC), BF16)
    pre = p.sb("pre", (128, DCH * TPC), F32)
    qr = p.sb("qr", (128, B * DCH * TPC), BF16)
    kr = p.sb("kr", (128, DCH * TPC), BF16)
    kr2 = p.sb("kr2", (128, 4 * NCORES * 130), BF16)
    wbuf = p.sb("wbuf", (128, DCH * D), BF16)
    cos_s = p.sb("cos_s", (128, 4 * TPC), F32)
    sin_s = p.sb("sin_s", (128, 4 * TPC), F32)
    sqw = p.sb("sqw", (128, DCH * TPC), BF16)
    srow = p.sb("srow", (1, 4 * TPC), F32)
    lnt = p.sb("lnt", (128, DCH * TPC), F32)
    cone = p.sb("cone", (128, 1), BF16)
    onesrow = p.sb("onesrow", (1, 128), F32)
    eps_col = p.sb("eps_col", (128, 1), F32)
    e16 = p.sb("e16", (128, DCH * NH), BF16)
    idt = p.sb("idt", (128, 128), BF16)
    idt32 = p.sb("idt32", (128, 128), F32)
    iwork = p.sb("iwork", (128, 128), I32)
    perm = p.sb("perm", (128, 4 * 128), BF16)
    mq = p.sb("mq", (128, B), F32)
    mkl = p.sb("mkl", (16, B), F32)
    mkg = p.sb("mkg", (16, B), F32)
    bias_bh = p.sb("bias_bh", (128, NH * B), F32)
    brow = p.sb("brow", (1, NH * B), F32)
    bsml = p.sb("bsml", (16, B), F32)
    kbuf = kr
    vbuf = kr2
    pk = sqw
    rbc = p.sb("rbc", (128, TPC), F32)
    denrow = p.sb("denrow", (128, 4 * TPC), F32)
    ones128 = p.sb("ones128", (128, 128), F32)
    wob_s = p.sb("wob_s", (128, DCH), F32)
    if causal:
        mbuf = p.sb("mbuf", (128, 2 * TPC), BF16)

    actx = xn
    ps_s = [p.ps("ps_s%d" % i) for i in range(4)]
    ps_o = [p.ps("ps_o%d" % i) for i in range(2)]
    ps_x = p.ps("ps_x")
    ps_t = p.ps("ps_t", (128, 128), BF16)
    ps_stat = ps_s[0]
    ps_b0 = ps_s[1]
    ps_b1 = ps_s[2]

    env = dict(stream=None, xn=xn, sq=None, sqw=sqw, srow=srow, lnt=lnt,
               cone=cone, onesrow=onesrow, ps_stat=ps_stat, ps_b0=ps_b0,
               ps_b1=ps_b1, eps_col=eps_col)


    def ld_sync(e):
        n = 0
        n += dma(e, p, "sync", wob_s[:, :], wob[:, :])
        for ch in range(4):
            n += dma(e, p, "sync", cos_s[:, ch * TPC:(ch + 1) * TPC],
                     cosd[ch * 128:(ch + 1) * 128, :])
            n += dma(e, p, "sync", sin_s[:, ch * TPC:(ch + 1) * TPC],
                     sind[ch * 128:(ch + 1) * 128, :])
        return n

    def init_v(v):
        v.memset(cone[:, :], 1.0 / 1024.0)
        v.memset(onesrow[:, :], 1.0)
        v.memset(ones128[:, :], 1.0)
        v.memset(eps_col[:, :], 1e-5)
        v.memset(e16[:, :], 0.0)
        v.drain()
        for ch in range(DCH):
            v.memset(e16[0:64, ch * NH + 2 * ch: ch * NH + 2 * ch + 1], 1.0)
            v.memset(e16[64:128, ch * NH + 2 * ch + 1: ch * NH + 2 * ch + 2], 1.0)

    def init_g(g):
        g.iota(iwork[:, :], pattern=[[1, 128]], base=0, channel_multiplier=-1)
    st.stage(sync=ld_sync, vector=init_v, gpsimd=init_g)

    def v_idt(v):
        v.tensor_scalar(out=idt[:, :], in0=iwork[:, :], scalar1=0.0, scalar2=None,
                        op0=OP.is_equal, op1=OP.bypass)
        v.tensor_scalar(out=idt32[:, :], in0=iwork[:, :], scalar1=0.0, scalar2=None,
                        op0=OP.is_equal, op1=OP.bypass)
        if causal:
            v.tensor_scalar(out=perm[:, 0:128], in0=iwork[:, :], scalar1=0.0,
                            scalar2=None, op0=OP.is_equal, op1=OP.bypass)
            v.tensor_scalar(out=perm[:, 128:256], in0=iwork[:, :], scalar1=-64.0,
                            scalar2=None, op0=OP.is_equal, op1=OP.bypass)
            v.tensor_scalar(out=perm[:, 256:384], in0=iwork[:, :], scalar1=64.0,
                            scalar2=None, op0=OP.is_equal, op1=OP.bypass)
            v.tensor_scalar(out=perm[:, 384:512], in0=iwork[:, :], scalar1=0.0,
                            scalar2=None, op0=OP.is_equal, op1=OP.bypass)
            v.drain()
            v.memset(perm[64:128, 0:128], 0.0)
            v.memset(perm[0:64, 384:512], 0.0)
    st.stage(vector=v_idt)

    def wtile(ic, oc):
        return wbuf[:, ic * D + oc * 128: ic * D + oc * 128 + 128]

    def xn_at(b, ch):
        o = (b * DCH + ch) * TPC
        return xn[:, o:o + TPC]

    for b in range(B):
        def ld_b(e, b=b):
            return dma(e, p, "sync",
                       sbx[:, :], xs_d[b])
        st.stage(sync=ld_b)
        emit_layernorm(st, env, b,
                       x_at=lambda ch: sbx[:, ch * TPC:(ch + 1) * TPC],
                       xn_at=lambda ch, b=b: xn_at(b, ch))

    def rope_into(v, dst_at):
        for r in range(4):
            qe = pre[:, r * TPC:(r + 1) * TPC]
            qo = pre[:, (4 + r) * TPC:(5 + r) * TPC]
            cs = cos_s[:, r * TPC:(r + 1) * TPC]
            sn = sin_s[:, r * TPC:(r + 1) * TPC]
            t1 = lnt[:, 0:TPC]
            t2 = lnt[:, TPC:2 * TPC]
            v.tensor_tensor(out=t1, in0=qe, in1=cs, op=OP.mult)
            v.tensor_tensor(out=t2, in0=qo, in1=sn, op=OP.mult)
            v.drain()
            v.tensor_tensor(out=dst_at(r), in0=t1, in1=t2, op=OP.subtract)
            v.drain()
            v.tensor_tensor(out=t1, in0=qe, in1=sn, op=OP.mult)
            v.tensor_tensor(out=t2, in0=qo, in1=cs, op=OP.mult)
            v.drain()
            v.tensor_tensor(out=dst_at(4 + r), in0=t1, in1=t2, op=OP.add)
            v.drain()

    def emit_proj(dest_fn, b):
        for oc in range(DCH):
            def t_p(t, oc=oc, b=b):
                for ic in range(DCH):
                    t.matmul(ps_x[:, 0:TPC], wtile(ic, oc), xn_at(b, ic),
                             start=(ic == 0), stop=(ic == DCH - 1))

            def v_e(v, oc=oc):
                dest_fn(v, oc)
            st.stage(tensor=t_p)
            st.stage(vector=v_e)

    def emit_regroup(srcbuf, dst_at):
        for hp in range(8):
            a_ch = hp // 2
            lo = hp % 2

            def t_rg(t, a_ch=a_ch, lo=lo, hp=hp):
                selA = perm[:, 0:128] if lo == 0 else perm[:, 128:256]
                selB = perm[:, 256:384] if lo == 0 else perm[:, 384:512]
                t.matmul(ps_x[:, 0:TPC], selA,
                         srcbuf[:, a_ch * TPC:(a_ch + 1) * TPC],
                         start=True, stop=False)
                t.matmul(ps_x[:, 0:TPC], selB,
                         srcbuf[:, (4 + a_ch) * TPC:(5 + a_ch) * TPC],
                         start=False, stop=True)

            def v_rg(v, hp=hp):
                v.tensor_copy(out=dst_at(hp), in_=ps_x[:, 0:TPC])
            st.stage(tensor=t_rg)
            st.stage(vector=v_rg)

    def emit_headnorm(srcat, dst, b):
        def s_sq(s):
            for ch in range(DCH):
                s.activation(out=sqw[:, ch * TPC:(ch + 1) * TPC], in_=srcat(ch),
                             func=AF.Square)
        st.stage(scalar=s_sq)

        def t_hs(t):
            for ch in range(DCH):
                t.matmul(ps_x[0:NH, 0:TPC], e16[:, ch * NH:(ch + 1) * NH],
                         sqw[:, ch * TPC:(ch + 1) * TPC],
                         start=(ch == 0), stop=(ch == DCH - 1))
        st.stage(tensor=t_hs)

        def v_m(v, b=b):
            v.tensor_reduce(out=dst[0:NH, b:b + 1], in_=ps_x[0:NH, 0:TPC],
                            axis=AX, op=OP.max)
        st.stage(vector=v_m)

    def ld_wq(e):
        return dma(e, p, "gpsimd",
                   wbuf[:, :].rearrange("q (ic o) -> q ic o", ic=DCH),
                   wqT.rearrange("(ic q) o -> q ic o", q=128))
    st.stage(gpsimd=ld_wq)
    for b in range(B):
        emit_proj(lambda v, oc: v.tensor_copy(out=pre[:, oc * TPC:(oc + 1) * TPC],
                                              in_=ps_x[:, 0:TPC]), b)
        if causal:
            def v_rq(v):
                rope_into(v, lambda r: kr2[:, r * TPC:(r + 1) * TPC])
            st.stage(vector=v_rq)
            emit_regroup(kr2, lambda hp, b=b: qr[:, (b * DCH + hp) * TPC:
                                                 (b * DCH + hp) * TPC + TPC])
        else:
            def v_rq(v, b=b):
                rope_into(v, lambda r, b=b: qr[:, (b * DCH + r) * TPC:
                                               (b * DCH + r) * TPC + TPC])
            st.stage(vector=v_rq)
        emit_headnorm(lambda ch, b=b: qr[:, (b * DCH + ch) * TPC:
                                         (b * DCH + ch) * TPC + TPC], mq, b)

    def ld_wk(e):
        return dma(e, p, "gpsimd",
                   wbuf[:, :].rearrange("q (ic o) -> q ic o", ic=DCH),
                   wkT.rearrange("(ic q) o -> q ic o", q=128))
    st.stage(gpsimd=ld_wk)
    for b in range(B):
        emit_proj(lambda v, oc: v.tensor_copy(out=pre[:, oc * TPC:(oc + 1) * TPC],
                                              in_=ps_x[:, 0:TPC]), b)
        if causal:
            def v_rk(v):
                rope_into(v, lambda r: kr2[:, r * TPC:(r + 1) * TPC])
            st.stage(vector=v_rk)
            emit_regroup(kr2, lambda hp: kr[:, hp * TPC:(hp + 1) * TPC])
        else:
            def v_rk(v):
                rope_into(v, lambda r: kr[:, r * TPC:(r + 1) * TPC])
            st.stage(vector=v_rk)
        emit_headnorm(lambda ch: kr[:, ch * TPC:(ch + 1) * TPC], mkl, b)

        def st_k(e, b=b):
            n = 0
            for hp in range(8):
                n += dma(e, p, "sync", kv_loc[0, b, hp],
                         kr[:, hp * TPC:(hp + 1) * TPC])
            return n
        st.stage(sync=st_k)

    def ld_wv(e):
        return dma(e, p, "gpsimd",
                   wbuf[:, :].rearrange("q (ic o) -> q ic o", ic=DCH),
                   wvT.rearrange("(ic q) o -> q ic o", q=128))
    st.stage(gpsimd=ld_wv)
    for b in range(B):
        emit_proj(lambda v, oc: v.tensor_copy(out=kr[:, oc * TPC:(oc + 1) * TPC],
                                              in_=ps_x[:, 0:TPC]), b)
        for ch in range(DCH):
            for tq in range(4):
                def t_tr(t, ch=ch, tq=tq):
                    t.transpose(ps_t[:, 0:128],
                                kr[:, ch * TPC + tq * 128: ch * TPC + tq * 128 + 128],
                                idt[:, :])

                def v_tr(v):
                    v.tensor_copy(out=sqw[:, 0:128], in_=ps_t[:, 0:128])

                def st_v(e, ch=ch, tq=tq, b=b):
                    dst = kv_loc[1, b, ch].rearrange("e t -> (e t)")
                    dst = dst.rearrange("(a q f) -> q a f", a=4, q=128)[:, tq, :]
                    return dma(e, p, "sync", dst, sqw[:, 0:128])
                st.stage(tensor=t_tr)
                st.stage(vector=v_tr)
                st.stage(sync=st_v)

    def st_bnd(e):
        return dma(e, p, "sync", bnd_loc[:, :], mkl[0:16, 0:B])
    st.stage(sync=st_bnd)

    def g_cc(g):
        cs = p.sem("cc")
        g.collective_compute("AllGather", OP.bypass,
                             replica_groups=[list(range(NCORES))],
                             ins=[kv_loc[:, :, :, :, :].opt()],
                             outs=[kv_all[:, :, :, :, :, :].opt()]).then_inc(cs[0], 1)
        cs[1] += 1
        g.collective_compute("AllReduce", OP.max,
                             replica_groups=[list(range(NCORES))],
                             ins=[bnd_loc[:, :].opt()],
                             outs=[bnd_all[:, :].opt()]).then_inc(cs[0], 1)
        cs[1] += 1
        g.wait_ge(cs[0], cs[1])
    st.stage(gpsimd=g_cc)

    def ld_mkg(e):
        return dma(e, p, "sync", mkg[0:16, 0:B], bnd_all[:, :])
    st.stage(sync=ld_mkg)

    scl = 1.0 / math.sqrt(HD)

    def s_bnd(s):
        s.activation(out=mq[0:NH, 0:B], in_=mq[0:NH, 0:B], func=AF.Sqrt)
        s.drain()
        s.activation(out=mkg[0:NH, 0:B], in_=mkg[0:NH, 0:B], func=AF.Sqrt)
    st.stage(scalar=s_bnd)

    def v_bnd(v):
        v.tensor_tensor(out=bsml[0:NH, 0:B], in0=mq[0:NH, 0:B], in1=mkg[0:NH, 0:B],
                        op=OP.mult)
        v.drain()
        v.tensor_scalar(out=bsml[0:NH, 0:B], in0=bsml[0:NH, 0:B], scalar1=-scl,
                        scalar2=None, op0=OP.mult, op1=OP.bypass)
    st.stage(vector=v_bnd)

    def t_bt(t):
        t.transpose(ps_x[0:B, 0:NH], bsml[0:NH, 0:B], idt32[0:NH, 0:NH])
    st.stage(tensor=t_bt)

    def v_bt(v):
        v.tensor_copy(out=lnt[0:B, 0:NH], in_=ps_x[0:B, 0:NH])
    st.stage(vector=v_bt)

    def st_br(e):
        return dma(e, p, "sync", brow_d[:, :], lnt[0:B, 0:NH])
    st.stage(sync=st_br)

    def ld_br(e):
        return dma(e, p, "sync", brow[0:1, :], brow_d.rearrange("b h -> (b h)"))
    st.stage(sync=ld_br)

    def t_pb(t):
        t.matmul(ps_x[:, 0:NH * B], onesrow[0:1, 0:128], brow[0:1, :],
                 start=True, stop=True)
    st.stage(tensor=t_pb)

    def v_pb(v):
        v.tensor_copy(out=bias_bh[:, :], in_=ps_x[:, 0:NH * B])
    st.stage(vector=v_pb)

    def v_vones(v):
        for z in range(4 * NCORES):
            for hi in range(2):
                v.memset(vbuf[:, z * 130 + hi * 65 + 64: z * 130 + hi * 65 + 65], 1.0)
    st.stage(vector=v_vones)

    qk_sem = p.sem("qk")
    ex_sem = p.sem("ex")
    exs = p.sem("exs")

    for b in range(B):
        def ld_strm(e, b=b):
            return dma(e, p, "sync",
                       sbx[:, :], xs_d[b])
        st.stage(sync=ld_strm)

        for hp in range(8):
            h0 = 2 * hp
            is_c = causal and hp < 4

            def d_kv(e, b=b, hp=hp):
                n = 0
                for cpr in range(NCORES):
                    n += dma(e, p, "sync",
                             kbuf[:, cpr * TPC:(cpr + 1) * TPC], kv_all[cpr, 0, b, hp])
                    srcv = kv_all[cpr, 1, b, hp].rearrange("e t -> (e t)")
                    srcv = srcv.rearrange("(a q h e) -> q a h e", q=128, h=2, e=64)
                    for hi in range(2):
                        dst = vbuf[:, 0:32 * 130].rearrange("q (z s) -> q z s", s=130)[
                            :, cpr * 4:(cpr + 1) * 4, hi * 65:hi * 65 + 64]
                        n += dma(e, p, "sync", dst, srcv[:, :, hi, :])
                return n
            st.stage(sync=d_kv)

            qk_base = qk_sem[1]
            ex_base = ex_sem[1]
            exs_base = exs[1]

            def t_att(t, b=b, hp=hp, qk_base=qk_base, ex_base=ex_base):
                for kc in range(KCH):
                    par = kc % 2
                    if kc >= 2:
                        t.wait_ge(ex_sem[0], ex_base + (kc - 1))
                    for hi in range(2):
                        mm = t.matmul(ps_s[hi * 2 + par][:, 0:TPC],
                                 kbuf[hi * 64:(hi + 1) * 64, kc * 128:kc * 128 + 128],
                                 qr[hi * 64:(hi + 1) * 64,
                                    (b * DCH + hp) * TPC:(b * DCH + hp) * TPC + TPC],
                                 start=True, stop=True)
                        if hi == 1:
                            mm.then_inc(qk_sem[0], 1)
                    if kc >= 1:
                        t.wait_ge(ex_sem[0], ex_base + kc)
                        pb = (kc - 1) % 2
                        for hi in range(2):
                            t.matmul(ps_o[hi][0:65, 0:TPC],
                                     vbuf[:, (kc - 1) * 130 + hi * 65:
                                          (kc - 1) * 130 + hi * 65 + 65],
                                     pk[:, (hi * 2 + pb) * TPC:(hi * 2 + pb) * TPC + TPC],
                                     start=(kc == 1), stop=False)
                t.wait_ge(ex_sem[0], ex_base + KCH)
                pb = (KCH - 1) % 2
                for hi in range(2):
                    mm = t.matmul(ps_o[hi][0:65, 0:TPC],
                             vbuf[:, (KCH - 1) * 130 + hi * 65:
                                  (KCH - 1) * 130 + hi * 65 + 65],
                             pk[:, (hi * 2 + pb) * TPC:(hi * 2 + pb) * TPC + TPC],
                             start=False, stop=True)
                    if hi == 1:
                        mm.then_inc(qk_sem[0], 1)

            def s_exp(s, b=b, hp=hp, qk_base=qk_base, is_c=is_c):
                tgt = exs if is_c else ex_sem
                for kc in range(KCH):
                    par = kc % 2
                    s.wait_ge(qk_sem[0], qk_base + kc + 1)
                    for hi in range(2):
                        act = s.activation(
                            out=pk[:, (hi * 2 + par) * TPC:
                                   (hi * 2 + par) * TPC + TPC],
                            in_=ps_s[hi * 2 + par][:, 0:TPC], func=AF.Exp,
                            scale=scl,
                            bias=bias_bh[:, b * NH + 2 * hp + hi:
                                         b * NH + 2 * hp + hi + 1])
                        if hi == 1:
                            act.then_inc(tgt[0], 1)

            if not is_c:
                qk_sem[1] += KCH + 1
                ex_sem[1] += KCH
                st.stage(tensor=t_att, scalar=s_exp)
            else:
                mk_base = p.sem("dma_sync")[1]

                def d_mask(e, ex_base=ex_base):
                    n = 0
                    for kc in range(KCH):
                        if kc >= 2:
                            e.wait_ge(ex_sem[0], ex_base + (kc - 1))
                        n += dma(e, p, "sync",
                                 mbuf[:, (kc % 2) * TPC:(kc % 2) * TPC + TPC],
                                 mask_d[kc])
                    return n

                def v_mask(v, exs_base=exs_base, ex_base=ex_base, mk_base=mk_base):
                    for kc in range(KCH):
                        par = kc % 2
                        v.wait_ge(exs[0], exs_base + kc + 1)
                        v.wait_ge(p.sem("dma_sync")[0], mk_base + 16 * (kc + 1))
                        for hi in range(2):
                            sl = pk[:, (hi * 2 + par) * TPC:(hi * 2 + par) * TPC + TPC]
                            tt = v.tensor_tensor(out=sl, in0=sl,
                                                 in1=mbuf[:, par * TPC:(par + 1) * TPC],
                                                 op=OP.mult)
                            if hi == 1:
                                tt.then_inc(ex_sem[0], 1)

                qk_sem[1] += KCH + 1
                exs[1] += KCH
                ex_sem[1] += KCH
                st.stage(tensor=t_att, scalar=s_exp, sync=d_mask, vector=v_mask)

            def v_rec(v):
                v.tensor_copy(out=denrow[64:65, 2 * TPC:3 * TPC],
                              in_=ps_o[0][64:65, 0:TPC])
                v.tensor_copy(out=denrow[64:65, 3 * TPC:4 * TPC],
                              in_=ps_o[1][64:65, 0:TPC])
                v.drain()
                v.reciprocal(out=denrow[64:65, 0:TPC],
                             in_=denrow[64:65, 2 * TPC:3 * TPC])
                v.reciprocal(out=denrow[64:65, TPC:2 * TPC],
                             in_=denrow[64:65, 3 * TPC:4 * TPC])
            st.stage(vector=v_rec)

            def t_rb(t):
                t.matmul(ps_s[0][0:64, 0:TPC], ones128[64:65, 0:64],
                         denrow[64:65, 0:TPC], start=True, stop=True)
                t.matmul(ps_s[1][0:64, 0:TPC], ones128[64:65, 0:64],
                         denrow[64:65, TPC:2 * TPC], start=True, stop=True)
            st.stage(tensor=t_rb)

            def v_rbc(v):
                v.tensor_copy(out=rbc[0:64, 0:TPC], in_=ps_s[0][0:64, 0:TPC])
                v.tensor_copy(out=rbc[64:128, 0:TPC], in_=ps_s[1][0:64, 0:TPC])
            st.stage(vector=v_rbc)

            def v_ctx(v, hp=hp):
                v.tensor_tensor(out=actx[0:64, hp * TPC:hp * TPC + TPC],
                                in0=ps_o[0][0:64, 0:TPC], in1=rbc[0:64, 0:TPC],
                                op=OP.mult)
                v.tensor_tensor(out=actx[64:128, hp * TPC:hp * TPC + TPC],
                                in0=ps_o[1][0:64, 0:TPC], in1=rbc[64:128, 0:TPC],
                                op=OP.mult)
            st.stage(vector=v_ctx)

        if b == 0:
            def ld_wo(e):
                return dma(e, p, "gpsimd",
                           wbuf[:, :].rearrange("q (ic o) -> q ic o", ic=DCH),
                           woT.rearrange("(ic q) o -> q ic o", q=128))
            st.stage(gpsimd=ld_wo)

        for oc in range(DCH):
            def t_o(t, oc=oc):
                for ic in range(DCH):
                    t.matmul(ps_x[:, 0:TPC], wtile(ic, oc),
                             actx[:, ic * TPC:(ic + 1) * TPC],
                             start=(ic == 0), stop=(ic == DCH - 1))

            def v_o(v, oc=oc):
                sl = sbx[:, oc * TPC:(oc + 1) * TPC]
                v.scalar_tensor_tensor(out=sl, in0=ps_x[:, 0:TPC],
                                       scalar=wob_s[:, oc:oc + 1], in1=sl,
                                       op0=OP.add, op1=OP.add)
            st.stage(tensor=t_o)
            st.stage(vector=v_o)

        def st_b(e, b=b):
            return dma(e, p, "sync",
                       xo_d[b], sbx[:, :])
        st.stage(sync=st_b)




def build_attn(layer):
    p = Prog("attn%d" % layer)
    p.pf = "a%d_" % layer
    xs_d = p.inp("xs", (B, 128, DCH * TPC))
    xo_d = p.outp("xo", (B, 128, DCH * TPC))
    p.start_block()
    st = Stager(p)
    emit_attn(p, st, layer, xs_d, xo_d)
    st.stage(sync=lambda e: None)
    return p

def _rope_tables_l2(core):
    pos = np.arange(core * TPC, (core + 1) * TPC, dtype=np.float64)[None, :]
    r = np.arange(512, dtype=np.float64)[:, None]
    freqs = np.exp(r * 2 * (-math.log(10000.0) / 1024.0))
    ang = pos * freqs
    return (np.cos(ang).astype(np.float32), np.sin(ang).astype(np.float32))


def _rope_tables_l1(core):
    inv = 1.0 / (10000.0 ** (np.arange(0, 64, 2, dtype=np.float64) / 64.0))
    t = np.arange(core * TPC, (core + 1) * TPC, dtype=np.float64)
    fr = t[:, None] * inv[None, :]
    cos = np.cos(fr).T
    sin = np.sin(fr).T
    cosT = np.tile(cos, (16, 1)).astype(np.float32)
    sinT = np.tile(sin, (16, 1)).astype(np.float32)
    return np.ascontiguousarray(cosT), np.ascontiguousarray(sinT)


def _perm_cols(wT, layer):
    if layer == 2:
        idx = np.concatenate([np.arange(0, D, 2), np.arange(1, D, 2)])
    else:
        heads = np.arange(NH)
        a_idx = (heads[:, None] * 64 + np.arange(32)[None, :]).ravel()
        b_idx = (heads[:, None] * 64 + 32 + np.arange(32)[None, :]).ravel()
        idx = np.concatenate([a_idx, b_idx])
    return np.ascontiguousarray(wT[:, idx])


def _causal_masks(core):
    import ml_dtypes
    m = np.zeros((KCH, 128, TPC), ml_dtypes.bfloat16)
    q_glob = core * TPC + np.arange(TPC)[None, :]
    for kc in range(KCH):
        k_glob = kc * 128 + np.arange(128)[:, None]
        m[kc] = (q_glob >= k_glob).astype(ml_dtypes.bfloat16)
    return m


def maps_attn(layer, I):
    if layer == 2:
        base = {
            "wqT": _perm_cols(_t(I["L2_attn_wq"]), 2),
            "wkT": _perm_cols(_t(I["L2_attn_wk"]), 2),
            "wvT": _t(I["L2_attn_wv"]),
            "woT": _t(I["L2_attn_wo_w"]),
            "wob": _col(I["L2_attn_wo_b"]),
        }
    else:
        base = {
            "wqT": _perm_cols(np.ascontiguousarray(np.asarray(I["L1_attn_Wq"], np.float32)), 1),
            "wkT": _perm_cols(np.ascontiguousarray(np.asarray(I["L1_attn_Wk"], np.float32)), 1),
            "wvT": np.ascontiguousarray(np.asarray(I["L1_attn_Wv"], np.float32)),
            "woT": np.ascontiguousarray(np.asarray(I["L1_attn_Wo"], np.float32)),
            "wob": np.zeros((128, DCH), np.float32),
        }
    pf = "a%d_" % layer
    in_maps = []
    for c in range(NCORES):
        m = dict(base)
        cosT, sinT = (_rope_tables_l2(c) if layer == 2 else _rope_tables_l1(c))
        m["cosT"] = cosT
        m["sinT"] = sinT
        if layer == 1:
            m["maskT"] = _causal_masks(c)
        in_maps.append({pf + k: v for k, v in m.items()})
    return in_maps


def run_attn(layer, stream_shards, I):
    p = build_attn(layer)
    pf = "a%d_" % layer
    in_maps = maps_attn(layer, I)
    for c in range(NCORES):
        in_maps[c][pf + "xs"] = _to_qmaj(stream_shards[c])
    res = run_bass_kernel_spmd(p.nc, in_maps, core_ids=list(range(NCORES)))
    out = [_from_qmaj(res.results[c][pf + "xo"]) for c in range(NCORES)]
    p.close()
    return out




def emit_l1ffn(p, st, xs_d, xo_d):
    p.pf = "f1_"
    t1r_d = p.inp("t1r", (4, 512, 512))
    t2r_d = p.inp("t2r", (4, 512, 512))
    t1i_d = p.inp("t1i", (4, 512, 512))
    t2i_d = p.inp("t2i", (4, 512, 512))
    br_d = p.inp("br", (128, 4))
    bi_d = p.inp("bi", (128, 4))

    sbx = p.sb("sbx", (128, DCH * TPC), F32)
    xn = p.sb("xn", (128, DCH * TPC), BF16)
    wre = p.sb("wre", (128, DCH * 512), BF16)
    wim = p.sb("wim", (128, DCH * 512), BF16)
    tta = p.sb("tta", (128, 4 * 512), F32)
    ttb = p.sb("ttb", (128, 4 * 512), F32)
    acc = p.sb("acc", (128, 4 * 512), F32)
    tmp = p.sb("tmp", (128, 4 * 512), F32)
    sqw = p.sb("sqw", (128, DCH * TPC), BF16)
    srow = p.sb("srow", (1, 4 * TPC), F32)
    lnt = p.sb("lnt", (128, DCH * TPC), F32)
    cone = p.sb("cone", (128, 1), BF16)
    onesrow = p.sb("onesrow", (1, 128), F32)
    eps_col = p.sb("eps_col", (128, 1), F32)
    bcol = p.sb("bcol", (128, 8), F32)

    ps_s = [p.ps("ps_s%d" % i) for i in range(4)]
    ps_x = p.ps("ps_x")
    env = dict(stream=None, xn=xn, sq=None, sqw=sqw, srow=srow, lnt=lnt,
               cone=cone, onesrow=onesrow, ps_stat=ps_s[0], ps_b0=ps_s[1],
               ps_b1=ps_s[2], eps_col=eps_col)


    def ld0(e):
        n = dma(e, p, "sync", bcol[:, 0:4], br_d[:, :])
        n += dma(e, p, "sync", bcol[:, 4:8], bi_d[:, :])
        return n

    def iv(v):
        v.memset(cone[:, :], 1.0 / 1024.0)
        v.memset(onesrow[:, :], 1.0)
        v.memset(eps_col[:, :], 1e-5)
    st.stage(sync=ld0, vector=iv)

    for part, (t1d, t2d) in [(0, (t1r_d, t2r_d)), (1, (t1i_d, t2i_d))]:
        for c in range(4):
            def ldc(e, t1d=t1d, t2d=t2d, c=c):
                n = dma(e, p, "sync",
                        tta[:, :].rearrange("q (a s) -> q a s", a=4),
                        t1d[c].rearrange("(a q) s -> q a s", q=128))
                n += dma(e, p, "sync",
                         ttb[:, :].rearrange("q (a s) -> q a s", a=4),
                         t2d[c].rearrange("(a q) s -> q a s", q=128))
                return n
            st.stage(sync=ldc)

            def vmul(v, c=c):
                if c == 0:
                    v.tensor_tensor(out=acc[:, :], in0=tta[:, :], in1=ttb[:, :],
                                    op=OP.mult)
                else:
                    v.tensor_tensor(out=tmp[:, :], in0=tta[:, :], in1=ttb[:, :],
                                    op=OP.mult)
                    v.drain()
                    v.tensor_tensor(out=acc[:, :], in0=acc[:, :], in1=tmp[:, :],
                                    op=OP.add)
            st.stage(vector=vmul)

        def vst(v, part=part):
            for ch in range(4):
                sl = acc[:, ch * 512:(ch + 1) * 512]
                if part == 0:
                    v.tensor_copy(out=wre[:, ch * 512:ch * 512 + 512], in_=sl)
                    v.tensor_copy(out=wim[:, (4 + ch) * 512:(4 + ch) * 512 + 512],
                                  in_=sl)
                else:
                    v.tensor_copy(out=wim[:, ch * 512:ch * 512 + 512], in_=sl)
            if part == 1:
                v.drain()
                for ch in range(4):
                    v.tensor_scalar(out=wre[:, (4 + ch) * 512:(4 + ch) * 512 + 512],
                                    in0=acc[:, ch * 512:(ch + 1) * 512],
                                    scalar1=-1.0, scalar2=None,
                                    op0=OP.mult, op1=OP.bypass)
        st.stage(vector=vst)

    for b in range(B):
        def ld_b(e, b=b):
            return dma(e, p, "sync",
                       sbx[:, :], xs_d[b])
        st.stage(sync=ld_b)
        emit_layernorm(st, env, 0,
                       x_at=lambda ch: sbx[:, ch * TPC:(ch + 1) * TPC],
                       xn_at=lambda ch: xn[:, ch * TPC:(ch + 1) * TPC])

        for oc in range(DCH):
            wsel = wre if oc < 4 else wim
            occ = oc % 4

            def t_p(t, wsel=wsel, occ=occ):
                for ic in range(DCH):
                    t.matmul(ps_x[:, 0:TPC],
                             wsel[:, ic * 512 + occ * 128: ic * 512 + occ * 128 + 128],
                             xn[:, ic * TPC:(ic + 1) * TPC],
                             start=(ic == 0), stop=(ic == DCH - 1))

            def v_o(v, oc=oc):
                sl = sbx[:, oc * TPC:(oc + 1) * TPC]
                v.scalar_tensor_tensor(out=sl, in0=ps_x[:, 0:TPC],
                                       scalar=bcol[:, oc:oc + 1], in1=sl,
                                       op0=OP.add, op1=OP.add)
            st.stage(tensor=t_p)
            st.stage(vector=v_o)

        def st_b(e, b=b):
            return dma(e, p, "sync",
                       xo_d[b], sbx[:, :])
        st.stage(sync=st_b)




def build_l1ffn():
    p = Prog("l1ffn")
    p.pf = "f1_"
    xs_d = p.inp("xs", (B, 128, DCH * TPC))
    xo_d = p.outp("xo", (B, 128, DCH * TPC))
    p.start_block()
    st = Stager(p)
    emit_l1ffn(p, st, xs_d, xo_d)
    st.stage(sync=lambda e: None)
    return p

def maps_l1ffn(I):
    G1R = np.asarray(I["L1_ffn_G1_R"], np.float32)[0]
    G2R = np.asarray(I["L1_ffn_G2_R"], np.float32)[:, :, 0, :]
    G1I = np.asarray(I["L1_ffn_G1_I"], np.float32)[0]
    G2I = np.asarray(I["L1_ffn_G2_I"], np.float32)[:, :, 0, :]

    def factors(G1, G2):
        t1 = np.zeros((4, 512, 512), np.float32)
        t2 = np.zeros((4, 512, 512), np.float32)
        for c in range(4):
            A = G1[:, c, :].T
            Bm = G2[c]
            t1[c] = np.repeat(np.repeat(A, 16, 0), 16, 1)
            t2[c] = np.tile(Bm.T, (32, 32))
        return t1, t2

    t1r, t2r = factors(G1R, G2R)
    t1i, t2i = factors(G1I, G2I)
    base = {
        "t1r": t1r, "t2r": t2r, "t1i": t1i, "t2i": t2i,
        "br": _col(I["L1_ffn_bias_real"], 4),
        "bi": _col(I["L1_ffn_bias_imag"], 4),
    }
    return [{"f1_" + k: v for k, v in base.items()} for c in range(NCORES)]


def run_l1ffn(stream_shards, I):
    p = build_l1ffn()
    in_maps = maps_l1ffn(I)
    for c in range(NCORES):
        in_maps[c]["f1_xs"] = _to_qmaj(stream_shards[c])
    res = run_bass_kernel_spmd(p.nc, in_maps, core_ids=list(range(NCORES)))
    out = [_from_qmaj(res.results[c]["f1_xo"]) for c in range(NCORES)]
    p.close()
    return out




def emit_quant(st, Q, src_ap, dst_ap, W, mbits, elo, ehi, signed):
    p = st.p
    f = Q["f"][:, 0:W]
    g_ = Q["g"][:, 0:W]
    t_ = Q["t"][:, 0:W]
    si = Q["si"][:, 0:W]
    u0 = Q["u0"][:, 0:W]
    u1 = Q["u1"][:, 0:W]
    b0 = Q["b0"][:, 0:W]
    ei = Q["ei"][:, 0:W]
    two = Q["two"][:, 0:W]
    sg = Q["sg"][:, 0:W]

    def s1(s):
        if signed:
            s.activation(out=sg, in_=src_ap, func=AF.Sign)
        s.activation(out=g_, in_=src_ap, func=AF.Square)
        s.drain()
        s.activation(out=f, in_=g_, func=AF.Ln)
        s.drain()
        s.activation(out=f, in_=f, func=AF.Relu, scale=0.5 * INV_LN2,
                     bias=Q["bl4" if elo == -6 else "bl5"][:, 0:1])
    st.stage(scalar=s1)

    def v1(v):
        v.tensor_scalar(out=g_, in0=f, scalar1=float(ehi - elo),
                        scalar2=float(elo) - 0.5, op0=OP.min, op1=OP.add)
        v.drain()
        v.tensor_copy(out=ei, in_=g_)
        v.drain()
        v.tensor_copy(out=t_, in_=ei)
        v.drain()
        v.tensor_scalar(out=t_, in0=t_, scalar1=-1.0, scalar2=float(mbits),
                        op0=OP.mult, op1=OP.add)
    st.stage(vector=v1)

    def v_pow(v):
        v.tensor_scalar(out=b0, in0=t_, scalar1=7.0, scalar2=None,
                        op0=OP.add, op1=OP.bypass)
        v.memset(si, 2.0 ** -7)
        v.drain()
        for k in (16, 8, 4, 2, 1):
            v.tensor_scalar(out=u0, in0=b0, scalar1=float(k), scalar2=None,
                            op0=OP.is_ge, op1=OP.bypass)
            v.drain()
            v.tensor_scalar(out=u1, in0=u0, scalar1=float(2 ** k - 1),
                            scalar2=1.0, op0=OP.mult, op1=OP.add)
            v.scalar_tensor_tensor(out=b0, in0=u0, scalar=-float(k), in1=b0,
                                   op0=OP.mult, op1=OP.add)
            v.drain()
            v.tensor_tensor(out=si, in0=si, in1=u1, op=OP.mult)
    st.stage(vector=v_pow)

    def v2a(v):
        v.tensor_tensor(out=u0, in0=src_ap, in1=si, op=OP.mult)
    st.stage(vector=v2a)

    if signed:
        def s2(s):
            s.activation(out=u1, in_=u0, func=AF.Abs)
        st.stage(scalar=s2)
        ua = u1
    else:
        ua = u0

    def v2(v):
        if mbits == 3:
            cur = ua
            slots = [g_, t_]
            for i, wt in enumerate([16.0, 8.0, 4.0, 2.0, 1.0]):
                v.tensor_scalar(out=b0, in0=cur, scalar1=wt - 0.5, scalar2=-wt,
                                op0=OP.is_ge, op1=OP.mult)
                v.drain()
                nxt = slots[i % 2]
                v.tensor_tensor(out=nxt, in0=cur, in1=b0, op=OP.add)
                v.drain()
                cur = nxt
            v.scalar_tensor_tensor(out=b0, in0=cur, scalar=-1.0, in1=ua,
                                   op0=OP.mult, op1=OP.add)
            v.drain()
            v.reciprocal(out=si, in_=si)
            v.drain()
            if signed:
                v.tensor_tensor(out=g_, in0=b0, in1=si, op=OP.mult)
                v.drain()
                v.tensor_tensor(out=dst_ap, in0=g_, in1=sg, op=OP.mult)
            else:
                v.tensor_tensor(out=dst_ap, in0=b0, in1=si, op=OP.mult)
        else:
            v.tensor_scalar(out=b0, in0=ua, scalar1=1.0 / 64.0, scalar2=None,
                            op0=OP.mult, op1=OP.bypass)
            v.drain()
            v.tensor_copy(out=ei, in_=b0)
            v.drain()
            v.tensor_copy(out=f, in_=ei)
            v.drain()
            v.scalar_tensor_tensor(out=u0 if signed else g_, in0=f, scalar=-64.0,
                                   in1=ua, op0=OP.mult, op1=OP.add)
            w_ = u0 if signed else g_
            v.drain()
            v.tensor_scalar(out=w_, in0=w_, scalar1=64.0, scalar2=None,
                            op0=OP.add, op1=OP.bypass)
            v.drain()
            cur = w_
            slots = [t_, g_] if signed else [t_, u1]
            for i, wt in enumerate([64.0, 32.0, 16.0, 8.0, 4.0, 2.0, 1.0]):
                v.tensor_scalar(out=b0, in0=cur, scalar1=wt - 0.5, scalar2=-wt,
                                op0=OP.is_ge, op1=OP.mult)
                v.drain()
                nxt = slots[i % 2]
                v.tensor_tensor(out=nxt, in0=cur, in1=b0, op=OP.add)
                v.drain()
                cur = nxt
            v.scalar_tensor_tensor(out=b0, in0=cur, scalar=-1.0, in1=w_,
                                   op0=OP.mult, op1=OP.add)
            v.drain()
            v.scalar_tensor_tensor(out=w_, in0=f, scalar=64.0, in1=b0,
                                   op0=OP.mult, op1=OP.add)
            v.drain()
            v.tensor_scalar(out=w_, in0=w_, scalar1=-64.0, scalar2=None,
                            op0=OP.add, op1=OP.bypass)
            v.drain()
            v.reciprocal(out=si, in_=si)
            v.drain()
            if signed:
                v.tensor_tensor(out=g_, in0=w_, in1=si, op=OP.mult)
                v.drain()
                v.tensor_tensor(out=dst_ap, in0=g_, in1=sg, op=OP.mult)
            else:
                v.tensor_tensor(out=dst_ap, in0=w_, in1=si, op=OP.mult)
    st.stage(vector=v2)


def emit_l2ffn(p, st, xs_d, xo_d):
    p.pf = "f2_"
    HCH = 32
    w1T_d = p.inp("w1T", (D, 4096))
    b1_d = p.inp("b1", (128, HCH))
    b2_d = p.inp("b2", (128, DCH))
    w2slice_d = p.inp("w2slice", (512, D))

    w2q_loc = p.dram("w2q_loc", (512, D), BF16)
    w2q_all = p.dram("w2q_all", (4096, D), BF16)

    sbx = p.sb("sbx", (128, DCH * TPC), F32)
    xn = p.sb("xn", (128, DCH * TPC), BF16)
    w1sb = p.sb("w1sb", (128, DCH * 4096), BF16)
    w2blk = p.sb("w2blk", (128, HCH * 128), BF16)
    hbuf = p.sb("hbuf", (128, 2 * TPC), F32)
    hq = p.sb("hq", (128, HCH * TPC), BF16)
    obuf = p.sb("obuf", (128, DCH * TPC), F32)
    b1c = p.sb("b1c", (128, HCH), F32)
    b2c = p.sb("b2c", (128, DCH), F32)
    b2q = p.sb("b2q", (128, DCH), F32)
    w2src = obuf
    w2qs = hq
    srow = p.sb("srow", (1, 4 * TPC), F32)
    cone = p.sb("cone", (128, 1), BF16)
    onesrow = p.sb("onesrow", (1, 128), F32)
    eps_col = p.sb("eps_col", (128, 1), F32)

    QW = 1024
    Q = {}
    for nm in ["f", "g", "t", "si", "u0", "u1", "b0", "sg", "two"]:
        Q[nm] = p.sb("q_" + nm, (128, QW), F32)
    Q["ei"] = p.sb("q_ei", (128, QW), I32)
    Q["bl4"] = p.sb("q_bl4", (128, 1), F32)
    Q["bl5"] = p.sb("q_bl5", (128, 1), F32)
    lnt = p.sb("lnt", (128, 2 * TPC), F32)

    ps_s = [p.ps("ps_s%d" % i) for i in range(4)]
    ps_x = p.ps("ps_x")
    env = dict(stream=None, xn=xn, sq=None, sqw=hq, srow=srow, lnt=lnt,
               cone=cone, onesrow=onesrow, ps_stat=ps_s[0], ps_b0=ps_s[1],
               ps_b1=ps_s[2], eps_col=eps_col)


    def ld0(e):
        n = dma(e, p, "sync", b1c[:, :], b1_d[:, :])
        n += dma(e, p, "sync", b2c[:, :], b2_d[:, :])
        n += dma(e, p, "sync",
                 w2src[:, :].rearrange("q (a o) -> q a o", a=4),
                 w2slice_d.rearrange("(a q) o -> q a o", q=128))
        return n

    def iv(v):
        v.memset(cone[:, :], 1.0 / 1024.0)
        v.memset(onesrow[:, :], 1.0)
        v.memset(eps_col[:, :], 1e-5)
        v.memset(Q["two"][:, :], 2.0)
        v.memset(Q["bl4"][:, :], 6.0)
        v.memset(Q["bl5"][:, :], 14.0)
    st.stage(sync=ld0, vector=iv)
    w1r = w1T_d.rearrange("(ic q) o -> q ic o", q=128)
    cast_stream(st, p,
                [(w1sb[:, ic * 4096 + o4 * 1024:ic * 4096 + (o4 + 1) * 1024],
                  w1r[:, ic, o4 * 1024:(o4 + 1) * 1024], 1024)
                 for ic in range(DCH) for o4 in range(4)],
                hbuf[:, 0:1024], lnt[:, 0:1024])

    for a in range(4):
        emit_quant(st, Q, w2src[:, a * D:(a + 1) * D], w2qs[:, a * D:(a + 1) * D],
                   D, 3, -6, 9.9, True)
    emit_quant(st, Q, b2c[:, 0:DCH], b2q[:, 0:DCH], DCH, 10, -14, 15.9, True)

    def stw2(e):
        return dma(e, p, "sync",
                   w2q_loc.rearrange("(a q) o -> q a o", q=128),
                   w2qs[:, 0:4 * D].rearrange("q (a o) -> q a o", a=4))
    st.stage(sync=stw2)

    def g_cc(g):
        cs = p.sem("cc")
        g.collective_compute("AllGather", OP.bypass,
                             replica_groups=[list(range(NCORES))],
                             ins=[w2q_loc[:, :].opt()],
                             outs=[w2q_all[:, :].opt()]).then_inc(cs[0], 1)
        cs[1] += 1
        g.wait_ge(cs[0], cs[1])
    st.stage(gpsimd=g_cc)

    for b in range(B):
        def ld_b(e, b=b):
            return dma(e, p, "sync",
                       sbx[:, :], xs_d[b])
        st.stage(sync=ld_b)
        emit_layernorm(st, env, 0,
                       x_at=lambda ch: sbx[:, ch * TPC:(ch + 1) * TPC],
                       xn_at=lambda ch: xn[:, ch * TPC:(ch + 1) * TPC])

        for hg in range(16):
            for j in range(2):
                hc = hg * 2 + j

                def t_h(t, hc=hc):
                    for ic in range(DCH):
                        t.matmul(ps_x[:, 0:TPC],
                                 w1sb[:, ic * 4096 + hc * 128: ic * 4096 + hc * 128 + 128],
                                 xn[:, ic * TPC:(ic + 1) * TPC],
                                 start=(ic == 0), stop=(ic == DCH - 1))

                def s_h(s, hc=hc, j=j):
                    s.activation(out=hbuf[:, j * TPC:(j + 1) * TPC],
                                 in_=ps_x[:, 0:TPC], func=AF.Relu,
                                 bias=b1c[:, hc:hc + 1])
                st.stage(tensor=t_h)
                st.stage(scalar=s_h)
            emit_quant(st, Q, hbuf[:, 0:2 * TPC],
                       hq[:, hg * 2 * TPC:(hg + 1) * 2 * TPC],
                       2 * TPC, 3, -6, 9.9, False)

        for oc in range(DCH):
            def ld_w2b(e, oc=oc):
                return dma(e, p, "sync",
                           w2blk[:, :].rearrange("q (ic w) -> q ic w", ic=HCH),
                           w2q_all.rearrange("(ic q) o -> q ic o", q=128)[
                               :, :, oc * 128:(oc + 1) * 128])
            st.stage(sync=ld_w2b)

            def t_o(t, oc=oc):
                for ic in range(HCH):
                    t.matmul(ps_x[:, 0:TPC],
                             w2blk[:, ic * 128:ic * 128 + 128],
                             hq[:, ic * TPC:(ic + 1) * TPC],
                             start=(ic == 0), stop=(ic == HCH - 1))

            def s_o(s, oc=oc):
                s.activation(out=obuf[:, oc * TPC:(oc + 1) * TPC],
                             in_=ps_x[:, 0:TPC], func=AF.Identity,
                             bias=b2q[:, oc:oc + 1])
            st.stage(tensor=t_o)
            st.stage(scalar=s_o)

        for quarter in range(4):
            sl = obuf[:, quarter * 2 * TPC:(quarter + 1) * 2 * TPC]
            emit_quant(st, Q, sl, sl, 2 * TPC, 10, -14, 15.9, True)

        def v_res(v):
            for ch in range(DCH):
                sl = sbx[:, ch * TPC:(ch + 1) * TPC]
                v.tensor_tensor(out=sl, in0=sl, in1=obuf[:, ch * TPC:(ch + 1) * TPC],
                                op=OP.add)
        st.stage(vector=v_res)

        def st_b(e, b=b):
            return dma(e, p, "sync",
                       xo_d[b], sbx[:, :])
        st.stage(sync=st_b)




def build_l2ffn():
    p = Prog("l2ffn")
    p.pf = "f2_"
    xs_d = p.inp("xs", (B, 128, DCH * TPC))
    xo_d = p.outp("xo", (B, 128, DCH * TPC))
    p.start_block()
    st = Stager(p)
    emit_l2ffn(p, st, xs_d, xo_d)
    st.stage(sync=lambda e: None)
    return p

def maps_l2ffn(I):
    w2T = _t(I["L2_ffn_w2"])
    base = {
        "w1T": _t(I["L2_ffn_w1"]),
        "b1": _col(I["L2_ffn_b1"], 32),
        "b2": _col(I["L2_ffn_b2"]),
    }
    in_maps = []
    for c in range(NCORES):
        m = dict(base)
        m["w2slice"] = np.ascontiguousarray(w2T[c * 512:(c + 1) * 512])
        in_maps.append({"f2_" + k: v for k, v in m.items()})
    return in_maps


def run_l2ffn(stream_shards, I):
    p = build_l2ffn()
    in_maps = maps_l2ffn(I)
    for c in range(NCORES):
        in_maps[c]["f2_xs"] = _to_qmaj(stream_shards[c])
    res = run_bass_kernel_spmd(p.nc, in_maps, core_ids=list(range(NCORES)))
    out = [_from_qmaj(res.results[c]["f2_xo"]) for c in range(NCORES)]
    p.close()
    return out




def emit_l3ffn(p, st, xs_d, xo_d):
    p.pf = "f3_"
    HCH = 32
    wexpTs_d = p.inp("wexpTs", (4096, 128))
    bexp_d = p.inp("bexpc", (128, HCH))
    wcons_d = p.inp("wcons", (4096, 128))
    stU_d = p.inp("stU", (7, 32, 128, 128))
    stD_d = p.inp("stD", (7, 32, 128, 128))
    wcU_d = p.inp("wcU", (5, 128, 64))
    wcD_d = p.inp("wcD", (5, 128, 64))
    upb_d = p.inp("upb", (128, HCH))
    bsum_d = p.inp("bsumc", (128, DCH))
    dnb_d = p.inp("dnbc", (128, DCH))

    w1_loc = p.dram("w1_loc", (128, 4096), BF16)
    w1_all = p.dram("w1_all", (NCORES, 128, 4096), BF16)
    w2_loc = p.dram("w2_loc", (4096, 128), BF16)
    w2_all = p.dram("w2_all", (NCORES, 4096, 128), BF16)

    sbx = p.sb("sbx", (128, DCH * TPC), F32)
    xn = p.sb("xn", (128, DCH * TPC), BF16)
    w1sb = p.sb("w1sb", (128, DCH * 4096), BF16)
    h2 = p.sb("h2", (128, HCH * TPC), BF16)
    w2blk = p.sb("w2blk", (128, HCH * 128), BF16)
    wt = p.sb("wt", (128, 32 * 132), BF16)
    wt2 = p.sb("wt2", (128, 32 * 132), BF16)
    stile = p.sb("stile", (128, 4 * 128), BF16)
    wcol = p.sb("wcol", (128, 64), F32)
    tmp1 = p.sb("tmp1", (128, 132), F32)
    srow = p.sb("srow", (1, 4 * TPC), F32)
    lnt = p.sb("lnt", (128, 2 * TPC), F32)
    cone = p.sb("cone", (128, 1), BF16)
    onesrow = p.sb("onesrow", (1, 128), F32)
    eps_col = p.sb("eps_col", (128, 1), F32)
    idt = p.sb("idt", (128, 128), BF16)
    iw = p.sb("iw", (128, 128), I32)
    b1c = p.sb("b1c", (128, HCH), F32)
    upbc = p.sb("upbc", (128, HCH), F32)
    boc = p.sb("boc", (128, DCH), F32)
    dnbc = p.sb("dnbc_s", (128, DCH), F32)

    ps_s = [p.ps("ps_s%d" % i) for i in range(4)]
    ps_x = p.ps("ps_x")
    ps_t = p.ps("ps_t", (128, 128), BF16)
    env = dict(stream=None, xn=xn, sq=None, sqw=h2, srow=srow, lnt=lnt,
               cone=cone, onesrow=onesrow, ps_stat=ps_s[0], ps_b0=ps_s[1],
               ps_b1=ps_s[2], eps_col=eps_col)


    def ld0(e):
        n = dma(e, p, "sync", upbc[:, :], upb_d[:, :])
        n += dma(e, p, "sync", boc[:, :], bsum_d[:, :])
        n += dma(e, p, "sync", dnbc[:, :], dnb_d[:, :])
        n += dma(e, p, "sync", b1c[:, :], bexp_d[:, :])
        return n

    def ld0g(e):
        n = dma(e, p, "gpsimd",
                wt[:, :].rearrange("q (jb s) -> q jb s", s=132)[:, :, 0:128],
                wexpTs_d.rearrange("(jb q) d -> q jb d", q=128))
        return n

    def iv(v):
        v.memset(cone[:, :], 1.0 / 1024.0)
        v.memset(onesrow[:, :], 1.0)
        v.memset(eps_col[:, :], 1e-5)
        v.memset(wt[:, :].rearrange("q (jb s) -> q jb s", s=132)[:, :, 128:132], 0.0)
        v.drain()
        for jb in range(HCH):
            v.tensor_copy(out=wt[:, jb * 132 + 128:jb * 132 + 129],
                          in_=b1c[:, jb:jb + 1])
        v.drain()
        v.tensor_tensor(out=boc[:, :], in0=boc[:, :], in1=dnbc[:, :], op=OP.add)
    st.stage(sync=ld0, gpsimd=ld0g)
    st.stage(vector=iv)

    def g_iota(g):
        g.iota(iw[:, :], pattern=[[1, 128]], base=0, channel_multiplier=-1)
    st.stage(gpsimd=g_iota)

    def v_idt(v):
        v.tensor_scalar(out=idt[:, :], in0=iw[:, :], scalar1=0.0, scalar2=None,
                        op0=OP.is_equal, op1=OP.bypass)
    st.stage(vector=v_idt)

    def fold(std_d, wc_d, a_order, b_first, b_trans, width):
        bufs = [wt, wt2]
        cur = 0

        def tile(bufi, jb):
            return bufs[bufi][:, jb * 132:jb * 132 + width]

        def do_bstages():
            nonlocal cur
            for si, s in enumerate((range(4, -1, -1) if b_trans else range(5))):
                sb = s
                def ld_wc(e, sb=sb):
                    return dma(e, p, "sync", wcol[:, :], wc_d[sb])
                st.stage(sync=ld_wc)

                def v_b(v, sb=sb, cur=cur):
                    pairs = [ba for ba in range(32) if not (ba >> sb) & 1]
                    for pi, ba in enumerate(pairs):
                        bb_ = ba + (1 << sb)
                        a_ap = tile(cur, ba)
                        b_ap = tile(cur, bb_)
                        c00 = wcol[:, pi * 4 + 0:pi * 4 + 1]
                        c01 = wcol[:, pi * 4 + 1:pi * 4 + 2]
                        c10 = wcol[:, pi * 4 + 2:pi * 4 + 3]
                        c11 = wcol[:, pi * 4 + 3:pi * 4 + 4]
                        if b_trans:
                            c01, c10 = c10, c01
                        v.tensor_scalar(out=tmp1[:, 0:width], in0=a_ap, scalar1=c00,
                                        scalar2=None, op0=OP.mult, op1=OP.bypass)
                        v.drain()
                        v.scalar_tensor_tensor(out=tile(1 - cur, ba), in0=b_ap,
                                               scalar=c01, in1=tmp1[:, 0:width],
                                               op0=OP.mult, op1=OP.add)
                        v.drain()
                        v.tensor_scalar(out=tmp1[:, 0:width], in0=a_ap, scalar1=c10,
                                        scalar2=None, op0=OP.mult, op1=OP.bypass)
                        v.drain()
                        v.scalar_tensor_tensor(out=tile(1 - cur, bb_), in0=b_ap,
                                               scalar=c11, in1=tmp1[:, 0:width],
                                               op0=OP.mult, op1=OP.add)
                        v.drain()
                st.stage(vector=v_b)
                cur = 1 - cur

        def do_astages():
            nonlocal cur
            for s in a_order:
                def ld_st(e, s=s):
                    n = 0
                    for g4 in range(8):
                        pass
                    return 0
                for g4 in range(8):
                    def ld_s4(e, s=s, g4=g4):
                        return dma(e, p, "gpsimd",
                                   stile[:, :].rearrange("q (k z) -> q k z", k=4),
                                   std_d[s, 4 * g4:4 * g4 + 4].rearrange(
                                       "k q z -> q k z"))
                    st.stage(sync=lambda e: None, gpsimd=ld_s4)

                    def t_a(t, g4=g4, cur=cur):
                        for k in range(4):
                            t.matmul(ps_s[k][:, 0:width],
                                     stile[:, k * 128:(k + 1) * 128],
                                     tile(cur, 4 * g4 + k), start=True, stop=True)
                    st.stage(tensor=t_a)

                    def v_a(v, g4=g4, cur=cur):
                        for k in range(4):
                            v.tensor_copy(out=tile(1 - cur, 4 * g4 + k),
                                          in_=ps_s[k][:, 0:width])
                    st.stage(vector=v_a)
                cur = 1 - cur

        if b_first:
            do_bstages()
            do_astages()
        else:
            do_astages()
            do_bstages()
        return cur

    cur = fold(stU_d, wcU_d, list(range(7)), False, False, 132)

    def v_b1(v, cur=cur):
        for jb in range(HCH):
            v.tensor_copy(out=b1c[:, jb:jb + 1],
                          in_=(wt if cur == 0 else wt2)[:, jb * 132 + 128:jb * 132 + 129])
        v.drain()
        v.tensor_tensor(out=b1c[:, :], in0=b1c[:, :], in1=upbc[:, :], op=OP.add)
    st.stage(vector=v_b1)

    for jb in range(HCH):
        def t_tr(t, jb=jb, cur=cur):
            t.transpose(ps_t[:, 0:128],
                        (wt if cur == 0 else wt2)[:, jb * 132:jb * 132 + 128],
                        idt[:, :])
        st.stage(tensor=t_tr)

        def v_tr(v, jb=jb):
            v.tensor_copy(out=wt2[:, 0:128] if False else stile[:, 0:128],
                          in_=ps_t[:, 0:128])
        st.stage(vector=v_tr)

        def st_w1(e, jb=jb):
            return dma(e, p, "sync", w1_loc[:, jb * 128:(jb + 1) * 128],
                       stile[:, 0:128])
        st.stage(sync=st_w1)

    def g_ag1(g):
        cs = p.sem("cc")
        g.collective_compute("AllGather", OP.bypass,
                             replica_groups=[list(range(NCORES))],
                             ins=[w1_loc[:, :].opt()],
                             outs=[w1_all[:, :, :].opt()]).then_inc(cs[0], 1)
        cs[1] += 1
        g.wait_ge(cs[0], cs[1])
    st.stage(gpsimd=g_ag1)

    def ld_wc2(e):
        return dma(e, p, "gpsimd",
                   wt[:, :].rearrange("q (jb s) -> q jb s", s=132)[:, :, 0:128],
                   wcons_d.rearrange("(jb q) d -> q jb d", q=128))
    st.stage(gpsimd=ld_wc2)
    cur2 = fold(stD_d, wcD_d, list(range(6, -1, -1)), True, True, 128)

    for jb in range(HCH):
        def st_w2(e, jb=jb, cur2=cur2):
            return dma(e, p, "sync", w2_loc[jb * 128:(jb + 1) * 128, :],
                       (wt if cur2 == 0 else wt2)[:, jb * 132:jb * 132 + 128])
        st.stage(sync=st_w2)

    def g_ag2(g):
        cs = p.sem("cc")
        g.collective_compute("AllGather", OP.bypass,
                             replica_groups=[list(range(NCORES))],
                             ins=[w2_loc[:, :].opt()],
                             outs=[w2_all[:, :, :].opt()]).then_inc(cs[0], 1)
        cs[1] += 1
        g.wait_ge(cs[0], cs[1])
    st.stage(gpsimd=g_ag2)

    def ld_w1(e):
        n = 0
        for dc in range(NCORES):
            n += dma(e, p, "sync", w1sb[:, dc * 4096:(dc + 1) * 4096], w1_all[dc])
        return n
    st.stage(sync=ld_w1)

    for b in range(B):
        def ld_b(e, b=b):
            return dma(e, p, "sync",
                       sbx[:, :], xs_d[b])
        st.stage(sync=ld_b)
        emit_layernorm(st, env, 0,
                       x_at=lambda ch: sbx[:, ch * TPC:(ch + 1) * TPC],
                       xn_at=lambda ch: xn[:, ch * TPC:(ch + 1) * TPC])

        for hc in range(HCH):
            def t_h(t, hc=hc):
                for ic in range(DCH):
                    t.matmul(ps_x[:, 0:TPC],
                             w1sb[:, ic * 4096 + hc * 128: ic * 4096 + hc * 128 + 128],
                             xn[:, ic * TPC:(ic + 1) * TPC],
                             start=(ic == 0), stop=(ic == DCH - 1))

            def s_h(s, hc=hc):
                s.activation(out=h2[:, hc * TPC:(hc + 1) * TPC],
                             in_=ps_x[:, 0:TPC], func=AF.Gelu,
                             bias=b1c[:, hc:hc + 1])
            st.stage(tensor=t_h)
            st.stage(scalar=s_h)

        for oc in range(DCH):
            def ld_w2b(e, oc=oc):
                n = 0
                for icq in range(NCORES):
                    pass
                return dma(e, p, "sync",
                           w2blk[:, :].rearrange("q (ic w) -> q ic w", ic=HCH),
                           w2_all[oc].rearrange("(ic q) w -> q ic w", q=128))
            st.stage(sync=ld_w2b)

            def t_o(t, oc=oc):
                for ic in range(HCH):
                    t.matmul(ps_x[:, 0:TPC],
                             w2blk[:, ic * 128:ic * 128 + 128],
                             h2[:, ic * TPC:(ic + 1) * TPC],
                             start=(ic == 0), stop=(ic == HCH - 1))

            def v_o(v, oc=oc):
                sl = sbx[:, oc * TPC:(oc + 1) * TPC]
                v.scalar_tensor_tensor(out=sl, in0=ps_x[:, 0:TPC],
                                       scalar=boc[:, oc:oc + 1], in1=sl,
                                       op0=OP.add, op1=OP.add)
            st.stage(tensor=t_o)
            st.stage(vector=v_o)

        def st_b(e, b=b):
            return dma(e, p, "sync",
                       xo_d[b], sbx[:, :])
        st.stage(sync=st_b)




def build_l3ffn():
    p = Prog("l3ffn")
    p.pf = "f3_"
    xs_d = p.inp("xs", (B, 128, DCH * TPC))
    xo_d = p.outp("xo", (B, 128, DCH * TPC))
    p.start_block()
    st = Stager(p)
    emit_l3ffn(p, st, xs_d, xo_d)
    st.stage(sync=lambda e: None)
    return p

def _l3_stage_tiles(weights, transposed):
    out = np.zeros((7, 32, 128, 128), np.float32)
    w = np.asarray(weights, np.float32)
    for s in range(7):
        sp = 2 ** s
        for blk in range(32):
            for kl in range(64):
                k = blk * 64 + kl
                a = (kl // sp) * 2 * sp + kl % sp
                bcol = a + sp
                w00, w01 = w[s, k, 0, 0], w[s, k, 0, 1]
                w10, w11 = w[s, k, 1, 0], w[s, k, 1, 1]
                out[s, blk, a, a] = w00
                out[s, blk, a, bcol] = w01
                out[s, blk, bcol, a] = w10
                out[s, blk, bcol, bcol] = w11
    if transposed:
        out = np.ascontiguousarray(out.transpose(0, 1, 3, 2))
    return np.ascontiguousarray(out)


def _l3_wcols(weights):
    w = np.asarray(weights, np.float32)
    out = np.zeros((5, 128, 64), np.float32)
    for sb in range(5):
        s = sb + 7
        sp = 2 ** sb
        pairs = [ba for ba in range(32) if not (ba >> sb) & 1]
        for pi, ba in enumerate(pairs):
            kbase = (ba >> (sb + 1)) * 128 * sp + (ba & (sp - 1)) * 128
            for comp in range(4):
                i_, j_ = comp // 2, comp % 2
                out[sb, :, pi * 4 + comp] = w[s, kbase:kbase + 128, i_, j_]
    return np.ascontiguousarray(out)


def maps_l3ffn(I):
    wexpT = _t(np.asarray(I["L3_ffn_W_exp"]))
    wcon = np.ascontiguousarray(np.asarray(I["L3_ffn_W_con"], np.float32))
    base = {
        "stU": _l3_stage_tiles(I["L3_ffn_up_weights"], True),
        "stD": _l3_stage_tiles(I["L3_ffn_down_weights"], False),
        "wcU": _l3_wcols(I["L3_ffn_up_weights"]),
        "wcD": _l3_wcols(I["L3_ffn_down_weights"]),
        "bexpc": _col(I["L3_ffn_b_exp"], 32),
        "upb": _col(I["L3_ffn_up_bias"], 32),
        "bsumc": _col(I["L3_ffn_b_con"]),
        "dnbc": _col(I["L3_ffn_down_bias"]),
    }
    in_maps = []
    for c in range(NCORES):
        m = dict(base)
        m["wexpTs"] = np.ascontiguousarray(wexpT[:, c * 128:(c + 1) * 128])
        m["wcons"] = np.ascontiguousarray(wcon[:, c * 128:(c + 1) * 128])
        in_maps.append({"f3_" + k: v for k, v in m.items()})
    return in_maps


def run_l3ffn(stream_shards, I):
    p = build_l3ffn()
    in_maps = maps_l3ffn(I)
    for c in range(NCORES):
        in_maps[c]["f3_xs"] = _to_qmaj(stream_shards[c])
    res = run_bass_kernel_spmd(p.nc, in_maps, core_ids=list(range(NCORES)))
    out = [_from_qmaj(res.results[c]["f3_xo"]) for c in range(NCORES)]
    p.close()
    return out




EPS_RMS = 1.1920929e-07


def emit_l3attn(p, st, xs_d, xo_d):
    p.pf = "a3_"
    CH2 = 2
    CSZ = 16
    NCH = TPC // CSZ
    NG = B * CH2

    wmT_d = p.inp("wmT", (D, D))
    wpT_d = p.inp("wpT", (512, D))
    vc_d = p.inp("vc", (128, 2 * CH2))
    osc_d = p.inp("osc", (128, DCH))
    ohc_d = p.inp("ohc", (128, NCORES))
    idc_d = p.inp("idc", (128, 1))

    car_loc = p.dram("car_loc", (128, 4 * NG), F32)
    car_all = p.dram("car_all", (NCORES, 128, 4 * NG), F32)

    sbx = p.sb("sbx", (128, DCH * TPC), F32)
    xn = p.sb("xn", (128, DCH * TPC), BF16)
    xr = p.sb("xr", (128, DCH * TPC), BF16)
    wm = p.sb("wm", (128, DCH * D), BF16)
    wp = p.sb("wp", (128, 4 * D), BF16)
    mcall = p.sb("mcall", (128, B * 4 * CH2 * TPC), F32)
    pref = p.sb("pref", (128, NG * 4 * NCH), F32)
    hh = p.sb("hh", (128, 2 * CH2 * TPC), F32)
    hbf = p.sb("hbf", (128, 2 * CH2 * TPC), BF16)
    n2 = p.sb("n2", (128, CH2 * TPC), F32)
    ssc = p.sb("ssc", (128, 16 * NCH), F32)
    car = p.sb("car", (128, 4 * NG), F32)
    carg = p.sb("carg", (128, 4 * NG), F32)
    rrun = p.sb("rrun", (128, 2 * 4 * NG), F32)
    hown = p.sb("hown", (128, 4 * NG), F32)
    gtile = p.sb("gtile", (128, 4 * (NCH + 1)), F32)
    wtl = p.sb("wtl", (128, 6 * NCH), F32)
    ones32 = p.sb("ones32", (128, NCH), F32)
    vcc = p.sb("vcc", (128, 2 * CH2), F32)
    osc = p.sb("osc_s", (128, DCH), F32)
    ohc = p.sb("ohc_s", (128, NCORES), F32)
    idc = p.sb("idc_s", (128, 1), F32)
    sqw = p.sb("sqw", (128, DCH * TPC), BF16)
    srow = p.sb("srow", (1, 4 * TPC), F32)
    lnt = p.sb("lnt", (128, DCH * TPC), F32)
    cone = p.sb("cone", (128, 1), BF16)
    onesrow = p.sb("onesrow", (1, 128), F32)
    eps_col = p.sb("eps_col", (128, 1), F32)
    eps_col2 = p.sb("eps_col2", (128, 1), F32)

    ps_s = [p.ps("ps_s%d" % i) for i in range(4)]
    ps_x = p.ps("ps_x")
    env = dict(stream=None, xn=xn, sq=None, sqw=sqw, srow=srow, lnt=lnt,
               cone=cone, onesrow=onesrow, ps_stat=ps_s[0], ps_b0=ps_s[1],
               ps_b1=ps_s[2], eps_col=eps_col, eps_col2=eps_col2)


    def S(i):
        return ssc[:, i * NCH:(i + 1) * NCH]

    def MC(b, comp, cc):
        o = ((b * 4 + comp) * CH2 + cc) * TPC
        return mcall[:, o:o + TPC]

    def PF(g, comp):
        o = (g * 4 + comp) * NCH
        return pref[:, o:o + NCH]

    def ld0(e):
        n = dma(e, p, "sync", vcc[:, :], vc_d[:, :])
        n += dma(e, p, "sync", osc[:, :], osc_d[:, :])
        n += dma(e, p, "sync", ohc[:, :], ohc_d[:, :])
        n += dma(e, p, "sync", idc[:, :], idc_d[:, :])
        return n

    def ldg(e):
        n = dma(e, p, "gpsimd",
                wm[:, :].rearrange("q (ic o) -> q ic o", ic=DCH),
                wmT_d.rearrange("(ic q) o -> q ic o", q=128))
        n += dma(e, p, "gpsimd",
                 wp[:, :].rearrange("q (ic o) -> q ic o", ic=4),
                 wpT_d.rearrange("(ic q) o -> q ic o", q=128))
        return n

    def iv(v):
        v.memset(cone[:, :], 1.0 / 1024.0)
        v.memset(onesrow[:, :], 1.0)
        v.memset(eps_col[:, :], 1e-5)
        v.memset(eps_col2[:, :], EPS_RMS)
        v.memset(ones32[:, :], 1.0)
    st.stage(sync=ld0, gpsimd=ldg, vector=iv)

    def mm22(v, O, A, Bt, t1, t2):
        for (oo, x1, y1, x2, y2) in [(O[0], A[0], Bt[0], A[1], Bt[2]),
                                     (O[1], A[0], Bt[1], A[1], Bt[3]),
                                     (O[2], A[2], Bt[0], A[3], Bt[2]),
                                     (O[3], A[2], Bt[1], A[3], Bt[3])]:
            v.tensor_tensor(out=t1, in0=x1, in1=y1, op=OP.mult)
            v.tensor_tensor(out=t2, in0=x2, in1=y2, op=OP.mult)
            v.drain()
            v.tensor_tensor(out=oo, in0=t1, in1=t2, op=OP.add)
            v.drain()

    for b in range(B):
        def ld_b(e, b=b):
            return dma(e, p, "sync",
                       sbx[:, :], xs_d[b])
        st.stage(sync=ld_b)
        emit_layernorm(st, env, 0,
                       x_at=lambda ch: sbx[:, ch * TPC:(ch + 1) * TPC],
                       xn_at=lambda ch: xn[:, ch * TPC:(ch + 1) * TPC])
        emit_layernorm(st, env, 0, rms=True, eps_override=EPS_RMS,
                       x_at=lambda ch: xn[:, ch * TPC:(ch + 1) * TPC],
                       xn_at=lambda ch: xr[:, ch * TPC:(ch + 1) * TPC])

        def v_resx(v):
            for ch in range(DCH):
                sl = sbx[:, ch * TPC:(ch + 1) * TPC]
                v.tensor_tensor(out=sl, in0=sl, in1=xn[:, ch * TPC:(ch + 1) * TPC],
                                op=OP.add)
        st.stage(vector=v_resx)

        def st_bx(e, b=b):
            return dma(e, p, "sync",
                       xo_d[b], sbx[:, :])
        st.stage(sync=st_bx)

        for oc in range(DCH):
            def t_p(t, oc=oc):
                for ic in range(DCH):
                    t.matmul(ps_x[:, 0:TPC],
                             wm[:, ic * D + oc * 128: ic * D + oc * 128 + 128],
                             xr[:, ic * TPC:(ic + 1) * TPC],
                             start=(ic == 0), stop=(ic == DCH - 1))

            def v_e(v, oc=oc, b=b):
                comp, cc = oc // 2, oc % 2
                v.tensor_copy(out=MC(b, comp, cc), in_=ps_x[:, 0:TPC])
            st.stage(tensor=t_p)
            st.stage(vector=v_e)

        def s_sq0(s, b=b):
            for cc in range(CH2):
                s.activation(out=n2[:, cc * TPC:(cc + 1) * TPC],
                             in_=MC(b, 0, cc), func=AF.Square)
        st.stage(scalar=s_sq0)
        for comp in range(1, 4):
            def s_sqk(s, comp=comp, b=b):
                for cc in range(CH2):
                    s.activation(out=lnt[:, cc * TPC:(cc + 1) * TPC],
                                 in_=MC(b, comp, cc), func=AF.Square)
            st.stage(scalar=s_sqk)

            def v_acc(v):
                for cc in range(CH2):
                    v.tensor_tensor(out=n2[:, cc * TPC:(cc + 1) * TPC],
                                    in0=n2[:, cc * TPC:(cc + 1) * TPC],
                                    in1=lnt[:, cc * TPC:(cc + 1) * TPC], op=OP.add)
            st.stage(vector=v_acc)

        def v_cl(v):
            for cc in range(CH2):
                v.tensor_scalar(out=n2[:, cc * TPC:(cc + 1) * TPC],
                                in0=n2[:, cc * TPC:(cc + 1) * TPC],
                                scalar1=1e-12, scalar2=None, op0=OP.max,
                                op1=OP.bypass)
        st.stage(vector=v_cl)

        def s_rn(s):
            for cc in range(CH2):
                s.activation(out=n2[:, cc * TPC:(cc + 1) * TPC],
                             in_=n2[:, cc * TPC:(cc + 1) * TPC], func=AF.Ln)
            s.drain()
            for cc in range(CH2):
                s.activation(out=n2[:, cc * TPC:(cc + 1) * TPC],
                             in_=n2[:, cc * TPC:(cc + 1) * TPC], func=AF.Exp,
                             scale=-0.5)
        st.stage(scalar=s_rn)

        def v_nm(v, b=b):
            for comp in range(4):
                for cc in range(CH2):
                    v.tensor_tensor(out=MC(b, comp, cc), in0=MC(b, comp, cc),
                                    in1=n2[:, cc * TPC:(cc + 1) * TPC], op=OP.mult)
                v.drain()
        st.stage(vector=v_nm)

        for cc in range(CH2):
            g = b * CH2 + cc

            def msl(comp, i, b=b, cc=cc):
                return MC(b, comp, cc).rearrange("q (c z) -> q c z", z=CSZ)[
                    :, :, i]

            def v_p12(v, b=b, cc=cc, g=g):
                for comp in range(4):
                    v.tensor_copy(out=S(comp), in_=msl(comp, 0))
                v.drain()
                cur = 0
                for i in range(1, CSZ):
                    co, no = cur * 4, (1 - cur) * 4
                    mm22(v, [S(no + k) for k in range(4)],
                         [msl(k, i) for k in range(4)],
                         [S(co + k) for k in range(4)], S(8), S(9))
                    cur = 1 - cur
                if cur == 1:
                    for comp in range(4):
                        v.tensor_copy(out=S(comp), in_=S(4 + comp))
                    v.drain()
                cur = 0
                for k in range(5):
                    sh = 1 << k
                    co, no = cur * 4, (1 - cur) * 4
                    for comp in range(4):
                        v.tensor_copy(out=S(no + comp)[:, 0:sh],
                                      in_=S(co + comp)[:, 0:sh])
                    v.drain()
                    W = NCH - sh
                    mm22(v, [S(no + j)[:, sh:] for j in range(4)],
                         [S(co + j)[:, sh:] for j in range(4)],
                         [S(co + j)[:, 0:W] for j in range(4)],
                         S(8)[:, 0:W], S(9)[:, 0:W])
                    cur = 1 - cur
                base = cur * 4
                for comp in range(4):
                    v.tensor_copy(out=PF(g, comp), in_=S(base + comp))
                v.drain()
                for comp in range(4):
                    v.tensor_copy(out=car[:, g * 4 + comp:g * 4 + comp + 1],
                                  in_=S(base + comp)[:, NCH - 1:NCH])
            st.stage(vector=v_p12)

    def st_car(e):
        return dma(e, p, "sync", car_loc[:, :], car[:, :])
    st.stage(sync=st_car)

    def g_cc(g):
        cs = p.sem("cc")
        g.collective_compute("AllGather", OP.bypass,
                             replica_groups=[list(range(NCORES))],
                             ins=[car_loc[:, :].opt()],
                             outs=[car_all[:, :, :].opt()]).then_inc(cs[0], 1)
        cs[1] += 1
        g.wait_ge(cs[0], cs[1])
    st.stage(gpsimd=g_cc)

    def v_hinit(v):
        v.memset(hown[:, :], 0.0)
        v.drain()
        for g_ in range(NG):
            v.tensor_scalar(out=hown[:, g_ * 4 + 0:g_ * 4 + 1], in0=idc[:, 0:1],
                            scalar1=1.0, scalar2=None, op0=OP.mult, op1=OP.bypass)
            v.tensor_scalar(out=hown[:, g_ * 4 + 3:g_ * 4 + 4], in0=idc[:, 0:1],
                            scalar1=1.0, scalar2=None, op0=OP.mult, op1=OP.bypass)
    st.stage(vector=v_hinit)

    for j in range(NCORES - 1):
        def ld_cj(e, j=j):
            return dma(e, p, "sync", carg[:, :], car_all[j])
        st.stage(sync=ld_cj)

        def v_comb(v, j=j):
            co = (j % 2) * 4 * NG
            no = ((j + 1) % 2) * 4 * NG
            if j == 0:
                v.tensor_copy(out=rrun[:, no:no + 4 * NG], in_=carg[:, :])
                v.drain()
            else:
                for g_ in range(NG):
                    A = [carg[:, g_ * 4 + k:g_ * 4 + k + 1] for k in range(4)]
                    Bt = [rrun[:, co + g_ * 4 + k:co + g_ * 4 + k + 1]
                          for k in range(4)]
                    O = [rrun[:, no + g_ * 4 + k:no + g_ * 4 + k + 1]
                         for k in range(4)]
                    mm22(v, O, A, Bt, S(8)[:, 0:1], S(9)[:, 0:1])
            v.drain()
            for k in range(4 * NG):
                pass
            v.scalar_tensor_tensor(out=hown[:, :], in0=rrun[:, no:no + 4 * NG],
                                   scalar=ohc[:, j:j + 1], in1=hown[:, :],
                                   op0=OP.mult, op1=OP.add)
        st.stage(vector=v_comb)

    for b in range(B):
        def ld_b2(e, b=b):
            return dma(e, p, "sync",
                       sbx[:, :], xo_d[b])
        st.stage(sync=ld_b2)

        for cc in range(CH2):
            g = b * CH2 + cc

            def hcol(comp, g=g):
                return hown[:, g * 4 + comp:g * 4 + comp + 1]

            def v_G(v, g=g, cc=cc):
                for (comp, x1, h1, x2, h2) in [(0, 0, 0, 1, 2), (1, 0, 1, 1, 3),
                                               (2, 2, 0, 3, 2), (3, 2, 1, 3, 3)]:
                    v.tensor_scalar(out=S(8), in0=PF(g, x1), scalar1=hcol(h1),
                                    scalar2=None, op0=OP.mult, op1=OP.bypass)
                    v.drain()
                    v.scalar_tensor_tensor(out=gtile[:, comp * 33 + 1:comp * 33 + 33],
                                           in0=PF(g, x2), scalar=hcol(h2),
                                           in1=S(8), op0=OP.mult, op1=OP.add)
                    v.drain()
                    v.tensor_scalar(out=gtile[:, comp * 33:comp * 33 + 1],
                                    in0=hcol(comp), scalar1=1.0, scalar2=None,
                                    op0=OP.mult, op1=OP.bypass)
                v.drain()
                v1c = vcc[:, cc:cc + 1]
                v2c = vcc[:, CH2 + cc:CH2 + cc + 1]
                v.tensor_scalar(out=S(8), in0=gtile[:, 0:NCH], scalar1=v1c,
                                scalar2=None, op0=OP.mult, op1=OP.bypass)
                v.drain()
                v.scalar_tensor_tensor(out=wtl[:, 0:NCH], in0=gtile[:, 33:33 + NCH],
                                       scalar=v2c, in1=S(8), op0=OP.mult, op1=OP.add)
                v.tensor_scalar(out=S(8), in0=gtile[:, 66:66 + NCH], scalar1=v1c,
                                scalar2=None, op0=OP.mult, op1=OP.bypass)
                v.drain()
                v.scalar_tensor_tensor(out=wtl[:, NCH:2 * NCH],
                                       in0=gtile[:, 99:99 + NCH],
                                       scalar=v2c, in1=S(8), op0=OP.mult, op1=OP.add)
                v.drain()
            st.stage(vector=v_G)

            def v_inner(v, b=b, cc=cc):
                def msl(comp, i):
                    return MC(b, comp, cc).rearrange("q (c z) -> q c z", z=CSZ)[
                        :, :, i]

                def hsl(h, i):
                    return hh[:, (h * CH2 + cc) * TPC:(h * CH2 + cc) * TPC + TPC]\
                        .rearrange("q (c z) -> q c z", z=CSZ)[:, :, i]
                cur = 0
                for i in range(CSZ):
                    co, no = cur * 2, 2 + (1 - cur) * 2 - 2
                    w1 = wtl[:, cur * 2 * NCH:cur * 2 * NCH + NCH]
                    w2 = wtl[:, cur * 2 * NCH + NCH:cur * 2 * NCH + 2 * NCH]
                    nw1 = wtl[:, (1 - cur) * 2 * NCH:(1 - cur) * 2 * NCH + NCH]
                    nw2 = wtl[:, (1 - cur) * 2 * NCH + NCH:
                              (1 - cur) * 2 * NCH + 2 * NCH]
                    v.tensor_tensor(out=S(8), in0=msl(0, i), in1=w1, op=OP.mult)
                    v.tensor_tensor(out=S(9), in0=msl(1, i), in1=w2, op=OP.mult)
                    v.drain()
                    v.tensor_tensor(out=nw1, in0=S(8), in1=S(9), op=OP.add)
                    v.tensor_tensor(out=S(8), in0=msl(2, i), in1=w1, op=OP.mult)
                    v.tensor_tensor(out=S(9), in0=msl(3, i), in1=w2, op=OP.mult)
                    v.drain()
                    v.tensor_tensor(out=nw2, in0=S(8), in1=S(9), op=OP.add)
                    v.drain()
                    v.tensor_copy(out=hsl(0, i), in_=nw1)
                    v.tensor_copy(out=hsl(1, i), in_=nw2)
                    v.drain()
                    cur = 1 - cur
            st.stage(vector=v_inner)

        def v_hc(v):
            for i in range(2 * CH2):
                v.tensor_copy(out=hbf[:, i * TPC:(i + 1) * TPC],
                              in_=hh[:, i * TPC:(i + 1) * TPC])
        st.stage(vector=v_hc)

        for oc in range(DCH):
            def t_o(t, oc=oc):
                for ic in range(4):
                    t.matmul(ps_x[:, 0:TPC],
                             wp[:, ic * D + oc * 128: ic * D + oc * 128 + 128],
                             hbf[:, ic * TPC:(ic + 1) * TPC],
                             start=(ic == 0), stop=(ic == 3))

            def v_o(v, oc=oc):
                sl = sbx[:, oc * TPC:(oc + 1) * TPC]
                v.scalar_tensor_tensor(out=sl, in0=ps_x[:, 0:TPC],
                                       scalar=osc[:, oc:oc + 1], in1=sl,
                                       op0=OP.mult, op1=OP.add)
            st.stage(tensor=t_o)
            st.stage(vector=v_o)

        def st_b2(e, b=b):
            return dma(e, p, "sync",
                       xo_d[b], sbx[:, :])
        st.stage(sync=st_b2)




def build_l3attn():
    p = Prog("l3attn")
    p.pf = "a3_"
    xs_d = p.inp("xs", (B, 128, DCH * TPC))
    xo_d = p.outp("xo", (B, 128, DCH * TPC))
    p.start_block()
    st = Stager(p)
    emit_l3attn(p, st, xs_d, xo_d)
    st.stage(sync=lambda e: None)
    return p

def maps_l3attn(I):
    CH2 = 2
    wm = np.asarray(I["L3_attn_W_m"], np.float32)
    wmT = _t(wm)
    idx = np.concatenate([np.arange(256) * 4 + comp for comp in range(4)])
    wmTp = np.ascontiguousarray(wmT[:, idx])
    wp = np.asarray(I["L3_attn_W_proj"], np.float32)
    wpT = _t(wp)
    ridx = np.concatenate([np.arange(256) * 2 + h for h in range(2)])
    wpTp = np.ascontiguousarray(wpT[ridx, :])
    v_vec = np.asarray(I["L3_attn_v_vec"], np.float32)[:, :, 0]
    vc = np.zeros((128, 2 * CH2), np.float32)
    for cc in range(CH2):
        vc[:, cc] = v_vec[cc * 128:(cc + 1) * 128, 0]
        vc[:, CH2 + cc] = v_vec[cc * 128:(cc + 1) * 128, 1]
    base = {
        "wmT": wmTp,
        "wpT": wpTp,
        "vc": vc,
        "osc": _col(I["L3_attn_out_scale"]),
    }
    in_maps = []
    for c in range(NCORES):
        m = dict(base)
        oh = np.zeros((128, NCORES), np.float32)
        if c > 0:
            oh[:, c - 1] = 1.0
        m["ohc"] = oh
        m["idc"] = np.full((128, 1), 1.0 if c == 0 else 0.0, np.float32)
        in_maps.append({"a3_" + k: v for k, v in m.items()})
    return in_maps


def run_l3attn(stream_shards, I):
    p = build_l3attn()
    in_maps = maps_l3attn(I)
    for c in range(NCORES):
        in_maps[c]["a3_xs"] = _to_qmaj(stream_shards[c])
    res = run_bass_kernel_spmd(p.nc, in_maps, core_ids=list(range(NCORES)))
    out = [_from_qmaj(res.results[c]["a3_xo"]) for c in range(NCORES)]
    p.close()
    return out




DI = 2048
NHM = 32
HDM = 64
NST = 128
CVD = 2304
DIP = 4384
CCH = 18
MCH = 128
NMC = TPC // MCH


def emit_l0attn(p, st, us_d, xo_d):
    p.pf = "a0_"
    ipT_d = p.inp("ipT", (D, DIP))
    cvw_d = p.inp("cvw", (128, CCH * 4))
    cvb_d = p.inp("cvb", (128, CCH))
    dtb_d = p.inp("dtb", (128, 1))
    alog_d = p.inp("alog", (3, 32))
    dcol_d = p.inp("dcol", (128, 16))
    nw_d = p.inp("nw", (128, 16))
    opT_d = p.inp("opT", (DI, D))
    ohc_d = p.inp("ohc", (128, NCORES))

    op_dram = p.dram("op_dram", (DCH, 128, 16 * 128), BF16)
    ip_bf = p.dram("ip_bf", (128, DCH, DIP), BF16)
    yz_dram = p.dram("yz_dram", (B, 2, 128, 16 * TPC), BF16)
    acg_dram = p.dram("acg_dram", (B, NMC, 32, MCH), F32)
    acl_dram = p.dram("acl_dram", (B, NMC, 32, MCH), F32)
    h_loc = p.dram("h_loc", (B, 128, DI), BF16)
    h_all = p.dram("h_all", (NCORES, B, 128, DI), BF16)
    dk_loc = p.dram("dk_loc", (NHM, B), F32)
    dk_all = p.dram("dk_all", (NCORES, NHM, B), F32)

    HW = 1536
    sbx = p.sb("sbx", (128, DCH * TPC), F32)
    xn = p.sb("xn", (128, DCH * (TPC + 3)), BF16)
    ipw = p.sb("ipw", (128, DCH * HW), BF16)
    opb = p.sb("opb", (128, 16 * 128), BF16)
    cvin = p.sb("cvin", (128, CCH * (TPC + 3)), BF16)
    xfm = p.sb("xfm", (128, 16 * TPC), BF16)
    xtm = p.sb("xtm", (128, 16 * TPC), BF16)
    bct = p.sb("bct", (128, 2 * TPC), BF16)
    call = p.sb("call", (128, B * TPC), BF16)
    ysb = cvin
    gsb = ysb
    zs3 = xfm
    hst = p.sb("hst", (128, DI), F32)
    hnw = p.sb("hnw", (128, DI), F32)
    hjb = p.sb("hjb", (128, DI), BF16)
    dtr = p.sb("dtr", (128, TPC), F32)
    dts = p.sb("dts", (128, 3 * TPC), F32)
    acg = p.sb("acg", (128, TPC), F32)
    offc = p.sb("offc", (128, 2), F32)
    acT = p.sb("acT", (128, NHM), F32)
    dtT = p.sb("dtT", (128, NHM), F32)
    wvt = p.sb("wvt", (128, 3 * NHM), F32)
    rowA = p.sb("rowA", (1, NHM * MCH + 256), F32)
    rowE = p.sb("rowE", (1, NHM * MCH), BF16)
    onesbf = p.sb("onesbf", (1, 128), BF16)
    tri = p.sb("tri", (128, 128), F32)
    iw = p.sb("iw", (128, 128), I32)
    idt = p.sb("idt", (128, 128), BF16)
    idt32 = p.sb("idt32m", (128, 128), F32)
    gwk = p.sb("gwk", (128, 8 * MCH), F32)
    gbf = p.sb("gbf", (128, 2 * MCH), BF16)
    xw = p.sb("xw", (128, DI), BF16)
    dec = p.sb("dec", (128, NHM), F32)
    dkb = p.sb("dkb", (32, B), F32)
    drow = p.sb("drow", (1, NCORES * NHM * B), F32)
    acol = p.sb("acol", (128, 4), F32)
    srow = p.sb("srow", (1, 4 * TPC), F32)
    lnt = p.sb("lnt", (128, 2 * TPC), F32)
    cone = p.sb("cone", (128, 1), BF16)
    c2 = p.sb("c2", (128, 1), BF16)
    onesrow = p.sb("onesrow", (1, 128), F32)
    eps_col = p.sb("eps_col", (128, 1), F32)
    eps_col2 = p.sb("eps_col2", (128, 1), F32)
    cvwc = p.sb("cvwc", (128, CCH * 4), F32)
    cvbc = p.sb("cvbc", (128, CCH), F32)
    dtbc = p.sb("dtbc", (128, 1), F32)
    dcc = p.sb("dcc", (128, 16), F32)
    nwc = p.sb("nwc", (128, 16), F32)
    ohc = p.sb("ohc_s", (128, NCORES), F32)
    alg = p.sb("alg", (3, 32), F32)
    onec = p.sb("onec", (128, 1), F32)

    ps_s = [p.ps("ps_s%d" % i) for i in range(4)]
    ps_x = p.ps("ps_x")
    ps_g = p.ps("ps_g")
    ps_t = p.ps("ps_t", (128, 128), BF16)
    ps_t2 = p.ps("ps_t2", (128, 128), BF16)
    sqw = xtm
    env = dict(stream=None, xn=xn, sq=None, sqw=sqw, srow=srow, lnt=lnt,
               cone=cone, cone2=c2, onesrow=onesrow, ps_stat=ps_s[0],
               ps_b0=ps_s[1], ps_b1=ps_s[2], eps_col=eps_col, eps_col2=eps_col2)


    TP3 = TPC + 3

    def ld0(e):
        n = dma(e, p, "sync", cvwc[:, :], cvw_d[:, :])
        n += dma(e, p, "sync", cvbc[:, :], cvb_d[:, :])
        n += dma(e, p, "sync", dtbc[:, :], dtb_d[:, :])
        n += dma(e, p, "sync", dcc[:, :], dcol_d[:, :])
        n += dma(e, p, "sync", nwc[:, :], nw_d[:, :])
        n += dma(e, p, "sync", ohc[:, :], ohc_d[:, :])
        n += dma(e, p, "sync", alg[:, :], alog_d[:, :])
        return n

    def iv(v):
        v.memset(cone[:, :], 1.0 / 1024.0)
        v.memset(c2[:, :], 1.0 / DI)
        v.memset(onesrow[:, :], 1.0)
        v.memset(eps_col[:, :], 1e-5)
        v.memset(eps_col2[:, :], 1e-5)
        v.memset(onec[:, :], 1.0)
        v.memset(rowA[0:1, NHM * MCH:NHM * MCH + 128], 1.0)
        v.memset(rowA[0:1, NHM * MCH + 128:NHM * MCH + 256], -1.0)
        v.memset(onesbf[0:1, :], 1.0)

    def ig(g):
        g.iota(iw[:, :], pattern=[[1, 128]], base=0, channel_multiplier=-1)
    st.stage(sync=ld0, vector=iv, gpsimd=ig)

    def v_tri(v):
        v.tensor_scalar(out=tri[:, :], in0=iw[:, :], scalar1=0.0, scalar2=None,
                        op0=OP.is_ge, op1=OP.bypass)
        v.drain()
        v.tensor_scalar(out=tri[:, :], in0=tri[:, :], scalar1=-1.0, scalar2=1e30,
                        op0=OP.add, op1=OP.mult)
        v.drain()
        v.tensor_scalar(out=idt[:, :], in0=iw[:, :], scalar1=0.0, scalar2=None,
                        op0=OP.is_equal, op1=OP.bypass)
        v.tensor_scalar(out=idt32[:, :], in0=iw[:, :], scalar1=0.0, scalar2=None,
                        op0=OP.is_equal, op1=OP.bypass)
    st.stage(vector=v_tri)

    def s_a(s):
        s.activation(out=alg[:, :], in_=alg[:, :], func=AF.Exp)
    st.stage(scalar=s_a)

    def v_a0(v):
        v.memset(acol[:, 1:2], -1.0 / 3.0)
    st.stage(vector=v_a0)

    def t_a(t):
        t.matmul(ps_x[0:32, 0:1], alg[0:3, :], acol[0:3, 1:2], start=True, stop=True)
    st.stage(tensor=t_a)

    def v_a1(v):
        v.tensor_copy(out=acol[:, 0:1], in_=ps_x[:, 0:1])
    st.stage(vector=v_a1)

    HH = DIP // 4
    ipchunks = [(ic, hf) for ic in range(DCH) for hf in range(4)]
    for k in range(len(ipchunks) + 2):
        em = {}

        def sy_ip(e, k=k):
            n = 0
            if k < len(ipchunks):
                ic, hf = ipchunks[k]
                n += dma(e, p, "sync", sbx[:, (k % 2) * HH:(k % 2) * HH + HH],
                         ipT_d[ic * 128:(ic + 1) * 128, hf * HH:(hf + 1) * HH])
            if k >= 2:
                ic, hf = ipchunks[k - 2]
                n += dma(e, p, "sync", ip_bf[:, ic, hf * HH:(hf + 1) * HH],
                         xn[:, (k % 2) * HH:(k % 2) * HH + HH])
            return n
        em["sync"] = sy_ip
        if 1 <= k <= len(ipchunks):
            def s_ip(s, k=k):
                s.activation(out=xn[:, ((k - 1) % 2) * HH:((k - 1) % 2) * HH + HH],
                             in_=sbx[:, ((k - 1) % 2) * HH:((k - 1) % 2) * HH + HH],
                             func=AF.Identity)
            em["scalar"] = s_ip
        st.stage(**em)

    for k in range(18):
        em = {}

        def sy_op(e, k=k):
            n = 0
            if k < 16:
                n += dma(e, p, "sync", sbx[:, (k % 2) * 1024:(k % 2) * 1024 + D],
                         opT_d[k * 128:(k + 1) * 128, :])
            if k >= 2:
                ic = k - 2
                n += dma(e, p, "sync",
                         op_dram.rearrange("o q i -> q o i")[
                             :, :, ic * 128:(ic + 1) * 128],
                         opb[:, (ic % 2) * 1024:(ic % 2) * 1024 + D]
                         .rearrange("q (o w) -> q o w", o=DCH))
            return n
        em["sync"] = sy_op
        if 1 <= k <= 16:
            ic = k - 1

            def s_op(s, ic=ic):
                s.activation(out=opb[:, (ic % 2) * 1024:(ic % 2) * 1024 + D],
                             in_=sbx[:, (ic % 2) * 1024:(ic % 2) * 1024 + D],
                             func=AF.Identity)
            em["scalar"] = s_op
        st.stage(**em)

    for b in range(B):
        def ld_b(e, b=b):
            return dma(e, p, "sync",
                       sbx[:, :].rearrange("q (c t) -> q c t", c=DCH),
                       us_d[b].rearrange("q (c t) -> q c t", t=TP3)[:, :, 3:TP3])
        st.stage(sync=ld_b)
        emit_layernorm(st, env, 0,
                       x_at=lambda ch: sbx[:, ch * TPC:(ch + 1) * TPC],
                       xn_at=lambda ch: xn[:, ch * TP3:ch * TP3 + TPC])
        def ld_b3(e, b=b):
            return dma(e, p, "sync",
                       lnt[:, 0:DCH * 3].rearrange("q (c t) -> q c t", c=DCH),
                       us_d[b].rearrange("q (c t) -> q c t", t=TP3)[:, :, 0:3])
        st.stage(sync=ld_b3)
        emit_layernorm(st, env, 0, T=3,
                       x_at=lambda ch: lnt[:, ch * 3:(ch + 1) * 3],
                       xn_at=lambda ch: xn[:, ch * TP3 + TPC:ch * TP3 + TPC + 3])

        def xnat(ic):
            return xn[:, ic * TP3:ic * TP3 + TPC]

        def xnat3(ic):
            return xn[:, ic * TP3 + TPC:ic * TP3 + TPC + 3]

        zst = bct
        for third in range(3):
            base_oc = third * 12
            ocs = list(range(base_oc, min(base_oc + 12, 35)))

            def ld_ip(e, third=third):
                lo = third * HW
                w = min(HW, DIP - lo)
                return dma(e, p, "sync",
                           ipw[:, :].rearrange("q (ic o) -> q ic o", o=HW)[
                               :, :, 0:w],
                           ip_bf[:, :, lo:lo + w])
            st.stage(sync=ld_ip)

            n_oc = len(ocs)
            for k in range(n_oc + 2):
                em = {}
                if k < n_oc:
                    oc = ocs[k]
                    po = 128 if oc < 34 else 32
                    ocl = oc - base_oc
                    pm = ps_x if k % 2 == 0 else ps_s[0]
                    pl = ps_s[3] if k % 2 == 0 else ps_s[1]

                    def t_p(t, oc=oc, po=po, ocl=ocl, pm=pm, pl=pl):
                        for ic in range(DCH):
                            t.matmul(pm[0:po, 0:TPC],
                                     ipw[:, ic * HW + ocl * 128:
                                         ic * HW + ocl * 128 + po],
                                     xnat(ic), start=(ic == 0),
                                     stop=(ic == DCH - 1))
                        if 16 <= oc < 34:
                            for ic in range(DCH):
                                t.matmul(pl[0:po, 0:3],
                                         ipw[:, ic * HW + ocl * 128:
                                             ic * HW + ocl * 128 + po],
                                         xnat3(ic), start=(ic == 0),
                                         stop=(ic == DCH - 1))
                    em["tensor"] = t_p
                if 1 <= k <= n_oc:
                    oc = ocs[k - 1]
                    pm = ps_x if (k - 1) % 2 == 0 else ps_s[0]
                    pl = ps_s[3] if (k - 1) % 2 == 0 else ps_s[1]

                    def v_e(v, oc=oc, pm=pm, pl=pl, k=k):
                        if oc < 16:
                            v.tensor_copy(out=zst[:, ((k - 1) % 2) * TPC:
                                                  ((k - 1) % 2) * TPC + TPC],
                                          in_=pm[:, 0:TPC])
                        elif oc < 34:
                            cc = oc - 16
                            v.tensor_copy(out=cvin[:, cc * TP3 + 3:
                                                   cc * TP3 + 3 + TPC],
                                          in_=pm[:, 0:TPC])
                            v.tensor_copy(out=cvin[:, cc * TP3:cc * TP3 + 3],
                                          in_=pl[:, 0:3])
                        else:
                            v.tensor_copy(out=dtr[0:32, 0:TPC],
                                          in_=pm[0:32, 0:TPC])
                    em["vector"] = v_e
                if 2 <= k and k - 2 < n_oc and ocs[k - 2] < 16:
                    oc2 = ocs[k - 2]

                    def st_z(e, oc2=oc2, k=k, b=b):
                        return dma(e, p, "sync",
                                   yz_dram[b, 1][:, oc2 * TPC:(oc2 + 1) * TPC],
                                   zst[:, ((k - 2) % 2) * TPC:
                                       ((k - 2) % 2) * TPC + TPC])
                    em["sync"] = st_z
                st.stage(**em)

        def s_sp(s):
            s.activation(out=dts[0:32, 0:TPC], in_=dtr[0:32, 0:TPC], func=AF.Abs,
                         bias=dtbc[0:32, 0:1])
            s.activation(out=dts[0:32, TPC:2 * TPC], in_=dtr[0:32, 0:TPC],
                         func=AF.Relu, bias=dtbc[0:32, 0:1])
            s.drain()
            s.activation(out=dts[0:32, 0:TPC], in_=dts[0:32, 0:TPC], func=AF.Exp,
                         scale=-1.0)
            s.drain()
            s.activation(out=dts[0:32, 0:TPC], in_=dts[0:32, 0:TPC], func=AF.Ln,
                         bias=onec[0:32, 0:1])
        st.stage(scalar=s_sp)

        def v_sp(v):
            v.tensor_tensor(out=dts[0:32, 0:TPC], in0=dts[0:32, 0:TPC],
                            in1=dts[0:32, TPC:2 * TPC], op=OP.add)
            v.drain()
            v.tensor_scalar(out=dts[0:32, TPC:2 * TPC], in0=dts[0:32, 0:TPC],
                            scalar1=acol[0:32, 0:1], scalar2=None,
                            op0=OP.mult, op1=OP.bypass)
            v.drain()
            v.memset(offc[0:32, 0:1], 0.0)
        st.stage(vector=v_sp)

        def v_ac(v, b=b):
            for c_ in range(NMC):
                v.tensor_tensor_scan(out=dts[0:32, 2 * TPC + c_ * MCH:
                                             2 * TPC + (c_ + 1) * MCH],
                                     data0=dts[0:32, TPC + c_ * MCH:
                                               TPC + (c_ + 1) * MCH],
                                     data1=dts[0:32, TPC + c_ * MCH:
                                               TPC + (c_ + 1) * MCH],
                                     initial=0.0, op0=OP.add, op1=OP.bypass)
            v.drain()
            for c_ in range(NMC):
                v.tensor_scalar(out=acg[0:32, c_ * MCH:(c_ + 1) * MCH],
                                in0=dts[0:32, 2 * TPC + c_ * MCH:
                                        2 * TPC + (c_ + 1) * MCH],
                                scalar1=offc[0:32, 0:1], scalar2=None,
                                op0=OP.add, op1=OP.bypass)
                v.drain()
                v.tensor_copy(out=offc[0:32, 1:2],
                              in_=acg[0:32, (c_ + 1) * MCH - 1:(c_ + 1) * MCH])
                v.drain()
                v.tensor_copy(out=offc[0:32, 0:1], in_=offc[0:32, 1:2])
                v.drain()
        st.stage(vector=v_ac)

        def s_dk(s, b=b):
            s.activation(out=dkb[0:32, b:b + 1], in_=acg[0:32, TPC - 1:TPC],
                         func=AF.Exp)
        st.stage(scalar=s_dk)

        def st_acg(e, b=b):
            n = dma(e, p, "sync", acg_dram[b].rearrange("c h t -> h c t"),
                    acg[0:32, :].rearrange("h (c t) -> h c t", t=MCH))
            n += dma(e, p, "sync", acl_dram[b].rearrange("c h t -> h c t"),
                     dts[0:32, 2 * TPC:3 * TPC].rearrange("h (c t) -> h c t",
                                                          t=MCH))
            return n
        st.stage(sync=st_acg)

        cbufs = (lnt, sbx)
        for k in range(CCH + 1):
            em = {}
            if k < CCH:
                def v_cv(v, cc=k, buf=cbufs[k % 2]):
                    base = cc * TP3
                    v.tensor_scalar(out=buf[:, 0:TPC],
                                    in0=cvin[:, base:base + TPC],
                                    scalar1=cvwc[:, cc * 4:cc * 4 + 1],
                                    scalar2=None, op0=OP.mult, op1=OP.bypass)
                    v.drain()
                    for j in range(1, 4):
                        v.scalar_tensor_tensor(
                            out=buf[:, TPC:2 * TPC] if j % 2 else buf[:, 0:TPC],
                            in0=cvin[:, base + j:base + j + TPC],
                            scalar=cvwc[:, cc * 4 + j:cc * 4 + j + 1],
                            in1=buf[:, 0:TPC] if j % 2 else buf[:, TPC:2 * TPC],
                            op0=OP.mult, op1=OP.add)
                        v.drain()
                em["vector"] = v_cv
            if k >= 1:
                def s_cv(s, cc=k - 1, buf=cbufs[(k - 1) % 2]):
                    dst = (xfm[:, cc * TPC:(cc + 1) * TPC] if cc < 16 else
                           bct[:, (cc - 16) * TPC:(cc - 15) * TPC])
                    s.activation(out=dst, in_=buf[:, TPC:2 * TPC], func=AF.Silu,
                                 bias=cvbc[:, cc:cc + 1])
                em["scalar"] = s_cv
            st.stage(**em)

        def v_ckeep(v, b=b):
            v.tensor_copy(out=call[:, b * TPC:(b + 1) * TPC],
                          in_=bct[:, TPC:2 * TPC])
        st.stage(vector=v_ckeep)

        pbanks = (ps_t, ps_t2)
        trs = [(cc, tq) for cc in range(16) for tq in range(4)]
        for k in range(len(trs) + 1):
            em = {}
            if k < len(trs):
                cc, tq = trs[k]

                def t_tr(t, cc=cc, tq=tq, bk=pbanks[k % 2]):
                    t.transpose(bk[:, 0:128],
                                xfm[:, cc * TPC + tq * 128:
                                    cc * TPC + tq * 128 + 128],
                                idt[:, :])
                em["tensor"] = t_tr
            if k >= 1:
                cc, tq = trs[k - 1]

                def v_tr(v, cc=cc, tq=tq, bk=pbanks[(k - 1) % 2]):
                    v.tensor_copy(out=xtm[:, tq * 2048 + cc * 128:
                                          tq * 2048 + cc * 128 + 128],
                                  in_=bk[:, 0:128])
                em["vector"] = v_tr
            st.stage(**em)

        def v_h0(v):
            v.memset(hst[:, :], 0.0)
        st.stage(vector=v_h0)

        for c_ in range(NMC):
            def st_r(e, c_=c_, b=b):
                return dma(e, p, "sync", rowA[0:1, 0:NHM * MCH],
                           acl_dram[b, c_].rearrange("h t -> (h t)"))
            st.stage(sync=st_r)

            def s_er(s):
                s.activation(out=rowE[0:1, :], in_=rowA[0:1, 0:NHM * MCH],
                             func=AF.Exp)

            def t_pre(t, c_=c_):
                t.transpose(ps_t[:, 0:128], bct[:, c_ * MCH:(c_ + 1) * MCH],
                            idt[:, :])
                t.matmul(ps_s[3][:, 0:MCH], bct[:, c_ * MCH:(c_ + 1) * MCH],
                         bct[:, TPC + c_ * MCH:TPC + (c_ + 1) * MCH],
                         start=True, stop=True)
                t.transpose(ps_s[0][:, 0:32], dts[0:32, 2 * TPC + c_ * MCH:
                                                  2 * TPC + (c_ + 1) * MCH],
                            idt32[0:32, 0:32])
                t.transpose(ps_s[1][:, 0:32], dts[0:32, c_ * MCH:(c_ + 1) * MCH],
                            idt32[0:32, 0:32])
                t.matmul(ps_s[2][:, 0:32], onesrow[0:1, 0:128],
                         rowA[0:1, 0:NHM * MCH].rearrange(
                             "o (h t) -> o h t", t=MCH)[:, :, MCH - 1],
                         start=True, stop=True)
            st.stage(scalar=s_er, tensor=t_pre)

            def v_pre(v):
                v.tensor_copy(out=gbf[:, MCH:2 * MCH], in_=ps_t[:, 0:128])
                v.tensor_copy(out=acT[:, :], in_=ps_s[0][:, 0:32])
                v.tensor_copy(out=dtT[:, :], in_=ps_s[1][:, 0:32])
                v.drain()
                v.tensor_tensor(out=wvt[:, 0:NHM], in0=ps_s[2][:, 0:32],
                                in1=acT[:, :], op=OP.subtract)
            st.stage(vector=v_pre)

            def s_wv(s):
                s.activation(out=wvt[:, NHM:2 * NHM], in_=wvt[:, 0:NHM],
                             func=AF.Exp)
                s.activation(out=dec[:, :], in_=ps_s[2][:, 0:32], func=AF.Exp)
            st.stage(scalar=s_wv)

            def v_wv2(v):
                v.tensor_tensor(out=wvt[:, 2 * NHM:3 * NHM],
                                in0=wvt[:, NHM:2 * NHM], in1=dtT[:, :], op=OP.mult)
            st.stage(vector=v_wv2)

            for g in range(4):
                hs = g * 8
                W = NHM * MCH

                def t_g1(t, hs=hs):
                    for bank, p4 in ((ps_s[0], 0), (ps_s[1], 1)):
                        h4 = hs + p4 * 4
                        for j in range(4):
                            h = h4 + j
                            t.matmul(bank[:, j * MCH:(j + 1) * MCH],
                                     rowA[0:1, h * MCH:(h + 1) * MCH],
                                     rowA[0:1, W + 128:W + 256],
                                     start=True, stop=False)
                            t.matmul(bank[:, j * MCH:(j + 1) * MCH],
                                     rowA[0:1, W:W + 128],
                                     rowA[0:1, h * MCH:(h + 1) * MCH],
                                     start=False, stop=True)
                    t.matmul(ps_s[2][:, 0:512], onesbf[0:1, 0:128],
                             rowE[0:1, hs * MCH:hs * MCH + 512],
                             start=True, stop=True)
                    t.matmul(ps_g[:, 0:512], onesbf[0:1, 0:128],
                             rowE[0:1, (hs + 4) * MCH:(hs + 4) * MCH + 512],
                             start=True, stop=True)
                st.stage(tensor=t_g1)

                def v_g2(v):
                    for j in range(8):
                        bank = ps_s[0] if j < 4 else ps_s[1]
                        v.tensor_tensor(out=gwk[:, j * MCH:(j + 1) * MCH],
                                        in0=bank[:, (j % 4) * MCH:
                                                 (j % 4 + 1) * MCH],
                                        in1=tri[:, :], op=OP.min)
                st.stage(vector=v_g2)

                def s_g3(s):
                    s.activation(out=sbx[:, 1024:1536], in_=gwk[:, 0:512],
                                 func=AF.Exp)
                    s.activation(out=sbx[:, 1536:2048], in_=gwk[:, 512:1024],
                                 func=AF.Exp)
                st.stage(scalar=s_g3)

                def v_g4(v, hs=hs, c_=c_):
                    for j in range(8):
                        v.tensor_tensor(out=gwk[:, j * MCH:(j + 1) * MCH],
                                        in0=sbx[:, 1024 + j * MCH:
                                                1024 + (j + 1) * MCH],
                                        in1=ps_s[3][:, 0:MCH], op=OP.mult)
                    for j in range(8):
                        pm = ps_s[2] if j < 4 else ps_g
                        v.tensor_tensor(out=sbx[:, j * MCH:(j + 1) * MCH],
                                        in0=bct[:, TPC + c_ * MCH:
                                                TPC + (c_ + 1) * MCH],
                                        in1=pm[:, (j % 4) * MCH:
                                               (j % 4 + 1) * MCH], op=OP.mult)
                    v.drain()
                    for j in range(8):
                        v.tensor_scalar(out=xw[:, j * MCH:(j + 1) * MCH],
                                        in0=gwk[:, j * MCH:(j + 1) * MCH],
                                        scalar1=dtT[:, hs + j:hs + j + 1],
                                        scalar2=None, op0=OP.mult, op1=OP.bypass)
                st.stage(vector=v_g4)

                def t_g5(t, hs=hs, c_=c_):
                    for j in range(8):
                        h = hs + j
                        out = ps_x[(h % 2) * 64:(h % 2) * 64 + 64,
                                   ((h // 2) % 4) * MCH:((h // 2) % 4 + 1) * MCH]
                        t.matmul(out, hst[:, h * 64:(h + 1) * 64],
                                 sbx[:, j * MCH:(j + 1) * MCH],
                                 start=True, stop=False)
                        t.matmul(out, xtm[:, c_ * 2048 + h * 64:
                                          c_ * 2048 + (h + 1) * 64],
                                 xw[:, j * MCH:(j + 1) * MCH],
                                 start=False, stop=True)
                st.stage(tensor=t_g5)

                def v_g6(v, hs=hs, c_=c_):
                    for jp in range(4):
                        hp2 = hs // 2 + jp
                        sl = hp2 * TPC + c_ * MCH
                        v.scalar_tensor_tensor(out=ysb[:, sl:sl + MCH],
                                               in0=xfm[:, sl:sl + MCH],
                                               scalar=dcc[:, hp2:hp2 + 1],
                                               in1=ps_x[:, jp * MCH:
                                                       (jp + 1) * MCH],
                                               op0=OP.mult, op1=OP.add)
                st.stage(vector=v_g6)

            def v_xw(v, c_=c_):
                for h in range(NHM):
                    v.tensor_scalar(out=xw[:, h * 64:(h + 1) * 64],
                                    in0=xtm[:, c_ * 2048 + h * 64:
                                            c_ * 2048 + (h + 1) * 64],
                                    scalar1=wvt[:, 2 * NHM + h:2 * NHM + h + 1],
                                    scalar2=None, op0=OP.mult, op1=OP.bypass)
            st.stage(vector=v_xw)

            for q4 in range(4):
                def t_hd(t, q4=q4):
                    t.matmul(ps_s[q4][:, 0:512], gbf[:, MCH:2 * MCH],
                             xw[:, q4 * 512:(q4 + 1) * 512], start=True, stop=True)
                st.stage(tensor=t_hd)

            def v_hu(v):
                for h in range(NHM):
                    v.tensor_scalar(out=hnw[:, h * 64:(h + 1) * 64],
                                    in0=hst[:, h * 64:(h + 1) * 64],
                                    scalar1=dec[:, h:h + 1], scalar2=None,
                                    op0=OP.mult, op1=OP.bypass)
                v.drain()
                for q4 in range(4):
                    v.tensor_tensor(out=hnw[:, q4 * 512:(q4 + 1) * 512],
                                    in0=hnw[:, q4 * 512:(q4 + 1) * 512],
                                    in1=ps_s[q4][:, 0:512], op=OP.add)
                v.drain()
                for q4 in range(4):
                    v.tensor_copy(out=hst[:, q4 * 512:(q4 + 1) * 512],
                                  in_=hnw[:, q4 * 512:(q4 + 1) * 512])
            st.stage(vector=v_hu)

        def g_hl(e, b=b):
            return dma(e, p, "gpsimd", h_loc[b], hst[:, :])

        def st_y(e, b=b):
            return dma(e, p, "sync", yz_dram[b, 0], ysb[:, 0:16 * TPC])
        st.stage(sync=st_y, gpsimd=g_hl)

    def st_dk(e):
        return dma(e, p, "sync", dk_loc, dkb[0:32, :])
    st.stage(sync=st_dk)

    def g_cc(g):
        cs = p.sem("cc")
        g.collective_compute("AllGather", OP.bypass,
                             replica_groups=[list(range(NCORES))],
                             ins=[h_loc[:, :, :].opt()],
                             outs=[h_all[:, :, :, :].opt()]).then_inc(cs[0], 1)
        cs[1] += 1
        g.collective_compute("AllGather", OP.bypass,
                             replica_groups=[list(range(NCORES))],
                             ins=[dk_loc[:, :].opt()],
                             outs=[dk_all[:, :, :].opt()]).then_inc(cs[0], 1)
        cs[1] += 1
        g.wait_ge(cs[0], cs[1])
    st.stage(gpsimd=g_cc)

    def ld_dk(e):
        return dma(e, p, "sync", drow[0:1, :],
                   dk_all.rearrange("c h b -> (c h b)"))
    st.stage(sync=ld_dk)

    for b in range(B):
        def ld_yz(e, b=b):
            n = dma(e, p, "sync", ysb[:, 0:16 * TPC], yz_dram[b, 0])
            n += dma(e, p, "sync", zs3[:, :], yz_dram[b, 1])
            n += dma(e, p, "sync",
                     sbx[:, :].rearrange("q (c t) -> q c t", c=DCH),
                     us_d[b].rearrange("q (c t) -> q c t", t=TP3)[:, :, 3:TP3])
            return n
        st.stage(sync=ld_yz)

        def v_hi0(v):
            v.memset(hst[:, :], 0.0)
            v.memset(hnw[:, :], 0.0)
        st.stage(vector=v_hi0)

        for j in range(NCORES - 1):
            def ld_hj(e, j=j, b=b):
                return dma(e, p, "sync", hjb[:, :], h_all[j, b])
            st.stage(sync=ld_hj)

            def t_db(t, j=j, b=b):
                t.matmul(ps_s[2][:, 0:NHM], onesrow[0:1, 0:128],
                         drow[0:1, :].rearrange("o (c h b) -> o c h b",
                                                h=NHM, b=B)[:, j, :, b],
                         start=True, stop=True)
            st.stage(tensor=t_db)

            def v_hi(v, j=j):
                v.tensor_copy(out=dec[:, :], in_=ps_s[2][:, 0:NHM])
                v.drain()
                for h in range(NHM):
                    v.tensor_scalar(out=hst[:, h * 64:(h + 1) * 64],
                                    in0=hst[:, h * 64:(h + 1) * 64],
                                    scalar1=dec[:, h:h + 1], scalar2=None,
                                    op0=OP.mult, op1=OP.bypass)
                v.drain()
                v.tensor_tensor(out=hst[:, :], in0=hst[:, :], in1=hjb[:, :],
                                op=OP.add)
                v.drain()
                v.scalar_tensor_tensor(out=hnw[:, :], in0=hst[:, :],
                                       scalar=ohc[:, j:j + 1], in1=hnw[:, :],
                                       op0=OP.mult, op1=OP.add)
                v.drain()
            st.stage(vector=v_hi)

        for c_ in range(NMC):
            def ld_r(e, c_=c_, b=b):
                return dma(e, p, "sync", rowA[0:1, 0:NHM * MCH],
                           acg_dram[b, c_].rearrange("h t -> (h t)"))
            st.stage(sync=ld_r)

            def s_er2(s):
                s.activation(out=rowE[0:1, :], in_=rowA[0:1, 0:NHM * MCH],
                             func=AF.Exp)
            st.stage(scalar=s_er2)

            for g in range(4):
                hs = g * 8

                def t_gv2(t, hs=hs):
                    t.matmul(ps_s[2][:, 0:512], onesbf[0:1, 0:128],
                             rowE[0:1, hs * MCH:hs * MCH + 512],
                             start=True, stop=True)
                    t.matmul(ps_g[:, 0:512], onesbf[0:1, 0:128],
                             rowE[0:1, (hs + 4) * MCH:(hs + 4) * MCH + 512],
                             start=True, stop=True)
                st.stage(tensor=t_gv2)

                def v_gv2(v, c_=c_, b=b):
                    for j in range(8):
                        pm = ps_s[2] if j < 4 else ps_g
                        v.tensor_tensor(out=dts[:, j * MCH:(j + 1) * MCH],
                                        in0=call[:, b * TPC + c_ * MCH:
                                                b * TPC + (c_ + 1) * MCH],
                                        in1=pm[:, (j % 4) * MCH:
                                               (j % 4 + 1) * MCH], op=OP.mult)
                st.stage(vector=v_gv2)

                def t_yc(t, hs=hs):
                    for j in range(8):
                        h = hs + j
                        t.matmul(ps_x[(h % 2) * 64:(h % 2) * 64 + 64,
                                      ((h // 2) % 4) * MCH:
                                      ((h // 2) % 4 + 1) * MCH],
                                 hnw[:, h * 64:(h + 1) * 64],
                                 dts[:, j * MCH:(j + 1) * MCH],
                                 start=True, stop=True)
                st.stage(tensor=t_yc)

                def v_yc(v, hs=hs, c_=c_):
                    for jp in range(4):
                        hp2 = hs // 2 + jp
                        sl = hp2 * TPC + c_ * MCH
                        v.tensor_tensor(out=ysb[:, sl:sl + MCH],
                                        in0=ysb[:, sl:sl + MCH],
                                        in1=ps_x[:, jp * MCH:(jp + 1) * MCH],
                                        op=OP.add)
                st.stage(vector=v_yc)

        emit_layernorm(st, env, 0, rms=True, nchunks=16,
                       x_at=lambda ch: ysb[:, ch * TPC:(ch + 1) * TPC],
                       xn_at=lambda ch: gsb[:, ch * TPC:(ch + 1) * TPC])

        def v_nwm(v):
            for ch in range(16):
                v.tensor_scalar(out=gsb[:, ch * TPC:(ch + 1) * TPC],
                                in0=gsb[:, ch * TPC:(ch + 1) * TPC],
                                scalar1=nwc[:, ch:ch + 1], scalar2=None,
                                op0=OP.mult, op1=OP.bypass)
        st.stage(vector=v_nwm)

        def s_sz(s):
            for ch in range(16):
                s.activation(out=zs3[:, ch * TPC:(ch + 1) * TPC],
                             in_=zs3[:, ch * TPC:(ch + 1) * TPC], func=AF.Silu)
        st.stage(scalar=s_sz)

        def v_gt(v):
            for ch in range(16):
                v.tensor_tensor(out=gsb[:, ch * TPC:(ch + 1) * TPC],
                                in0=gsb[:, ch * TPC:(ch + 1) * TPC],
                                in1=zs3[:, ch * TPC:(ch + 1) * TPC], op=OP.mult)
        st.stage(vector=v_gt)

        for oc in range(DCH):
            def ld_ob(e, oc=oc):
                return dma(e, p, "sync", opb[:, :], op_dram[oc])
            st.stage(sync=ld_ob)

            def t_o(t, oc=oc):
                for ic in range(16):
                    t.matmul(ps_x[:, 0:TPC], opb[:, ic * 128:(ic + 1) * 128],
                             gsb[:, ic * TPC:(ic + 1) * TPC],
                             start=(ic == 0), stop=(ic == 15))
            st.stage(tensor=t_o)

            def v_o(v, oc=oc):
                sl = sbx[:, oc * TPC:(oc + 1) * TPC]
                v.tensor_tensor(out=sl, in0=sl, in1=ps_x[:, 0:TPC], op=OP.add)
            st.stage(vector=v_o)

        def st_b(e, b=b):
            return dma(e, p, "sync",
                       xo_d[b], sbx[:, :])
        st.stage(sync=st_b)




def build_l0attn():
    p = Prog("l0attn")
    p.pf = "a0_"
    us_d = p.inp("us", (B, 128, DCH * (TPC + 3)))
    xo_d = p.outp("xo", (B, 128, DCH * TPC))
    p.start_block()
    st = Stager(p)
    emit_l0attn(p, st, us_d, xo_d)
    st.stage(sync=lambda e: None)
    return p

def maps_l0attn(I):
    cw = np.asarray(I["L0_attn_conv_w"], np.float32)[:, 0, :]
    cvw = np.zeros((128, CCH * 4), np.float32)
    for cc in range(CCH):
        for j in range(4):
            cvw[:, cc * 4 + j] = cw[cc * 128:(cc + 1) * 128, j]
    dtb = np.zeros((128, 1), np.float32)
    dtb[0:32, 0] = np.asarray(I["L0_attn_dt_bias"], np.float32)
    Dv = np.asarray(I["L0_attn_D"], np.float32)
    dcol = np.zeros((128, 16), np.float32)
    for m in range(16):
        dcol[0:64, m] = Dv[2 * m]
        dcol[64:128, m] = Dv[2 * m + 1]
    base = {
        "ipT": _t(I["L0_attn_in_proj_w"]),
        "cvw": cvw,
        "cvb": _col(I["L0_attn_conv_b"], CCH),
        "dtb": dtb,
        "alog": np.asarray(I["L0_attn_A_log"], np.float32),
        "dcol": dcol,
        "nw": _col(I["L0_attn_norm_w"], 16),
        "opT": _t(I["L0_attn_out_proj_w"]),
    }
    in_maps = []
    for c in range(NCORES):
        m = dict(base)
        oh = np.zeros((128, NCORES), np.float32)
        if c > 0:
            oh[:, c - 1] = 1.0
        m["ohc"] = oh
        in_maps.append({"a0_" + k: v for k, v in m.items()})
    return in_maps


def run_l0attn(u_shards3, I):
    p = build_l0attn()
    in_maps = maps_l0attn(I)
    for c in range(NCORES):
        in_maps[c]["a0_us"] = _to_qmaj(u_shards3[c])
    res = run_bass_kernel_spmd(p.nc, in_maps, core_ids=list(range(NCORES)))
    out = [_from_qmaj(res.results[c]["a0_xo"]) for c in range(NCORES)]
    p.close()
    return out



def _t(w):
    return np.ascontiguousarray(np.asarray(w, np.float32).T)


def _col(v, nch=DCH):
    return np.ascontiguousarray(np.asarray(v, np.float32).reshape(nch, 128).T)


def maps_l0ffn(I):
    base = {
        "wT_wf": _t(I["L0_ffn_wf"]), "wT_uf": _t(I["L0_ffn_uf"]),
        "wT_wi": _t(I["L0_ffn_wi"]), "wT_ui": _t(I["L0_ffn_ui"]),
        "wT_wc": _t(I["L0_ffn_wc"]), "wT_uc": _t(I["L0_ffn_uc"]),
        "wT_wo": _t(I["L0_ffn_wo"]), "wT_uo": _t(I["L0_ffn_uo"]),
        "b_bf": _col(I["L0_ffn_bf"]),
        "b_bi": _col(I["L0_ffn_bi"]),
        "b_bc": _col(I["L0_ffn_bc"]),
        "b_bo": _col(I["L0_ffn_bo"]),
        "in_c": _col(I["L0_ffn_in_c"]),
    }
    return [{"f0_" + k: v for k, v in base.items()} for c in range(NCORES)]


def run_l0ffn(stream_shards, I):
    p = build_l0ffn()
    in_maps = maps_l0ffn(I)
    for c in range(NCORES):
        in_maps[c]["f0_xs"] = _to_qmaj(stream_shards[c])
    res = run_bass_kernel_spmd(p.nc, in_maps, core_ids=list(range(NCORES)))
    out = [_from_qmaj(res.results[c]["f0_xo"]) for c in range(NCORES)]
    p.close()
    return out


def _to_qmaj(a):
    Bq, Dq, T = a.shape
    return np.ascontiguousarray(
        a.reshape(Bq, DCH, 128, T).transpose(0, 2, 1, 3).reshape(Bq, 128, DCH * T))


def _from_qmaj(a):
    Bq, _, CT = a.shape
    T = CT // DCH
    return np.ascontiguousarray(
        a.reshape(Bq, 128, DCH, T).transpose(0, 2, 1, 3).reshape(Bq, DCH * 128, T))


def _shard_u3(x):
    fm = np.ascontiguousarray(x.transpose(0, 2, 1))
    out = []
    for c in range(NCORES):
        sl = np.zeros((B, D, TPC + 3), np.float32)
        lo = c * TPC
        lo3 = max(lo - 3, 0)
        pad = lo - lo3
        sl[:, :, 3 - pad:] = fm[:, :, lo3:lo + TPC]
        out.append(sl)
    return out


def build_fused():
    p = Prog("fused")
    p.pf = "a0_"
    us_d = p.inp("us", (B, 128, DCH * (TPC + 3)))
    p.pf = "out_"
    xo_d = p.outp("xo", (B, 128, DCH * TPC))
    strm = [p.dram("strm%d" % k, (B, 128, DCH * TPC)) for k in range(7)]
    p.pf = ""
    p.start_block()
    st = Stager(p)
    blocks = [
        (emit_l0attn, (us_d, strm[0])),
        (emit_l0ffn, (strm[0], strm[1])),
        (emit_attn, (1, strm[1], strm[2])),
        (emit_l1ffn, (strm[2], strm[3])),
        (emit_attn, (2, strm[3], strm[4])),
        (emit_l2ffn, (strm[4], strm[5])),
        (emit_l3attn, (strm[5], strm[6])),
        (emit_l3ffn, (strm[6], xo_d)),
    ]
    for fn, args in blocks:
        p.push_region()
        fn(p, st, *args)
        p.pop_region()
    st.stage(sync=lambda e: None)
    return p


def run_fused(I):
    p = build_fused()
    x = np.ascontiguousarray(np.asarray(I["L0_attn_u"], np.float32))
    us = _shard_u3(x)
    allmaps = [maps_l0attn(I), maps_l0ffn(I), maps_attn(1, I), maps_l1ffn(I),
               maps_attn(2, I), maps_l2ffn(I), maps_l3attn(I), maps_l3ffn(I)]
    in_maps = []
    for c in range(NCORES):
        m = {}
        for bm in allmaps:
            m.update(bm[c])
        m["a0_us"] = _to_qmaj(us[c])
        in_maps.append(m)
    res = run_bass_kernel_spmd(p.nc, in_maps, core_ids=list(range(NCORES)))
    out = [_from_qmaj(res.results[c]["out_xo"]) for c in range(NCORES)]
    p.close()
    fm = np.concatenate(out, axis=2)
    return np.ascontiguousarray(fm.transpose(0, 2, 1)).astype(np.float32)


def kernel(**inputs):
    return run_fused(inputs)


# version 07
# speedup vs baseline: 1.0999x; 1.0016x over previous
import math
import contextlib
import numpy as np
import concourse.bass as bass
import concourse.mybir as mybir
from concourse.alu_op_type import AluOpType
from concourse.bass_utils import run_bass_kernel_spmd
from concourse import library_config

F32 = mybir.dt.float32
BF16 = mybir.dt.bfloat16
I32 = mybir.dt.int32
U8 = mybir.dt.uint8
AF = mybir.ActivationFunctionType
AX = mybir.AxisListType.X
OP = AluOpType

NCORES = 8
B = 4
SEQ = 4096
TPC = SEQ // NCORES
D = 1024
DCH = D // 128

LN2 = float(np.float32(0.6931471805599453))
INV_LN2 = float(np.float32(1.4426950408889634))
LN2_LO = float(np.float32(np.float64(0.69314718055994530941723) - np.float64(np.float32(0.6931471805599453))))



class Prog:

    def __init__(self, name):
        self.nc = bass.Bass("TRN2", target_bir_lowering=False)
        self.es = contextlib.ExitStack()
        self.block = None
        self.semc = {}
        self.inputs = {}
        self.outputs = {}
        self._sb = {}
        self.pf = ""
        self.rstack = []

    def push_region(self):
        self.rstack.append(contextlib.ExitStack())

    def pop_region(self):
        self.rstack.pop().close()

    def _es(self):
        return self.rstack[-1] if self.rstack else self.es

    def start_block(self):
        self.block = self.es.enter_context(self.nc.Block())

    def sem(self, name):
        if name not in self.semc:
            h = self.es.enter_context(self.nc.semaphore(name))
            self.semc[name] = [h, 0]
        return self.semc[name]

    def inp(self, name, shape, dtype=F32):
        name = self.pf + name
        ap = self.nc.dram_tensor(name, list(shape), dtype, kind="ExternalInput").ap()
        self.inputs[name] = ap
        return ap

    def outp(self, name, shape, dtype=F32):
        name = self.pf + name
        ap = self.nc.dram_tensor(name, list(shape), dtype, kind="ExternalOutput").ap()
        self.outputs[name] = ap
        return ap

    def dram(self, name, shape, dtype=F32):
        return self.nc.dram_tensor(self.pf + name, list(shape), dtype).ap()

    def sb(self, name, shape, dtype=F32):
        name = self.pf + name
        t = self._es().enter_context(self.nc.sbuf_tensor(name, list(shape), dtype))
        self._sb[name] = t
        return t

    def ps(self, name, shape=(128, 512), dtype=F32):
        return self._es().enter_context(
            self.nc.psum_tensor(self.pf + name, list(shape), dtype))

    def close(self):
        self.es.close()


class Stager:

    ENGS = ("tensor", "vector", "scalar", "gpsimd", "sync")
    DSEMS = ("dma_sync", "dma_gpsimd")

    def __init__(self, prog):
        self.p = prog
        for e in self.ENGS:
            prog.sem("st_" + e)
        for d in self.DSEMS:
            prog.sem(d)
        self.seen = {e: {} for e in self.ENGS}

    def _engine_section(self, ename, fn, waits):
        p = self.p
        seen = self.seen[ename]

        def body(eng):
            for (nm, h, c) in waits:
                if c > 0 and seen.get(nm, 0) < c:
                    eng.wait_ge(h, c)
                    seen[nm] = c
            n_dma = fn(eng)
            if n_dma is None:
                n_dma = 0
            sem = p.sem("st_" + ename)
            eng.drain().then_inc(sem[0], 1)
            sem[1] += 1
            if n_dma:
                p.sem("dma_" + ename)[1] += 16 * n_dma

        getattr(self.p.block, ename)(body)

    def stage(self, **emits):
        waits = [("st_" + e, self.p.sem("st_" + e)[0], self.p.sem("st_" + e)[1])
                 for e in self.ENGS]
        waits += [(d, self.p.sem(d)[0], self.p.sem(d)[1]) for d in self.DSEMS]
        for ename, fn in emits.items():
            self._engine_section(ename, fn, waits)


def dma(eng, p, qname, out, in_):
    sem = p.sem("dma_" + qname)
    eng.dma_start(out=out, in_=in_).then_inc(sem[0], 16)
    return 1


def cast_stream(st, p, items, stg0, stg1):
    n = len(items)
    for k in range(n + 1):
        em = {}
        if k < n:
            d_, s_, w_ = items[k]
            stg = (stg0, stg1)[k % 2]

            def ld(e, s_=s_, stg=stg, w_=w_):
                return dma(e, p, "sync", stg[:, 0:w_], s_)
            em["sync"] = ld
        if k > 0:
            d0, s0, w0 = items[k - 1]
            stgp = (stg0, stg1)[(k - 1) % 2]

            def cst(s, d0=d0, stgp=stgp, w0=w0):
                s.activation(out=d0, in_=stgp[:, 0:w0], func=AF.Identity)
            em["scalar"] = cst
        st.stage(**em)



def mm_tiled(t, psum_ap, lhsT_tiles, rhs_tiles, n_cols, start=True):
    n = len(lhsT_tiles)
    for k in range(n):
        t.matmul(psum_ap, lhsT_tiles[k], rhs_tiles[k],
                 start=(start and k == 0), stop=(k == n - 1))



def emit_layernorm(st, env, b, T=TPC, rms=False, nchunks=DCH, x_at=None, xn_at=None,
                   eps_override=None):
    p = st.p
    xs = env["stream"]
    xn = env["xn"]
    sq = env["sq"]
    cone = env["cone"] if nchunks == DCH else env["cone2"]
    onesrow = env["onesrow"]
    srow = env["srow"]
    ps_stat = env["ps_stat"]
    ps_b0 = env["ps_b0"]
    ps_b1 = env["ps_b1"]
    nf = nchunks * 128
    eps = 1e-5 if not rms else 1e-5

    def xat(ch):
        if x_at is not None:
            return x_at(ch)
        return xs[:, (b * DCH + ch) * TPC:(b * DCH + ch) * TPC + T]

    def xnat(ch):
        if xn_at is not None:
            return xn_at(ch)
        return xn[:, (b * DCH + ch) * TPC:(b * DCH + ch) * TPC + T]

    def v_cast(v):
        for ch in range(nchunks):
            v.tensor_copy(out=xnat(ch), in_=xat(ch))
    st.stage(vector=v_cast)

    sqw = env["sqw"]

    def s_sq(s):
        for ch in range(nchunks):
            s.activation(out=sqw[:, ch * T:(ch + 1) * T], in_=xnat(ch), func=AF.Square)
    st.stage(scalar=s_sq)

    def t_stats(t):
        for ch in range(nchunks):
            t.matmul(ps_stat[0:1, 0:T], cone[:, 0:1], xnat(ch),
                     start=(ch == 0), stop=(ch == nchunks - 1))
        for ch in range(nchunks):
            t.matmul(ps_b0[0:1, 0:T], cone[:, 0:1], sqw[:, ch * T:(ch + 1) * T],
                     start=(ch == 0), stop=(ch == nchunks - 1))
    st.stage(tensor=t_stats)

    r_mean = srow[0:1, 0:T]
    r_var = srow[0:1, T:2 * T]
    r_rstd = srow[0:1, 2 * T:3 * T]
    r_negm = srow[0:1, 3 * T:4 * T]

    if not rms:
        def v_row1(v):
            v.tensor_copy(out=r_mean, in_=ps_stat[0:1, 0:T])
        st.stage(vector=v_row1)

        def s_row2(s):
            s.activation(out=r_var, in_=r_mean, func=AF.Square)
        st.stage(scalar=s_row2)

        def v_row3(v):
            v.tensor_tensor(out=r_var, in0=ps_b0[0:1, 0:T], in1=r_var, op=OP.subtract)
        st.stage(vector=v_row3)
    else:
        def v_row3b(v):
            v.tensor_copy(out=r_var, in_=ps_b0[0:1, 0:T])
        st.stage(vector=v_row3b)

    eps_col = env["eps_col"] if eps_override is None else env["eps_col2"]

    def s_row4(s):
        s.activation(out=r_rstd, in_=r_var, func=AF.Ln, bias=eps_col[0:1, 0:1])
        s.drain()
        s.activation(out=r_rstd, in_=r_rstd, func=AF.Exp, scale=-0.5)
    st.stage(scalar=s_row4)

    if not rms:
        def v_row5(v):
            v.scalar_tensor_tensor(out=r_negm, in0=r_mean, scalar=-1.0, in1=r_rstd,
                                   op0=OP.mult, op1=OP.mult)
        st.stage(vector=v_row5)

    def t_bc(t):
        t.matmul(ps_b0[:, 0:T], onesrow[0:1, :], r_rstd, start=True, stop=True)
        if not rms:
            t.matmul(ps_b1[:, 0:T], onesrow[0:1, :], r_negm, start=True, stop=True)
    st.stage(tensor=t_bc)

    tmp = env["lnt"]

    slots = max(1, tmp.shape[1] // T) if not rms else nchunks

    def v_apply(v):
        if rms:
            for ch in range(nchunks):
                v.tensor_tensor(out=xnat(ch), in0=xat(ch), in1=ps_b0[:, 0:T], op=OP.mult)
        else:
            for g0 in range(0, nchunks, slots):
                grp = range(g0, min(g0 + slots, nchunks))
                for ch in grp:
                    v.tensor_tensor(out=tmp[:, (ch - g0) * T:(ch - g0 + 1) * T],
                                    in0=xat(ch), in1=ps_b0[:, 0:T], op=OP.mult)
                v.drain()
                for ch in grp:
                    v.tensor_tensor(out=xnat(ch), in0=tmp[:, (ch - g0) * T:(ch - g0 + 1) * T],
                                    in1=ps_b1[:, 0:T], op=OP.add)
                v.drain()
    st.stage(vector=v_apply)



def emit_l0ffn(p, st, xs_d, xo_d):
    p.pf = "f0_"
    nc = p.nc
    w_names = ["wf", "uf", "wi", "ui", "wc", "uc", "wo", "uo"]
    wd = {n: p.inp("wT_" + n, (D, D)) for n in w_names}
    bd = {n: p.inp("b_" + n, (128, DCH)) for n in ["bf", "bi", "bc", "bo"]}
    inc_d = p.inp("in_c", (128, DCH))

    sbx = p.sb("sbx", (128, DCH * TPC), F32)
    xn = p.sb("xn", (128, B * DCH * TPC), BF16)
    wsb = {n: p.sb("w_" + n, (128, DCH * D), BF16)
           for n in ["wf", "uf", "wi", "ui", "wc", "uc"]}
    wsb["wo"] = wsb["wf"]
    wsb["uo"] = wsb["uf"]
    bias = p.sb("bias", (128, 5 * DCH), F32)
    cstate = p.sb("cstate", (128, DCH * TPC), F32)
    cbf = p.sb("cbf", (128, DCH * TPC), BF16)
    gates = p.sb("gates", (128, 3 * TPC), BF16)
    sqw = p.sb("sqw", (128, DCH * TPC), BF16)
    srow = p.sb("srow", (1, 4 * TPC), F32)
    lnt = p.sb("lnt", (128, DCH * TPC), F32)
    cone = p.sb("cone", (128, 1), BF16)
    onesrow = p.sb("onesrow", (1, 128), F32)
    onescol = p.sb("onescol", (128, TPC), F32)
    htile = p.sb("htile", (128, TPC), F32)

    ps_stat = p.ps("ps_stat")
    ps_b0 = p.ps("ps_b0")
    ps_b1 = p.ps("ps_b1")
    ps_g = [p.ps("ps_g%d" % i) for i in range(3)]
    ps_h = p.ps("ps_h")

    eps_col = p.sb("eps_col", (128, 1), F32)
    env = dict(stream=None, xn=xn, sq=None, sqw=sqw, srow=srow, lnt=lnt,
               cone=cone, onesrow=onesrow, ps_stat=ps_stat, ps_b0=ps_b0,
               ps_b1=ps_b1, eps_col=eps_col)


    def ld_sync(e):
        n = 0
        for gi, bn in enumerate(["bf", "bi", "bc", "bo"]):
            n += dma(e, p, "sync", bias[:, gi * DCH:(gi + 1) * DCH], bd[bn][:, :])
        n += dma(e, p, "sync", bias[:, 4 * DCH:5 * DCH], inc_d[:, :])
        return n

    def ld_gp(e):
        n = 0
        for wn in ["wf", "uf", "wi", "ui", "wc", "uc"]:
            n += dma(e, p, "gpsimd",
                     wsb[wn][:, :].rearrange("q (ic o) -> q ic o", ic=DCH),
                     wd[wn].rearrange("(ic q) o -> q ic o", q=128))
        return n

    def init_v(v):
        v.memset(cone[:, :], 1.0 / 1024.0)
        v.memset(onesrow[:, :], 1.0)
        v.memset(onescol[:, :], 1.0)
        v.memset(eps_col[:, :], 1e-5)
    st.stage(sync=ld_sync, gpsimd=ld_gp, vector=init_v)

    for b in range(B):
        def ld_b(e, b=b):
            return dma(e, p, "sync",
                       sbx[:, :], xs_d[b])
        st.stage(sync=ld_b)
        emit_layernorm(st, env, b,
                       x_at=lambda ch: sbx[:, ch * TPC:(ch + 1) * TPC])

    def v_c0(v):
        for ch in range(DCH):
            v.tensor_scalar(out=cstate[:, ch * TPC:(ch + 1) * TPC],
                            in0=onescol[:, :],
                            scalar1=bias[:, 4 * DCH + ch:4 * DCH + ch + 1],
                            scalar2=None, op0=OP.mult, op1=OP.bypass)
    st.stage(vector=v_c0)

    def wtile(wn, ic, oc):
        return wsb[wn][:, ic * D + oc * 128: ic * D + oc * 128 + 128]

    def xn_at(t_step, ch):
        o = (t_step * DCH + ch) * TPC
        return xn[:, o:o + TPC]

    for t_step in range(B):
        def v_castc(v):
            for ch in range(DCH):
                v.tensor_copy(out=cbf[:, ch * TPC:(ch + 1) * TPC],
                              in_=cstate[:, ch * TPC:(ch + 1) * TPC])
        st.stage(vector=v_castc)

        for oc in range(DCH):
            def t_mm(t, oc=oc, t_step=t_step):
                for gi, (wn, un) in enumerate([("wf", "uf"), ("wi", "ui"), ("wc", "uc")]):
                    for ic in range(DCH):
                        t.matmul(ps_g[gi][:, 0:TPC], wtile(wn, ic, oc),
                                 xn_at(t_step, ic), start=(ic == 0), stop=False)
                    for ic in range(DCH):
                        t.matmul(ps_g[gi][:, 0:TPC], wtile(un, ic, oc),
                                 cbf[:, ic * TPC:(ic + 1) * TPC],
                                 start=False, stop=(ic == DCH - 1))

            def s_evac(s, oc=oc):
                for gi, fn in [(0, AF.Sigmoid), (1, AF.Sigmoid), (2, AF.Tanh)]:
                    s.activation(out=gates[:, gi * TPC:(gi + 1) * TPC],
                                 in_=ps_g[gi][:, 0:TPC], func=fn,
                                 bias=bias[:, gi * DCH + oc:gi * DCH + oc + 1])

            def v_update(v, oc=oc):
                cs = cstate[:, oc * TPC:(oc + 1) * TPC]
                f_ = gates[:, 0:TPC]
                i_ = gates[:, TPC:2 * TPC]
                g_ = gates[:, 2 * TPC:3 * TPC]
                v.tensor_tensor(out=cs, in0=cs, in1=f_, op=OP.mult)
                v.tensor_tensor(out=lnt[:, 0:TPC], in0=i_, in1=g_, op=OP.mult)
                v.drain()
                v.tensor_tensor(out=cs, in0=cs, in1=lnt[:, 0:TPC], op=OP.add)
                v.drain()
            st.stage(tensor=t_mm)
            st.stage(scalar=s_evac)
            st.stage(vector=v_update)

    def v_castc2(v):
        for ch in range(DCH):
            v.tensor_copy(out=cbf[:, ch * TPC:(ch + 1) * TPC],
                          in_=cstate[:, ch * TPC:(ch + 1) * TPC])

    def ld_wo(e):
        n = 0
        for wn in ["wo", "uo"]:
            n += dma(e, p, "gpsimd",
                     wsb[wn][:, :].rearrange("q (ic o) -> q ic o", ic=DCH),
                     wd[wn].rearrange("(ic q) o -> q ic o", q=128))
        return n
    st.stage(vector=v_castc2, gpsimd=ld_wo)

    for t_step in range(B):
        def ld_b2(e, t_step=t_step):
            return dma(e, p, "sync",
                       sbx[:, :], xs_d[t_step])
        st.stage(sync=ld_b2)
        for oc in range(DCH):
            def t_mmh(t, oc=oc, t_step=t_step):
                for ic in range(DCH):
                    t.matmul(ps_h[:, 0:TPC], wtile("wo", ic, oc),
                             xn_at(t_step, ic), start=(ic == 0), stop=False)
                for ic in range(DCH):
                    t.matmul(ps_h[:, 0:TPC], wtile("uo", ic, oc),
                             cbf[:, ic * TPC:(ic + 1) * TPC],
                             start=False, stop=(ic == DCH - 1))

            def s_evh(s, oc=oc):
                s.activation(out=htile[:, 0:TPC], in_=ps_h[:, 0:TPC], func=AF.Tanh,
                             bias=bias[:, 3 * DCH + oc:3 * DCH + oc + 1])

            def v_res(v, oc=oc):
                sl = sbx[:, oc * TPC:(oc + 1) * TPC]
                v.tensor_tensor(out=sl, in0=sl, in1=htile[:, 0:TPC], op=OP.add)
            st.stage(tensor=t_mmh)
            st.stage(scalar=s_evh)
            st.stage(vector=v_res)

        def st_b(e, t_step=t_step):
            return dma(e, p, "sync",
                       xo_d[t_step], sbx[:, :])
        st.stage(sync=st_b)





NH = 16
HD = 64
KCH = SEQ // 128



def build_l0ffn():
    p = Prog("l0ffn")
    p.pf = "f0_"
    xs_d = p.inp("xs", (B, 128, DCH * TPC))
    xo_d = p.outp("xo", (B, 128, DCH * TPC))
    p.start_block()
    st = Stager(p)
    emit_l0ffn(p, st, xs_d, xo_d)
    st.stage(sync=lambda e: None)
    return p

def emit_attn(p, st, layer, xs_d, xo_d):
    causal = (layer == 1)
    p.pf = "a%d_" % layer
    wqT = p.inp("wqT", (D, D))
    wkT = p.inp("wkT", (D, D))
    wvT = p.inp("wvT", (D, D))
    woT = p.inp("woT", (D, D))
    wob = p.inp("wob", (128, DCH))
    cosd = p.inp("cosT", (512, TPC))
    sind = p.inp("sinT", (512, TPC))
    if causal:
        mask_d = p.inp("maskT", (KCH, 128, TPC), BF16)

    kv_loc = p.dram("kv_loc", (2, B, 8, 128, TPC), BF16)
    kv_all = p.dram("kv_all", (NCORES, 2, B, 8, 128, TPC), BF16)
    bnd_loc = p.dram("bnd_loc", (NH, B), F32)
    bnd_all = p.dram("bnd_all", (NH, B), F32)
    brow_d = p.dram("brow_d", (B, NH), F32)

    sbx = p.sb("sbx", (128, DCH * TPC), F32)
    xn = p.sb("xn", (128, B * DCH * TPC), BF16)
    pre = p.sb("pre", (128, DCH * TPC), F32)
    qr = p.sb("qr", (128, B * DCH * TPC), BF16)
    kr = p.sb("kr", (128, DCH * TPC), BF16)
    kr2 = p.sb("kr2", (128, 4 * NCORES * 130), BF16)
    wbuf = p.sb("wbuf", (128, DCH * D), BF16)
    cos_s = p.sb("cos_s", (128, 4 * TPC), F32)
    sin_s = p.sb("sin_s", (128, 4 * TPC), F32)
    sqw = p.sb("sqw", (128, DCH * TPC), BF16)
    srow = p.sb("srow", (1, 4 * TPC), F32)
    lnt = p.sb("lnt", (128, DCH * TPC), F32)
    cone = p.sb("cone", (128, 1), BF16)
    onesrow = p.sb("onesrow", (1, 128), F32)
    eps_col = p.sb("eps_col", (128, 1), F32)
    e16 = p.sb("e16", (128, DCH * NH), BF16)
    idt = p.sb("idt", (128, 128), BF16)
    idt32 = p.sb("idt32", (128, 128), F32)
    iwork = p.sb("iwork", (128, 128), I32)
    perm = p.sb("perm", (128, 4 * 128), BF16)
    mq = p.sb("mq", (128, B), F32)
    mkl = p.sb("mkl", (16, B), F32)
    mkg = p.sb("mkg", (16, B), F32)
    bias_bh = p.sb("bias_bh", (128, NH * B), F32)
    brow = p.sb("brow", (1, NH * B), F32)
    bsml = p.sb("bsml", (16, B), F32)
    kbuf = kr
    vbuf = kr2
    pk = sqw
    rbc = p.sb("rbc", (128, TPC), F32)
    denrow = p.sb("denrow", (128, 4 * TPC), F32)
    ones128 = p.sb("ones128", (128, 128), F32)
    wob_s = p.sb("wob_s", (128, DCH), F32)
    if causal:
        mbuf = p.sb("mbuf", (128, 2 * TPC), BF16)

    actx = xn
    ps_s = [p.ps("ps_s%d" % i) for i in range(4)]
    ps_o = [p.ps("ps_o%d" % i) for i in range(2)]
    ps_x = p.ps("ps_x")
    ps_t = p.ps("ps_t", (128, 128), BF16)
    ps_stat = ps_s[0]
    ps_b0 = ps_s[1]
    ps_b1 = ps_s[2]

    env = dict(stream=None, xn=xn, sq=None, sqw=sqw, srow=srow, lnt=lnt,
               cone=cone, onesrow=onesrow, ps_stat=ps_stat, ps_b0=ps_b0,
               ps_b1=ps_b1, eps_col=eps_col)


    def ld_sync(e):
        n = 0
        n += dma(e, p, "sync", wob_s[:, :], wob[:, :])
        for ch in range(4):
            n += dma(e, p, "sync", cos_s[:, ch * TPC:(ch + 1) * TPC],
                     cosd[ch * 128:(ch + 1) * 128, :])
            n += dma(e, p, "sync", sin_s[:, ch * TPC:(ch + 1) * TPC],
                     sind[ch * 128:(ch + 1) * 128, :])
        return n

    def init_v(v):
        v.memset(cone[:, :], 1.0 / 1024.0)
        v.memset(onesrow[:, :], 1.0)
        v.memset(ones128[:, :], 1.0)
        v.memset(eps_col[:, :], 1e-5)
        v.memset(e16[:, :], 0.0)
        v.drain()
        for ch in range(DCH):
            v.memset(e16[0:64, ch * NH + 2 * ch: ch * NH + 2 * ch + 1], 1.0)
            v.memset(e16[64:128, ch * NH + 2 * ch + 1: ch * NH + 2 * ch + 2], 1.0)

    def init_g(g):
        g.iota(iwork[:, :], pattern=[[1, 128]], base=0, channel_multiplier=-1)
    st.stage(sync=ld_sync, vector=init_v, gpsimd=init_g)

    def v_idt(v):
        v.tensor_scalar(out=idt[:, :], in0=iwork[:, :], scalar1=0.0, scalar2=None,
                        op0=OP.is_equal, op1=OP.bypass)
        v.tensor_scalar(out=idt32[:, :], in0=iwork[:, :], scalar1=0.0, scalar2=None,
                        op0=OP.is_equal, op1=OP.bypass)
        if causal:
            v.tensor_scalar(out=perm[:, 0:128], in0=iwork[:, :], scalar1=0.0,
                            scalar2=None, op0=OP.is_equal, op1=OP.bypass)
            v.tensor_scalar(out=perm[:, 128:256], in0=iwork[:, :], scalar1=-64.0,
                            scalar2=None, op0=OP.is_equal, op1=OP.bypass)
            v.tensor_scalar(out=perm[:, 256:384], in0=iwork[:, :], scalar1=64.0,
                            scalar2=None, op0=OP.is_equal, op1=OP.bypass)
            v.tensor_scalar(out=perm[:, 384:512], in0=iwork[:, :], scalar1=0.0,
                            scalar2=None, op0=OP.is_equal, op1=OP.bypass)
            v.drain()
            v.memset(perm[64:128, 0:128], 0.0)
            v.memset(perm[0:64, 384:512], 0.0)
    st.stage(vector=v_idt)

    def wtile(ic, oc):
        return wbuf[:, ic * D + oc * 128: ic * D + oc * 128 + 128]

    def xn_at(b, ch):
        o = (b * DCH + ch) * TPC
        return xn[:, o:o + TPC]

    for b in range(B):
        def ld_b(e, b=b):
            return dma(e, p, "sync",
                       sbx[:, :], xs_d[b])
        st.stage(sync=ld_b)
        emit_layernorm(st, env, b,
                       x_at=lambda ch: sbx[:, ch * TPC:(ch + 1) * TPC],
                       xn_at=lambda ch, b=b: xn_at(b, ch))

    def rope_into(v, dst_at):
        for r in range(4):
            qe = pre[:, r * TPC:(r + 1) * TPC]
            qo = pre[:, (4 + r) * TPC:(5 + r) * TPC]
            cs = cos_s[:, r * TPC:(r + 1) * TPC]
            sn = sin_s[:, r * TPC:(r + 1) * TPC]
            t1 = lnt[:, 0:TPC]
            t2 = lnt[:, TPC:2 * TPC]
            v.tensor_tensor(out=t1, in0=qe, in1=cs, op=OP.mult)
            v.tensor_tensor(out=t2, in0=qo, in1=sn, op=OP.mult)
            v.drain()
            v.tensor_tensor(out=dst_at(r), in0=t1, in1=t2, op=OP.subtract)
            v.drain()
            v.tensor_tensor(out=t1, in0=qe, in1=sn, op=OP.mult)
            v.tensor_tensor(out=t2, in0=qo, in1=cs, op=OP.mult)
            v.drain()
            v.tensor_tensor(out=dst_at(4 + r), in0=t1, in1=t2, op=OP.add)
            v.drain()

    def emit_proj(dest_fn, b):
        for oc in range(DCH):
            def t_p(t, oc=oc, b=b):
                for ic in range(DCH):
                    t.matmul(ps_x[:, 0:TPC], wtile(ic, oc), xn_at(b, ic),
                             start=(ic == 0), stop=(ic == DCH - 1))

            def v_e(v, oc=oc):
                dest_fn(v, oc)
            st.stage(tensor=t_p)
            st.stage(vector=v_e)

    def emit_regroup(srcbuf, dst_at):
        for hp in range(8):
            a_ch = hp // 2
            lo = hp % 2

            def t_rg(t, a_ch=a_ch, lo=lo, hp=hp):
                selA = perm[:, 0:128] if lo == 0 else perm[:, 128:256]
                selB = perm[:, 256:384] if lo == 0 else perm[:, 384:512]
                t.matmul(ps_x[:, 0:TPC], selA,
                         srcbuf[:, a_ch * TPC:(a_ch + 1) * TPC],
                         start=True, stop=False)
                t.matmul(ps_x[:, 0:TPC], selB,
                         srcbuf[:, (4 + a_ch) * TPC:(5 + a_ch) * TPC],
                         start=False, stop=True)

            def v_rg(v, hp=hp):
                v.tensor_copy(out=dst_at(hp), in_=ps_x[:, 0:TPC])
            st.stage(tensor=t_rg)
            st.stage(vector=v_rg)

    def emit_headnorm(srcat, dst, b):
        def s_sq(s):
            for ch in range(DCH):
                s.activation(out=sqw[:, ch * TPC:(ch + 1) * TPC], in_=srcat(ch),
                             func=AF.Square)
        st.stage(scalar=s_sq)

        def t_hs(t):
            for ch in range(DCH):
                t.matmul(ps_x[0:NH, 0:TPC], e16[:, ch * NH:(ch + 1) * NH],
                         sqw[:, ch * TPC:(ch + 1) * TPC],
                         start=(ch == 0), stop=(ch == DCH - 1))
        st.stage(tensor=t_hs)

        def v_m(v, b=b):
            v.tensor_reduce(out=dst[0:NH, b:b + 1], in_=ps_x[0:NH, 0:TPC],
                            axis=AX, op=OP.max)
        st.stage(vector=v_m)

    def ld_wq(e):
        return dma(e, p, "gpsimd",
                   wbuf[:, :].rearrange("q (ic o) -> q ic o", ic=DCH),
                   wqT.rearrange("(ic q) o -> q ic o", q=128))
    st.stage(gpsimd=ld_wq)
    for b in range(B):
        emit_proj(lambda v, oc: v.tensor_copy(out=pre[:, oc * TPC:(oc + 1) * TPC],
                                              in_=ps_x[:, 0:TPC]), b)
        if causal:
            def v_rq(v):
                rope_into(v, lambda r: kr2[:, r * TPC:(r + 1) * TPC])
            st.stage(vector=v_rq)
            emit_regroup(kr2, lambda hp, b=b: qr[:, (b * DCH + hp) * TPC:
                                                 (b * DCH + hp) * TPC + TPC])
        else:
            def v_rq(v, b=b):
                rope_into(v, lambda r, b=b: qr[:, (b * DCH + r) * TPC:
                                               (b * DCH + r) * TPC + TPC])
            st.stage(vector=v_rq)
        emit_headnorm(lambda ch, b=b: qr[:, (b * DCH + ch) * TPC:
                                         (b * DCH + ch) * TPC + TPC], mq, b)

    def ld_wk(e):
        return dma(e, p, "gpsimd",
                   wbuf[:, :].rearrange("q (ic o) -> q ic o", ic=DCH),
                   wkT.rearrange("(ic q) o -> q ic o", q=128))
    st.stage(gpsimd=ld_wk)
    for b in range(B):
        emit_proj(lambda v, oc: v.tensor_copy(out=pre[:, oc * TPC:(oc + 1) * TPC],
                                              in_=ps_x[:, 0:TPC]), b)
        if causal:
            def v_rk(v):
                rope_into(v, lambda r: kr2[:, r * TPC:(r + 1) * TPC])
            st.stage(vector=v_rk)
            emit_regroup(kr2, lambda hp: kr[:, hp * TPC:(hp + 1) * TPC])
        else:
            def v_rk(v):
                rope_into(v, lambda r: kr[:, r * TPC:(r + 1) * TPC])
            st.stage(vector=v_rk)
        emit_headnorm(lambda ch: kr[:, ch * TPC:(ch + 1) * TPC], mkl, b)

        def st_k(e, b=b):
            n = 0
            for hp in range(8):
                n += dma(e, p, "sync", kv_loc[0, b, hp],
                         kr[:, hp * TPC:(hp + 1) * TPC])
            return n
        st.stage(sync=st_k)

    def ld_wv(e):
        return dma(e, p, "gpsimd",
                   wbuf[:, :].rearrange("q (ic o) -> q ic o", ic=DCH),
                   wvT.rearrange("(ic q) o -> q ic o", q=128))
    st.stage(gpsimd=ld_wv)
    for b in range(B):
        emit_proj(lambda v, oc: v.tensor_copy(out=kr[:, oc * TPC:(oc + 1) * TPC],
                                              in_=ps_x[:, 0:TPC]), b)
        for ch in range(DCH):
            for tq in range(4):
                def t_tr(t, ch=ch, tq=tq):
                    t.transpose(ps_t[:, 0:128],
                                kr[:, ch * TPC + tq * 128: ch * TPC + tq * 128 + 128],
                                idt[:, :])

                def v_tr(v):
                    v.tensor_copy(out=sqw[:, 0:128], in_=ps_t[:, 0:128])

                def st_v(e, ch=ch, tq=tq, b=b):
                    dst = kv_loc[1, b, ch].rearrange("e t -> (e t)")
                    dst = dst.rearrange("(a q f) -> q a f", a=4, q=128)[:, tq, :]
                    return dma(e, p, "sync", dst, sqw[:, 0:128])
                st.stage(tensor=t_tr)
                st.stage(vector=v_tr)
                st.stage(sync=st_v)

    def st_bnd(e):
        return dma(e, p, "sync", bnd_loc[:, :], mkl[0:16, 0:B])
    st.stage(sync=st_bnd)

    def g_cc(g):
        cs = p.sem("cc")
        g.collective_compute("AllGather", OP.bypass,
                             replica_groups=[list(range(NCORES))],
                             ins=[kv_loc[:, :, :, :, :].opt()],
                             outs=[kv_all[:, :, :, :, :, :].opt()]).then_inc(cs[0], 1)
        cs[1] += 1
        g.collective_compute("AllReduce", OP.max,
                             replica_groups=[list(range(NCORES))],
                             ins=[bnd_loc[:, :].opt()],
                             outs=[bnd_all[:, :].opt()]).then_inc(cs[0], 1)
        cs[1] += 1
        g.wait_ge(cs[0], cs[1])
    st.stage(gpsimd=g_cc)

    def ld_mkg(e):
        return dma(e, p, "sync", mkg[0:16, 0:B], bnd_all[:, :])
    st.stage(sync=ld_mkg)

    scl = 1.0 / math.sqrt(HD)

    def s_bnd(s):
        s.activation(out=mq[0:NH, 0:B], in_=mq[0:NH, 0:B], func=AF.Sqrt)
        s.drain()
        s.activation(out=mkg[0:NH, 0:B], in_=mkg[0:NH, 0:B], func=AF.Sqrt)
    st.stage(scalar=s_bnd)

    def v_bnd(v):
        v.tensor_tensor(out=bsml[0:NH, 0:B], in0=mq[0:NH, 0:B], in1=mkg[0:NH, 0:B],
                        op=OP.mult)
        v.drain()
        v.tensor_scalar(out=bsml[0:NH, 0:B], in0=bsml[0:NH, 0:B], scalar1=-scl,
                        scalar2=None, op0=OP.mult, op1=OP.bypass)
    st.stage(vector=v_bnd)

    def t_bt(t):
        t.transpose(ps_x[0:B, 0:NH], bsml[0:NH, 0:B], idt32[0:NH, 0:NH])
    st.stage(tensor=t_bt)

    def v_bt(v):
        v.tensor_copy(out=lnt[0:B, 0:NH], in_=ps_x[0:B, 0:NH])
    st.stage(vector=v_bt)

    def st_br(e):
        return dma(e, p, "sync", brow_d[:, :], lnt[0:B, 0:NH])
    st.stage(sync=st_br)

    def ld_br(e):
        return dma(e, p, "sync", brow[0:1, :], brow_d.rearrange("b h -> (b h)"))
    st.stage(sync=ld_br)

    def t_pb(t):
        t.matmul(ps_x[:, 0:NH * B], onesrow[0:1, 0:128], brow[0:1, :],
                 start=True, stop=True)
    st.stage(tensor=t_pb)

    def v_pb(v):
        v.tensor_copy(out=bias_bh[:, :], in_=ps_x[:, 0:NH * B])
    st.stage(vector=v_pb)

    def v_vones(v):
        for z in range(4 * NCORES):
            for hi in range(2):
                v.memset(vbuf[:, z * 130 + hi * 65 + 64: z * 130 + hi * 65 + 65], 1.0)
    st.stage(vector=v_vones)

    qk_sem = p.sem("qk")
    ex_sem = p.sem("ex")
    exs = p.sem("exs")

    for b in range(B):
        def ld_strm(e, b=b):
            return dma(e, p, "sync",
                       sbx[:, :], xs_d[b])
        st.stage(sync=ld_strm)

        for hp in range(8):
            h0 = 2 * hp
            is_c = causal and hp < 4

            def d_kv(e, b=b, hp=hp):
                n = 0
                for cpr in range(NCORES):
                    n += dma(e, p, "sync",
                             kbuf[:, cpr * TPC:(cpr + 1) * TPC], kv_all[cpr, 0, b, hp])
                    srcv = kv_all[cpr, 1, b, hp].rearrange("e t -> (e t)")
                    srcv = srcv.rearrange("(a q h e) -> q a h e", q=128, h=2, e=64)
                    for hi in range(2):
                        dst = vbuf[:, 0:32 * 130].rearrange("q (z s) -> q z s", s=130)[
                            :, cpr * 4:(cpr + 1) * 4, hi * 65:hi * 65 + 64]
                        n += dma(e, p, "sync", dst, srcv[:, :, hi, :])
                return n
            st.stage(sync=d_kv)

            qk_base = qk_sem[1]
            ex_base = ex_sem[1]
            exs_base = exs[1]

            def t_att(t, b=b, hp=hp, qk_base=qk_base, ex_base=ex_base):
                for kc in range(KCH):
                    par = kc % 2
                    if kc >= 2:
                        t.wait_ge(ex_sem[0], ex_base + (kc - 1))
                    for hi in range(2):
                        mm = t.matmul(ps_s[hi * 2 + par][:, 0:TPC],
                                 kbuf[hi * 64:(hi + 1) * 64, kc * 128:kc * 128 + 128],
                                 qr[hi * 64:(hi + 1) * 64,
                                    (b * DCH + hp) * TPC:(b * DCH + hp) * TPC + TPC],
                                 start=True, stop=True)
                        if hi == 1:
                            mm.then_inc(qk_sem[0], 1)
                    if kc >= 1:
                        t.wait_ge(ex_sem[0], ex_base + kc)
                        pb = (kc - 1) % 2
                        for hi in range(2):
                            t.matmul(ps_o[hi][0:65, 0:TPC],
                                     vbuf[:, (kc - 1) * 130 + hi * 65:
                                          (kc - 1) * 130 + hi * 65 + 65],
                                     pk[:, (hi * 2 + pb) * TPC:(hi * 2 + pb) * TPC + TPC],
                                     start=(kc == 1), stop=False)
                t.wait_ge(ex_sem[0], ex_base + KCH)
                pb = (KCH - 1) % 2
                for hi in range(2):
                    mm = t.matmul(ps_o[hi][0:65, 0:TPC],
                             vbuf[:, (KCH - 1) * 130 + hi * 65:
                                  (KCH - 1) * 130 + hi * 65 + 65],
                             pk[:, (hi * 2 + pb) * TPC:(hi * 2 + pb) * TPC + TPC],
                             start=False, stop=True)
                    if hi == 1:
                        mm.then_inc(qk_sem[0], 1)

            def s_exp(s, b=b, hp=hp, qk_base=qk_base, is_c=is_c):
                tgt = exs if is_c else ex_sem
                for kc in range(KCH):
                    par = kc % 2
                    s.wait_ge(qk_sem[0], qk_base + kc + 1)
                    for hi in range(2):
                        act = s.activation(
                            out=pk[:, (hi * 2 + par) * TPC:
                                   (hi * 2 + par) * TPC + TPC],
                            in_=ps_s[hi * 2 + par][:, 0:TPC], func=AF.Exp,
                            scale=scl,
                            bias=bias_bh[:, b * NH + 2 * hp + hi:
                                         b * NH + 2 * hp + hi + 1])
                        if hi == 1:
                            act.then_inc(tgt[0], 1)

            if not is_c:
                qk_sem[1] += KCH + 1
                ex_sem[1] += KCH
                st.stage(tensor=t_att, scalar=s_exp)
            else:
                msems = (p.sem("dma_maskA"), p.sem("dma_maskB"))
                mk_base = (msems[0][1], msems[1][1])

                def d_mask(e, ex_base=ex_base):
                    for kc in range(KCH):
                        if kc >= 2:
                            e.wait_ge(ex_sem[0], ex_base + (kc - 1))
                        e.dma_start(out=mbuf[:, (kc % 2) * TPC:
                                             (kc % 2) * TPC + TPC],
                                    in_=mask_d[kc]).then_inc(msems[kc % 2][0], 16)
                    return 0
                msems[0][1] += 16 * (KCH // 2)
                msems[1][1] += 16 * (KCH // 2)

                def v_mask(v, exs_base=exs_base, ex_base=ex_base, mk_base=mk_base):
                    for kc in range(KCH):
                        par = kc % 2
                        v.wait_ge(exs[0], exs_base + kc + 1)
                        v.wait_ge(msems[kc % 2][0],
                                  mk_base[kc % 2] + 16 * (kc // 2 + 1))
                        for hi in range(2):
                            sl = pk[:, (hi * 2 + par) * TPC:(hi * 2 + par) * TPC + TPC]
                            tt = v.tensor_tensor(out=sl, in0=sl,
                                                 in1=mbuf[:, par * TPC:(par + 1) * TPC],
                                                 op=OP.mult)
                            if hi == 1:
                                tt.then_inc(ex_sem[0], 1)

                qk_sem[1] += KCH + 1
                exs[1] += KCH
                ex_sem[1] += KCH
                st.stage(tensor=t_att, scalar=s_exp, sync=d_mask, vector=v_mask)

            def v_rec(v):
                v.tensor_copy(out=denrow[64:65, 2 * TPC:3 * TPC],
                              in_=ps_o[0][64:65, 0:TPC])
                v.tensor_copy(out=denrow[64:65, 3 * TPC:4 * TPC],
                              in_=ps_o[1][64:65, 0:TPC])
                v.drain()
                v.reciprocal(out=denrow[64:65, 0:TPC],
                             in_=denrow[64:65, 2 * TPC:3 * TPC])
                v.reciprocal(out=denrow[64:65, TPC:2 * TPC],
                             in_=denrow[64:65, 3 * TPC:4 * TPC])
            st.stage(vector=v_rec)

            def t_rb(t):
                t.matmul(ps_s[0][0:64, 0:TPC], ones128[64:65, 0:64],
                         denrow[64:65, 0:TPC], start=True, stop=True)
                t.matmul(ps_s[1][0:64, 0:TPC], ones128[64:65, 0:64],
                         denrow[64:65, TPC:2 * TPC], start=True, stop=True)
            st.stage(tensor=t_rb)

            def v_rbc(v):
                v.tensor_copy(out=rbc[0:64, 0:TPC], in_=ps_s[0][0:64, 0:TPC])
                v.tensor_copy(out=rbc[64:128, 0:TPC], in_=ps_s[1][0:64, 0:TPC])
            st.stage(vector=v_rbc)

            def v_ctx(v, hp=hp):
                v.tensor_tensor(out=actx[0:64, hp * TPC:hp * TPC + TPC],
                                in0=ps_o[0][0:64, 0:TPC], in1=rbc[0:64, 0:TPC],
                                op=OP.mult)
                v.tensor_tensor(out=actx[64:128, hp * TPC:hp * TPC + TPC],
                                in0=ps_o[1][0:64, 0:TPC], in1=rbc[64:128, 0:TPC],
                                op=OP.mult)
            st.stage(vector=v_ctx)

        if b == 0:
            def ld_wo(e):
                return dma(e, p, "gpsimd",
                           wbuf[:, :].rearrange("q (ic o) -> q ic o", ic=DCH),
                           woT.rearrange("(ic q) o -> q ic o", q=128))
            st.stage(gpsimd=ld_wo)

        for oc in range(DCH):
            def t_o(t, oc=oc):
                for ic in range(DCH):
                    t.matmul(ps_x[:, 0:TPC], wtile(ic, oc),
                             actx[:, ic * TPC:(ic + 1) * TPC],
                             start=(ic == 0), stop=(ic == DCH - 1))

            def v_o(v, oc=oc):
                sl = sbx[:, oc * TPC:(oc + 1) * TPC]
                v.scalar_tensor_tensor(out=sl, in0=ps_x[:, 0:TPC],
                                       scalar=wob_s[:, oc:oc + 1], in1=sl,
                                       op0=OP.add, op1=OP.add)
            st.stage(tensor=t_o)
            st.stage(vector=v_o)

        def st_b(e, b=b):
            return dma(e, p, "sync",
                       xo_d[b], sbx[:, :])
        st.stage(sync=st_b)




def build_attn(layer):
    p = Prog("attn%d" % layer)
    p.pf = "a%d_" % layer
    xs_d = p.inp("xs", (B, 128, DCH * TPC))
    xo_d = p.outp("xo", (B, 128, DCH * TPC))
    p.start_block()
    st = Stager(p)
    emit_attn(p, st, layer, xs_d, xo_d)
    st.stage(sync=lambda e: None)
    return p

def _rope_tables_l2(core):
    pos = np.arange(core * TPC, (core + 1) * TPC, dtype=np.float64)[None, :]
    r = np.arange(512, dtype=np.float64)[:, None]
    freqs = np.exp(r * 2 * (-math.log(10000.0) / 1024.0))
    ang = pos * freqs
    return (np.cos(ang).astype(np.float32), np.sin(ang).astype(np.float32))


def _rope_tables_l1(core):
    inv = 1.0 / (10000.0 ** (np.arange(0, 64, 2, dtype=np.float64) / 64.0))
    t = np.arange(core * TPC, (core + 1) * TPC, dtype=np.float64)
    fr = t[:, None] * inv[None, :]
    cos = np.cos(fr).T
    sin = np.sin(fr).T
    cosT = np.tile(cos, (16, 1)).astype(np.float32)
    sinT = np.tile(sin, (16, 1)).astype(np.float32)
    return np.ascontiguousarray(cosT), np.ascontiguousarray(sinT)


def _perm_cols(wT, layer):
    if layer == 2:
        idx = np.concatenate([np.arange(0, D, 2), np.arange(1, D, 2)])
    else:
        heads = np.arange(NH)
        a_idx = (heads[:, None] * 64 + np.arange(32)[None, :]).ravel()
        b_idx = (heads[:, None] * 64 + 32 + np.arange(32)[None, :]).ravel()
        idx = np.concatenate([a_idx, b_idx])
    return np.ascontiguousarray(wT[:, idx])


def _causal_masks(core):
    import ml_dtypes
    m = np.zeros((KCH, 128, TPC), ml_dtypes.bfloat16)
    q_glob = core * TPC + np.arange(TPC)[None, :]
    for kc in range(KCH):
        k_glob = kc * 128 + np.arange(128)[:, None]
        m[kc] = (q_glob >= k_glob).astype(ml_dtypes.bfloat16)
    return m


def maps_attn(layer, I):
    if layer == 2:
        base = {
            "wqT": _perm_cols(_t(I["L2_attn_wq"]), 2),
            "wkT": _perm_cols(_t(I["L2_attn_wk"]), 2),
            "wvT": _t(I["L2_attn_wv"]),
            "woT": _t(I["L2_attn_wo_w"]),
            "wob": _col(I["L2_attn_wo_b"]),
        }
    else:
        base = {
            "wqT": _perm_cols(np.ascontiguousarray(np.asarray(I["L1_attn_Wq"], np.float32)), 1),
            "wkT": _perm_cols(np.ascontiguousarray(np.asarray(I["L1_attn_Wk"], np.float32)), 1),
            "wvT": np.ascontiguousarray(np.asarray(I["L1_attn_Wv"], np.float32)),
            "woT": np.ascontiguousarray(np.asarray(I["L1_attn_Wo"], np.float32)),
            "wob": np.zeros((128, DCH), np.float32),
        }
    pf = "a%d_" % layer
    in_maps = []
    for c in range(NCORES):
        m = dict(base)
        cosT, sinT = (_rope_tables_l2(c) if layer == 2 else _rope_tables_l1(c))
        m["cosT"] = cosT
        m["sinT"] = sinT
        if layer == 1:
            m["maskT"] = _causal_masks(c)
        in_maps.append({pf + k: v for k, v in m.items()})
    return in_maps


def run_attn(layer, stream_shards, I):
    p = build_attn(layer)
    pf = "a%d_" % layer
    in_maps = maps_attn(layer, I)
    for c in range(NCORES):
        in_maps[c][pf + "xs"] = _to_qmaj(stream_shards[c])
    res = run_bass_kernel_spmd(p.nc, in_maps, core_ids=list(range(NCORES)))
    out = [_from_qmaj(res.results[c][pf + "xo"]) for c in range(NCORES)]
    p.close()
    return out




def emit_l1ffn(p, st, xs_d, xo_d):
    p.pf = "f1_"
    t1r_d = p.inp("t1r", (4, 512, 512))
    t2r_d = p.inp("t2r", (4, 512, 512))
    t1i_d = p.inp("t1i", (4, 512, 512))
    t2i_d = p.inp("t2i", (4, 512, 512))
    br_d = p.inp("br", (128, 4))
    bi_d = p.inp("bi", (128, 4))

    sbx = p.sb("sbx", (128, DCH * TPC), F32)
    xn = p.sb("xn", (128, DCH * TPC), BF16)
    wre = p.sb("wre", (128, DCH * 512), BF16)
    wim = p.sb("wim", (128, DCH * 512), BF16)
    tta = p.sb("tta", (128, 4 * 512), F32)
    ttb = p.sb("ttb", (128, 4 * 512), F32)
    acc = p.sb("acc", (128, 4 * 512), F32)
    tmp = p.sb("tmp", (128, 4 * 512), F32)
    sqw = p.sb("sqw", (128, DCH * TPC), BF16)
    srow = p.sb("srow", (1, 4 * TPC), F32)
    lnt = p.sb("lnt", (128, DCH * TPC), F32)
    cone = p.sb("cone", (128, 1), BF16)
    onesrow = p.sb("onesrow", (1, 128), F32)
    eps_col = p.sb("eps_col", (128, 1), F32)
    bcol = p.sb("bcol", (128, 8), F32)

    ps_s = [p.ps("ps_s%d" % i) for i in range(4)]
    ps_x = p.ps("ps_x")
    env = dict(stream=None, xn=xn, sq=None, sqw=sqw, srow=srow, lnt=lnt,
               cone=cone, onesrow=onesrow, ps_stat=ps_s[0], ps_b0=ps_s[1],
               ps_b1=ps_s[2], eps_col=eps_col)


    def ld0(e):
        n = dma(e, p, "sync", bcol[:, 0:4], br_d[:, :])
        n += dma(e, p, "sync", bcol[:, 4:8], bi_d[:, :])
        return n

    def iv(v):
        v.memset(cone[:, :], 1.0 / 1024.0)
        v.memset(onesrow[:, :], 1.0)
        v.memset(eps_col[:, :], 1e-5)
    st.stage(sync=ld0, vector=iv)

    for part, (t1d, t2d) in [(0, (t1r_d, t2r_d)), (1, (t1i_d, t2i_d))]:
        for c in range(4):
            def ldc(e, t1d=t1d, t2d=t2d, c=c):
                n = dma(e, p, "sync",
                        tta[:, :].rearrange("q (a s) -> q a s", a=4),
                        t1d[c].rearrange("(a q) s -> q a s", q=128))
                n += dma(e, p, "sync",
                         ttb[:, :].rearrange("q (a s) -> q a s", a=4),
                         t2d[c].rearrange("(a q) s -> q a s", q=128))
                return n
            st.stage(sync=ldc)

            def vmul(v, c=c):
                if c == 0:
                    v.tensor_tensor(out=acc[:, :], in0=tta[:, :], in1=ttb[:, :],
                                    op=OP.mult)
                else:
                    v.tensor_tensor(out=tmp[:, :], in0=tta[:, :], in1=ttb[:, :],
                                    op=OP.mult)
                    v.drain()
                    v.tensor_tensor(out=acc[:, :], in0=acc[:, :], in1=tmp[:, :],
                                    op=OP.add)
            st.stage(vector=vmul)

        def vst(v, part=part):
            for ch in range(4):
                sl = acc[:, ch * 512:(ch + 1) * 512]
                if part == 0:
                    v.tensor_copy(out=wre[:, ch * 512:ch * 512 + 512], in_=sl)
                    v.tensor_copy(out=wim[:, (4 + ch) * 512:(4 + ch) * 512 + 512],
                                  in_=sl)
                else:
                    v.tensor_copy(out=wim[:, ch * 512:ch * 512 + 512], in_=sl)
            if part == 1:
                v.drain()
                for ch in range(4):
                    v.tensor_scalar(out=wre[:, (4 + ch) * 512:(4 + ch) * 512 + 512],
                                    in0=acc[:, ch * 512:(ch + 1) * 512],
                                    scalar1=-1.0, scalar2=None,
                                    op0=OP.mult, op1=OP.bypass)
        st.stage(vector=vst)

    for b in range(B):
        def ld_b(e, b=b):
            return dma(e, p, "sync",
                       sbx[:, :], xs_d[b])
        st.stage(sync=ld_b)
        emit_layernorm(st, env, 0,
                       x_at=lambda ch: sbx[:, ch * TPC:(ch + 1) * TPC],
                       xn_at=lambda ch: xn[:, ch * TPC:(ch + 1) * TPC])

        for oc in range(DCH):
            wsel = wre if oc < 4 else wim
            occ = oc % 4

            def t_p(t, wsel=wsel, occ=occ):
                for ic in range(DCH):
                    t.matmul(ps_x[:, 0:TPC],
                             wsel[:, ic * 512 + occ * 128: ic * 512 + occ * 128 + 128],
                             xn[:, ic * TPC:(ic + 1) * TPC],
                             start=(ic == 0), stop=(ic == DCH - 1))

            def v_o(v, oc=oc):
                sl = sbx[:, oc * TPC:(oc + 1) * TPC]
                v.scalar_tensor_tensor(out=sl, in0=ps_x[:, 0:TPC],
                                       scalar=bcol[:, oc:oc + 1], in1=sl,
                                       op0=OP.add, op1=OP.add)
            st.stage(tensor=t_p)
            st.stage(vector=v_o)

        def st_b(e, b=b):
            return dma(e, p, "sync",
                       xo_d[b], sbx[:, :])
        st.stage(sync=st_b)




def build_l1ffn():
    p = Prog("l1ffn")
    p.pf = "f1_"
    xs_d = p.inp("xs", (B, 128, DCH * TPC))
    xo_d = p.outp("xo", (B, 128, DCH * TPC))
    p.start_block()
    st = Stager(p)
    emit_l1ffn(p, st, xs_d, xo_d)
    st.stage(sync=lambda e: None)
    return p

def maps_l1ffn(I):
    G1R = np.asarray(I["L1_ffn_G1_R"], np.float32)[0]
    G2R = np.asarray(I["L1_ffn_G2_R"], np.float32)[:, :, 0, :]
    G1I = np.asarray(I["L1_ffn_G1_I"], np.float32)[0]
    G2I = np.asarray(I["L1_ffn_G2_I"], np.float32)[:, :, 0, :]

    def factors(G1, G2):
        t1 = np.zeros((4, 512, 512), np.float32)
        t2 = np.zeros((4, 512, 512), np.float32)
        for c in range(4):
            A = G1[:, c, :].T
            Bm = G2[c]
            t1[c] = np.repeat(np.repeat(A, 16, 0), 16, 1)
            t2[c] = np.tile(Bm.T, (32, 32))
        return t1, t2

    t1r, t2r = factors(G1R, G2R)
    t1i, t2i = factors(G1I, G2I)
    base = {
        "t1r": t1r, "t2r": t2r, "t1i": t1i, "t2i": t2i,
        "br": _col(I["L1_ffn_bias_real"], 4),
        "bi": _col(I["L1_ffn_bias_imag"], 4),
    }
    return [{"f1_" + k: v for k, v in base.items()} for c in range(NCORES)]


def run_l1ffn(stream_shards, I):
    p = build_l1ffn()
    in_maps = maps_l1ffn(I)
    for c in range(NCORES):
        in_maps[c]["f1_xs"] = _to_qmaj(stream_shards[c])
    res = run_bass_kernel_spmd(p.nc, in_maps, core_ids=list(range(NCORES)))
    out = [_from_qmaj(res.results[c]["f1_xo"]) for c in range(NCORES)]
    p.close()
    return out




def emit_quant(st, Q, src_ap, dst_ap, W, mbits, elo, ehi, signed):
    p = st.p
    f = Q["f"][:, 0:W]
    g_ = Q["g"][:, 0:W]
    t_ = Q["t"][:, 0:W]
    si = Q["si"][:, 0:W]
    u0 = Q["u0"][:, 0:W]
    u1 = Q["u1"][:, 0:W]
    b0 = Q["b0"][:, 0:W]
    ei = Q["ei"][:, 0:W]
    two = Q["two"][:, 0:W]
    sg = Q["sg"][:, 0:W]

    def s1(s):
        if signed:
            s.activation(out=sg, in_=src_ap, func=AF.Sign)
        s.activation(out=g_, in_=src_ap, func=AF.Square)
        s.drain()
        s.activation(out=f, in_=g_, func=AF.Ln, bias=Q["tn"][:, 0:1])
        s.drain()
        s.activation(out=f, in_=f, func=AF.Relu, scale=0.5 * INV_LN2,
                     bias=Q["bl4" if elo == -6 else "bl5"][:, 0:1])
    st.stage(scalar=s1)

    def v1(v):
        v.tensor_scalar(out=g_, in0=f, scalar1=float(ehi - elo),
                        scalar2=float(elo) - 0.5, op0=OP.min, op1=OP.add)
        v.drain()
        v.tensor_copy(out=ei, in_=g_)
        v.drain()
        v.tensor_copy(out=t_, in_=ei)
        v.drain()
        v.tensor_scalar(out=t_, in0=t_, scalar1=-1.0, scalar2=float(mbits),
                        op0=OP.mult, op1=OP.add)
    st.stage(vector=v1)

    def v_pow(v):
        v.tensor_scalar(out=b0, in0=t_, scalar1=7.0, scalar2=None,
                        op0=OP.add, op1=OP.bypass)
        v.memset(si, 2.0 ** -7)
        v.drain()
        for k in (16, 8, 4, 2, 1):
            v.tensor_scalar(out=u0, in0=b0, scalar1=float(k), scalar2=None,
                            op0=OP.is_ge, op1=OP.bypass)
            v.drain()
            v.tensor_scalar(out=u1, in0=u0, scalar1=float(2 ** k - 1),
                            scalar2=1.0, op0=OP.mult, op1=OP.add)
            v.scalar_tensor_tensor(out=b0, in0=u0, scalar=-float(k), in1=b0,
                                   op0=OP.mult, op1=OP.add)
            v.drain()
            v.tensor_tensor(out=si, in0=si, in1=u1, op=OP.mult)
    st.stage(vector=v_pow)

    def v2a(v):
        v.tensor_tensor(out=u0, in0=src_ap, in1=si, op=OP.mult)
    st.stage(vector=v2a)

    if signed:
        def s2(s):
            s.activation(out=u1, in_=u0, func=AF.Abs)
        st.stage(scalar=s2)
        ua = u1
    else:
        ua = u0

    def v2(v):
        if mbits == 3:
            cur = ua
            slots = [g_, t_]
            for i, wt in enumerate([16.0, 8.0, 4.0, 2.0, 1.0]):
                v.tensor_scalar(out=b0, in0=cur, scalar1=wt - 0.5, scalar2=-wt,
                                op0=OP.is_ge, op1=OP.mult)
                v.drain()
                nxt = slots[i % 2]
                v.tensor_tensor(out=nxt, in0=cur, in1=b0, op=OP.add)
                v.drain()
                cur = nxt
            v.scalar_tensor_tensor(out=b0, in0=cur, scalar=-1.0, in1=ua,
                                   op0=OP.mult, op1=OP.add)
            v.drain()
            v.reciprocal(out=si, in_=si)
            v.drain()
            if signed:
                v.tensor_tensor(out=g_, in0=b0, in1=si, op=OP.mult)
                v.drain()
                v.tensor_tensor(out=dst_ap, in0=g_, in1=sg, op=OP.mult)
            else:
                v.tensor_tensor(out=dst_ap, in0=b0, in1=si, op=OP.mult)
        else:
            v.tensor_scalar(out=b0, in0=ua, scalar1=1.0 / 64.0, scalar2=None,
                            op0=OP.mult, op1=OP.bypass)
            v.drain()
            v.tensor_copy(out=ei, in_=b0)
            v.drain()
            v.tensor_copy(out=f, in_=ei)
            v.drain()
            v.scalar_tensor_tensor(out=u0 if signed else g_, in0=f, scalar=-64.0,
                                   in1=ua, op0=OP.mult, op1=OP.add)
            w_ = u0 if signed else g_
            v.drain()
            v.tensor_scalar(out=w_, in0=w_, scalar1=64.0, scalar2=None,
                            op0=OP.add, op1=OP.bypass)
            v.drain()
            cur = w_
            slots = [t_, g_] if signed else [t_, u1]
            for i, wt in enumerate([64.0, 32.0, 16.0, 8.0, 4.0, 2.0, 1.0]):
                v.tensor_scalar(out=b0, in0=cur, scalar1=wt - 0.5, scalar2=-wt,
                                op0=OP.is_ge, op1=OP.mult)
                v.drain()
                nxt = slots[i % 2]
                v.tensor_tensor(out=nxt, in0=cur, in1=b0, op=OP.add)
                v.drain()
                cur = nxt
            v.scalar_tensor_tensor(out=b0, in0=cur, scalar=-1.0, in1=w_,
                                   op0=OP.mult, op1=OP.add)
            v.drain()
            v.scalar_tensor_tensor(out=w_, in0=f, scalar=64.0, in1=b0,
                                   op0=OP.mult, op1=OP.add)
            v.drain()
            v.tensor_scalar(out=w_, in0=w_, scalar1=-64.0, scalar2=None,
                            op0=OP.add, op1=OP.bypass)
            v.drain()
            v.reciprocal(out=si, in_=si)
            v.drain()
            if signed:
                v.tensor_tensor(out=g_, in0=w_, in1=si, op=OP.mult)
                v.drain()
                v.tensor_tensor(out=dst_ap, in0=g_, in1=sg, op=OP.mult)
            else:
                v.tensor_tensor(out=dst_ap, in0=w_, in1=si, op=OP.mult)
    st.stage(vector=v2)


def emit_l2ffn(p, st, xs_d, xo_d):
    p.pf = "f2_"
    HCH = 32
    w1T_d = p.inp("w1T", (D, 4096))
    b1_d = p.inp("b1", (128, HCH))
    b2_d = p.inp("b2", (128, DCH))
    w2slice_d = p.inp("w2slice", (512, D))

    w2q_loc = p.dram("w2q_loc", (512, D), BF16)
    w2q_all = p.dram("w2q_all", (4096, D), BF16)

    sbx = p.sb("sbx", (128, DCH * TPC), F32)
    xn = p.sb("xn", (128, DCH * TPC), BF16)
    w1sb = p.sb("w1sb", (128, DCH * 4096), BF16)
    w2blk = p.sb("w2blk", (128, HCH * 128), BF16)
    hbuf = p.sb("hbuf", (128, 2 * TPC), F32)
    hq = p.sb("hq", (128, HCH * TPC), BF16)
    obuf = p.sb("obuf", (128, DCH * TPC), F32)
    b1c = p.sb("b1c", (128, HCH), F32)
    b2c = p.sb("b2c", (128, DCH), F32)
    b2q = p.sb("b2q", (128, DCH), F32)
    w2src = obuf
    w2qs = hq
    srow = p.sb("srow", (1, 4 * TPC), F32)
    cone = p.sb("cone", (128, 1), BF16)
    onesrow = p.sb("onesrow", (1, 128), F32)
    eps_col = p.sb("eps_col", (128, 1), F32)

    QW = 1024
    Q = {}
    for nm in ["f", "g", "t", "si", "u0", "u1", "b0", "sg", "two"]:
        Q[nm] = p.sb("q_" + nm, (128, QW), F32)
    Q["ei"] = p.sb("q_ei", (128, QW), I32)
    Q["bl4"] = p.sb("q_bl4", (128, 1), F32)
    Q["bl5"] = p.sb("q_bl5", (128, 1), F32)
    Q["tn"] = p.sb("q_tn", (128, 1), F32)
    lnt = p.sb("lnt", (128, 2 * TPC), F32)

    ps_s = [p.ps("ps_s%d" % i) for i in range(4)]
    ps_x = p.ps("ps_x")
    env = dict(stream=None, xn=xn, sq=None, sqw=hq, srow=srow, lnt=lnt,
               cone=cone, onesrow=onesrow, ps_stat=ps_s[0], ps_b0=ps_s[1],
               ps_b1=ps_s[2], eps_col=eps_col)


    def ld0(e):
        n = dma(e, p, "sync", b1c[:, :], b1_d[:, :])
        n += dma(e, p, "sync", b2c[:, :], b2_d[:, :])
        n += dma(e, p, "sync",
                 w2src[:, :].rearrange("q (a o) -> q a o", a=4),
                 w2slice_d.rearrange("(a q) o -> q a o", q=128))
        return n

    def iv(v):
        v.memset(cone[:, :], 1.0 / 1024.0)
        v.memset(onesrow[:, :], 1.0)
        v.memset(eps_col[:, :], 1e-5)
        v.memset(Q["two"][:, :], 2.0)
        v.memset(Q["bl4"][:, :], 6.0)
        v.memset(Q["bl5"][:, :], 14.0)
        v.memset(Q["tn"][:, :], 1e-38)
    st.stage(sync=ld0, vector=iv)
    w1r = w1T_d.rearrange("(ic q) o -> q ic o", q=128)
    cast_stream(st, p,
                [(w1sb[:, ic * 4096 + o4 * 1024:ic * 4096 + (o4 + 1) * 1024],
                  w1r[:, ic, o4 * 1024:(o4 + 1) * 1024], 1024)
                 for ic in range(DCH) for o4 in range(4)],
                hbuf[:, 0:1024], lnt[:, 0:1024])

    for a in range(4):
        emit_quant(st, Q, w2src[:, a * D:(a + 1) * D], w2qs[:, a * D:(a + 1) * D],
                   D, 3, -6, 9.9, True)
    emit_quant(st, Q, b2c[:, 0:DCH], b2q[:, 0:DCH], DCH, 10, -14, 15.9, True)

    def stw2(e):
        return dma(e, p, "sync",
                   w2q_loc.rearrange("(a q) o -> q a o", q=128),
                   w2qs[:, 0:4 * D].rearrange("q (a o) -> q a o", a=4))
    st.stage(sync=stw2)

    def g_cc(g):
        cs = p.sem("cc")
        g.collective_compute("AllGather", OP.bypass,
                             replica_groups=[list(range(NCORES))],
                             ins=[w2q_loc[:, :].opt()],
                             outs=[w2q_all[:, :].opt()]).then_inc(cs[0], 1)
        cs[1] += 1
        g.wait_ge(cs[0], cs[1])
    st.stage(gpsimd=g_cc)

    for b in range(B):
        def ld_b(e, b=b):
            return dma(e, p, "sync",
                       sbx[:, :], xs_d[b])
        st.stage(sync=ld_b)
        emit_layernorm(st, env, 0,
                       x_at=lambda ch: sbx[:, ch * TPC:(ch + 1) * TPC],
                       xn_at=lambda ch: xn[:, ch * TPC:(ch + 1) * TPC])

        for hg in range(16):
            for j in range(2):
                hc = hg * 2 + j

                def t_h(t, hc=hc):
                    for ic in range(DCH):
                        t.matmul(ps_x[:, 0:TPC],
                                 w1sb[:, ic * 4096 + hc * 128: ic * 4096 + hc * 128 + 128],
                                 xn[:, ic * TPC:(ic + 1) * TPC],
                                 start=(ic == 0), stop=(ic == DCH - 1))

                def s_h(s, hc=hc, j=j):
                    s.activation(out=hbuf[:, j * TPC:(j + 1) * TPC],
                                 in_=ps_x[:, 0:TPC], func=AF.Relu,
                                 bias=b1c[:, hc:hc + 1])
                st.stage(tensor=t_h)
                st.stage(scalar=s_h)
            emit_quant(st, Q, hbuf[:, 0:2 * TPC],
                       hq[:, hg * 2 * TPC:(hg + 1) * 2 * TPC],
                       2 * TPC, 3, -6, 9.9, False)

        for oc in range(DCH):
            def ld_w2b(e, oc=oc):
                return dma(e, p, "sync",
                           w2blk[:, :].rearrange("q (ic w) -> q ic w", ic=HCH),
                           w2q_all.rearrange("(ic q) o -> q ic o", q=128)[
                               :, :, oc * 128:(oc + 1) * 128])
            st.stage(sync=ld_w2b)

            def t_o(t, oc=oc):
                for ic in range(HCH):
                    t.matmul(ps_x[:, 0:TPC],
                             w2blk[:, ic * 128:ic * 128 + 128],
                             hq[:, ic * TPC:(ic + 1) * TPC],
                             start=(ic == 0), stop=(ic == HCH - 1))

            def s_o(s, oc=oc):
                s.activation(out=obuf[:, oc * TPC:(oc + 1) * TPC],
                             in_=ps_x[:, 0:TPC], func=AF.Identity,
                             bias=b2q[:, oc:oc + 1])
            st.stage(tensor=t_o)
            st.stage(scalar=s_o)

        for quarter in range(4):
            sl = obuf[:, quarter * 2 * TPC:(quarter + 1) * 2 * TPC]
            emit_quant(st, Q, sl, sl, 2 * TPC, 10, -14, 15.9, True)

        def v_res(v):
            for ch in range(DCH):
                sl = sbx[:, ch * TPC:(ch + 1) * TPC]
                v.tensor_tensor(out=sl, in0=sl, in1=obuf[:, ch * TPC:(ch + 1) * TPC],
                                op=OP.add)
        st.stage(vector=v_res)

        def st_b(e, b=b):
            return dma(e, p, "sync",
                       xo_d[b], sbx[:, :])
        st.stage(sync=st_b)




def build_l2ffn():
    p = Prog("l2ffn")
    p.pf = "f2_"
    xs_d = p.inp("xs", (B, 128, DCH * TPC))
    xo_d = p.outp("xo", (B, 128, DCH * TPC))
    p.start_block()
    st = Stager(p)
    emit_l2ffn(p, st, xs_d, xo_d)
    st.stage(sync=lambda e: None)
    return p

def maps_l2ffn(I):
    w2T = _t(I["L2_ffn_w2"])
    base = {
        "w1T": _t(I["L2_ffn_w1"]),
        "b1": _col(I["L2_ffn_b1"], 32),
        "b2": _col(I["L2_ffn_b2"]),
    }
    in_maps = []
    for c in range(NCORES):
        m = dict(base)
        m["w2slice"] = np.ascontiguousarray(w2T[c * 512:(c + 1) * 512])
        in_maps.append({"f2_" + k: v for k, v in m.items()})
    return in_maps


def run_l2ffn(stream_shards, I):
    p = build_l2ffn()
    in_maps = maps_l2ffn(I)
    for c in range(NCORES):
        in_maps[c]["f2_xs"] = _to_qmaj(stream_shards[c])
    res = run_bass_kernel_spmd(p.nc, in_maps, core_ids=list(range(NCORES)))
    out = [_from_qmaj(res.results[c]["f2_xo"]) for c in range(NCORES)]
    p.close()
    return out




def emit_l3ffn(p, st, xs_d, xo_d):
    p.pf = "f3_"
    HCH = 32
    wexpTs_d = p.inp("wexpTs", (4096, 128))
    bexp_d = p.inp("bexpc", (128, HCH))
    wcons_d = p.inp("wcons", (4096, 128))
    stU_d = p.inp("stU", (7, 32, 128, 128))
    stD_d = p.inp("stD", (7, 32, 128, 128))
    wcU_d = p.inp("wcU", (5, 128, 64))
    wcD_d = p.inp("wcD", (5, 128, 64))
    upb_d = p.inp("upb", (128, HCH))
    bsum_d = p.inp("bsumc", (128, DCH))
    dnb_d = p.inp("dnbc", (128, DCH))

    w1_loc = p.dram("w1_loc", (128, 4096), BF16)
    w1_all = p.dram("w1_all", (NCORES, 128, 4096), BF16)
    w2_loc = p.dram("w2_loc", (4096, 128), BF16)
    w2_all = p.dram("w2_all", (NCORES, 4096, 128), BF16)

    sbx = p.sb("sbx", (128, DCH * TPC), F32)
    xn = p.sb("xn", (128, DCH * TPC), BF16)
    w1sb = p.sb("w1sb", (128, DCH * 4096), BF16)
    h2 = p.sb("h2", (128, HCH * TPC), BF16)
    w2blk = p.sb("w2blk", (128, HCH * 128), BF16)
    wt = p.sb("wt", (128, 32 * 132), BF16)
    wt2 = p.sb("wt2", (128, 32 * 132), BF16)
    stile = p.sb("stile", (128, 4 * 128), BF16)
    wcol = p.sb("wcol", (128, 64), F32)
    tmp1 = p.sb("tmp1", (128, 132), F32)
    srow = p.sb("srow", (1, 4 * TPC), F32)
    lnt = p.sb("lnt", (128, 2 * TPC), F32)
    cone = p.sb("cone", (128, 1), BF16)
    onesrow = p.sb("onesrow", (1, 128), F32)
    eps_col = p.sb("eps_col", (128, 1), F32)
    idt = p.sb("idt", (128, 128), BF16)
    iw = p.sb("iw", (128, 128), I32)
    b1c = p.sb("b1c", (128, HCH), F32)
    upbc = p.sb("upbc", (128, HCH), F32)
    boc = p.sb("boc", (128, DCH), F32)
    dnbc = p.sb("dnbc_s", (128, DCH), F32)

    ps_s = [p.ps("ps_s%d" % i) for i in range(4)]
    ps_x = p.ps("ps_x")
    ps_t = p.ps("ps_t", (128, 128), BF16)
    env = dict(stream=None, xn=xn, sq=None, sqw=h2, srow=srow, lnt=lnt,
               cone=cone, onesrow=onesrow, ps_stat=ps_s[0], ps_b0=ps_s[1],
               ps_b1=ps_s[2], eps_col=eps_col)


    def ld0(e):
        n = dma(e, p, "sync", upbc[:, :], upb_d[:, :])
        n += dma(e, p, "sync", boc[:, :], bsum_d[:, :])
        n += dma(e, p, "sync", dnbc[:, :], dnb_d[:, :])
        n += dma(e, p, "sync", b1c[:, :], bexp_d[:, :])
        return n

    def ld0g(e):
        n = dma(e, p, "gpsimd",
                wt[:, :].rearrange("q (jb s) -> q jb s", s=132)[:, :, 0:128],
                wexpTs_d.rearrange("(jb q) d -> q jb d", q=128))
        return n

    def iv(v):
        v.memset(cone[:, :], 1.0 / 1024.0)
        v.memset(onesrow[:, :], 1.0)
        v.memset(eps_col[:, :], 1e-5)
        v.memset(wt[:, :].rearrange("q (jb s) -> q jb s", s=132)[:, :, 128:132], 0.0)
        v.drain()
        for jb in range(HCH):
            v.tensor_copy(out=wt[:, jb * 132 + 128:jb * 132 + 129],
                          in_=b1c[:, jb:jb + 1])
        v.drain()
        v.tensor_tensor(out=boc[:, :], in0=boc[:, :], in1=dnbc[:, :], op=OP.add)
    st.stage(sync=ld0, gpsimd=ld0g)
    st.stage(vector=iv)

    def g_iota(g):
        g.iota(iw[:, :], pattern=[[1, 128]], base=0, channel_multiplier=-1)
    st.stage(gpsimd=g_iota)

    def v_idt(v):
        v.tensor_scalar(out=idt[:, :], in0=iw[:, :], scalar1=0.0, scalar2=None,
                        op0=OP.is_equal, op1=OP.bypass)
    st.stage(vector=v_idt)

    def fold(std_d, wc_d, a_order, b_first, b_trans, width):
        bufs = [wt, wt2]
        cur = 0

        def tile(bufi, jb):
            return bufs[bufi][:, jb * 132:jb * 132 + width]

        def do_bstages():
            nonlocal cur
            for si, s in enumerate((range(4, -1, -1) if b_trans else range(5))):
                sb = s
                def ld_wc(e, sb=sb):
                    return dma(e, p, "sync", wcol[:, :], wc_d[sb])
                st.stage(sync=ld_wc)

                def v_b(v, sb=sb, cur=cur):
                    pairs = [ba for ba in range(32) if not (ba >> sb) & 1]
                    for pi, ba in enumerate(pairs):
                        bb_ = ba + (1 << sb)
                        a_ap = tile(cur, ba)
                        b_ap = tile(cur, bb_)
                        c00 = wcol[:, pi * 4 + 0:pi * 4 + 1]
                        c01 = wcol[:, pi * 4 + 1:pi * 4 + 2]
                        c10 = wcol[:, pi * 4 + 2:pi * 4 + 3]
                        c11 = wcol[:, pi * 4 + 3:pi * 4 + 4]
                        if b_trans:
                            c01, c10 = c10, c01
                        v.tensor_scalar(out=tmp1[:, 0:width], in0=a_ap, scalar1=c00,
                                        scalar2=None, op0=OP.mult, op1=OP.bypass)
                        v.drain()
                        v.scalar_tensor_tensor(out=tile(1 - cur, ba), in0=b_ap,
                                               scalar=c01, in1=tmp1[:, 0:width],
                                               op0=OP.mult, op1=OP.add)
                        v.drain()
                        v.tensor_scalar(out=tmp1[:, 0:width], in0=a_ap, scalar1=c10,
                                        scalar2=None, op0=OP.mult, op1=OP.bypass)
                        v.drain()
                        v.scalar_tensor_tensor(out=tile(1 - cur, bb_), in0=b_ap,
                                               scalar=c11, in1=tmp1[:, 0:width],
                                               op0=OP.mult, op1=OP.add)
                        v.drain()
                st.stage(vector=v_b)
                cur = 1 - cur

        def do_astages():
            nonlocal cur
            for s in a_order:
                def ld_st(e, s=s):
                    n = 0
                    for g4 in range(8):
                        pass
                    return 0
                for g4 in range(8):
                    def ld_s4(e, s=s, g4=g4):
                        return dma(e, p, "gpsimd",
                                   stile[:, :].rearrange("q (k z) -> q k z", k=4),
                                   std_d[s, 4 * g4:4 * g4 + 4].rearrange(
                                       "k q z -> q k z"))
                    st.stage(sync=lambda e: None, gpsimd=ld_s4)

                    def t_a(t, g4=g4, cur=cur):
                        for k in range(4):
                            t.matmul(ps_s[k][:, 0:width],
                                     stile[:, k * 128:(k + 1) * 128],
                                     tile(cur, 4 * g4 + k), start=True, stop=True)
                    st.stage(tensor=t_a)

                    def v_a(v, g4=g4, cur=cur):
                        for k in range(4):
                            v.tensor_copy(out=tile(1 - cur, 4 * g4 + k),
                                          in_=ps_s[k][:, 0:width])
                    st.stage(vector=v_a)
                cur = 1 - cur

        if b_first:
            do_bstages()
            do_astages()
        else:
            do_astages()
            do_bstages()
        return cur

    cur = fold(stU_d, wcU_d, list(range(7)), False, False, 132)

    def v_b1(v, cur=cur):
        for jb in range(HCH):
            v.tensor_copy(out=b1c[:, jb:jb + 1],
                          in_=(wt if cur == 0 else wt2)[:, jb * 132 + 128:jb * 132 + 129])
        v.drain()
        v.tensor_tensor(out=b1c[:, :], in0=b1c[:, :], in1=upbc[:, :], op=OP.add)
    st.stage(vector=v_b1)

    for jb in range(HCH):
        def t_tr(t, jb=jb, cur=cur):
            t.transpose(ps_t[:, 0:128],
                        (wt if cur == 0 else wt2)[:, jb * 132:jb * 132 + 128],
                        idt[:, :])
        st.stage(tensor=t_tr)

        def v_tr(v, jb=jb):
            v.tensor_copy(out=wt2[:, 0:128] if False else stile[:, 0:128],
                          in_=ps_t[:, 0:128])
        st.stage(vector=v_tr)

        def st_w1(e, jb=jb):
            return dma(e, p, "sync", w1_loc[:, jb * 128:(jb + 1) * 128],
                       stile[:, 0:128])
        st.stage(sync=st_w1)

    def g_ag1(g):
        cs = p.sem("cc")
        g.collective_compute("AllGather", OP.bypass,
                             replica_groups=[list(range(NCORES))],
                             ins=[w1_loc[:, :].opt()],
                             outs=[w1_all[:, :, :].opt()]).then_inc(cs[0], 1)
        cs[1] += 1
        g.wait_ge(cs[0], cs[1])
    st.stage(gpsimd=g_ag1)

    def ld_wc2(e):
        return dma(e, p, "gpsimd",
                   wt[:, :].rearrange("q (jb s) -> q jb s", s=132)[:, :, 0:128],
                   wcons_d.rearrange("(jb q) d -> q jb d", q=128))
    st.stage(gpsimd=ld_wc2)
    cur2 = fold(stD_d, wcD_d, list(range(6, -1, -1)), True, True, 128)

    for jb in range(HCH):
        def st_w2(e, jb=jb, cur2=cur2):
            return dma(e, p, "sync", w2_loc[jb * 128:(jb + 1) * 128, :],
                       (wt if cur2 == 0 else wt2)[:, jb * 132:jb * 132 + 128])
        st.stage(sync=st_w2)

    def g_ag2(g):
        cs = p.sem("cc")
        g.collective_compute("AllGather", OP.bypass,
                             replica_groups=[list(range(NCORES))],
                             ins=[w2_loc[:, :].opt()],
                             outs=[w2_all[:, :, :].opt()]).then_inc(cs[0], 1)
        cs[1] += 1
        g.wait_ge(cs[0], cs[1])
    st.stage(gpsimd=g_ag2)

    def ld_w1(e):
        n = 0
        for dc in range(NCORES):
            n += dma(e, p, "sync", w1sb[:, dc * 4096:(dc + 1) * 4096], w1_all[dc])
        return n
    st.stage(sync=ld_w1)

    for b in range(B):
        def ld_b(e, b=b):
            return dma(e, p, "sync",
                       sbx[:, :], xs_d[b])
        st.stage(sync=ld_b)
        emit_layernorm(st, env, 0,
                       x_at=lambda ch: sbx[:, ch * TPC:(ch + 1) * TPC],
                       xn_at=lambda ch: xn[:, ch * TPC:(ch + 1) * TPC])

        for hc in range(HCH):
            def t_h(t, hc=hc):
                for ic in range(DCH):
                    t.matmul(ps_x[:, 0:TPC],
                             w1sb[:, ic * 4096 + hc * 128: ic * 4096 + hc * 128 + 128],
                             xn[:, ic * TPC:(ic + 1) * TPC],
                             start=(ic == 0), stop=(ic == DCH - 1))

            def s_h(s, hc=hc):
                s.activation(out=h2[:, hc * TPC:(hc + 1) * TPC],
                             in_=ps_x[:, 0:TPC], func=AF.Gelu,
                             bias=b1c[:, hc:hc + 1])
            st.stage(tensor=t_h)
            st.stage(scalar=s_h)

        for oc in range(DCH):
            def ld_w2b(e, oc=oc):
                n = 0
                for icq in range(NCORES):
                    pass
                return dma(e, p, "sync",
                           w2blk[:, :].rearrange("q (ic w) -> q ic w", ic=HCH),
                           w2_all[oc].rearrange("(ic q) w -> q ic w", q=128))
            st.stage(sync=ld_w2b)

            def t_o(t, oc=oc):
                for ic in range(HCH):
                    t.matmul(ps_x[:, 0:TPC],
                             w2blk[:, ic * 128:ic * 128 + 128],
                             h2[:, ic * TPC:(ic + 1) * TPC],
                             start=(ic == 0), stop=(ic == HCH - 1))

            def v_o(v, oc=oc):
                sl = sbx[:, oc * TPC:(oc + 1) * TPC]
                v.scalar_tensor_tensor(out=sl, in0=ps_x[:, 0:TPC],
                                       scalar=boc[:, oc:oc + 1], in1=sl,
                                       op0=OP.add, op1=OP.add)
            st.stage(tensor=t_o)
            st.stage(vector=v_o)

        def st_b(e, b=b):
            return dma(e, p, "sync",
                       xo_d[b], sbx[:, :])
        st.stage(sync=st_b)




def build_l3ffn():
    p = Prog("l3ffn")
    p.pf = "f3_"
    xs_d = p.inp("xs", (B, 128, DCH * TPC))
    xo_d = p.outp("xo", (B, 128, DCH * TPC))
    p.start_block()
    st = Stager(p)
    emit_l3ffn(p, st, xs_d, xo_d)
    st.stage(sync=lambda e: None)
    return p

def _l3_stage_tiles(weights, transposed):
    out = np.zeros((7, 32, 128, 128), np.float32)
    w = np.asarray(weights, np.float32)
    for s in range(7):
        sp = 2 ** s
        for blk in range(32):
            for kl in range(64):
                k = blk * 64 + kl
                a = (kl // sp) * 2 * sp + kl % sp
                bcol = a + sp
                w00, w01 = w[s, k, 0, 0], w[s, k, 0, 1]
                w10, w11 = w[s, k, 1, 0], w[s, k, 1, 1]
                out[s, blk, a, a] = w00
                out[s, blk, a, bcol] = w01
                out[s, blk, bcol, a] = w10
                out[s, blk, bcol, bcol] = w11
    if transposed:
        out = np.ascontiguousarray(out.transpose(0, 1, 3, 2))
    return np.ascontiguousarray(out)


def _l3_wcols(weights):
    w = np.asarray(weights, np.float32)
    out = np.zeros((5, 128, 64), np.float32)
    for sb in range(5):
        s = sb + 7
        sp = 2 ** sb
        pairs = [ba for ba in range(32) if not (ba >> sb) & 1]
        for pi, ba in enumerate(pairs):
            kbase = (ba >> (sb + 1)) * 128 * sp + (ba & (sp - 1)) * 128
            for comp in range(4):
                i_, j_ = comp // 2, comp % 2
                out[sb, :, pi * 4 + comp] = w[s, kbase:kbase + 128, i_, j_]
    return np.ascontiguousarray(out)


def maps_l3ffn(I):
    wexpT = _t(np.asarray(I["L3_ffn_W_exp"]))
    wcon = np.ascontiguousarray(np.asarray(I["L3_ffn_W_con"], np.float32))
    base = {
        "stU": _l3_stage_tiles(I["L3_ffn_up_weights"], True),
        "stD": _l3_stage_tiles(I["L3_ffn_down_weights"], False),
        "wcU": _l3_wcols(I["L3_ffn_up_weights"]),
        "wcD": _l3_wcols(I["L3_ffn_down_weights"]),
        "bexpc": _col(I["L3_ffn_b_exp"], 32),
        "upb": _col(I["L3_ffn_up_bias"], 32),
        "bsumc": _col(I["L3_ffn_b_con"]),
        "dnbc": _col(I["L3_ffn_down_bias"]),
    }
    in_maps = []
    for c in range(NCORES):
        m = dict(base)
        m["wexpTs"] = np.ascontiguousarray(wexpT[:, c * 128:(c + 1) * 128])
        m["wcons"] = np.ascontiguousarray(wcon[:, c * 128:(c + 1) * 128])
        in_maps.append({"f3_" + k: v for k, v in m.items()})
    return in_maps


def run_l3ffn(stream_shards, I):
    p = build_l3ffn()
    in_maps = maps_l3ffn(I)
    for c in range(NCORES):
        in_maps[c]["f3_xs"] = _to_qmaj(stream_shards[c])
    res = run_bass_kernel_spmd(p.nc, in_maps, core_ids=list(range(NCORES)))
    out = [_from_qmaj(res.results[c]["f3_xo"]) for c in range(NCORES)]
    p.close()
    return out




EPS_RMS = 1.1920929e-07


def emit_l3attn(p, st, xs_d, xo_d):
    p.pf = "a3_"
    CH2 = 2
    CSZ = 16
    NCH = TPC // CSZ
    NG = B * CH2

    wmT_d = p.inp("wmT", (D, D))
    wpT_d = p.inp("wpT", (512, D))
    vc_d = p.inp("vc", (128, 2 * CH2))
    osc_d = p.inp("osc", (128, DCH))
    ohc_d = p.inp("ohc", (128, NCORES))
    idc_d = p.inp("idc", (128, 1))

    car_loc = p.dram("car_loc", (128, 4 * NG), F32)
    car_all = p.dram("car_all", (NCORES, 128, 4 * NG), F32)

    sbx = p.sb("sbx", (128, DCH * TPC), F32)
    xn = p.sb("xn", (128, DCH * TPC), BF16)
    xr = p.sb("xr", (128, DCH * TPC), BF16)
    wm = p.sb("wm", (128, DCH * D), BF16)
    wp = p.sb("wp", (128, 4 * D), BF16)
    mcall = p.sb("mcall", (128, B * 4 * CH2 * TPC), F32)
    pref = p.sb("pref", (128, NG * 4 * NCH), F32)
    hh = p.sb("hh", (128, 2 * CH2 * TPC), F32)
    hbf = p.sb("hbf", (128, 2 * CH2 * TPC), BF16)
    n2 = p.sb("n2", (128, CH2 * TPC), F32)
    ssc = p.sb("ssc", (128, 16 * NCH), F32)
    car = p.sb("car", (128, 4 * NG), F32)
    carg = p.sb("carg", (128, 4 * NG), F32)
    rrun = p.sb("rrun", (128, 2 * 4 * NG), F32)
    hown = p.sb("hown", (128, 4 * NG), F32)
    gtile = p.sb("gtile", (128, 4 * (NCH + 1)), F32)
    wtl = p.sb("wtl", (128, 6 * NCH), F32)
    ones32 = p.sb("ones32", (128, NCH), F32)
    vcc = p.sb("vcc", (128, 2 * CH2), F32)
    osc = p.sb("osc_s", (128, DCH), F32)
    ohc = p.sb("ohc_s", (128, NCORES), F32)
    idc = p.sb("idc_s", (128, 1), F32)
    sqw = p.sb("sqw", (128, DCH * TPC), BF16)
    srow = p.sb("srow", (1, 4 * TPC), F32)
    lnt = p.sb("lnt", (128, DCH * TPC), F32)
    cone = p.sb("cone", (128, 1), BF16)
    onesrow = p.sb("onesrow", (1, 128), F32)
    eps_col = p.sb("eps_col", (128, 1), F32)
    eps_col2 = p.sb("eps_col2", (128, 1), F32)

    ps_s = [p.ps("ps_s%d" % i) for i in range(4)]
    ps_x = p.ps("ps_x")
    env = dict(stream=None, xn=xn, sq=None, sqw=sqw, srow=srow, lnt=lnt,
               cone=cone, onesrow=onesrow, ps_stat=ps_s[0], ps_b0=ps_s[1],
               ps_b1=ps_s[2], eps_col=eps_col, eps_col2=eps_col2)


    def S(i):
        return ssc[:, i * NCH:(i + 1) * NCH]

    def MC(b, comp, cc):
        o = ((b * 4 + comp) * CH2 + cc) * TPC
        return mcall[:, o:o + TPC]

    def PF(g, comp):
        o = (g * 4 + comp) * NCH
        return pref[:, o:o + NCH]

    def ld0(e):
        n = dma(e, p, "sync", vcc[:, :], vc_d[:, :])
        n += dma(e, p, "sync", osc[:, :], osc_d[:, :])
        n += dma(e, p, "sync", ohc[:, :], ohc_d[:, :])
        n += dma(e, p, "sync", idc[:, :], idc_d[:, :])
        return n

    def ldg(e):
        n = dma(e, p, "gpsimd",
                wm[:, :].rearrange("q (ic o) -> q ic o", ic=DCH),
                wmT_d.rearrange("(ic q) o -> q ic o", q=128))
        n += dma(e, p, "gpsimd",
                 wp[:, :].rearrange("q (ic o) -> q ic o", ic=4),
                 wpT_d.rearrange("(ic q) o -> q ic o", q=128))
        return n

    def iv(v):
        v.memset(cone[:, :], 1.0 / 1024.0)
        v.memset(onesrow[:, :], 1.0)
        v.memset(eps_col[:, :], 1e-5)
        v.memset(eps_col2[:, :], EPS_RMS)
        v.memset(ones32[:, :], 1.0)
    st.stage(sync=ld0, gpsimd=ldg, vector=iv)

    def mm22(v, O, A, Bt, t1, t2):
        for (oo, x1, y1, x2, y2) in [(O[0], A[0], Bt[0], A[1], Bt[2]),
                                     (O[1], A[0], Bt[1], A[1], Bt[3]),
                                     (O[2], A[2], Bt[0], A[3], Bt[2]),
                                     (O[3], A[2], Bt[1], A[3], Bt[3])]:
            v.tensor_tensor(out=t1, in0=x1, in1=y1, op=OP.mult)
            v.tensor_tensor(out=t2, in0=x2, in1=y2, op=OP.mult)
            v.drain()
            v.tensor_tensor(out=oo, in0=t1, in1=t2, op=OP.add)
            v.drain()

    for b in range(B):
        def ld_b(e, b=b):
            return dma(e, p, "sync",
                       sbx[:, :], xs_d[b])
        st.stage(sync=ld_b)
        emit_layernorm(st, env, 0,
                       x_at=lambda ch: sbx[:, ch * TPC:(ch + 1) * TPC],
                       xn_at=lambda ch: xn[:, ch * TPC:(ch + 1) * TPC])
        emit_layernorm(st, env, 0, rms=True, eps_override=EPS_RMS,
                       x_at=lambda ch: xn[:, ch * TPC:(ch + 1) * TPC],
                       xn_at=lambda ch: xr[:, ch * TPC:(ch + 1) * TPC])

        def v_resx(v):
            for ch in range(DCH):
                sl = sbx[:, ch * TPC:(ch + 1) * TPC]
                v.tensor_tensor(out=sl, in0=sl, in1=xn[:, ch * TPC:(ch + 1) * TPC],
                                op=OP.add)
        st.stage(vector=v_resx)

        def st_bx(e, b=b):
            return dma(e, p, "sync",
                       xo_d[b], sbx[:, :])
        st.stage(sync=st_bx)

        for oc in range(DCH):
            def t_p(t, oc=oc):
                for ic in range(DCH):
                    t.matmul(ps_x[:, 0:TPC],
                             wm[:, ic * D + oc * 128: ic * D + oc * 128 + 128],
                             xr[:, ic * TPC:(ic + 1) * TPC],
                             start=(ic == 0), stop=(ic == DCH - 1))

            def v_e(v, oc=oc, b=b):
                comp, cc = oc // 2, oc % 2
                v.tensor_copy(out=MC(b, comp, cc), in_=ps_x[:, 0:TPC])
            st.stage(tensor=t_p)
            st.stage(vector=v_e)

        def s_sq0(s, b=b):
            for cc in range(CH2):
                s.activation(out=n2[:, cc * TPC:(cc + 1) * TPC],
                             in_=MC(b, 0, cc), func=AF.Square)
        st.stage(scalar=s_sq0)
        for comp in range(1, 4):
            def s_sqk(s, comp=comp, b=b):
                for cc in range(CH2):
                    s.activation(out=lnt[:, cc * TPC:(cc + 1) * TPC],
                                 in_=MC(b, comp, cc), func=AF.Square)
            st.stage(scalar=s_sqk)

            def v_acc(v):
                for cc in range(CH2):
                    v.tensor_tensor(out=n2[:, cc * TPC:(cc + 1) * TPC],
                                    in0=n2[:, cc * TPC:(cc + 1) * TPC],
                                    in1=lnt[:, cc * TPC:(cc + 1) * TPC], op=OP.add)
            st.stage(vector=v_acc)

        def v_cl(v):
            for cc in range(CH2):
                v.tensor_scalar(out=n2[:, cc * TPC:(cc + 1) * TPC],
                                in0=n2[:, cc * TPC:(cc + 1) * TPC],
                                scalar1=1e-12, scalar2=None, op0=OP.max,
                                op1=OP.bypass)
        st.stage(vector=v_cl)

        def s_rn(s):
            for cc in range(CH2):
                s.activation(out=n2[:, cc * TPC:(cc + 1) * TPC],
                             in_=n2[:, cc * TPC:(cc + 1) * TPC], func=AF.Ln)
            s.drain()
            for cc in range(CH2):
                s.activation(out=n2[:, cc * TPC:(cc + 1) * TPC],
                             in_=n2[:, cc * TPC:(cc + 1) * TPC], func=AF.Exp,
                             scale=-0.5)
        st.stage(scalar=s_rn)

        def v_nm(v, b=b):
            for comp in range(4):
                for cc in range(CH2):
                    v.tensor_tensor(out=MC(b, comp, cc), in0=MC(b, comp, cc),
                                    in1=n2[:, cc * TPC:(cc + 1) * TPC], op=OP.mult)
                v.drain()
        st.stage(vector=v_nm)

        for cc in range(CH2):
            g = b * CH2 + cc

            def msl(comp, i, b=b, cc=cc):
                return MC(b, comp, cc).rearrange("q (c z) -> q c z", z=CSZ)[
                    :, :, i]

            def v_p12(v, b=b, cc=cc, g=g):
                for comp in range(4):
                    v.tensor_copy(out=S(comp), in_=msl(comp, 0))
                v.drain()
                cur = 0
                for i in range(1, CSZ):
                    co, no = cur * 4, (1 - cur) * 4
                    mm22(v, [S(no + k) for k in range(4)],
                         [msl(k, i) for k in range(4)],
                         [S(co + k) for k in range(4)], S(8), S(9))
                    cur = 1 - cur
                if cur == 1:
                    for comp in range(4):
                        v.tensor_copy(out=S(comp), in_=S(4 + comp))
                    v.drain()
                cur = 0
                for k in range(5):
                    sh = 1 << k
                    co, no = cur * 4, (1 - cur) * 4
                    for comp in range(4):
                        v.tensor_copy(out=S(no + comp)[:, 0:sh],
                                      in_=S(co + comp)[:, 0:sh])
                    v.drain()
                    W = NCH - sh
                    mm22(v, [S(no + j)[:, sh:] for j in range(4)],
                         [S(co + j)[:, sh:] for j in range(4)],
                         [S(co + j)[:, 0:W] for j in range(4)],
                         S(8)[:, 0:W], S(9)[:, 0:W])
                    cur = 1 - cur
                base = cur * 4
                for comp in range(4):
                    v.tensor_copy(out=PF(g, comp), in_=S(base + comp))
                v.drain()
                for comp in range(4):
                    v.tensor_copy(out=car[:, g * 4 + comp:g * 4 + comp + 1],
                                  in_=S(base + comp)[:, NCH - 1:NCH])
            st.stage(vector=v_p12)

    def st_car(e):
        return dma(e, p, "sync", car_loc[:, :], car[:, :])
    st.stage(sync=st_car)

    def g_cc(g):
        cs = p.sem("cc")
        g.collective_compute("AllGather", OP.bypass,
                             replica_groups=[list(range(NCORES))],
                             ins=[car_loc[:, :].opt()],
                             outs=[car_all[:, :, :].opt()]).then_inc(cs[0], 1)
        cs[1] += 1
        g.wait_ge(cs[0], cs[1])
    st.stage(gpsimd=g_cc)

    def v_hinit(v):
        v.memset(hown[:, :], 0.0)
        v.drain()
        for g_ in range(NG):
            v.tensor_scalar(out=hown[:, g_ * 4 + 0:g_ * 4 + 1], in0=idc[:, 0:1],
                            scalar1=1.0, scalar2=None, op0=OP.mult, op1=OP.bypass)
            v.tensor_scalar(out=hown[:, g_ * 4 + 3:g_ * 4 + 4], in0=idc[:, 0:1],
                            scalar1=1.0, scalar2=None, op0=OP.mult, op1=OP.bypass)
    st.stage(vector=v_hinit)

    for j in range(NCORES - 1):
        def ld_cj(e, j=j):
            return dma(e, p, "sync", carg[:, :], car_all[j])
        st.stage(sync=ld_cj)

        def v_comb(v, j=j):
            co = (j % 2) * 4 * NG
            no = ((j + 1) % 2) * 4 * NG
            if j == 0:
                v.tensor_copy(out=rrun[:, no:no + 4 * NG], in_=carg[:, :])
                v.drain()
            else:
                for g_ in range(NG):
                    A = [carg[:, g_ * 4 + k:g_ * 4 + k + 1] for k in range(4)]
                    Bt = [rrun[:, co + g_ * 4 + k:co + g_ * 4 + k + 1]
                          for k in range(4)]
                    O = [rrun[:, no + g_ * 4 + k:no + g_ * 4 + k + 1]
                         for k in range(4)]
                    mm22(v, O, A, Bt, S(8)[:, 0:1], S(9)[:, 0:1])
            v.drain()
            for k in range(4 * NG):
                pass
            v.scalar_tensor_tensor(out=hown[:, :], in0=rrun[:, no:no + 4 * NG],
                                   scalar=ohc[:, j:j + 1], in1=hown[:, :],
                                   op0=OP.mult, op1=OP.add)
        st.stage(vector=v_comb)

    for b in range(B):
        def ld_b2(e, b=b):
            return dma(e, p, "sync",
                       sbx[:, :], xo_d[b])
        st.stage(sync=ld_b2)

        for cc in range(CH2):
            g = b * CH2 + cc

            def hcol(comp, g=g):
                return hown[:, g * 4 + comp:g * 4 + comp + 1]

            def v_G(v, g=g, cc=cc):
                for (comp, x1, h1, x2, h2) in [(0, 0, 0, 1, 2), (1, 0, 1, 1, 3),
                                               (2, 2, 0, 3, 2), (3, 2, 1, 3, 3)]:
                    v.tensor_scalar(out=S(8), in0=PF(g, x1), scalar1=hcol(h1),
                                    scalar2=None, op0=OP.mult, op1=OP.bypass)
                    v.drain()
                    v.scalar_tensor_tensor(out=gtile[:, comp * 33 + 1:comp * 33 + 33],
                                           in0=PF(g, x2), scalar=hcol(h2),
                                           in1=S(8), op0=OP.mult, op1=OP.add)
                    v.drain()
                    v.tensor_scalar(out=gtile[:, comp * 33:comp * 33 + 1],
                                    in0=hcol(comp), scalar1=1.0, scalar2=None,
                                    op0=OP.mult, op1=OP.bypass)
                v.drain()
                v1c = vcc[:, cc:cc + 1]
                v2c = vcc[:, CH2 + cc:CH2 + cc + 1]
                v.tensor_scalar(out=S(8), in0=gtile[:, 0:NCH], scalar1=v1c,
                                scalar2=None, op0=OP.mult, op1=OP.bypass)
                v.drain()
                v.scalar_tensor_tensor(out=wtl[:, 0:NCH], in0=gtile[:, 33:33 + NCH],
                                       scalar=v2c, in1=S(8), op0=OP.mult, op1=OP.add)
                v.drain()
                v.tensor_scalar(out=S(8), in0=gtile[:, 66:66 + NCH], scalar1=v1c,
                                scalar2=None, op0=OP.mult, op1=OP.bypass)
                v.drain()
                v.scalar_tensor_tensor(out=wtl[:, NCH:2 * NCH],
                                       in0=gtile[:, 99:99 + NCH],
                                       scalar=v2c, in1=S(8), op0=OP.mult, op1=OP.add)
                v.drain()
            st.stage(vector=v_G)

            def v_inner(v, b=b, cc=cc):
                def msl(comp, i):
                    return MC(b, comp, cc).rearrange("q (c z) -> q c z", z=CSZ)[
                        :, :, i]

                def hsl(h, i):
                    return hh[:, (h * CH2 + cc) * TPC:(h * CH2 + cc) * TPC + TPC]\
                        .rearrange("q (c z) -> q c z", z=CSZ)[:, :, i]
                cur = 0
                for i in range(CSZ):
                    co, no = cur * 2, 2 + (1 - cur) * 2 - 2
                    w1 = wtl[:, cur * 2 * NCH:cur * 2 * NCH + NCH]
                    w2 = wtl[:, cur * 2 * NCH + NCH:cur * 2 * NCH + 2 * NCH]
                    nw1 = wtl[:, (1 - cur) * 2 * NCH:(1 - cur) * 2 * NCH + NCH]
                    nw2 = wtl[:, (1 - cur) * 2 * NCH + NCH:
                              (1 - cur) * 2 * NCH + 2 * NCH]
                    v.tensor_tensor(out=S(8), in0=msl(0, i), in1=w1, op=OP.mult)
                    v.tensor_tensor(out=S(9), in0=msl(1, i), in1=w2, op=OP.mult)
                    v.drain()
                    v.tensor_tensor(out=nw1, in0=S(8), in1=S(9), op=OP.add)
                    v.tensor_tensor(out=S(10), in0=msl(2, i), in1=w1, op=OP.mult)
                    v.tensor_tensor(out=S(11), in0=msl(3, i), in1=w2, op=OP.mult)
                    v.drain()
                    v.tensor_tensor(out=nw2, in0=S(10), in1=S(11), op=OP.add)
                    v.drain()
                    v.tensor_copy(out=hsl(0, i), in_=nw1)
                    v.tensor_copy(out=hsl(1, i), in_=nw2)
                    v.drain()
                    cur = 1 - cur
            st.stage(vector=v_inner)

        def v_hc(v):
            for i in range(2 * CH2):
                v.tensor_copy(out=hbf[:, i * TPC:(i + 1) * TPC],
                              in_=hh[:, i * TPC:(i + 1) * TPC])
        st.stage(vector=v_hc)

        for oc in range(DCH):
            def t_o(t, oc=oc):
                for ic in range(4):
                    t.matmul(ps_x[:, 0:TPC],
                             wp[:, ic * D + oc * 128: ic * D + oc * 128 + 128],
                             hbf[:, ic * TPC:(ic + 1) * TPC],
                             start=(ic == 0), stop=(ic == 3))

            def v_o(v, oc=oc):
                sl = sbx[:, oc * TPC:(oc + 1) * TPC]
                v.scalar_tensor_tensor(out=sl, in0=ps_x[:, 0:TPC],
                                       scalar=osc[:, oc:oc + 1], in1=sl,
                                       op0=OP.mult, op1=OP.add)
            st.stage(tensor=t_o)
            st.stage(vector=v_o)

        def st_b2(e, b=b):
            return dma(e, p, "sync",
                       xo_d[b], sbx[:, :])
        st.stage(sync=st_b2)




def build_l3attn():
    p = Prog("l3attn")
    p.pf = "a3_"
    xs_d = p.inp("xs", (B, 128, DCH * TPC))
    xo_d = p.outp("xo", (B, 128, DCH * TPC))
    p.start_block()
    st = Stager(p)
    emit_l3attn(p, st, xs_d, xo_d)
    st.stage(sync=lambda e: None)
    return p

def maps_l3attn(I):
    CH2 = 2
    wm = np.asarray(I["L3_attn_W_m"], np.float32)
    wmT = _t(wm)
    idx = np.concatenate([np.arange(256) * 4 + comp for comp in range(4)])
    wmTp = np.ascontiguousarray(wmT[:, idx])
    wp = np.asarray(I["L3_attn_W_proj"], np.float32)
    wpT = _t(wp)
    ridx = np.concatenate([np.arange(256) * 2 + h for h in range(2)])
    wpTp = np.ascontiguousarray(wpT[ridx, :])
    v_vec = np.asarray(I["L3_attn_v_vec"], np.float32)[:, :, 0]
    vc = np.zeros((128, 2 * CH2), np.float32)
    for cc in range(CH2):
        vc[:, cc] = v_vec[cc * 128:(cc + 1) * 128, 0]
        vc[:, CH2 + cc] = v_vec[cc * 128:(cc + 1) * 128, 1]
    base = {
        "wmT": wmTp,
        "wpT": wpTp,
        "vc": vc,
        "osc": _col(I["L3_attn_out_scale"]),
    }
    in_maps = []
    for c in range(NCORES):
        m = dict(base)
        oh = np.zeros((128, NCORES), np.float32)
        if c > 0:
            oh[:, c - 1] = 1.0
        m["ohc"] = oh
        m["idc"] = np.full((128, 1), 1.0 if c == 0 else 0.0, np.float32)
        in_maps.append({"a3_" + k: v for k, v in m.items()})
    return in_maps


def run_l3attn(stream_shards, I):
    p = build_l3attn()
    in_maps = maps_l3attn(I)
    for c in range(NCORES):
        in_maps[c]["a3_xs"] = _to_qmaj(stream_shards[c])
    res = run_bass_kernel_spmd(p.nc, in_maps, core_ids=list(range(NCORES)))
    out = [_from_qmaj(res.results[c]["a3_xo"]) for c in range(NCORES)]
    p.close()
    return out




DI = 2048
NHM = 32
HDM = 64
NST = 128
CVD = 2304
DIP = 4384
CCH = 18
MCH = 128
NMC = TPC // MCH


def emit_l0attn(p, st, us_d, xo_d):
    p.pf = "a0_"
    ipT_d = p.inp("ipT", (D, DIP))
    cvw_d = p.inp("cvw", (128, CCH * 4))
    cvb_d = p.inp("cvb", (128, CCH))
    dtb_d = p.inp("dtb", (128, 1))
    alog_d = p.inp("alog", (3, 32))
    dcol_d = p.inp("dcol", (128, 16))
    nw_d = p.inp("nw", (128, 16))
    opT_d = p.inp("opT", (DI, D))
    ohc_d = p.inp("ohc", (128, NCORES))

    op_dram = p.dram("op_dram", (DCH, 128, 16 * 128), BF16)
    ip_bf = p.dram("ip_bf", (128, DCH, DIP), BF16)
    yz_dram = p.dram("yz_dram", (B, 2, 128, 16 * TPC), BF16)
    acg_dram = p.dram("acg_dram", (B, NMC, 32, MCH), F32)
    acl_dram = p.dram("acl_dram", (B, NMC, 32, MCH), F32)
    h_loc = p.dram("h_loc", (B, 128, DI), BF16)
    h_all = p.dram("h_all", (NCORES, B, 128, DI), BF16)
    dk_loc = p.dram("dk_loc", (NHM, B), F32)
    dk_all = p.dram("dk_all", (NCORES, NHM, B), F32)

    HW = 1536
    sbx = p.sb("sbx", (128, DCH * TPC), F32)
    xn = p.sb("xn", (128, DCH * (TPC + 3)), BF16)
    ipw = p.sb("ipw", (128, DCH * HW), BF16)
    opb = p.sb("opb", (128, 16 * 128), BF16)
    cvin = p.sb("cvin", (128, CCH * (TPC + 3)), BF16)
    xfm = p.sb("xfm", (128, 16 * TPC), BF16)
    xtm = p.sb("xtm", (128, 16 * TPC), BF16)
    bct = p.sb("bct", (128, 2 * TPC), BF16)
    call = p.sb("call", (128, B * TPC), BF16)
    ysb = cvin
    gsb = ysb
    zs3 = xfm
    hst = p.sb("hst", (128, DI), F32)
    hnw = p.sb("hnw", (128, DI), F32)
    hjb = p.sb("hjb", (128, DI), BF16)
    dtr = p.sb("dtr", (128, TPC), F32)
    dts = p.sb("dts", (128, 3 * TPC), F32)
    acg = p.sb("acg", (128, TPC), F32)
    offc = p.sb("offc", (128, 2), F32)
    acT = p.sb("acT", (128, NHM), F32)
    dtT = p.sb("dtT", (128, NHM), F32)
    wvt = p.sb("wvt", (128, 3 * NHM), F32)
    rowA = p.sb("rowA", (1, NHM * MCH + 256), F32)
    rowE = p.sb("rowE", (1, NHM * MCH), BF16)
    onesbf = p.sb("onesbf", (1, 128), BF16)
    tri = p.sb("tri", (128, 128), F32)
    iw = p.sb("iw", (128, 128), I32)
    idt = p.sb("idt", (128, 128), BF16)
    idt32 = p.sb("idt32m", (128, 128), F32)
    gwk = p.sb("gwk", (128, 8 * MCH), F32)
    gbf = p.sb("gbf", (128, 2 * MCH), BF16)
    xw = p.sb("xw", (128, DI), BF16)
    dec = p.sb("dec", (128, NHM), F32)
    dkb = p.sb("dkb", (32, B), F32)
    drow = p.sb("drow", (1, NCORES * NHM * B), F32)
    acol = p.sb("acol", (128, 4), F32)
    srow = p.sb("srow", (1, 4 * TPC), F32)
    lnt = p.sb("lnt", (128, 2 * TPC), F32)
    cone = p.sb("cone", (128, 1), BF16)
    c2 = p.sb("c2", (128, 1), BF16)
    onesrow = p.sb("onesrow", (1, 128), F32)
    eps_col = p.sb("eps_col", (128, 1), F32)
    eps_col2 = p.sb("eps_col2", (128, 1), F32)
    cvwc = p.sb("cvwc", (128, CCH * 4), F32)
    cvbc = p.sb("cvbc", (128, CCH), F32)
    dtbc = p.sb("dtbc", (128, 1), F32)
    dcc = p.sb("dcc", (128, 16), F32)
    nwc = p.sb("nwc", (128, 16), F32)
    ohc = p.sb("ohc_s", (128, NCORES), F32)
    alg = p.sb("alg", (3, 32), F32)
    onec = p.sb("onec", (128, 1), F32)

    ps_s = [p.ps("ps_s%d" % i) for i in range(4)]
    ps_x = p.ps("ps_x")
    ps_g = p.ps("ps_g")
    ps_t = p.ps("ps_t", (128, 128), BF16)
    ps_t2 = p.ps("ps_t2", (128, 128), BF16)
    sqw = xtm
    env = dict(stream=None, xn=xn, sq=None, sqw=sqw, srow=srow, lnt=lnt,
               cone=cone, cone2=c2, onesrow=onesrow, ps_stat=ps_s[0],
               ps_b0=ps_s[1], ps_b1=ps_s[2], eps_col=eps_col, eps_col2=eps_col2)


    TP3 = TPC + 3

    def ld0(e):
        n = dma(e, p, "sync", cvwc[:, :], cvw_d[:, :])
        n += dma(e, p, "sync", cvbc[:, :], cvb_d[:, :])
        n += dma(e, p, "sync", dtbc[:, :], dtb_d[:, :])
        n += dma(e, p, "sync", dcc[:, :], dcol_d[:, :])
        n += dma(e, p, "sync", nwc[:, :], nw_d[:, :])
        n += dma(e, p, "sync", ohc[:, :], ohc_d[:, :])
        n += dma(e, p, "sync", alg[:, :], alog_d[:, :])
        return n

    def iv(v):
        v.memset(cone[:, :], 1.0 / 1024.0)
        v.memset(c2[:, :], 1.0 / DI)
        v.memset(onesrow[:, :], 1.0)
        v.memset(eps_col[:, :], 1e-5)
        v.memset(eps_col2[:, :], 1e-5)
        v.memset(onec[:, :], 1.0)
        v.memset(rowA[0:1, NHM * MCH:NHM * MCH + 128], 1.0)
        v.memset(rowA[0:1, NHM * MCH + 128:NHM * MCH + 256], -1.0)
        v.memset(onesbf[0:1, :], 1.0)

    def ig(g):
        g.iota(iw[:, :], pattern=[[1, 128]], base=0, channel_multiplier=-1)
    st.stage(sync=ld0, vector=iv, gpsimd=ig)

    def v_tri(v):
        v.tensor_scalar(out=tri[:, :], in0=iw[:, :], scalar1=0.0, scalar2=None,
                        op0=OP.is_ge, op1=OP.bypass)
        v.drain()
        v.tensor_scalar(out=tri[:, :], in0=tri[:, :], scalar1=-1.0, scalar2=1e30,
                        op0=OP.add, op1=OP.mult)
        v.drain()
        v.tensor_scalar(out=idt[:, :], in0=iw[:, :], scalar1=0.0, scalar2=None,
                        op0=OP.is_equal, op1=OP.bypass)
        v.tensor_scalar(out=idt32[:, :], in0=iw[:, :], scalar1=0.0, scalar2=None,
                        op0=OP.is_equal, op1=OP.bypass)
    st.stage(vector=v_tri)

    def s_a(s):
        s.activation(out=alg[:, :], in_=alg[:, :], func=AF.Exp)
    st.stage(scalar=s_a)

    def v_a0(v):
        v.memset(acol[:, 1:2], -1.0 / 3.0)
    st.stage(vector=v_a0)

    def t_a(t):
        t.matmul(ps_x[0:32, 0:1], alg[0:3, :], acol[0:3, 1:2], start=True, stop=True)
    st.stage(tensor=t_a)

    def v_a1(v):
        v.tensor_copy(out=acol[0:32, 0:1], in_=ps_x[0:32, 0:1])
    st.stage(vector=v_a1)

    HH = DIP // 4
    ipchunks = [(ic, hf) for ic in range(DCH) for hf in range(4)]
    for k in range(len(ipchunks) + 2):
        em = {}

        def sy_ip(e, k=k):
            n = 0
            if k < len(ipchunks):
                ic, hf = ipchunks[k]
                n += dma(e, p, "sync", sbx[:, (k % 2) * HH:(k % 2) * HH + HH],
                         ipT_d[ic * 128:(ic + 1) * 128, hf * HH:(hf + 1) * HH])
            if k >= 2:
                ic, hf = ipchunks[k - 2]
                n += dma(e, p, "sync", ip_bf[:, ic, hf * HH:(hf + 1) * HH],
                         xn[:, (k % 2) * HH:(k % 2) * HH + HH])
            return n
        em["sync"] = sy_ip
        if 1 <= k <= len(ipchunks):
            def s_ip(s, k=k):
                s.activation(out=xn[:, ((k - 1) % 2) * HH:((k - 1) % 2) * HH + HH],
                             in_=sbx[:, ((k - 1) % 2) * HH:((k - 1) % 2) * HH + HH],
                             func=AF.Identity)
            em["scalar"] = s_ip
        st.stage(**em)

    for k in range(18):
        em = {}

        def sy_op(e, k=k):
            n = 0
            if k < 16:
                n += dma(e, p, "sync", sbx[:, (k % 2) * 1024:(k % 2) * 1024 + D],
                         opT_d[k * 128:(k + 1) * 128, :])
            if k >= 2:
                ic = k - 2
                n += dma(e, p, "sync",
                         op_dram.rearrange("o q i -> q o i")[
                             :, :, ic * 128:(ic + 1) * 128],
                         opb[:, (ic % 2) * 1024:(ic % 2) * 1024 + D]
                         .rearrange("q (o w) -> q o w", o=DCH))
            return n
        em["sync"] = sy_op
        if 1 <= k <= 16:
            ic = k - 1

            def s_op(s, ic=ic):
                s.activation(out=opb[:, (ic % 2) * 1024:(ic % 2) * 1024 + D],
                             in_=sbx[:, (ic % 2) * 1024:(ic % 2) * 1024 + D],
                             func=AF.Identity)
            em["scalar"] = s_op
        st.stage(**em)

    for b in range(B):
        def ld_b(e, b=b):
            return dma(e, p, "sync",
                       sbx[:, :].rearrange("q (c t) -> q c t", c=DCH),
                       us_d[b].rearrange("q (c t) -> q c t", t=TP3)[:, :, 3:TP3])
        st.stage(sync=ld_b)
        emit_layernorm(st, env, 0,
                       x_at=lambda ch: sbx[:, ch * TPC:(ch + 1) * TPC],
                       xn_at=lambda ch: xn[:, ch * TP3:ch * TP3 + TPC])
        def ld_b3(e, b=b):
            return dma(e, p, "sync",
                       lnt[:, 0:DCH * 3].rearrange("q (c t) -> q c t", c=DCH),
                       us_d[b].rearrange("q (c t) -> q c t", t=TP3)[:, :, 0:3])
        st.stage(sync=ld_b3)
        emit_layernorm(st, env, 0, T=3,
                       x_at=lambda ch: lnt[:, ch * 3:(ch + 1) * 3],
                       xn_at=lambda ch: xn[:, ch * TP3 + TPC:ch * TP3 + TPC + 3])

        def xnat(ic):
            return xn[:, ic * TP3:ic * TP3 + TPC]

        def xnat3(ic):
            return xn[:, ic * TP3 + TPC:ic * TP3 + TPC + 3]

        zst = bct
        for third in range(3):
            base_oc = third * 12
            ocs = list(range(base_oc, min(base_oc + 12, 35)))

            def ld_ip(e, third=third):
                lo = third * HW
                w = min(HW, DIP - lo)
                return dma(e, p, "sync",
                           ipw[:, :].rearrange("q (ic o) -> q ic o", o=HW)[
                               :, :, 0:w],
                           ip_bf[:, :, lo:lo + w])
            st.stage(sync=ld_ip)

            n_oc = len(ocs)
            for k in range(n_oc + 2):
                em = {}
                if k < n_oc:
                    oc = ocs[k]
                    po = 128 if oc < 34 else 32
                    ocl = oc - base_oc
                    pm = ps_x if k % 2 == 0 else ps_s[0]
                    pl = ps_s[3] if k % 2 == 0 else ps_s[1]

                    def t_p(t, oc=oc, po=po, ocl=ocl, pm=pm, pl=pl):
                        for ic in range(DCH):
                            t.matmul(pm[0:po, 0:TPC],
                                     ipw[:, ic * HW + ocl * 128:
                                         ic * HW + ocl * 128 + po],
                                     xnat(ic), start=(ic == 0),
                                     stop=(ic == DCH - 1))
                        if 16 <= oc < 34:
                            for ic in range(DCH):
                                t.matmul(pl[0:po, 0:3],
                                         ipw[:, ic * HW + ocl * 128:
                                             ic * HW + ocl * 128 + po],
                                         xnat3(ic), start=(ic == 0),
                                         stop=(ic == DCH - 1))
                    em["tensor"] = t_p
                if 1 <= k <= n_oc:
                    oc = ocs[k - 1]
                    pm = ps_x if (k - 1) % 2 == 0 else ps_s[0]
                    pl = ps_s[3] if (k - 1) % 2 == 0 else ps_s[1]

                    def v_e(v, oc=oc, pm=pm, pl=pl, k=k):
                        if oc < 16:
                            v.tensor_copy(out=zst[:, ((k - 1) % 2) * TPC:
                                                  ((k - 1) % 2) * TPC + TPC],
                                          in_=pm[:, 0:TPC])
                        elif oc < 34:
                            cc = oc - 16
                            v.tensor_copy(out=cvin[:, cc * TP3 + 3:
                                                   cc * TP3 + 3 + TPC],
                                          in_=pm[:, 0:TPC])
                            v.tensor_copy(out=cvin[:, cc * TP3:cc * TP3 + 3],
                                          in_=pl[:, 0:3])
                        else:
                            v.tensor_copy(out=dtr[0:32, 0:TPC],
                                          in_=pm[0:32, 0:TPC])
                    em["vector"] = v_e
                if 2 <= k and k - 2 < n_oc and ocs[k - 2] < 16:
                    oc2 = ocs[k - 2]

                    def st_z(e, oc2=oc2, k=k, b=b):
                        return dma(e, p, "sync",
                                   yz_dram[b, 1][:, oc2 * TPC:(oc2 + 1) * TPC],
                                   zst[:, ((k - 2) % 2) * TPC:
                                       ((k - 2) % 2) * TPC + TPC])
                    em["sync"] = st_z
                st.stage(**em)

        def s_sp(s):
            s.activation(out=dts[0:32, 0:TPC], in_=dtr[0:32, 0:TPC], func=AF.Abs,
                         bias=dtbc[0:32, 0:1])
            s.activation(out=dts[0:32, TPC:2 * TPC], in_=dtr[0:32, 0:TPC],
                         func=AF.Relu, bias=dtbc[0:32, 0:1])
            s.drain()
            s.activation(out=dts[0:32, 0:TPC], in_=dts[0:32, 0:TPC], func=AF.Exp,
                         scale=-1.0)
            s.drain()
            s.activation(out=dts[0:32, 0:TPC], in_=dts[0:32, 0:TPC], func=AF.Ln,
                         bias=onec[0:32, 0:1])
        st.stage(scalar=s_sp)

        def v_sp(v):
            v.tensor_tensor(out=dts[0:32, 0:TPC], in0=dts[0:32, 0:TPC],
                            in1=dts[0:32, TPC:2 * TPC], op=OP.add)
            v.drain()
            v.tensor_scalar(out=dts[0:32, TPC:2 * TPC], in0=dts[0:32, 0:TPC],
                            scalar1=acol[0:32, 0:1], scalar2=None,
                            op0=OP.mult, op1=OP.bypass)
            v.drain()
            v.memset(offc[0:32, 0:1], 0.0)
        st.stage(vector=v_sp)

        def v_ac(v, b=b):
            for c_ in range(NMC):
                v.tensor_tensor_scan(out=dts[0:32, 2 * TPC + c_ * MCH:
                                             2 * TPC + (c_ + 1) * MCH],
                                     data0=dts[0:32, TPC + c_ * MCH:
                                               TPC + (c_ + 1) * MCH],
                                     data1=dts[0:32, TPC + c_ * MCH:
                                               TPC + (c_ + 1) * MCH],
                                     initial=0.0, op0=OP.add, op1=OP.bypass)
            v.drain()
            for c_ in range(NMC):
                v.tensor_scalar(out=acg[0:32, c_ * MCH:(c_ + 1) * MCH],
                                in0=dts[0:32, 2 * TPC + c_ * MCH:
                                        2 * TPC + (c_ + 1) * MCH],
                                scalar1=offc[0:32, 0:1], scalar2=None,
                                op0=OP.add, op1=OP.bypass)
                v.drain()
                v.tensor_copy(out=offc[0:32, 1:2],
                              in_=acg[0:32, (c_ + 1) * MCH - 1:(c_ + 1) * MCH])
                v.drain()
                v.tensor_copy(out=offc[0:32, 0:1], in_=offc[0:32, 1:2])
                v.drain()
        st.stage(vector=v_ac)

        def s_dk(s, b=b):
            s.activation(out=dkb[0:32, b:b + 1], in_=acg[0:32, TPC - 1:TPC],
                         func=AF.Exp)
        st.stage(scalar=s_dk)

        def st_acg(e, b=b):
            n = dma(e, p, "sync", acg_dram[b].rearrange("c h t -> h c t"),
                    acg[0:32, :].rearrange("h (c t) -> h c t", t=MCH))
            n += dma(e, p, "sync", acl_dram[b].rearrange("c h t -> h c t"),
                     dts[0:32, 2 * TPC:3 * TPC].rearrange("h (c t) -> h c t",
                                                          t=MCH))
            return n
        st.stage(sync=st_acg)

        cbufs = (lnt, sbx)
        for k in range(CCH + 1):
            em = {}
            if k < CCH:
                def v_cv(v, cc=k, buf=cbufs[k % 2]):
                    base = cc * TP3
                    v.tensor_scalar(out=buf[:, 0:TPC],
                                    in0=cvin[:, base:base + TPC],
                                    scalar1=cvwc[:, cc * 4:cc * 4 + 1],
                                    scalar2=None, op0=OP.mult, op1=OP.bypass)
                    v.drain()
                    for j in range(1, 4):
                        v.scalar_tensor_tensor(
                            out=buf[:, TPC:2 * TPC] if j % 2 else buf[:, 0:TPC],
                            in0=cvin[:, base + j:base + j + TPC],
                            scalar=cvwc[:, cc * 4 + j:cc * 4 + j + 1],
                            in1=buf[:, 0:TPC] if j % 2 else buf[:, TPC:2 * TPC],
                            op0=OP.mult, op1=OP.add)
                        v.drain()
                em["vector"] = v_cv
            if k >= 1:
                def s_cv(s, cc=k - 1, buf=cbufs[(k - 1) % 2]):
                    dst = (xfm[:, cc * TPC:(cc + 1) * TPC] if cc < 16 else
                           bct[:, (cc - 16) * TPC:(cc - 15) * TPC])
                    s.activation(out=dst, in_=buf[:, TPC:2 * TPC], func=AF.Silu,
                                 bias=cvbc[:, cc:cc + 1])
                em["scalar"] = s_cv
            st.stage(**em)

        def v_ckeep(v, b=b):
            v.tensor_copy(out=call[:, b * TPC:(b + 1) * TPC],
                          in_=bct[:, TPC:2 * TPC])
        st.stage(vector=v_ckeep)

        pbanks = (ps_t, ps_t2)
        trs = [(cc, tq) for cc in range(16) for tq in range(4)]
        for k in range(len(trs) + 1):
            em = {}
            if k < len(trs):
                cc, tq = trs[k]

                def t_tr(t, cc=cc, tq=tq, bk=pbanks[k % 2]):
                    t.transpose(bk[:, 0:128],
                                xfm[:, cc * TPC + tq * 128:
                                    cc * TPC + tq * 128 + 128],
                                idt[:, :])
                em["tensor"] = t_tr
            if k >= 1:
                cc, tq = trs[k - 1]

                def v_tr(v, cc=cc, tq=tq, bk=pbanks[(k - 1) % 2]):
                    v.tensor_copy(out=xtm[:, tq * 2048 + cc * 128:
                                          tq * 2048 + cc * 128 + 128],
                                  in_=bk[:, 0:128])
                em["vector"] = v_tr
            st.stage(**em)

        def v_h0(v):
            v.memset(hst[:, :], 0.0)
        st.stage(vector=v_h0)

        for c_ in range(NMC):
            def st_r(e, c_=c_, b=b):
                return dma(e, p, "sync", rowA[0:1, 0:NHM * MCH],
                           acl_dram[b, c_].rearrange("h t -> (h t)"))
            st.stage(sync=st_r)

            def s_er(s):
                s.activation(out=rowE[0:1, :], in_=rowA[0:1, 0:NHM * MCH],
                             func=AF.Exp)

            def t_pre(t, c_=c_):
                t.transpose(ps_t[:, 0:128], bct[:, c_ * MCH:(c_ + 1) * MCH],
                            idt[:, :])
                t.matmul(ps_s[3][:, 0:MCH], bct[:, c_ * MCH:(c_ + 1) * MCH],
                         bct[:, TPC + c_ * MCH:TPC + (c_ + 1) * MCH],
                         start=True, stop=True)
                t.transpose(ps_s[0][:, 0:32], dts[0:32, 2 * TPC + c_ * MCH:
                                                  2 * TPC + (c_ + 1) * MCH],
                            idt32[0:32, 0:32])
                t.transpose(ps_s[1][:, 0:32], dts[0:32, c_ * MCH:(c_ + 1) * MCH],
                            idt32[0:32, 0:32])
                t.matmul(ps_s[2][:, 0:32], onesrow[0:1, 0:128],
                         rowA[0:1, 0:NHM * MCH].rearrange(
                             "o (h t) -> o h t", t=MCH)[:, :, MCH - 1],
                         start=True, stop=True)
            st.stage(scalar=s_er, tensor=t_pre)

            def v_pre(v):
                v.tensor_copy(out=gbf[:, MCH:2 * MCH], in_=ps_t[:, 0:128])
                v.tensor_copy(out=acT[:, :], in_=ps_s[0][:, 0:32])
                v.tensor_copy(out=dtT[:, :], in_=ps_s[1][:, 0:32])
                v.drain()
                v.tensor_tensor(out=wvt[:, 0:NHM], in0=ps_s[2][:, 0:32],
                                in1=acT[:, :], op=OP.subtract)
            st.stage(vector=v_pre)

            def s_wv(s):
                s.activation(out=wvt[:, NHM:2 * NHM], in_=wvt[:, 0:NHM],
                             func=AF.Exp)
                s.activation(out=dec[:, :], in_=ps_s[2][:, 0:32], func=AF.Exp)
            st.stage(scalar=s_wv)

            def v_wv2(v):
                v.tensor_tensor(out=wvt[:, 2 * NHM:3 * NHM],
                                in0=wvt[:, NHM:2 * NHM], in1=dtT[:, :], op=OP.mult)
            st.stage(vector=v_wv2)

            for g in range(4):
                hs = g * 8
                W = NHM * MCH

                def t_g1(t, hs=hs):
                    for bank, p4 in ((ps_s[0], 0), (ps_s[1], 1)):
                        h4 = hs + p4 * 4
                        for j in range(4):
                            h = h4 + j
                            t.matmul(bank[:, j * MCH:(j + 1) * MCH],
                                     rowA[0:1, h * MCH:(h + 1) * MCH],
                                     rowA[0:1, W + 128:W + 256],
                                     start=True, stop=False)
                            t.matmul(bank[:, j * MCH:(j + 1) * MCH],
                                     rowA[0:1, W:W + 128],
                                     rowA[0:1, h * MCH:(h + 1) * MCH],
                                     start=False, stop=True)
                    t.matmul(ps_s[2][:, 0:512], onesbf[0:1, 0:128],
                             rowE[0:1, hs * MCH:hs * MCH + 512],
                             start=True, stop=True)
                    t.matmul(ps_g[:, 0:512], onesbf[0:1, 0:128],
                             rowE[0:1, (hs + 4) * MCH:(hs + 4) * MCH + 512],
                             start=True, stop=True)
                st.stage(tensor=t_g1)

                def v_g2(v):
                    for j in range(8):
                        bank = ps_s[0] if j < 4 else ps_s[1]
                        v.tensor_tensor(out=gwk[:, j * MCH:(j + 1) * MCH],
                                        in0=bank[:, (j % 4) * MCH:
                                                 (j % 4 + 1) * MCH],
                                        in1=tri[:, :], op=OP.min)
                st.stage(vector=v_g2)

                def s_g3(s):
                    s.activation(out=sbx[:, 1024:1536], in_=gwk[:, 0:512],
                                 func=AF.Exp)
                    s.activation(out=sbx[:, 1536:2048], in_=gwk[:, 512:1024],
                                 func=AF.Exp)
                st.stage(scalar=s_g3)

                def v_g4(v, hs=hs, c_=c_):
                    for j in range(8):
                        v.tensor_tensor(out=gwk[:, j * MCH:(j + 1) * MCH],
                                        in0=sbx[:, 1024 + j * MCH:
                                                1024 + (j + 1) * MCH],
                                        in1=ps_s[3][:, 0:MCH], op=OP.mult)
                    for j in range(8):
                        pm = ps_s[2] if j < 4 else ps_g
                        v.tensor_tensor(out=sbx[:, j * MCH:(j + 1) * MCH],
                                        in0=bct[:, TPC + c_ * MCH:
                                                TPC + (c_ + 1) * MCH],
                                        in1=pm[:, (j % 4) * MCH:
                                               (j % 4 + 1) * MCH], op=OP.mult)
                    v.drain()
                    for j in range(8):
                        v.tensor_scalar(out=xw[:, j * MCH:(j + 1) * MCH],
                                        in0=gwk[:, j * MCH:(j + 1) * MCH],
                                        scalar1=dtT[:, hs + j:hs + j + 1],
                                        scalar2=None, op0=OP.mult, op1=OP.bypass)
                st.stage(vector=v_g4)

                def t_g5(t, hs=hs, c_=c_):
                    for j in range(8):
                        h = hs + j
                        out = ps_x[(h % 2) * 64:(h % 2) * 64 + 64,
                                   ((h // 2) % 4) * MCH:((h // 2) % 4 + 1) * MCH]
                        t.matmul(out, hst[:, h * 64:(h + 1) * 64],
                                 sbx[:, j * MCH:(j + 1) * MCH],
                                 start=True, stop=False)
                        t.matmul(out, xtm[:, c_ * 2048 + h * 64:
                                          c_ * 2048 + (h + 1) * 64],
                                 xw[:, j * MCH:(j + 1) * MCH],
                                 start=False, stop=True)
                st.stage(tensor=t_g5)

                def v_g6(v, hs=hs, c_=c_):
                    for jp in range(4):
                        hp2 = hs // 2 + jp
                        sl = hp2 * TPC + c_ * MCH
                        v.scalar_tensor_tensor(out=ysb[:, sl:sl + MCH],
                                               in0=xfm[:, sl:sl + MCH],
                                               scalar=dcc[:, hp2:hp2 + 1],
                                               in1=ps_x[:, jp * MCH:
                                                       (jp + 1) * MCH],
                                               op0=OP.mult, op1=OP.add)
                st.stage(vector=v_g6)

            def v_xw(v, c_=c_):
                for h in range(NHM):
                    v.tensor_scalar(out=xw[:, h * 64:(h + 1) * 64],
                                    in0=xtm[:, c_ * 2048 + h * 64:
                                            c_ * 2048 + (h + 1) * 64],
                                    scalar1=wvt[:, 2 * NHM + h:2 * NHM + h + 1],
                                    scalar2=None, op0=OP.mult, op1=OP.bypass)
            st.stage(vector=v_xw)

            for q4 in range(4):
                def t_hd(t, q4=q4):
                    t.matmul(ps_s[q4][:, 0:512], gbf[:, MCH:2 * MCH],
                             xw[:, q4 * 512:(q4 + 1) * 512], start=True, stop=True)
                st.stage(tensor=t_hd)

            def v_hu(v):
                for h in range(NHM):
                    v.tensor_scalar(out=hnw[:, h * 64:(h + 1) * 64],
                                    in0=hst[:, h * 64:(h + 1) * 64],
                                    scalar1=dec[:, h:h + 1], scalar2=None,
                                    op0=OP.mult, op1=OP.bypass)
                v.drain()
                for q4 in range(4):
                    v.tensor_tensor(out=hnw[:, q4 * 512:(q4 + 1) * 512],
                                    in0=hnw[:, q4 * 512:(q4 + 1) * 512],
                                    in1=ps_s[q4][:, 0:512], op=OP.add)
                v.drain()
                for q4 in range(4):
                    v.tensor_copy(out=hst[:, q4 * 512:(q4 + 1) * 512],
                                  in_=hnw[:, q4 * 512:(q4 + 1) * 512])
            st.stage(vector=v_hu)

        def g_hl(e, b=b):
            return dma(e, p, "gpsimd", h_loc[b], hst[:, :])

        def st_y(e, b=b):
            return dma(e, p, "sync", yz_dram[b, 0], ysb[:, 0:16 * TPC])
        st.stage(sync=st_y, gpsimd=g_hl)

    def st_dk(e):
        return dma(e, p, "sync", dk_loc, dkb[0:32, :])
    st.stage(sync=st_dk)

    def g_cc(g):
        cs = p.sem("cc")
        g.collective_compute("AllGather", OP.bypass,
                             replica_groups=[list(range(NCORES))],
                             ins=[h_loc[:, :, :].opt()],
                             outs=[h_all[:, :, :, :].opt()]).then_inc(cs[0], 1)
        cs[1] += 1
        g.collective_compute("AllGather", OP.bypass,
                             replica_groups=[list(range(NCORES))],
                             ins=[dk_loc[:, :].opt()],
                             outs=[dk_all[:, :, :].opt()]).then_inc(cs[0], 1)
        cs[1] += 1
        g.wait_ge(cs[0], cs[1])
    st.stage(gpsimd=g_cc)

    def ld_dk(e):
        return dma(e, p, "sync", drow[0:1, :],
                   dk_all.rearrange("c h b -> (c h b)"))
    st.stage(sync=ld_dk)

    for b in range(B):
        def ld_yz(e, b=b):
            n = dma(e, p, "sync", ysb[:, 0:16 * TPC], yz_dram[b, 0])
            n += dma(e, p, "sync", zs3[:, :], yz_dram[b, 1])
            n += dma(e, p, "sync",
                     sbx[:, :].rearrange("q (c t) -> q c t", c=DCH),
                     us_d[b].rearrange("q (c t) -> q c t", t=TP3)[:, :, 3:TP3])
            return n
        st.stage(sync=ld_yz)

        def v_hi0(v):
            v.memset(hst[:, :], 0.0)
            v.memset(hnw[:, :], 0.0)
        st.stage(vector=v_hi0)

        for j in range(NCORES - 1):
            def ld_hj(e, j=j, b=b):
                return dma(e, p, "sync", hjb[:, :], h_all[j, b])
            st.stage(sync=ld_hj)

            def t_db(t, j=j, b=b):
                t.matmul(ps_s[2][:, 0:NHM], onesrow[0:1, 0:128],
                         drow[0:1, :].rearrange("o (c h b) -> o c h b",
                                                h=NHM, b=B)[:, j, :, b],
                         start=True, stop=True)
            st.stage(tensor=t_db)

            def v_hi(v, j=j):
                v.tensor_copy(out=dec[:, :], in_=ps_s[2][:, 0:NHM])
                v.drain()
                for h in range(NHM):
                    v.tensor_scalar(out=hst[:, h * 64:(h + 1) * 64],
                                    in0=hst[:, h * 64:(h + 1) * 64],
                                    scalar1=dec[:, h:h + 1], scalar2=None,
                                    op0=OP.mult, op1=OP.bypass)
                v.drain()
                v.tensor_tensor(out=hst[:, :], in0=hst[:, :], in1=hjb[:, :],
                                op=OP.add)
                v.drain()
                v.scalar_tensor_tensor(out=hnw[:, :], in0=hst[:, :],
                                       scalar=ohc[:, j:j + 1], in1=hnw[:, :],
                                       op0=OP.mult, op1=OP.add)
                v.drain()
            st.stage(vector=v_hi)

        for c_ in range(NMC):
            def ld_r(e, c_=c_, b=b):
                return dma(e, p, "sync", rowA[0:1, 0:NHM * MCH],
                           acg_dram[b, c_].rearrange("h t -> (h t)"))
            st.stage(sync=ld_r)

            def s_er2(s):
                s.activation(out=rowE[0:1, :], in_=rowA[0:1, 0:NHM * MCH],
                             func=AF.Exp)
            st.stage(scalar=s_er2)

            for g in range(4):
                hs = g * 8

                def t_gv2(t, hs=hs):
                    t.matmul(ps_s[2][:, 0:512], onesbf[0:1, 0:128],
                             rowE[0:1, hs * MCH:hs * MCH + 512],
                             start=True, stop=True)
                    t.matmul(ps_g[:, 0:512], onesbf[0:1, 0:128],
                             rowE[0:1, (hs + 4) * MCH:(hs + 4) * MCH + 512],
                             start=True, stop=True)
                st.stage(tensor=t_gv2)

                def v_gv2(v, c_=c_, b=b):
                    for j in range(8):
                        pm = ps_s[2] if j < 4 else ps_g
                        v.tensor_tensor(out=dts[:, j * MCH:(j + 1) * MCH],
                                        in0=call[:, b * TPC + c_ * MCH:
                                                b * TPC + (c_ + 1) * MCH],
                                        in1=pm[:, (j % 4) * MCH:
                                               (j % 4 + 1) * MCH], op=OP.mult)
                st.stage(vector=v_gv2)

                def t_yc(t, hs=hs):
                    for j in range(8):
                        h = hs + j
                        t.matmul(ps_x[(h % 2) * 64:(h % 2) * 64 + 64,
                                      ((h // 2) % 4) * MCH:
                                      ((h // 2) % 4 + 1) * MCH],
                                 hnw[:, h * 64:(h + 1) * 64],
                                 dts[:, j * MCH:(j + 1) * MCH],
                                 start=True, stop=True)
                st.stage(tensor=t_yc)

                def v_yc(v, hs=hs, c_=c_):
                    for jp in range(4):
                        hp2 = hs // 2 + jp
                        sl = hp2 * TPC + c_ * MCH
                        v.tensor_tensor(out=ysb[:, sl:sl + MCH],
                                        in0=ysb[:, sl:sl + MCH],
                                        in1=ps_x[:, jp * MCH:(jp + 1) * MCH],
                                        op=OP.add)
                st.stage(vector=v_yc)

        emit_layernorm(st, env, 0, rms=True, nchunks=16,
                       x_at=lambda ch: ysb[:, ch * TPC:(ch + 1) * TPC],
                       xn_at=lambda ch: gsb[:, ch * TPC:(ch + 1) * TPC])

        def v_nwm(v):
            for ch in range(16):
                v.tensor_scalar(out=gsb[:, ch * TPC:(ch + 1) * TPC],
                                in0=gsb[:, ch * TPC:(ch + 1) * TPC],
                                scalar1=nwc[:, ch:ch + 1], scalar2=None,
                                op0=OP.mult, op1=OP.bypass)
        st.stage(vector=v_nwm)

        def s_sz(s):
            for ch in range(16):
                s.activation(out=zs3[:, ch * TPC:(ch + 1) * TPC],
                             in_=zs3[:, ch * TPC:(ch + 1) * TPC], func=AF.Silu)
        st.stage(scalar=s_sz)

        def v_gt(v):
            for ch in range(16):
                v.tensor_tensor(out=gsb[:, ch * TPC:(ch + 1) * TPC],
                                in0=gsb[:, ch * TPC:(ch + 1) * TPC],
                                in1=zs3[:, ch * TPC:(ch + 1) * TPC], op=OP.mult)
        st.stage(vector=v_gt)

        for oc in range(DCH):
            def ld_ob(e, oc=oc):
                return dma(e, p, "sync", opb[:, :], op_dram[oc])
            st.stage(sync=ld_ob)

            def t_o(t, oc=oc):
                for ic in range(16):
                    t.matmul(ps_x[:, 0:TPC], opb[:, ic * 128:(ic + 1) * 128],
                             gsb[:, ic * TPC:(ic + 1) * TPC],
                             start=(ic == 0), stop=(ic == 15))
            st.stage(tensor=t_o)

            def v_o(v, oc=oc):
                sl = sbx[:, oc * TPC:(oc + 1) * TPC]
                v.tensor_tensor(out=sl, in0=sl, in1=ps_x[:, 0:TPC], op=OP.add)
            st.stage(vector=v_o)

        def st_b(e, b=b):
            return dma(e, p, "sync",
                       xo_d[b], sbx[:, :])
        st.stage(sync=st_b)




def build_l0attn():
    p = Prog("l0attn")
    p.pf = "a0_"
    us_d = p.inp("us", (B, 128, DCH * (TPC + 3)))
    xo_d = p.outp("xo", (B, 128, DCH * TPC))
    p.start_block()
    st = Stager(p)
    emit_l0attn(p, st, us_d, xo_d)
    st.stage(sync=lambda e: None)
    return p

def maps_l0attn(I):
    cw = np.asarray(I["L0_attn_conv_w"], np.float32)[:, 0, :]
    cvw = np.zeros((128, CCH * 4), np.float32)
    for cc in range(CCH):
        for j in range(4):
            cvw[:, cc * 4 + j] = cw[cc * 128:(cc + 1) * 128, j]
    dtb = np.zeros((128, 1), np.float32)
    dtb[0:32, 0] = np.asarray(I["L0_attn_dt_bias"], np.float32)
    Dv = np.asarray(I["L0_attn_D"], np.float32)
    dcol = np.zeros((128, 16), np.float32)
    for m in range(16):
        dcol[0:64, m] = Dv[2 * m]
        dcol[64:128, m] = Dv[2 * m + 1]
    base = {
        "ipT": _t(I["L0_attn_in_proj_w"]),
        "cvw": cvw,
        "cvb": _col(I["L0_attn_conv_b"], CCH),
        "dtb": dtb,
        "alog": np.asarray(I["L0_attn_A_log"], np.float32),
        "dcol": dcol,
        "nw": _col(I["L0_attn_norm_w"], 16),
        "opT": _t(I["L0_attn_out_proj_w"]),
    }
    in_maps = []
    for c in range(NCORES):
        m = dict(base)
        oh = np.zeros((128, NCORES), np.float32)
        if c > 0:
            oh[:, c - 1] = 1.0
        m["ohc"] = oh
        in_maps.append({"a0_" + k: v for k, v in m.items()})
    return in_maps


def run_l0attn(u_shards3, I):
    p = build_l0attn()
    in_maps = maps_l0attn(I)
    for c in range(NCORES):
        in_maps[c]["a0_us"] = _to_qmaj(u_shards3[c])
    res = run_bass_kernel_spmd(p.nc, in_maps, core_ids=list(range(NCORES)))
    out = [_from_qmaj(res.results[c]["a0_xo"]) for c in range(NCORES)]
    p.close()
    return out



def _t(w):
    return np.ascontiguousarray(np.asarray(w, np.float32).T)


def _col(v, nch=DCH):
    return np.ascontiguousarray(np.asarray(v, np.float32).reshape(nch, 128).T)


def maps_l0ffn(I):
    base = {
        "wT_wf": _t(I["L0_ffn_wf"]), "wT_uf": _t(I["L0_ffn_uf"]),
        "wT_wi": _t(I["L0_ffn_wi"]), "wT_ui": _t(I["L0_ffn_ui"]),
        "wT_wc": _t(I["L0_ffn_wc"]), "wT_uc": _t(I["L0_ffn_uc"]),
        "wT_wo": _t(I["L0_ffn_wo"]), "wT_uo": _t(I["L0_ffn_uo"]),
        "b_bf": _col(I["L0_ffn_bf"]),
        "b_bi": _col(I["L0_ffn_bi"]),
        "b_bc": _col(I["L0_ffn_bc"]),
        "b_bo": _col(I["L0_ffn_bo"]),
        "in_c": _col(I["L0_ffn_in_c"]),
    }
    return [{"f0_" + k: v for k, v in base.items()} for c in range(NCORES)]


def run_l0ffn(stream_shards, I):
    p = build_l0ffn()
    in_maps = maps_l0ffn(I)
    for c in range(NCORES):
        in_maps[c]["f0_xs"] = _to_qmaj(stream_shards[c])
    res = run_bass_kernel_spmd(p.nc, in_maps, core_ids=list(range(NCORES)))
    out = [_from_qmaj(res.results[c]["f0_xo"]) for c in range(NCORES)]
    p.close()
    return out


def _to_qmaj(a):
    Bq, Dq, T = a.shape
    return np.ascontiguousarray(
        a.reshape(Bq, DCH, 128, T).transpose(0, 2, 1, 3).reshape(Bq, 128, DCH * T))


def _from_qmaj(a):
    Bq, _, CT = a.shape
    T = CT // DCH
    return np.ascontiguousarray(
        a.reshape(Bq, 128, DCH, T).transpose(0, 2, 1, 3).reshape(Bq, DCH * 128, T))


def _shard_u3(x):
    fm = np.ascontiguousarray(x.transpose(0, 2, 1))
    out = []
    for c in range(NCORES):
        sl = np.zeros((B, D, TPC + 3), np.float32)
        lo = c * TPC
        lo3 = max(lo - 3, 0)
        pad = lo - lo3
        sl[:, :, 3 - pad:] = fm[:, :, lo3:lo + TPC]
        out.append(sl)
    return out


def build_fused():
    p = Prog("fused")
    p.pf = "a0_"
    us_d = p.inp("us", (B, 128, DCH * (TPC + 3)))
    p.pf = "out_"
    xo_d = p.outp("xo", (B, 128, DCH * TPC))
    strm = [p.dram("strm%d" % k, (B, 128, DCH * TPC)) for k in range(7)]
    p.pf = ""
    p.start_block()
    st = Stager(p)
    blocks = [
        (emit_l0attn, (us_d, strm[0])),
        (emit_l0ffn, (strm[0], strm[1])),
        (emit_attn, (1, strm[1], strm[2])),
        (emit_l1ffn, (strm[2], strm[3])),
        (emit_attn, (2, strm[3], strm[4])),
        (emit_l2ffn, (strm[4], strm[5])),
        (emit_l3attn, (strm[5], strm[6])),
        (emit_l3ffn, (strm[6], xo_d)),
    ]
    for fn, args in blocks:
        p.push_region()
        fn(p, st, *args)
        p.pop_region()
    st.stage(sync=lambda e: None)
    return p


def run_fused(I):
    p = build_fused()
    x = np.ascontiguousarray(np.asarray(I["L0_attn_u"], np.float32))
    us = _shard_u3(x)
    allmaps = [maps_l0attn(I), maps_l0ffn(I), maps_attn(1, I), maps_l1ffn(I),
               maps_attn(2, I), maps_l2ffn(I), maps_l3attn(I), maps_l3ffn(I)]
    in_maps = []
    for c in range(NCORES):
        m = {}
        for bm in allmaps:
            m.update(bm[c])
        m["a0_us"] = _to_qmaj(us[c])
        in_maps.append(m)
    res = run_bass_kernel_spmd(p.nc, in_maps, core_ids=list(range(NCORES)))
    out = [_from_qmaj(res.results[c]["out_xo"]) for c in range(NCORES)]
    p.close()
    fm = np.concatenate(out, axis=2)
    return np.ascontiguousarray(fm.transpose(0, 2, 1)).astype(np.float32)


def kernel(**inputs):
    return run_fused(inputs)
